# version 09
# baseline (speedup 1.0000x reference)
import math
import numpy as np
import ml_dtypes

import concourse.bass as bass
import concourse.mybir as mybir
from concourse.bass_utils import run_bass_kernel_spmd

F32 = mybir.dt.float32
BF16 = mybir.dt.bfloat16
AF = mybir.ActivationFunctionType
ALU = mybir.AluOpType
AX = mybir.AxisListType

S, E, H = 4096, 2048, 16
TQ = 1024
KCH = 16
ICH = 44
LAMBDA_INIT = 0.8 - 0.6 * math.exp(-0.3 * 12)
ONE_MINUS_LI = 1.0 - LAMBDA_INIT
EPS = 1e-5
NPBF = ml_dtypes.bfloat16
SWAP_MASK = [i ^ 1 for i in range(32)]


class Prog:
    def __init__(self, nc, ent):
        self.nc = nc
        self.ent = ent
        self.ops = {e: [] for e in ("pe", "act", "dve", "pool", "sync")}
        self.sems = {}
        self.ct = {}
        for s in ("pe", "ac", "dv", "pl", "lds", "ldx0", "ldx1", "ldw0", "ldw1",
                  "ldk0", "ldk1", "st0", "st1", "sta"):
            self.sems[s] = ent(nc.semaphore(s))
            self.ct[s] = 0

    def emit(self, eng, fn):
        self.ops[eng].append(fn)

    def wait(self, eng, sem, val=None):
        v = self.ct[sem] if val is None else val
        if v <= 0:
            return
        s = self.sems[sem]
        self.emit(eng, lambda E, s=s, v=v: E.wait_ge(s, v))

    OWN = {"act": "ac", "dve": "dv", "pool": "pl"}

    def inc(self, eng, sem, n, mk):
        own = self.OWN.get(eng)
        if own is not None and self.ct[own] > 0:
            self.wait(eng, own)
        if n >= 16 and sem == "lds" and self.ct[sem] > 0:
            self.wait(eng, sem)
        s = self.sems[sem]
        self.ct[sem] += n
        v = self.ct[sem]
        self.emit(eng, lambda E, mk=mk, s=s, n=n: mk(E).then_inc(s, n))
        return v


class Ring:

    def __init__(self, p, name, nslots, queue):
        self.p, self.queue, self.n, self.name = p, queue, nslots, name
        for j in range(nslots):
            s = f"{name}{j}"
            if s not in p.sems:
                p.sems[s] = p.ent(p.nc.semaphore(s))
                p.ct[s] = 0
        self.i = 0
        self.done = []

    def load(self, mk_of_slot):
        i, n = self.i, self.n
        slot = i % n
        if i >= n:
            for sem, val in self.done[i - n].items():
                self.p.wait(self.queue, sem, val)
        v = self.p.inc(self.queue, f"{self.name}{slot}", 16, mk_of_slot(slot))
        self.done.append({})
        self.i += 1
        return i, slot, (f"{self.name}{slot}", v)

    def consumed(self, i, *sems):
        for s in sems:
            self.done[i][s] = self.p.ct[s]


def scope_barrier(p):
    for eng in ("pe", "act", "dve", "pool", "sync"):
        for sem in list(p.ct):
            if Prog.OWN.get(eng) == sem:
                continue
            p.wait(eng, sem)


def attach(p, block):
    @block.tensor
    def _(t):
        for fn in p.ops["pe"]:
            fn(t)

    @block.scalar
    def _(sc):
        for fn in p.ops["act"]:
            fn(sc)

    @block.vector
    def _(v):
        for fn in p.ops["dve"]:
            fn(v)

    @block.gpsimd
    def _(g):
        for fn in p.ops["pool"]:
            fn(g)

    @block.sync
    def _(s):
        for fn in p.ops["sync"]:
            fn(s)


def alloc_common(nc, ent, t):
    t["ps_a"] = ent(nc.psum_tensor("ps_a", [128, 2048], F32))
    t["ps_b"] = ent(nc.psum_tensor("ps_b", [128, 1024], F32))
    t["ps_d1"] = ent(nc.psum_tensor("ps_d1", [1, 512], F32))
    t["ps_d2"] = ent(nc.psum_tensor("ps_d2", [1, 512], F32))
    t["ones_col_bf"] = ent(nc.sbuf_tensor("ones_col_bf", [128, 1], BF16))
    t["ones_row_f"] = ent(nc.sbuf_tensor("ones_row_f", [1, 128], F32))
    t["neg_row_f"] = ent(nc.sbuf_tensor("neg_row_f", [1, 128], F32))
    t["ones128_bf"] = ent(nc.sbuf_tensor("ones128_bf", [128, 128], BF16))
    t["mask1_bf"] = ent(nc.sbuf_tensor("mask1_bf", [128, 1], BF16))
    t["mask2_bf"] = ent(nc.sbuf_tensor("mask2_bf", [128, 1], BF16))
    t["xc"] = ent(nc.sbuf_tensor("xc", [128, 4096], F32))
    t["xbb"] = ent(nc.sbuf_tensor("xbb", [128, 4096], BF16))
    t["x2b"] = ent(nc.sbuf_tensor("x2b", [128, 4096], BF16))
    t["row"] = ent(nc.sbuf_tensor("row", [1, 4096], F32))
    t["tmp0"] = ent(nc.sbuf_tensor("tmp0", [128, 1024], F32))
    t["tmp1"] = ent(nc.sbuf_tensor("tmp1", [128, 1024], F32))
    t["sq_b"] = ent(nc.sbuf_tensor("sq_b", [128, 1024], BF16))
    t["srow"] = ent(nc.sbuf_tensor("srow", [1, 64], F32))
    t["eps_col"] = ent(nc.sbuf_tensor("eps_col", [128, 1], F32))
    t["nrow"] = ent(nc.sbuf_tensor("nrow", [1, 1024], F32))
    t["trow"] = ent(nc.sbuf_tensor("trow", [1, 16], F32))


def emit_init(p, t):
    for name, sl, val in (("ones_col_bf", None, 1.0), ("ones_row_f", None, 1.0),
                          ("neg_row_f", None, -1.0), ("ones128_bf", None, 1.0),
                          ("mask1_bf", (0, 64), 1.0), ("mask1_bf", (64, 128), 0.0),
                          ("mask2_bf", (0, 64), 0.0), ("mask2_bf", (64, 128), 1.0),
                          ("srow", None, 0.0), ("eps_col", None, EPS)):
        def mk(E, name=name, sl=sl, val=val):
            ap = t[name][:] if sl is None else t[name][sl[0]:sl[1], :]
            return E.memset(ap, val)
        p.inc("pool", "pl", 1, mk)


def emit_ln(p, t, x_hbm, xn_bf, xn_seg, xn_col0, ring=None):
    xc, xbb, x2b, row = t["xc"], t["xbb"], t["x2b"], t["row"]
    ps_b, ps_d1, ps_d2 = t["ps_b"], t["ps_d1"], t["ps_d2"]
    if ring is None:
        ring = Ring(p, "rx", 2, "sync")
    p.wait("pe", "pl")

    ringb = t.setdefault("_lnringb", Ring(p, "rxb", 2, "pool"))

    def qload(q, tile):
        return ring.load(lambda s: (lambda E, q=q, tile=tile, s=s: E.dma_start(
            out=xc[:, s * 2048:(s + 1) * 2048].rearrange("p (c t) -> p c t", c=4),
            in_=x_hbm[q * 512:(q + 1) * 512,
                      tile * 512:(tile + 1) * 512].rearrange("(c p) t -> p c t", p=128))))

    def qload_bf(q, tile):
        return ringb.load(lambda s: (lambda E, q=q, tile=tile, s=s: E.dma_start(
            out=xbb[:, s * 2048:(s + 1) * 2048].rearrange("p (c t) -> p c t", c=4),
            in_=x_hbm[q * 512:(q + 1) * 512,
                      tile * 512:(tile + 1) * 512].rearrange("(c p) t -> p c t", p=128))))

    for tile in range(2):
        for q in range(4):
            i, sl, key = qload_bf(q, tile)
            p.wait("act", *key)
            for j in range(4):
                kch = q * 4 + j
                co = sl * 2048 + j * 512
                p.inc("act", "ac", 1, lambda E, co=co: E.activation(
                    x2b[:, co:co + 512], xbb[:, co:co + 512], AF.Square))
                p.wait("pe", *key)
                p.wait("pe", "ac")
                p.emit("pe", lambda E, co=co, kch=kch: E.matmul(
                    ps_d1[:], t["ones_col_bf"][:], xbb[:, co:co + 512],
                    start=(kch == 0), stop=(kch == KCH - 1)))
                mmv = p.inc("pe", "pe", 1, lambda E, co=co, kch=kch: E.matmul(
                    ps_d2[:], t["ones_col_bf"][:], x2b[:, co:co + 512],
                    start=(kch == 0), stop=(kch == KCH - 1)))
            ringb.consumed(i, "ac", "pe")
        p.wait("act", "pe", mmv)
        p.inc("act", "ac", 1, lambda E: E.activation(row[0:1, 0:512], ps_d1[:], AF.Copy, scale=1.0 / 2048.0))
        p.inc("act", "ac", 1, lambda E: E.activation(row[0:1, 512:1024], ps_d2[:], AF.Copy, scale=1.0 / 2048.0))
        p.inc("act", "ac", 1, lambda E: E.activation(row[0:1, 1024:1536], row[0:1, 0:512], AF.Square))
        p.wait("dve", "ac")
        p.inc("dve", "dv", 1, lambda E: E.tensor_tensor(
            row[0:1, 512:1024], row[0:1, 512:1024], row[0:1, 1024:1536], ALU.subtract))
        p.wait("act", "dv")
        p.inc("act", "ac", 1, lambda E: E.activation(row[0:1, 1536:2048], row[0:1, 512:1024], AF.Ln, bias=t["eps_col"][0:1, :]))
        p.inc("act", "ac", 1, lambda E: E.activation(row[0:1, 1536:2048], row[0:1, 1536:2048], AF.Exp, scale=-0.5))
        p.wait("pe", "ac")
        p.wait("pe", "dv")
        p.emit("pe", lambda E: E.matmul(ps_b[:, 0:512], t["ones_row_f"][:], row[0:1, 0:512],
                                        start=True, stop=True))
        bcv = p.inc("pe", "pe", 1, lambda E: E.matmul(
            ps_b[:, 512:1024], t["ones_row_f"][:], row[0:1, 1536:2048], start=True, stop=True))
        for q in range(4):
            i, sl, key = qload(q, tile)
            p.wait("dve", *key)
            p.wait("dve", "pe", bcv)
            for j in range(4):
                kch = q * 4 + j
                co = sl * 2048 + j * 512
                c0 = xn_col0 + tile * 512
                p.inc("dve", "dv", 1, lambda E, co=co: E.tensor_tensor(
                    t["tmp0"][:, 0:512], xc[:, co:co + 512], ps_b[:, 0:512], ALU.subtract))
                p.inc("dve", "dv", 1, lambda E, kch=kch, c0=c0: E.tensor_tensor(
                    xn_bf[:, kch * xn_seg + c0: kch * xn_seg + c0 + 512],
                    t["tmp0"][:, 0:512], ps_b[:, 512:1024], ALU.mult))
            ring.consumed(i, "dv")
    return {"dv": p.ct["dv"]}


def emit_rope(p, t, ps_view, ctab, stab, ccol0, dst_bf, dst_col0):
    qr, sw, ta = t["qraw_bf"], t["tswp_bf"], t["tmpa_bf"]
    p.wait("act", "pe")
    psfree = p.inc("act", "ac", 1, lambda E: E.activation(qr[:], ps_view, AF.Copy))
    p.wait("dve", "ac", psfree)
    p.inc("dve", "dv", 1, lambda E: E.stream_shuffle(sw[:], qr[:], SWAP_MASK))
    p.inc("dve", "dv", 1, lambda E: E.tensor_tensor(
        ta[:], qr[:], ctab[:, ccol0:ccol0 + 1024], ALU.mult))
    p.inc("dve", "dv", 1, lambda E: E.tensor_tensor(
        sw[:], sw[:], stab[:, ccol0:ccol0 + 1024], ALU.mult))
    wv = p.inc("dve", "dv", 1, lambda E: E.tensor_tensor(
        dst_bf[:, dst_col0:dst_col0 + 1024], ta[:], sw[:], ALU.add))
    return ("ac", psfree), wv


def emit_rope_sb(p, t, src_view, ctab, stab, ccol0, dst_bf, dst_col0):
    sw, ta = t["tswp_bf"], t["tmpa_bf"]
    p.inc("dve", "dv", 1, lambda E: E.stream_shuffle(sw[:], src_view, SWAP_MASK))
    srcfree = p.inc("dve", "dv", 1, lambda E: E.tensor_tensor(
        ta[:], src_view, ctab[:, ccol0:ccol0 + 1024], ALU.mult))
    p.inc("dve", "dv", 1, lambda E: E.tensor_tensor(
        sw[:], sw[:], stab[:, ccol0:ccol0 + 1024], ALU.mult))
    wv = p.inc("dve", "dv", 1, lambda E: E.tensor_tensor(
        dst_bf[:, dst_col0:dst_col0 + 1024], ta[:], sw[:], ALU.add))
    return srcfree, wv


def emit_headnorm(p, t, src_bf, col0, och, statbase=0):
    sq_b, ps_d1, ps_d2 = t["sq_b"], t["ps_d1"], t["ps_d2"]
    p.wait("act", "dv")
    p.wait("act", "pe")
    p.inc("act", "ac", 1, lambda E: E.activation(
        sq_b[:], src_bf[:, col0:col0 + 1024], AF.Square))
    p.wait("pe", "ac")
    for half, mask in ((0, "mask1_bf"), (1, "mask2_bf")):
        p.wait("pe", "ac", t.get("hn_ac", 0))
        p.emit("pe", lambda E, mask=mask: E.matmul(
            ps_d1[:], t[mask][:], sq_b[:, 0:512], start=True, stop=True))
        mv = p.inc("pe", "pe", 1, lambda E, mask=mask: E.matmul(
            ps_d2[:], t[mask][:], sq_b[:, 512:1024], start=True, stop=True))
        p.wait("act", "pe", mv)
        p.wait("act", "dv", t.get("hn_rv", 0))
        p.inc("act", "ac", 1, lambda E: E.activation(t["nrow"][0:1, 0:512], ps_d1[:], AF.Copy))
        nv = p.inc("act", "ac", 1, lambda E: E.activation(t["nrow"][0:1, 512:1024], ps_d2[:], AF.Copy))
        t["hn_ac"] = nv
        p.wait("dve", "ac", nv)
        p.wait("dve", "pl")
        rv_ = p.inc("dve", "dv", 1, lambda E: E.tensor_reduce(
            t["trow"][0:1, 2:3], t["nrow"][0:1, 0:1024], AX.X, ALU.max))
        t["hn_rv"] = rv_
        col = statbase + och + 16 * half
        p.inc("dve", "dv", 1, lambda E, col=col: E.tensor_tensor(
            t["srow"][0:1, col:col + 1], t["srow"][0:1, col:col + 1],
            t["trow"][0:1, 2:3], ALU.max))
    return p.ct["pe"]


def emit_fmproj(p, t, w_hbm, wbuf, xn_bf, xn_seg, xn_col0, consume, psfree0):
    ps_a = t["ps_a"]
    free = [dict(psfree0), dict(psfree0)]
    ldv = []
    use_done = []
    for och in range(16):
        par = och % 2
        if och >= 2:
            for sem, val in use_done[och - 2].items():
                p.wait("pool", sem, val)
        v = p.inc("pool", f"ldw{par}", 16, lambda E, och=och, par=par: E.dma_start(
            out=wbuf[:, par * 2048:(par + 1) * 2048], in_=w_hbm[och]))
        ldv.append(v)
        for sem, val in free[par].items():
            p.wait("pe", sem, val)
        p.wait("pe", f"ldw{par}", v)
        for kch in range(KCH):
            for half in range(2):
                mm = lambda E, och=och, kch=kch, half=half, par=par: E.matmul(
                    ps_a[:, par * 1024 + half * 512: par * 1024 + (half + 1) * 512],
                    wbuf[:, par * 2048 + kch * 128: par * 2048 + (kch + 1) * 128],
                    xn_bf[:, kch * xn_seg + xn_col0 + half * 512:
                          kch * xn_seg + xn_col0 + (half + 1) * 512],
                    start=(kch == 0), stop=(kch == KCH - 1))
                if kch == KCH - 1 and half == 1:
                    mmv = p.inc("pe", "pe", 1, mm)
                else:
                    p.emit("pe", mm)
        use_done.append({"pe": p.ct["pe"]})
        p.wait("dve", "pe", mmv)
        free[par] = consume(och, mmv)
    return free[0], free[1]


def emit_vproj(p, t, wv_hbm, wvbuf, xn_bf, xn_seg, xn_col0, v_stage, v_out, row0, psfree0):
    ps_a = t["ps_a"]

    def vps(j):
        return ps_a[:, j * 512:(j + 1) * 512]

    free = dict(psfree0)
    use_done = []
    gi = 0
    stv = []
    for halfq in range(2):
        for dvq in range(4):
            par = gi % 2
            if gi >= 2:
                for sem, val in use_done[gi - 2].items():
                    p.wait("pool", sem, val)
            v = p.inc("pool", f"ldw{par}", 16, lambda E, dvq=dvq, par=par: E.dma_start(
                out=wvbuf[:, par * 8192:(par + 1) * 8192], in_=wv_hbm[dvq]))
            for sem, val in free.items():
                p.wait("pe", sem, val)
            p.wait("pe", f"ldw{par}", v)
            for kch in range(KCH):
                for tc in range(4):
                    tok = xn_col0 + halfq * 512 + tc * 128
                    mm = lambda E, kch=kch, tc=tc, tok=tok, par=par: E.matmul(
                        vps(tc),
                        xn_bf[:, kch * xn_seg + tok: kch * xn_seg + tok + 128],
                        wvbuf[:, par * 8192 + kch * 512: par * 8192 + (kch + 1) * 512],
                        start=(kch == 0), stop=(kch == KCH - 1))
                    if kch == KCH - 1 and tc == 3:
                        p.inc("pe", "pe", 1, mm)
                    else:
                        p.emit("pe", mm)
            use_done.append({"pe": p.ct["pe"]})
            p.wait("dve", "pe")
            for tc in range(4):
                p.inc("dve", "dv", 1, lambda E, tc=tc, dvq=dvq: E.tensor_copy(
                    v_stage[:, tc, dvq * 512:(dvq + 1) * 512], vps(tc)))
            free = {"dv": p.ct["dv"]}
            gi += 1
        p.wait("sync", "dv")
        for tc in range(4):
            r0 = row0 + halfq * 512 + tc * 128
            stv.append(p.inc("sync", "sta", 16, lambda E, tc=tc, r0=r0: E.dma_start(
                out=v_out[r0:r0 + 128, :], in_=v_stage[:, tc, :])))
        if halfq == 0:
            p.wait("dve", "sta", stv[-1])
    return free


def build_A():
    nc = bass.Bass("TRN2", target_bir_lowering=False)
    xT = nc.dram_tensor("xT", [E, TQ], F32, kind="ExternalInput").ap()
    wk = nc.dram_tensor("wk", [16, 128, 2048], F32, kind="ExternalInput").ap()
    wv = nc.dram_tensor("wv", [4, 128, 8192], F32, kind="ExternalInput").ap()
    ct = nc.dram_tensor("ct", [128, TQ], BF16, kind="ExternalInput").ap()
    stb = nc.dram_tensor("stb", [128, TQ], BF16, kind="ExternalInput").ap()
    kT_out = nc.dram_tensor("kT_out", [E, TQ], BF16, kind="ExternalOutput").ap()
    v_out = nc.dram_tensor("v_out", [TQ, E], BF16, kind="ExternalOutput").ap()
    kstat_out = nc.dram_tensor("kstat_out", [1, 64], F32, kind="ExternalOutput").ap()

    from contextlib import ExitStack
    ctx = ExitStack()
    ent = ctx.enter_context
    t = {}
    alloc_common(nc, ent, t)
    t["xn_bf"] = ent(nc.sbuf_tensor("xn_bf", [128, 16 * 1024], BF16))
    t["wbuf"] = ent(nc.sbuf_tensor("wbuf", [128, 4096], BF16))
    t["wvbuf"] = ent(nc.sbuf_tensor("wvbuf", [128, 16384], BF16))
    t["v_stage"] = ent(nc.sbuf_tensor("v_stage", [128, 4, 2048], BF16))
    t["kstage"] = ent(nc.sbuf_tensor("kstage", [128, 2048], BF16))
    t["ctab"] = ent(nc.sbuf_tensor("ctab", [128, TQ], BF16))
    t["qraw_bf"] = ent(nc.sbuf_tensor("qraw_bf", [128, 1024], BF16))
    t["tswp_bf"] = ent(nc.sbuf_tensor("tswp_bf", [128, 1024], BF16))
    t["tmpa_bf"] = ent(nc.sbuf_tensor("tmpa_bf", [128, 1024], BF16))
    t["stab"] = ent(nc.sbuf_tensor("stab", [128, TQ], BF16))

    p = Prog(nc, ent)
    emit_init(p, t)
    p.inc("sync", "lds", 16, lambda E: E.dma_start(out=t["ctab"][:], in_=ct))
    tabv = p.inc("sync", "lds", 16, lambda E: E.dma_start(out=t["stab"][:], in_=stb))
    psfree = emit_ln(p, t, xT, t["xn_bf"], 1024, 0)
    p.wait("dve", "lds", tabv)
    p.wait("pool", "lds", tabv)

    kst_store = []
    ksq_done = []

    def consume_k(och, mmv):
        half = och % 2
        if och >= 2:
            p.wait("dve", f"st{half}", kst_store[och - 2])
            p.wait("dve", "ac", ksq_done[och - 2])
        fr, wv_ = emit_rope(p, t, t["ps_a"][:, half * 1024:(half + 1) * 1024],
                            t["ctab"], t["stab"], 0, t["kstage"], half * 1024)
        emit_headnorm(p, t, t["kstage"], half * 1024, och)
        ksq_done.append(p.ct["ac"])
        p.wait("sync", "dv", wv_)
        kst_store.append(p.inc("sync", f"st{half}", 16, lambda E, och=och, half=half: E.dma_start(
            out=kT_out[och * 128:(och + 1) * 128, :],
            in_=t["kstage"][:, half * 1024:(half + 1) * 1024])))
        return {fr[0]: fr[1]}

    f0, f1 = emit_fmproj(p, t, wk, t["wbuf"], t["xn_bf"], 1024, 0, consume_k, psfree)
    both = {k: max(f0.get(k, 0), f1.get(k, 0)) for k in set(f0) | set(f1)}
    emit_vproj(p, t, wv, t["wvbuf"], t["xn_bf"], 1024, 0, t["v_stage"], v_out, 0, both)
    p.wait("sync", "dv")
    p.inc("sync", "sta", 16, lambda E: E.dma_start(out=kstat_out, in_=t["srow"][:]))
    p.wait("sync", "st0")
    p.wait("sync", "st1")
    p.wait("sync", "sta")
    block = ent(nc.Block())
    attach(p, block)
    ctx.close()
    return nc


def build_B(debug=False):
    nc = bass.Bass("TRN2", target_bir_lowering=False)
    xT = nc.dram_tensor("xT", [E, TQ], F32, kind="ExternalInput").ap()
    kT = nc.dram_tensor("kT", [E, S], BF16, kind="ExternalInput").ap()
    vf = nc.dram_tensor("vf", [S, E], BF16, kind="ExternalInput").ap()
    ks = nc.dram_tensor("ks", [4, 64], F32, kind="ExternalInput").ap()
    wq = nc.dram_tensor("wq", [16, 128, 2048], F32, kind="ExternalInput").ap()
    wo = nc.dram_tensor("wo", [16, 128, 2048], F32, kind="ExternalInput").ap()
    gu = nc.dram_tensor("gu", [44, 128, 4096], F32, kind="ExternalInput").ap()
    dn = nc.dram_tensor("dn", [16, 128, 5632], F32, kind="ExternalInput").ap()
    gs = nc.dram_tensor("gs", [128, 44], F32, kind="ExternalInput").ap()
    us = nc.dram_tensor("us", [128, 44], F32, kind="ExternalInput").ap()
    ds = nc.dram_tensor("ds", [128, 16], F32, kind="ExternalInput").ap()
    gam = nc.dram_tensor("gam", [128, 1], F32, kind="ExternalInput").ap()
    lq1 = nc.dram_tensor("lq1", [1, 64], F32, kind="ExternalInput").ap()
    lk1 = nc.dram_tensor("lk1", [1, 64], F32, kind="ExternalInput").ap()
    lq2 = nc.dram_tensor("lq2", [1, 64], F32, kind="ExternalInput").ap()
    lk2 = nc.dram_tensor("lk2", [1, 64], F32, kind="ExternalInput").ap()
    ct = nc.dram_tensor("ct", [128, TQ], BF16, kind="ExternalInput").ap()
    stb = nc.dram_tensor("stb", [128, TQ], BF16, kind="ExternalInput").ap()
    outT = nc.dram_tensor("outT", [E, TQ], F32, kind="ExternalOutput").ap()
    dbg = {}
    if debug:
        for nm, shp, dt in (("d_xn", [E, TQ], BF16), ("d_q", [E, TQ], BF16),
                            ("d_attn", [E, TQ], BF16), ("d_acc", [E, TQ], F32),
                            ("d_bias", [128, 16], F32), ("d_srow", [1, 64], F32)):
            dbg[nm] = nc.dram_tensor(nm, shp, dt, kind="ExternalOutput").ap()

    from contextlib import ExitStack
    ctx = ExitStack()
    ent = ctx.enter_context
    t = {}
    t["ps_a"] = ent(nc.psum_tensor("ps_a", [128, 2048], F32))
    t["ps_b"] = ent(nc.psum_tensor("ps_b", [128, 1024], F32))
    t["ps_d1"] = ent(nc.psum_tensor("ps_d1", [1, 512], F32))
    t["ps_d2"] = ent(nc.psum_tensor("ps_d2", [1, 512], F32))
    t["ones_col_bf"] = ent(nc.sbuf_tensor("ones_col_bf", [128, 1], BF16))
    t["ones_row_f"] = ent(nc.sbuf_tensor("ones_row_f", [1, 128], F32))
    t["neg_row_f"] = ent(nc.sbuf_tensor("neg_row_f", [1, 128], F32))
    t["ones128_bf"] = ent(nc.sbuf_tensor("ones128_bf", [128, 128], BF16))
    t["mask1_bf"] = ent(nc.sbuf_tensor("mask1_bf", [128, 1], BF16))
    t["mask2_bf"] = ent(nc.sbuf_tensor("mask2_bf", [128, 1], BF16))
    t["tmp0"] = ent(nc.sbuf_tensor("tmp0", [128, 1024], F32))
    t["tmp1"] = ent(nc.sbuf_tensor("tmp1", [128, 1024], F32))
    t["srow"] = ent(nc.sbuf_tensor("srow", [1, 64], F32))
    t["trow"] = ent(nc.sbuf_tensor("trow", [1, 16], F32))
    t["eps_col"] = ent(nc.sbuf_tensor("eps_col", [128, 1], F32))
    t["nrow"] = ent(nc.sbuf_tensor("nrow", [1, 1024], F32))
    t["bias_sb"] = ent(nc.sbuf_tensor("bias_sb", [128, 16], F32))
    t["lam_vec"] = ent(nc.sbuf_tensor("lam_vec", [128, 1], F32))
    t["gam_eff"] = ent(nc.sbuf_tensor("gam_eff", [128, 1], F32))
    t["gs_sb"] = ent(nc.sbuf_tensor("gs_sb", [128, 44], F32))
    t["us_sb"] = ent(nc.sbuf_tensor("us_sb", [128, 44], F32))
    t["ds_sb"] = ent(nc.sbuf_tensor("ds_sb", [128, 16], F32))
    t["lrow"] = ent(nc.sbuf_tensor("lrow", [1, 256], F32))
    t["krow"] = ent(nc.sbuf_tensor("krow", [1, 256], F32))
    t["xn_bf"] = ent(nc.sbuf_tensor("xn_bf", [128, 16 * 1024], BF16))
    t["q_bf"] = ent(nc.sbuf_tensor("q_bf", [128, 16 * 1024], BF16))
    t["acc"] = ent(nc.sbuf_tensor("acc", [128, 16 * 1024], F32))

    p = Prog(nc, ent)
    emit_init(p, t)
    p.wait("pe", "pl")
    for name, src_ap in (("gs_sb", gs), ("us_sb", us), ("ds_sb", ds)):
        p.inc("sync", "lds", 16, lambda E, src_ap=src_ap, name=name: E.dma_start(
            out=t[name][:], in_=src_ap))
    p.inc("sync", "lds", 16, lambda E: E.dma_start(out=t["lrow"][0:1, 0:64], in_=lq1))
    p.inc("sync", "lds", 16, lambda E: E.dma_start(out=t["lrow"][0:1, 64:128], in_=lk1))
    p.inc("sync", "lds", 16, lambda E: E.dma_start(out=t["lrow"][0:1, 128:192], in_=lq2))
    p.inc("sync", "lds", 16, lambda E: E.dma_start(out=t["lrow"][0:1, 192:256], in_=lk2))
    for r in range(4):
        p.inc("sync", "lds", 16, lambda E, r=r: E.dma_start(
            out=t["krow"][0:1, r * 64:(r + 1) * 64], in_=ks[r:r + 1, :]))
    gamv = p.inc("sync", "lds", 16, lambda E: E.dma_start(out=t["tmp0"][:, 0:1], in_=gam))
    p.wait("dve", "lds", gamv)
    p.wait("pool", "lds", gamv)
    p.wait("act", "lds", gamv)
    p.inc("dve", "dv", 1, lambda E: E.tensor_scalar(
        t["gam_eff"][:], t["tmp0"][:, 0:1], ONE_MINUS_LI, 0.0, ALU.mult, ALU.add))
    p.inc("dve", "dv", 1, lambda E: E.tensor_tensor(
        t["lrow"][0:1, 0:64], t["lrow"][0:1, 0:64], t["lrow"][0:1, 64:128], ALU.mult))
    p.inc("dve", "dv", 1, lambda E: E.tensor_tensor(
        t["lrow"][0:1, 128:192], t["lrow"][0:1, 128:192], t["lrow"][0:1, 192:256], ALU.mult))
    p.inc("dve", "dv", 1, lambda E: E.tensor_reduce(
        t["trow"][0:1, 0:1], t["lrow"][0:1, 0:64], AX.X, ALU.add))
    p.inc("dve", "dv", 1, lambda E: E.tensor_reduce(
        t["trow"][0:1, 1:2], t["lrow"][0:1, 128:192], AX.X, ALU.add))
    p.wait("act", "dv")
    p.inc("act", "ac", 1, lambda E: E.activation(t["trow"][0:1, 2:3], t["trow"][0:1, 0:1], AF.Exp))
    p.inc("act", "ac", 1, lambda E: E.activation(t["trow"][0:1, 3:4], t["trow"][0:1, 1:2], AF.Exp))
    p.wait("dve", "ac")
    p.inc("dve", "dv", 1, lambda E: E.tensor_tensor(
        t["trow"][0:1, 2:3], t["trow"][0:1, 2:3], t["trow"][0:1, 3:4], ALU.subtract))
    lamv = p.inc("dve", "dv", 1, lambda E: E.tensor_scalar(
        t["trow"][0:1, 2:3], t["trow"][0:1, 2:3], 1.0, LAMBDA_INIT, ALU.mult, ALU.add))
    p.wait("pe", "dv", lamv)
    lbv = p.inc("pe", "pe", 1, lambda E: E.matmul(
        t["ps_b"][:, 0:1], t["ones_row_f"][:], t["trow"][0:1, 2:3], start=True, stop=True))
    p.wait("dve", "pe", lbv)
    p.inc("dve", "dv", 1, lambda E: E.tensor_copy(t["lam_vec"][:], t["ps_b"][:, 0:1]))

    ctx1 = ExitStack()
    ent1 = ctx1.enter_context
    t["xc"] = ent1(nc.sbuf_tensor("xc", [128, 4096], F32, side="right"))
    t["xbb"] = ent1(nc.sbuf_tensor("xbb", [128, 4096], BF16, side="right"))
    t["x2b"] = ent1(nc.sbuf_tensor("x2b", [128, 4096], BF16, side="right"))
    t["row"] = ent1(nc.sbuf_tensor("row", [1, 2048], F32, side="right"))
    t["sq_b"] = ent1(nc.sbuf_tensor("sq_b", [128, 1024], BF16, side="right"))
    t["ctab"] = ent1(nc.sbuf_tensor("ctab", [128, TQ], BF16, side="right"))
    t["qraw_bf"] = ent1(nc.sbuf_tensor("qraw_bf", [128, 1024], BF16, side="right"))
    t["tswp_bf"] = ent1(nc.sbuf_tensor("tswp_bf", [128, 1024], BF16, side="right"))
    t["tmpa_bf"] = ent1(nc.sbuf_tensor("tmpa_bf", [128, 1024], BF16, side="right"))
    t["stab"] = ent1(nc.sbuf_tensor("stab", [128, TQ], BF16, side="right"))
    t["wbuf"] = ent1(nc.sbuf_tensor("wbuf", [128, 4096], BF16, side="right"))

    p.inc("sync", "lds", 16, lambda E: E.dma_start(out=t["ctab"][:], in_=ct))
    tabv = p.inc("sync", "lds", 16, lambda E: E.dma_start(out=t["stab"][:], in_=stb))
    psfree = emit_ln(p, t, xT, t["xn_bf"], 1024, 0)
    p.wait("dve", "lds", tabv)
    p.wait("pool", "lds", tabv)
    if debug:
        p.wait("sync", "dv")
        for kch in range(KCH):
            p.inc("sync", "sta", 16, lambda E, kch=kch: E.dma_start(
                out=dbg["d_xn"][kch * 128:(kch + 1) * 128, :],
                in_=t["xn_bf"][:, kch * 1024:(kch + 1) * 1024]))

    def consume_q(och, mmv):
        fr, wv_ = emit_rope(p, t, t["ps_a"][:, (och % 2) * 1024:(och % 2 + 1) * 1024],
                            t["ctab"], t["stab"], 0, t["q_bf"], och * 1024)
        emit_headnorm(p, t, t["q_bf"], och * 1024, och)
        return {fr[0]: fr[1]}

    emit_fmproj(p, t, wq, t["wbuf"], t["xn_bf"], 1024, 0, consume_q, psfree)
    if debug:
        p.wait("sync", "dv")
        for kch in range(KCH):
            p.inc("sync", "sta", 16, lambda E, kch=kch: E.dma_start(
                out=dbg["d_q"][kch * 128:(kch + 1) * 128, :],
                in_=t["q_bf"][:, kch * 1024:(kch + 1) * 1024]))
        p.inc("sync", "sta", 16, lambda E: E.dma_start(out=dbg["d_srow"], in_=t["srow"][:]))

    p.wait("dve", "lds")
    for r in (1, 2, 3):
        p.inc("dve", "dv", 1, lambda E, r=r: E.tensor_tensor(
            t["krow"][0:1, 0:64], t["krow"][0:1, 0:64],
            t["krow"][0:1, r * 64:(r + 1) * 64], ALU.max))
    p.inc("dve", "dv", 1, lambda E: E.tensor_tensor(
        t["lrow"][0:1, 0:16], t["srow"][0:1, 0:16], t["krow"][0:1, 0:16], ALU.mult))
    p.inc("dve", "dv", 1, lambda E: E.tensor_tensor(
        t["lrow"][0:1, 16:32], t["srow"][0:1, 16:32], t["krow"][0:1, 16:32], ALU.mult))
    p.inc("dve", "dv", 1, lambda E: E.tensor_tensor(
        t["lrow"][0:1, 64:80], t["lrow"][0:1, 0:16], t["lrow"][0:1, 16:32], ALU.max))
    p.wait("act", "dv")
    brv = p.inc("act", "ac", 1, lambda E: E.activation(
        t["lrow"][0:1, 128:144], t["lrow"][0:1, 64:80], AF.Sqrt, scale=1.0 / 64.0))
    p.wait("pe", "ac", brv)
    p.wait("pe", "dv")
    bbv = p.inc("pe", "pe", 1, lambda E: E.matmul(
        t["ps_b"][:, 0:16], t["neg_row_f"][:], t["lrow"][0:1, 128:144], start=True, stop=True))
    p.wait("dve", "pe", bbv)
    biasv = p.inc("dve", "dv", 1, lambda E: E.tensor_copy(t["bias_sb"][:], t["ps_b"][:, 0:16]))
    if debug:
        p.wait("sync", "dv", biasv)
        p.inc("sync", "sta", 16, lambda E: E.dma_start(out=dbg["d_bias"], in_=t["bias_sb"][:]))
    ctx1.close()

    ctx2 = ExitStack()
    ent2 = ctx2.enter_context
    scope_barrier(p)
    t["h_bf"] = ent2(nc.sbuf_tensor("h_bf", [128, ICH * 512], BF16, side="right"))
    t["wffn"] = ent2(nc.sbuf_tensor("wffn", [128, 8448], BF16, side="right"))
    gring = Ring(p, "rg", 2, "pool")
    uring = Ring(p, "ru", 2, "pool")
    dring = Ring(p, "rd", 3, "pool")
    wffn = t["wffn"]
    UOFF = 4096
    for tile in range(2):
        gufree = {"dv": p.ct["dv"]}
        p.wait("dve", "pe")
        p.wait("pool", "pe")
        for ich in range(ICH):
            par = ich % 2
            gi, gsl, gkey = gring.load(lambda s: (lambda E, ich=ich, s=s: E.dma_start(
                out=wffn[:, s * 2048:(s + 1) * 2048], in_=gu[ich][:, 0:2048])))
            ui, usl, ukey = uring.load(lambda s: (lambda E, ich=ich, s=s: E.dma_start(
                out=wffn[:, UOFF + s * 2048:UOFF + (s + 1) * 2048], in_=gu[ich][:, 2048:4096])))
            for sem, val in gufree.items():
                p.wait("pe", sem, val)
            p.wait("pe", *gkey)
            p.wait("pe", *ukey)
            for kch in range(KCH):
                p.emit("pe", lambda E, kch=kch, par=par, tile=tile, gsl=gsl: E.matmul(
                    t["ps_a"][:, par * 1024: par * 1024 + 512],
                    wffn[:, gsl * 2048 + kch * 128: gsl * 2048 + (kch + 1) * 128],
                    t["xn_bf"][:, kch * 1024 + tile * 512: kch * 1024 + (tile + 1) * 512],
                    start=(kch == 0), stop=(kch == KCH - 1)))
                mm = lambda E, kch=kch, par=par, tile=tile, usl=usl: E.matmul(
                    t["ps_a"][:, par * 1024 + 512: par * 1024 + 1024],
                    wffn[:, UOFF + usl * 2048 + kch * 128: UOFF + usl * 2048 + (kch + 1) * 128],
                    t["xn_bf"][:, kch * 1024 + tile * 512: kch * 1024 + (tile + 1) * 512],
                    start=(kch == 0), stop=(kch == KCH - 1))
                if kch == KCH - 1:
                    gv = p.inc("pe", "pe", 1, mm)
                else:
                    p.emit("pe", mm)
            gring.consumed(gi, "pe")
            uring.consumed(ui, "pe")
            p.wait("act", "pe", gv)
            p.wait("act", "dv")
            p.inc("act", "ac", 1, lambda E, par=par, ich=ich: E.activation(
                t["tmp1"][:, 512:1024],
                t["ps_a"][:, par * 1024: par * 1024 + 512],
                AF.Silu, scale=t["gs_sb"][:, ich:ich + 1]))
            p.wait("dve", "ac")
            p.inc("dve", "dv", 1, lambda E, par=par, ich=ich: E.tensor_scalar(
                t["tmp0"][:, 0:512], t["ps_a"][:, par * 1024 + 512: par * 1024 + 1024],
                t["us_sb"][:, ich:ich + 1], 0.0, ALU.mult, ALU.add))
            p.inc("dve", "dv", 1, lambda E, par=par, ich=ich: E.tensor_tensor(
                t["h_bf"][:, ich * 512:(ich + 1) * 512], t["tmp0"][:, 0:512],
                t["tmp1"][:, 512:1024], ALU.mult))
            gufree = {"dv": p.ct["dv"]}
        dfree = [{"dv": p.ct["dv"]}, {"dv": p.ct["dv"]}]
        p.wait("pool", "pe")
        for hch in range(16):
            par = hch % 2
            d0, s0, k0 = dring.load(lambda s: (lambda E, hch=hch, s=s: E.dma_start(
                out=wffn[:, s * 2816:(s + 1) * 2816], in_=dn[hch][:, 0:2816])))
            d1, s1, k1 = dring.load(lambda s: (lambda E, hch=hch, s=s: E.dma_start(
                out=wffn[:, s * 2816:(s + 1) * 2816], in_=dn[hch][:, 2816:5632])))
            for sem, val in dfree[par].items():
                p.wait("pe", sem, val)
            if hch == 0:
                p.wait("pe", "dv", gufree["dv"])
            p.wait("pe", *k0)
            for ich in range(22):
                p.emit("pe", lambda E, ich=ich, par=par, s0=s0: E.matmul(
                    t["ps_b"][:, par * 512:(par + 1) * 512],
                    wffn[:, s0 * 2816 + ich * 128: s0 * 2816 + (ich + 1) * 128],
                    t["h_bf"][:, ich * 512:(ich + 1) * 512],
                    start=(ich == 0), stop=False))
            p.wait("pe", *k1)
            for ich in range(22, ICH):
                mm = lambda E, ich=ich, par=par, s1=s1: E.matmul(
                    t["ps_b"][:, par * 512:(par + 1) * 512],
                    wffn[:, s1 * 2816 + (ich - 22) * 128: s1 * 2816 + (ich - 21) * 128],
                    t["h_bf"][:, ich * 512:(ich + 1) * 512],
                    start=False, stop=(ich == ICH - 1))
                if ich == ICH - 1:
                    dv_ = p.inc("pe", "pe", 1, mm)
                else:
                    p.emit("pe", mm)
            dring.consumed(d0, "pe")
            dring.consumed(d1, "pe")
            p.wait("dve", "pe", dv_)
            p.inc("dve", "dv", 1, lambda E, hch=hch, par=par, tile=tile: E.tensor_scalar(
                t["acc"][:, hch * 1024 + tile * 512: hch * 1024 + (tile + 1) * 512],
                t["ps_b"][:, par * 512:(par + 1) * 512],
                t["ds_sb"][:, hch:hch + 1], 0.0, ALU.mult, ALU.add))
            dfree[par] = {"dv": p.ct["dv"]}
    ctx2.close()

    ctx3 = ExitStack()
    ent3 = ctx3.enter_context
    scope_barrier(p)
    t["e12"] = ent3(nc.sbuf_tensor("e12", [128, 2048], BF16, side="right"))
    t["ktb"] = ent3(nc.sbuf_tensor("ktb", [128, 8192], BF16, side="right"))
    t["vtb"] = ent3(nc.sbuf_tensor("vtb", [128, 8192], BF16, side="right"))
    t["r1B"] = ent3(nc.sbuf_tensor("r1B", [128, 512], F32, side="right"))
    t["r2B"] = ent3(nc.sbuf_tensor("r2B", [128, 512], F32, side="right"))
    t["uB"] = ent3(nc.sbuf_tensor("uB", [128, 512], F32, side="right"))
    t["rstdB"] = ent3(nc.sbuf_tensor("rstdB", [128, 512], F32, side="right"))
    t["sqa"] = ent3(nc.sbuf_tensor("sqa", [128, 512], BF16, side="right"))
    p.wait("act", "dv", biasv)
    attn_bf = t["xn_bf"]
    ps_a, ps_b, ps_d1, ps_d2 = t["ps_a"], t["ps_b"], t["ps_d1"], t["ps_d2"]
    kv_use = []
    ac_free = p.ct["ac"]
    dv_free = p.ct["dv"]
    for h in range(16):
        par_h = h % 2
        if h >= 2:
            p.wait("sync", "pe", kv_use[h - 2])
        v1 = p.inc("sync", f"ldk{par_h}", 16, lambda E, h=h, par_h=par_h: E.dma_start(
            out=t["ktb"][:, par_h * 4096:(par_h + 1) * 4096],
            in_=kT[h * 128:(h + 1) * 128, :]))
        v2 = p.inc("sync", f"ldk{par_h}", 16, lambda E, h=h, par_h=par_h: E.dma_start(
            out=t["vtb"].ap()[:, par_h * 4096:(par_h + 1) * 4096].rearrange(
                "p (b d) -> p b d", b=32),
            in_=vf.rearrange("(b q) d -> q b d", q=128)[:, :, h * 128:(h + 1) * 128]))
        for tile in range(2):
            koff = par_h * 4096
            qcol = h * 1024 + tile * 512
            p.wait("pe", f"ldk{par_h}", v2)
            p.wait("pe", "ac", ac_free)
            p.wait("pe", "dv", dv_free)
            for b in range(32):
                par = b % 2
                p.emit("pe", lambda E, b=b, par=par, koff=koff, qcol=qcol: E.matmul(
                    ps_a[:, par * 1024: par * 1024 + 512],
                    t["ktb"][0:64, koff + b * 128: koff + (b + 1) * 128],
                    t["q_bf"][0:64, qcol:qcol + 512],
                    start=True, stop=True, tile_position=(0, 0)))
                sv = p.inc("pe", "pe", 1, lambda E, b=b, par=par, koff=koff, qcol=qcol: E.matmul(
                    ps_a[:, par * 1024 + 512: par * 1024 + 1024],
                    t["ktb"][64:128, koff + b * 128: koff + (b + 1) * 128],
                    t["q_bf"][64:128, qcol:qcol + 512],
                    start=True, stop=True, tile_position=(64, 0)))
                p.wait("act", "pe", sv)
                ev = p.inc("act", "ac", 1, lambda E, par=par, h=h: E.activation(
                    t["e12"][:, par * 1024:(par + 1) * 1024],
                    ps_a[:, par * 1024:(par + 1) * 1024],
                    AF.Exp, bias=t["bias_sb"][:, h:h + 1], scale=0.125))
                p.wait("pe", "ac", ev)
                first, last = (b == 0), (b == 31)
                p.emit("pe", lambda E, b=b, par=par, koff=koff, first=first, last=last: E.matmul(
                    ps_b[:, 0:512],
                    t["vtb"][:, koff + b * 128: koff + (b + 1) * 128],
                    t["e12"][:, par * 1024: par * 1024 + 512],
                    start=first, stop=last))
                p.emit("pe", lambda E, b=b, par=par, koff=koff, first=first, last=last: E.matmul(
                    ps_b[:, 512:1024],
                    t["vtb"][:, koff + b * 128: koff + (b + 1) * 128],
                    t["e12"][:, par * 1024 + 512: par * 1024 + 1024],
                    start=first, stop=last))
                p.emit("pe", lambda E, par=par, first=first, last=last: E.matmul(
                    ps_d1[:], t["ones_col_bf"][:],
                    t["e12"][:, par * 1024: par * 1024 + 512],
                    start=first, stop=last))
                mmD = lambda E, par=par, first=first, last=last: E.matmul(
                    ps_d2[:], t["ones_col_bf"][:],
                    t["e12"][:, par * 1024 + 512: par * 1024 + 1024],
                    start=first, stop=last)
                if last:
                    pvv = p.inc("pe", "pe", 1, mmD)
                else:
                    p.emit("pe", mmD)
            p.wait("act", "pe", pvv)
            p.inc("act", "ac", 1, lambda E: E.activation(t["tmp1"][0:1, 0:512], ps_d1[:], AF.Copy))
            dcp = p.inc("act", "ac", 1, lambda E: E.activation(t["tmp1"][0:1, 512:1024], ps_d2[:], AF.Copy))
            p.wait("pe", "ac", dcp)
            p.emit("pe", lambda E: E.matmul(ps_a[:, 0:512], t["ones_row_f"][:],
                                            t["tmp1"][0:1, 0:512], start=True, stop=True))
            bv = p.inc("pe", "pe", 1, lambda E: E.matmul(
                ps_a[:, 512:1024], t["ones_row_f"][:], t["tmp1"][0:1, 512:1024],
                start=True, stop=True))
            p.wait("act", "pe", bv)
            p.inc("act", "ac", 1, lambda E: E.activation(t["uB"][:], ps_a[:, 0:512], AF.Ln))
            p.inc("act", "ac", 1, lambda E: E.activation(t["r1B"][:], t["uB"][:], AF.Exp, scale=-1.0))
            p.inc("act", "ac", 1, lambda E: E.activation(t["uB"][:], ps_a[:, 512:1024], AF.Ln))
            rv = p.inc("act", "ac", 1, lambda E: E.activation(t["r2B"][:], t["uB"][:], AF.Exp, scale=-1.0))
            p.wait("dve", "ac", rv)
            p.inc("dve", "dv", 1, lambda E: E.tensor_scalar(
                t["r2B"][:], t["r2B"][:], t["lam_vec"][:], 0.0, ALU.mult, ALU.add))
            p.inc("dve", "dv", 1, lambda E: E.tensor_tensor(
                t["tmp0"][:, 0:512], ps_b[:, 0:512], t["r1B"][:], ALU.mult))
            p.inc("dve", "dv", 1, lambda E: E.tensor_tensor(
                t["tmp0"][:, 512:1024], ps_b[:, 512:1024], t["r2B"][:], ALU.mult))
            yv = p.inc("dve", "dv", 1, lambda E: E.tensor_tensor(
                t["tmp0"][:, 0:512], t["tmp0"][:, 0:512], t["tmp0"][:, 512:1024], ALU.subtract))
            p.wait("act", "dv", yv)
            sqv = p.inc("act", "ac", 1, lambda E: E.activation(
                t["sqa"][:], t["tmp0"][:, 0:512], AF.Square))
            p.wait("pe", "ac", sqv)
            msv = p.inc("pe", "pe", 1, lambda E: E.matmul(
                ps_a[:, 1024:1536], t["ones128_bf"][:], t["sqa"][:], start=True, stop=True))
            p.wait("act", "pe", msv)
            p.inc("act", "ac", 1, lambda E: E.activation(
                t["uB"][:], ps_a[:, 1024:1536], AF.Ln, scale=1.0 / 128.0, bias=t["eps_col"][:]))
            rsv = p.inc("act", "ac", 1, lambda E: E.activation(
                t["rstdB"][:], t["uB"][:], AF.Exp, scale=-0.5))
            p.wait("dve", "ac", rsv)
            p.inc("dve", "dv", 1, lambda E: E.tensor_tensor(
                t["tmp1"][:, 0:512], t["tmp0"][:, 0:512], t["rstdB"][:], ALU.mult))
            dvf = p.inc("dve", "dv", 1, lambda E, h=h, tile=tile: E.tensor_scalar(
                attn_bf[:, h * 1024 + tile * 512: h * 1024 + (tile + 1) * 512],
                t["tmp1"][:, 0:512], t["gam_eff"][:], 0.0, ALU.mult, ALU.add))
            ac_free = rsv
            dv_free = dvf
        kv_use.append(p.ct["pe"])
    ctx3.close()

    if debug:
        p.wait("sync", "dv")
        for kch in range(KCH):
            p.inc("sync", "sta", 16, lambda E, kch=kch: E.dma_start(
                out=dbg["d_attn"][kch * 128:(kch + 1) * 128, :],
                in_=attn_bf[:, kch * 1024:(kch + 1) * 1024]))

    ctx4 = ExitStack()
    ent4 = ctx4.enter_context
    scope_barrier(p)
    t["wbuf2"] = ent4(nc.sbuf_tensor("wbuf2", [128, 4096], BF16, side="right"))
    wring = Ring(p, "rw", 2, "pool")
    wofree = [{"ac": ac_free, "dv": dv_free}, {"ac": ac_free, "dv": dv_free}]
    for och in range(16):
        par = och % 2
        wi, wsl, wkey = wring.load(lambda s: (lambda E, och=och, s=s: E.dma_start(
            out=t["wbuf2"][:, s * 2048:(s + 1) * 2048], in_=wo[och])))
        for sem, val in wofree[par].items():
            p.wait("pe", sem, val)
        p.wait("pe", *wkey)
        for kch in range(KCH):
            for half in range(2):
                mm = lambda E, och=och, kch=kch, half=half, wsl=wsl: E.matmul(
                    ps_a[:, (och % 2) * 1024 + half * 512: (och % 2) * 1024 + (half + 1) * 512],
                    t["wbuf2"][:, wsl * 2048 + kch * 128: wsl * 2048 + (kch + 1) * 128],
                    attn_bf[:, kch * 1024 + half * 512: kch * 1024 + (half + 1) * 512],
                    start=(kch == 0), stop=(kch == KCH - 1))
                if kch == KCH - 1 and half == 1:
                    ov = p.inc("pe", "pe", 1, mm)
                else:
                    p.emit("pe", mm)
        wring.consumed(wi, "pe")
        p.wait("dve", "pe", ov)
        fv = p.inc("dve", "dv", 1, lambda E, och=och, par=par: E.tensor_tensor(
            t["acc"][:, och * 1024:(och + 1) * 1024],
            t["acc"][:, och * 1024:(och + 1) * 1024],
            ps_a[:, par * 1024:(par + 1) * 1024], ALU.add))
        wofree[par] = {"dv": fv}
    ctx4.close()

    if debug:
        p.wait("sync", "dv")
        for kch in range(KCH):
            p.inc("sync", "sta", 16, lambda E, kch=kch: E.dma_start(
                out=dbg["d_acc"][kch * 128:(kch + 1) * 128, :],
                in_=t["acc"][:, kch * 1024:(kch + 1) * 1024]))

    ctx5 = ExitStack()
    ent5 = ctx5.enter_context
    scope_barrier(p)
    t["xe"] = ent5(nc.sbuf_tensor("xe", [128, 2048], F32, side="right"))
    t["out_sb"] = ent5(nc.sbuf_tensor("out_sb", [128, 2048], F32, side="right"))
    stv = []
    for och in range(16):
        par = och % 2
        if och >= 2:
            p.wait("sync", f"st{par}", stv[och - 2])
        v = p.inc("sync", f"ldx{par}", 16, lambda E, och=och, par=par: E.dma_start(
            out=t["xe"][:, par * 1024:(par + 1) * 1024],
            in_=xT[och * 128:(och + 1) * 128, :]))
        p.wait("dve", f"ldx{par}", v)
        p.wait("dve", "dv", wofree[par].get("dv", 0))
        ev = p.inc("dve", "dv", 1, lambda E, och=och, par=par: E.tensor_tensor(
            t["out_sb"][:, par * 1024:(par + 1) * 1024],
            t["acc"][:, och * 1024:(och + 1) * 1024],
            t["xe"][:, par * 1024:(par + 1) * 1024], ALU.add))
        p.wait("sync", "dv", ev)
        stv.append(p.inc("sync", f"st{par}", 16, lambda E, och=och, par=par: E.dma_start(
            out=outT[och * 128:(och + 1) * 128, :],
            in_=t["out_sb"][:, par * 1024:(par + 1) * 1024])))
    ctx5.close()
    p.wait("sync", "st0")
    p.wait("sync", "st1")
    if debug:
        p.wait("sync", "sta")
    block = ent(nc.Block())
    attach(p, block)
    ctx.close()
    return nc


def build_F(debug=False):
    nc = bass.Bass("TRN2", target_bir_lowering=False)
    xTb = nc.dram_tensor("xTb", [E, S], F32, kind="ExternalInput").ap()
    wq = nc.dram_tensor("wq", [16, 128, 2048], F32, kind="ExternalInput").ap()
    wk = nc.dram_tensor("wk", [16, 128, 2048], F32, kind="ExternalInput").ap()
    wo = nc.dram_tensor("wo", [16, 128, 2048], F32, kind="ExternalInput").ap()
    wv = nc.dram_tensor("wv", [4, 128, 8192], F32, kind="ExternalInput").ap()
    gu = nc.dram_tensor("gu", [44, 128, 4096], F32, kind="ExternalInput").ap()
    dn = nc.dram_tensor("dn", [16, 128, 5632], F32, kind="ExternalInput").ap()
    gs = nc.dram_tensor("gs", [128, 44], F32, kind="ExternalInput").ap()
    us = nc.dram_tensor("us", [128, 44], F32, kind="ExternalInput").ap()
    ds = nc.dram_tensor("ds", [128, 16], F32, kind="ExternalInput").ap()
    gam = nc.dram_tensor("gam", [128, 1], F32, kind="ExternalInput").ap()
    lq1 = nc.dram_tensor("lq1", [1, 64], F32, kind="ExternalInput").ap()
    lk1 = nc.dram_tensor("lk1", [1, 64], F32, kind="ExternalInput").ap()
    lq2 = nc.dram_tensor("lq2", [1, 64], F32, kind="ExternalInput").ap()
    lk2 = nc.dram_tensor("lk2", [1, 64], F32, kind="ExternalInput").ap()
    ctb = nc.dram_tensor("ctb", [128, S], BF16, kind="ExternalInput").ap()
    stbt = nc.dram_tensor("stbt", [128, S], BF16, kind="ExternalInput").ap()
    outT = nc.dram_tensor("outT", [E, TQ], F32, kind="ExternalOutput").ap()
    kT_s = nc.dram_tensor("kT_s", [E, S], BF16, kind="Internal").ap()
    v_s = nc.dram_tensor("v_s", [S, E], BF16, kind="Internal").ap()
    wk_c = nc.dram_tensor("wk_c", [16, 128, 2048], BF16, kind="Internal").ap()
    wv_c = nc.dram_tensor("wv_c", [4, 128, 8192], BF16, kind="Internal").ap()
    dbg = {}
    if debug:
        for nm, shp, dt in (("d_xn", [E, TQ], BF16), ("d_q", [E, TQ], BF16),
                            ("d_attn", [E, TQ], BF16), ("d_kt", [E, S], BF16),
                            ("d_vs", [S, E], BF16), ("d_srow", [1, 64], F32)):
            dbg[nm] = nc.dram_tensor(nm, shp, dt, kind="ExternalOutput").ap()

    from contextlib import ExitStack
    ctx = ExitStack()
    ent = ctx.enter_context
    t = {}
    t["ps_a"] = ent(nc.psum_tensor("ps_a", [128, 2048], F32))
    t["ps_b"] = ent(nc.psum_tensor("ps_b", [128, 1024], F32))
    t["ps_d1"] = ent(nc.psum_tensor("ps_d1", [1, 512], F32))
    t["ps_d2"] = ent(nc.psum_tensor("ps_d2", [1, 512], F32))
    t["ones_col_bf"] = ent(nc.sbuf_tensor("ones_col_bf", [128, 1], BF16))
    t["ones_row_f"] = ent(nc.sbuf_tensor("ones_row_f", [1, 128], F32))
    t["neg_row_f"] = ent(nc.sbuf_tensor("neg_row_f", [1, 128], F32))
    t["ones128_bf"] = ent(nc.sbuf_tensor("ones128_bf", [128, 128], BF16))
    t["mask1_bf"] = ent(nc.sbuf_tensor("mask1_bf", [128, 1], BF16))
    t["mask2_bf"] = ent(nc.sbuf_tensor("mask2_bf", [128, 1], BF16))
    t["tmp0"] = ent(nc.sbuf_tensor("tmp0", [128, 1024], F32))
    t["tmp1"] = ent(nc.sbuf_tensor("tmp1", [128, 1024], F32))
    t["srow"] = ent(nc.sbuf_tensor("srow", [1, 64], F32))
    t["trow"] = ent(nc.sbuf_tensor("trow", [1, 16], F32))
    t["eps_col"] = ent(nc.sbuf_tensor("eps_col", [128, 1], F32))
    t["nrow"] = ent(nc.sbuf_tensor("nrow", [1, 1024], F32))
    t["bias_sb"] = ent(nc.sbuf_tensor("bias_sb", [128, 16], F32))
    t["lam_vec"] = ent(nc.sbuf_tensor("lam_vec", [128, 1], F32))
    t["gam_eff"] = ent(nc.sbuf_tensor("gam_eff", [128, 1], F32))
    t["gs_sb"] = ent(nc.sbuf_tensor("gs_sb", [128, 44], F32))
    t["us_sb"] = ent(nc.sbuf_tensor("us_sb", [128, 44], F32))
    t["ds_sb"] = ent(nc.sbuf_tensor("ds_sb", [128, 16], F32))
    t["lrow"] = ent(nc.sbuf_tensor("lrow", [1, 256], F32))
    t["xn_bf"] = ent(nc.sbuf_tensor("xn_bf", [128, 16 * 1024], BF16))

    p = Prog(nc, ent)
    emit_init(p, t)
    p.wait("pe", "pl")
    for name, src_ap in (("gs_sb", gs), ("us_sb", us), ("ds_sb", ds)):
        p.inc("sync", "lds", 16, lambda E, src_ap=src_ap, name=name: E.dma_start(
            out=t[name][:], in_=src_ap))
    p.inc("sync", "lds", 16, lambda E: E.dma_start(out=t["lrow"][0:1, 0:64], in_=lq1))
    p.inc("sync", "lds", 16, lambda E: E.dma_start(out=t["lrow"][0:1, 64:128], in_=lk1))
    p.inc("sync", "lds", 16, lambda E: E.dma_start(out=t["lrow"][0:1, 128:192], in_=lq2))
    p.inc("sync", "lds", 16, lambda E: E.dma_start(out=t["lrow"][0:1, 192:256], in_=lk2))
    gamv = p.inc("sync", "lds", 16, lambda E: E.dma_start(out=t["tmp0"][:, 0:1], in_=gam))
    p.wait("dve", "lds", gamv)
    p.wait("pool", "lds", gamv)
    p.wait("act", "lds", gamv)
    p.inc("dve", "dv", 1, lambda E: E.tensor_scalar(
        t["gam_eff"][:], t["tmp0"][:, 0:1], ONE_MINUS_LI, 0.0, ALU.mult, ALU.add))
    p.inc("dve", "dv", 1, lambda E: E.tensor_tensor(
        t["lrow"][0:1, 0:64], t["lrow"][0:1, 0:64], t["lrow"][0:1, 64:128], ALU.mult))
    p.inc("dve", "dv", 1, lambda E: E.tensor_tensor(
        t["lrow"][0:1, 128:192], t["lrow"][0:1, 128:192], t["lrow"][0:1, 192:256], ALU.mult))
    p.inc("dve", "dv", 1, lambda E: E.tensor_reduce(
        t["trow"][0:1, 0:1], t["lrow"][0:1, 0:64], AX.X, ALU.add))
    p.inc("dve", "dv", 1, lambda E: E.tensor_reduce(
        t["trow"][0:1, 1:2], t["lrow"][0:1, 128:192], AX.X, ALU.add))
    p.wait("act", "dv")
    p.inc("act", "ac", 1, lambda E: E.activation(t["trow"][0:1, 2:3], t["trow"][0:1, 0:1], AF.Exp))
    p.inc("act", "ac", 1, lambda E: E.activation(t["trow"][0:1, 3:4], t["trow"][0:1, 1:2], AF.Exp))
    p.wait("dve", "ac")
    p.inc("dve", "dv", 1, lambda E: E.tensor_tensor(
        t["trow"][0:1, 2:3], t["trow"][0:1, 2:3], t["trow"][0:1, 3:4], ALU.subtract))
    lamv = p.inc("dve", "dv", 1, lambda E: E.tensor_scalar(
        t["trow"][0:1, 2:3], t["trow"][0:1, 2:3], 1.0, LAMBDA_INIT, ALU.mult, ALU.add))
    p.wait("pe", "dv", lamv)
    lbv = p.inc("pe", "pe", 1, lambda E: E.matmul(
        t["ps_b"][:, 0:1], t["ones_row_f"][:], t["trow"][0:1, 2:3], start=True, stop=True))
    p.wait("dve", "pe", lbv)
    p.inc("dve", "dv", 1, lambda E: E.tensor_copy(t["lam_vec"][:], t["ps_b"][:, 0:1]))

    ctx_tab = ExitStack()
    entt = ctx_tab.enter_context
    t["ctab"] = entt(nc.sbuf_tensor("ctab", [128, S], BF16, side="right"))
    t["qraw_bf"] = entt(nc.sbuf_tensor("qraw_bf", [128, 1024], BF16, side="right"))
    t["tswp_bf"] = entt(nc.sbuf_tensor("tswp_bf", [128, 1024], BF16, side="right"))
    t["tmpa_bf"] = entt(nc.sbuf_tensor("tmpa_bf", [128, 1024], BF16, side="right"))
    t["stab"] = entt(nc.sbuf_tensor("stab", [128, S], BF16, side="right"))
    t["sq_b"] = entt(nc.sbuf_tensor("sq_b", [128, 1024], BF16, side="right"))
    p.inc("sync", "lds", 16, lambda E: E.dma_start(out=t["ctab"][:], in_=ctb))
    tabv = p.inc("sync", "lds", 16, lambda E: E.dma_start(out=t["stab"][:], in_=stbt))
    p.wait("dve", "lds", tabv)

    ctx_kv = ExitStack()
    entk = ctx_kv.enter_context
    t["xc"] = entk(nc.sbuf_tensor("xc", [128, 4096], F32, side="right"))
    t["xbb"] = entk(nc.sbuf_tensor("xbb", [128, 4096], BF16, side="right"))
    t["x2b"] = entk(nc.sbuf_tensor("x2b", [128, 4096], BF16, side="right"))
    t["row"] = entk(nc.sbuf_tensor("row", [1, 2048], F32, side="right"))
    t["xnb"] = entk(nc.sbuf_tensor("xnb", [128, 16 * 1024], BF16, side="right"))
    t["kstage"] = entk(nc.sbuf_tensor("kstage", [128, 2048], BF16, side="right"))
    t["kraw"] = entk(nc.sbuf_tensor("kraw", [128, 4096], BF16, side="right"))
    t["v_stage"] = entk(nc.sbuf_tensor("v_stage", [128, 4, 2048], BF16, side="right"))
    t["wbuf"] = entk(nc.sbuf_tensor("wbuf", [128, 4096], BF16, side="right"))
    t["wvbuf"] = entk(nc.sbuf_tensor("wvbuf", [128, 12288], BF16, side="right"))
    kwring = Ring(p, "rkw", 2, "pool")
    vwring = Ring(p, "rvw", 3, "pool")
    lnring = Ring(p, "rx", 2, "sync")
    kst_store = []
    ksq_done = []

    for chunk in range(4):
        c0 = chunk * 1024
        xn_dst = t["xn_bf"] if chunk == 0 else t["xnb"]
        psfree = emit_ln(p, t, xTb[:, c0:c0 + 1024], xn_dst, 1024, 0, ring=lnring)

        def consume_k(och, o8, chunk=chunk, c0=c0):
            gidx = chunk * 16 + och
            half = och % 2
            if gidx >= 2:
                p.wait("dve", f"st{half}", kst_store[gidx - 2])
                p.wait("dve", "ac", ksq_done[gidx - 2])
            srcfree, wv_ = emit_rope_sb(p, t, t["kraw"][:, o8 * 1024:(o8 + 1) * 1024],
                                        t["ctab"], t["stab"], c0, t["kstage"], half * 1024)
            emit_headnorm(p, t, t["kstage"], half * 1024, och, statbase=0)
            ksq_done.append(p.ct["ac"])
            p.wait("sync", "dv", wv_)
            kst_store.append(p.inc("sync", f"st{half}", 16,
                                   lambda E, och=och, half=half, c0=c0: E.dma_start(
                out=kT_s[och * 128:(och + 1) * 128, c0:c0 + 1024],
                in_=t["kstage"][:, half * 1024:(half + 1) * 1024])))
            return srcfree

        free = [dict(psfree), dict(psfree)]
        kraw_free = {}
        for wave in range(4):
            cpvs = []
            for o8 in range(4):
                och = wave * 4 + o8
                par = och % 2
                if chunk == 0:
                    i, slot, key = kwring.load(lambda s: (lambda E, och=och, s=s: E.dma_start(
                        out=t["wbuf"][:, s * 2048:(s + 1) * 2048], in_=wk[och])))
                    p.wait("pool", *key)
                    wcv = p.inc("pool", f"rkw{slot}", 16, lambda E, och=och, s=slot: E.dma_start(
                        out=wk_c[och], in_=t["wbuf"][:, s * 2048:(s + 1) * 2048]))
                    kwring.done[i][f"rkw{slot}"] = wcv
                else:
                    i, slot, key = kwring.load(lambda s: (lambda E, och=och, s=s: E.dma_start(
                        out=t["wbuf"][:, s * 2048:(s + 1) * 2048], in_=wk_c[och])))
                for sem, val in free[par].items():
                    p.wait("pe", sem, val)
                p.wait("pe", key[0], key[1])
                for kch in range(KCH):
                    for half in range(2):
                        mm = lambda E, och=och, kch=kch, half=half, par=par, slot=slot, xn_dst=xn_dst: E.matmul(
                            t["ps_a"][:, par * 1024 + half * 512: par * 1024 + (half + 1) * 512],
                            t["wbuf"][:, slot * 2048 + kch * 128: slot * 2048 + (kch + 1) * 128],
                            xn_dst[:, kch * 1024 + half * 512: kch * 1024 + (half + 1) * 512],
                            start=(kch == 0), stop=(kch == KCH - 1))
                        if kch == KCH - 1 and half == 1:
                            mmv = p.inc("pe", "pe", 1, mm)
                        else:
                            p.emit("pe", mm)
                kwring.consumed(i, "pe")
                p.wait("act", "pe", mmv)
                if wave >= 1:
                    p.wait("act", "dv", kraw_free[o8])
                cpv = p.inc("act", "ac", 1, lambda E, par=par, o8=o8: E.activation(
                    t["kraw"][:, o8 * 1024:(o8 + 1) * 1024],
                    t["ps_a"][:, par * 1024:(par + 1) * 1024], AF.Copy))
                cpvs.append(cpv)
                free[par] = {"ac": cpv}
            for o8 in range(4):
                och = wave * 4 + o8
                p.wait("dve", "ac", cpvs[o8])
                kraw_free[o8] = consume_k(och, o8)
        both = {k: max(free[0].get(k, 0), free[1].get(k, 0)) for k in set(free[0]) | set(free[1])}
        emit_vproj_cache(p, t, wv, wv_c, vwring, xn_dst, 1024, 0,
                         t["v_stage"], v_s, c0, both, fill_cache=(chunk == 0))
    ctx_kv.close()
    if debug:
        p.wait("sync", "dv")
        p.wait("sync", "st0")
        p.wait("sync", "st1")
        p.wait("sync", "sta")
        p.inc("sync", "sta", 16, lambda E: E.dma_start(out=dbg["d_kt"], in_=kT_s))
        p.inc("sync", "sta", 16, lambda E: E.dma_start(out=dbg["d_vs"], in_=v_s))

    ctx_q = ExitStack()
    entq = ctx_q.enter_context
    scope_barrier(p)
    t["wbufq"] = entq(nc.sbuf_tensor("wbufq", [128, 4096], BF16, side="right"))
    t["q_bf"] = ent(nc.sbuf_tensor("q_bf", [128, 16 * 1024], BF16))

    def consume_q(och, mmv):
        fr, wv_ = emit_rope(p, t, t["ps_a"][:, (och % 2) * 1024:(och % 2 + 1) * 1024],
                            t["ctab"], t["stab"], 0, t["q_bf"], och * 1024)
        emit_headnorm(p, t, t["q_bf"], och * 1024, och, statbase=32)
        return {fr[0]: fr[1]}

    qfree = {"pe": p.ct["pe"], "dv": p.ct["dv"], "ac": p.ct["ac"]}
    emit_fmproj(p, t, wq, t["wbufq"], t["xn_bf"], 1024, 0, consume_q, qfree)
    if debug:
        p.wait("sync", "dv")
        for kch in range(KCH):
            p.inc("sync", "sta", 16, lambda E, kch=kch: E.dma_start(
                out=dbg["d_q"][kch * 128:(kch + 1) * 128, :],
                in_=t["q_bf"][:, kch * 1024:(kch + 1) * 1024]))
        p.inc("sync", "sta", 16, lambda E: E.dma_start(out=dbg["d_srow"], in_=t["srow"][:]))

    p.wait("dve", "lds")
    p.wait("dve", "pl")
    p.inc("dve", "dv", 1, lambda E: E.tensor_tensor(
        t["lrow"][0:1, 0:16], t["srow"][0:1, 32:48], t["srow"][0:1, 0:16], ALU.mult))
    p.inc("dve", "dv", 1, lambda E: E.tensor_tensor(
        t["lrow"][0:1, 16:32], t["srow"][0:1, 48:64], t["srow"][0:1, 16:32], ALU.mult))
    p.inc("dve", "dv", 1, lambda E: E.tensor_tensor(
        t["lrow"][0:1, 64:80], t["lrow"][0:1, 0:16], t["lrow"][0:1, 16:32], ALU.max))
    p.wait("act", "dv")
    brv = p.inc("act", "ac", 1, lambda E: E.activation(
        t["lrow"][0:1, 128:144], t["lrow"][0:1, 64:80], AF.Sqrt, scale=1.0 / 64.0))
    p.wait("pe", "ac", brv)
    p.wait("pe", "dv")
    bbv = p.inc("pe", "pe", 1, lambda E: E.matmul(
        t["ps_b"][:, 0:16], t["neg_row_f"][:], t["lrow"][0:1, 128:144], start=True, stop=True))
    p.wait("dve", "pe", bbv)
    biasv = p.inc("dve", "dv", 1, lambda E: E.tensor_copy(t["bias_sb"][:], t["ps_b"][:, 0:16]))
    ctx_q.close()
    ctx_tab.close()

    ctx2 = ExitStack()
    ent2 = ctx2.enter_context
    scope_barrier(p)
    t["acc"] = ent(nc.sbuf_tensor("acc", [128, 16 * 1024], F32))
    t["h_bf"] = ent2(nc.sbuf_tensor("h_bf", [128, ICH * 512], BF16, side="right"))
    t["wffn"] = ent2(nc.sbuf_tensor("wffn", [128, 8448], BF16, side="right"))
    gring = Ring(p, "rg", 2, "pool")
    uring = Ring(p, "ru", 2, "pool")
    dring = Ring(p, "rd", 3, "pool")
    wffn = t["wffn"]
    UOFF = 4096
    for tile in range(2):
        gufree = {"dv": p.ct["dv"]}
        p.wait("dve", "pe")
        p.wait("pool", "pe")
        for ich in range(ICH):
            par = ich % 2
            gi, gsl, gkey = gring.load(lambda s: (lambda E, ich=ich, s=s: E.dma_start(
                out=wffn[:, s * 2048:(s + 1) * 2048], in_=gu[ich][:, 0:2048])))
            ui, usl, ukey = uring.load(lambda s: (lambda E, ich=ich, s=s: E.dma_start(
                out=wffn[:, UOFF + s * 2048:UOFF + (s + 1) * 2048], in_=gu[ich][:, 2048:4096])))
            for sem, val in gufree.items():
                p.wait("pe", sem, val)
            p.wait("pe", *gkey)
            p.wait("pe", *ukey)
            for kch in range(KCH):
                p.emit("pe", lambda E, kch=kch, par=par, tile=tile, gsl=gsl: E.matmul(
                    t["ps_a"][:, par * 1024: par * 1024 + 512],
                    wffn[:, gsl * 2048 + kch * 128: gsl * 2048 + (kch + 1) * 128],
                    t["xn_bf"][:, kch * 1024 + tile * 512: kch * 1024 + (tile + 1) * 512],
                    start=(kch == 0), stop=(kch == KCH - 1)))
                mm = lambda E, kch=kch, par=par, tile=tile, usl=usl: E.matmul(
                    t["ps_a"][:, par * 1024 + 512: par * 1024 + 1024],
                    wffn[:, UOFF + usl * 2048 + kch * 128: UOFF + usl * 2048 + (kch + 1) * 128],
                    t["xn_bf"][:, kch * 1024 + tile * 512: kch * 1024 + (tile + 1) * 512],
                    start=(kch == 0), stop=(kch == KCH - 1))
                if kch == KCH - 1:
                    gv = p.inc("pe", "pe", 1, mm)
                else:
                    p.emit("pe", mm)
            gring.consumed(gi, "pe")
            uring.consumed(ui, "pe")
            p.wait("act", "pe", gv)
            p.wait("act", "dv")
            p.inc("act", "ac", 1, lambda E, par=par, ich=ich: E.activation(
                t["tmp1"][:, 512:1024],
                t["ps_a"][:, par * 1024: par * 1024 + 512],
                AF.Silu, scale=t["gs_sb"][:, ich:ich + 1]))
            p.wait("dve", "ac")
            p.inc("dve", "dv", 1, lambda E, par=par, ich=ich: E.tensor_scalar(
                t["tmp0"][:, 0:512], t["ps_a"][:, par * 1024 + 512: par * 1024 + 1024],
                t["us_sb"][:, ich:ich + 1], 0.0, ALU.mult, ALU.add))
            p.inc("dve", "dv", 1, lambda E, par=par, ich=ich: E.tensor_tensor(
                t["h_bf"][:, ich * 512:(ich + 1) * 512], t["tmp0"][:, 0:512],
                t["tmp1"][:, 512:1024], ALU.mult))
            gufree = {"dv": p.ct["dv"]}
        dfree = [{"dv": p.ct["dv"]}, {"dv": p.ct["dv"]}]
        p.wait("pool", "pe")
        for hch in range(16):
            par = hch % 2
            d0, s0, k0 = dring.load(lambda s: (lambda E, hch=hch, s=s: E.dma_start(
                out=wffn[:, s * 2816:(s + 1) * 2816], in_=dn[hch][:, 0:2816])))
            d1, s1, k1 = dring.load(lambda s: (lambda E, hch=hch, s=s: E.dma_start(
                out=wffn[:, s * 2816:(s + 1) * 2816], in_=dn[hch][:, 2816:5632])))
            for sem, val in dfree[par].items():
                p.wait("pe", sem, val)
            if hch == 0:
                p.wait("pe", "dv", gufree["dv"])
            p.wait("pe", *k0)
            for ich in range(22):
                p.emit("pe", lambda E, ich=ich, par=par, s0=s0: E.matmul(
                    t["ps_b"][:, par * 512:(par + 1) * 512],
                    wffn[:, s0 * 2816 + ich * 128: s0 * 2816 + (ich + 1) * 128],
                    t["h_bf"][:, ich * 512:(ich + 1) * 512],
                    start=(ich == 0), stop=False))
            p.wait("pe", *k1)
            for ich in range(22, ICH):
                mm = lambda E, ich=ich, par=par, s1=s1: E.matmul(
                    t["ps_b"][:, par * 512:(par + 1) * 512],
                    wffn[:, s1 * 2816 + (ich - 22) * 128: s1 * 2816 + (ich - 21) * 128],
                    t["h_bf"][:, ich * 512:(ich + 1) * 512],
                    start=False, stop=(ich == ICH - 1))
                if ich == ICH - 1:
                    dv_ = p.inc("pe", "pe", 1, mm)
                else:
                    p.emit("pe", mm)
            dring.consumed(d0, "pe")
            dring.consumed(d1, "pe")
            p.wait("dve", "pe", dv_)
            p.inc("dve", "dv", 1, lambda E, hch=hch, par=par, tile=tile: E.tensor_scalar(
                t["acc"][:, hch * 1024 + tile * 512: hch * 1024 + (tile + 1) * 512],
                t["ps_b"][:, par * 512:(par + 1) * 512],
                t["ds_sb"][:, hch:hch + 1], 0.0, ALU.mult, ALU.add))
            dfree[par] = {"dv": p.ct["dv"]}
    ctx2.close()

    ctx3 = ExitStack()
    ent3 = ctx3.enter_context
    scope_barrier(p)
    t["e12"] = ent3(nc.sbuf_tensor("e12", [128, 2048], BF16, side="right"))
    t["ktb"] = ent3(nc.sbuf_tensor("ktb", [128, 8192], BF16, side="right"))
    t["vtb"] = ent3(nc.sbuf_tensor("vtb", [128, 8192], BF16, side="right"))
    t["r1B"] = ent3(nc.sbuf_tensor("r1B", [128, 512], F32, side="right"))
    t["r2B"] = ent3(nc.sbuf_tensor("r2B", [128, 512], F32, side="right"))
    t["uB"] = ent3(nc.sbuf_tensor("uB", [128, 512], F32, side="right"))
    t["rstdB"] = ent3(nc.sbuf_tensor("rstdB", [128, 512], F32, side="right"))
    t["sqa"] = ent3(nc.sbuf_tensor("sqa", [128, 512], BF16, side="right"))
    p.wait("act", "dv", biasv)
    p.wait("sync", "st0")
    p.wait("sync", "st1")
    p.wait("sync", "sta")
    attn_bf = t["xn_bf"]
    ps_a, ps_b, ps_d1, ps_d2 = t["ps_a"], t["ps_b"], t["ps_d1"], t["ps_d2"]
    kv_use = []
    ac_free = p.ct["ac"]
    dv_free = p.ct["dv"]
    for h in range(16):
        par_h = h % 2
        if h >= 2:
            p.wait("sync", "pe", kv_use[h - 2])
        v1 = p.inc("sync", f"ldk{par_h}", 16, lambda E, h=h, par_h=par_h: E.dma_start(
            out=t["ktb"][:, par_h * 4096:(par_h + 1) * 4096],
            in_=kT_s[h * 128:(h + 1) * 128, :]))
        v2 = p.inc("sync", f"ldk{par_h}", 16, lambda E, h=h, par_h=par_h: E.dma_start(
            out=t["vtb"].ap()[:, par_h * 4096:(par_h + 1) * 4096].rearrange(
                "p (b d) -> p b d", b=32),
            in_=v_s.rearrange("(b q) d -> q b d", q=128)[:, :, h * 128:(h + 1) * 128]))
        for tile in range(2):
            koff = par_h * 4096
            qcol = h * 1024 + tile * 512
            p.wait("pe", f"ldk{par_h}", v2)
            p.wait("pe", "ac", ac_free)
            p.wait("pe", "dv", dv_free)

            def emit_S(b):
                par = b % 2
                p.emit("pe", lambda E, b=b, par=par, koff=koff, qcol=qcol: E.matmul(
                    ps_a[:, par * 1024: par * 1024 + 512],
                    t["ktb"][0:64, koff + b * 128: koff + (b + 1) * 128],
                    t["q_bf"][0:64, qcol:qcol + 512],
                    start=True, stop=True, tile_position=(0, 0)))
                return p.inc("pe", "pe", 1, lambda E, b=b, par=par, koff=koff, qcol=qcol: E.matmul(
                    ps_a[:, par * 1024 + 512: par * 1024 + 1024],
                    t["ktb"][64:128, koff + b * 128: koff + (b + 1) * 128],
                    t["q_bf"][64:128, qcol:qcol + 512],
                    start=True, stop=True, tile_position=(64, 0)))

            sv = [None] * 32
            pvv_l = [None] * 32
            sv[0] = emit_S(0)
            sv[1] = emit_S(1)
            for b in range(32):
                par = b % 2
                if b >= 2:
                    p.wait("act", "pe", pvv_l[b - 2])
                p.wait("act", "pe", sv[b])
                ev = p.inc("act", "ac", 1, lambda E, par=par, h=h: E.activation(
                    t["e12"][:, par * 1024:(par + 1) * 1024],
                    ps_a[:, par * 1024:(par + 1) * 1024],
                    AF.Exp, bias=t["bias_sb"][:, h:h + 1], scale=0.125))
                p.wait("pe", "ac", ev)
                first, last = (b == 0), (b == 31)
                p.emit("pe", lambda E, b=b, par=par, koff=koff, first=first, last=last: E.matmul(
                    ps_b[:, 0:512],
                    t["vtb"][:, koff + b * 128: koff + (b + 1) * 128],
                    t["e12"][:, par * 1024: par * 1024 + 512],
                    start=first, stop=last))
                p.emit("pe", lambda E, b=b, par=par, koff=koff, first=first, last=last: E.matmul(
                    ps_b[:, 512:1024],
                    t["vtb"][:, koff + b * 128: koff + (b + 1) * 128],
                    t["e12"][:, par * 1024 + 512: par * 1024 + 1024],
                    start=first, stop=last))
                p.emit("pe", lambda E, par=par, first=first, last=last: E.matmul(
                    ps_d1[:], t["ones_col_bf"][:],
                    t["e12"][:, par * 1024: par * 1024 + 512],
                    start=first, stop=last))
                mmD = lambda E, par=par, first=first, last=last: E.matmul(
                    ps_d2[:], t["ones_col_bf"][:],
                    t["e12"][:, par * 1024 + 512: par * 1024 + 1024],
                    start=first, stop=last)
                pvv_l[b] = p.inc("pe", "pe", 1, mmD)
                if b + 2 < 32:
                    sv[b + 2] = emit_S(b + 2)
            pvv = pvv_l[31]
            p.wait("act", "pe", pvv)
            p.inc("act", "ac", 1, lambda E: E.activation(t["tmp1"][0:1, 0:512], ps_d1[:], AF.Copy))
            dcp = p.inc("act", "ac", 1, lambda E: E.activation(t["tmp1"][0:1, 512:1024], ps_d2[:], AF.Copy))
            p.wait("pe", "ac", dcp)
            p.emit("pe", lambda E: E.matmul(ps_a[:, 0:512], t["ones_row_f"][:],
                                            t["tmp1"][0:1, 0:512], start=True, stop=True))
            bv = p.inc("pe", "pe", 1, lambda E: E.matmul(
                ps_a[:, 512:1024], t["ones_row_f"][:], t["tmp1"][0:1, 512:1024],
                start=True, stop=True))
            p.wait("act", "pe", bv)
            p.inc("act", "ac", 1, lambda E: E.activation(t["uB"][:], ps_a[:, 0:512], AF.Ln))
            p.inc("act", "ac", 1, lambda E: E.activation(t["r1B"][:], t["uB"][:], AF.Exp, scale=-1.0))
            p.inc("act", "ac", 1, lambda E: E.activation(t["uB"][:], ps_a[:, 512:1024], AF.Ln))
            rv = p.inc("act", "ac", 1, lambda E: E.activation(t["r2B"][:], t["uB"][:], AF.Exp, scale=-1.0))
            p.wait("dve", "ac", rv)
            p.inc("dve", "dv", 1, lambda E: E.tensor_scalar(
                t["r2B"][:], t["r2B"][:], t["lam_vec"][:], 0.0, ALU.mult, ALU.add))
            p.inc("dve", "dv", 1, lambda E: E.tensor_tensor(
                t["tmp0"][:, 0:512], ps_b[:, 0:512], t["r1B"][:], ALU.mult))
            p.inc("dve", "dv", 1, lambda E: E.tensor_tensor(
                t["tmp0"][:, 512:1024], ps_b[:, 512:1024], t["r2B"][:], ALU.mult))
            yv = p.inc("dve", "dv", 1, lambda E: E.tensor_tensor(
                t["tmp0"][:, 0:512], t["tmp0"][:, 0:512], t["tmp0"][:, 512:1024], ALU.subtract))
            p.wait("act", "dv", yv)
            sqv = p.inc("act", "ac", 1, lambda E: E.activation(
                t["sqa"][:], t["tmp0"][:, 0:512], AF.Square))
            p.wait("pe", "ac", sqv)
            msv = p.inc("pe", "pe", 1, lambda E: E.matmul(
                ps_a[:, 1024:1536], t["ones128_bf"][:], t["sqa"][:], start=True, stop=True))
            p.wait("act", "pe", msv)
            p.inc("act", "ac", 1, lambda E: E.activation(
                t["uB"][:], ps_a[:, 1024:1536], AF.Ln, scale=1.0 / 128.0, bias=t["eps_col"][:]))
            rsv = p.inc("act", "ac", 1, lambda E: E.activation(
                t["rstdB"][:], t["uB"][:], AF.Exp, scale=-0.5))
            p.wait("dve", "ac", rsv)
            p.inc("dve", "dv", 1, lambda E: E.tensor_tensor(
                t["tmp1"][:, 0:512], t["tmp0"][:, 0:512], t["rstdB"][:], ALU.mult))
            dvf = p.inc("dve", "dv", 1, lambda E, h=h, tile=tile: E.tensor_scalar(
                attn_bf[:, h * 1024 + tile * 512: h * 1024 + (tile + 1) * 512],
                t["tmp1"][:, 0:512], t["gam_eff"][:], 0.0, ALU.mult, ALU.add))
            ac_free = rsv
            dv_free = dvf
        kv_use.append(p.ct["pe"])
    ctx3.close()
    if debug:
        p.wait("sync", "dv")
        for kch in range(KCH):
            p.inc("sync", "sta", 16, lambda E, kch=kch: E.dma_start(
                out=dbg["d_attn"][kch * 128:(kch + 1) * 128, :],
                in_=attn_bf[:, kch * 1024:(kch + 1) * 1024]))

    ctx4 = ExitStack()
    ent4 = ctx4.enter_context
    scope_barrier(p)
    t["wbuf2"] = ent4(nc.sbuf_tensor("wbuf2", [128, 4096], BF16, side="right"))
    t["xe"] = ent4(nc.sbuf_tensor("xe", [128, 2048], F32, side="right"))
    t["out_sb"] = ent4(nc.sbuf_tensor("out_sb", [128, 2048], F32, side="right"))
    wring = Ring(p, "rw", 2, "pool")
    wofree = [{"ac": ac_free, "dv": dv_free}, {"ac": ac_free, "dv": dv_free}]
    stv = []
    for och in range(16):
        par = och % 2
        wi, wsl, wkey = wring.load(lambda s: (lambda E, och=och, s=s: E.dma_start(
            out=t["wbuf2"][:, s * 2048:(s + 1) * 2048], in_=wo[och])))
        if och >= 2:
            p.wait("sync", f"st{par}", stv[och - 2])
        xv = p.inc("sync", f"ldx{par}", 16, lambda E, och=och, par=par: E.dma_start(
            out=t["xe"][:, par * 1024:(par + 1) * 1024],
            in_=xTb[och * 128:(och + 1) * 128, 0:1024]))
        for sem, val in wofree[par].items():
            p.wait("pe", sem, val)
        p.wait("pe", *wkey)
        for kch in range(KCH):
            for half in range(2):
                mm = lambda E, och=och, kch=kch, half=half, wsl=wsl: E.matmul(
                    ps_a[:, (och % 2) * 1024 + half * 512: (och % 2) * 1024 + (half + 1) * 512],
                    t["wbuf2"][:, wsl * 2048 + kch * 128: wsl * 2048 + (kch + 1) * 128],
                    attn_bf[:, kch * 1024 + half * 512: kch * 1024 + (half + 1) * 512],
                    start=(kch == 0), stop=(kch == KCH - 1))
                if kch == KCH - 1 and half == 1:
                    ov = p.inc("pe", "pe", 1, mm)
                else:
                    p.emit("pe", mm)
        wring.consumed(wi, "pe")
        p.wait("dve", "pe", ov)
        p.wait("dve", f"ldx{par}", xv)
        if och >= 2:
            p.wait("dve", f"st{par}", stv[och - 2])
        p.inc("dve", "dv", 1, lambda E, och=och, par=par: E.tensor_tensor(
            t["out_sb"][:, par * 1024:(par + 1) * 1024],
            t["acc"][:, och * 1024:(och + 1) * 1024],
            ps_a[:, par * 1024:(par + 1) * 1024], ALU.add))
        fv = p.inc("dve", "dv", 1, lambda E, och=och, par=par: E.tensor_tensor(
            t["out_sb"][:, par * 1024:(par + 1) * 1024],
            t["out_sb"][:, par * 1024:(par + 1) * 1024],
            t["xe"][:, par * 1024:(par + 1) * 1024], ALU.add))
        p.wait("sync", "dv", fv)
        stv.append(p.inc("sync", f"st{par}", 16, lambda E, och=och, par=par: E.dma_start(
            out=outT[och * 128:(och + 1) * 128, :],
            in_=t["out_sb"][:, par * 1024:(par + 1) * 1024])))
        wofree[par] = {"dv": fv}
    ctx4.close()
    p.wait("sync", "st0")
    p.wait("sync", "st1")
    if debug:
        p.wait("sync", "sta")
    block = ent(nc.Block())
    attach(p, block)
    ctx.close()
    return nc


def emit_vproj_cache(p, t, wv_hbm, wv_cache, ring, xn_bf, xn_seg, xn_col0,
                     v_stage, v_out, row0, psfree0, fill_cache):
    ps_a = t["ps_a"]

    def vps(j):
        return ps_a[:, j * 512:(j + 1) * 512]

    free = dict(psfree0)
    stv = []
    p.wait("act", "sta")
    for halfq in range(2):
        for dvq in range(4):
            slots = []
            for wh in range(2):
                if fill_cache and halfq == 0:
                    i, slot, key = ring.load(lambda s: (lambda E, dvq=dvq, wh=wh, s=s: E.dma_start(
                        out=t["wvbuf"][:, s * 4096:(s + 1) * 4096],
                        in_=wv_hbm[dvq][:, wh * 4096:(wh + 1) * 4096])))
                    p.wait("pool", *key)
                    wcv = p.inc("pool", f"rvw{slot}", 16, lambda E, dvq=dvq, wh=wh, s=slot: E.dma_start(
                        out=wv_cache[dvq][:, wh * 4096:(wh + 1) * 4096],
                        in_=t["wvbuf"][:, s * 4096:(s + 1) * 4096]))
                    ring.done[i][f"rvw{slot}"] = wcv
                else:
                    i, slot, key = ring.load(lambda s: (lambda E, dvq=dvq, wh=wh, s=s: E.dma_start(
                        out=t["wvbuf"][:, s * 4096:(s + 1) * 4096],
                        in_=wv_cache[dvq][:, wh * 4096:(wh + 1) * 4096])))
                slots.append((i, slot, key))
            for sem, val in free.items():
                p.wait("pe", sem, val)
            for wh in range(2):
                i, slot, key = slots[wh]
                p.wait("pe", key[0], key[1])
                for kch in range(wh * 8, wh * 8 + 8):
                    for tc in range(4):
                        tok = xn_col0 + halfq * 512 + tc * 128
                        mm = lambda E, kch=kch, tc=tc, tok=tok, slot=slot: E.matmul(
                            vps(tc),
                            xn_bf[:, kch * xn_seg + tok: kch * xn_seg + tok + 128],
                            t["wvbuf"][:, slot * 4096 + (kch % 8) * 512:
                                       slot * 4096 + (kch % 8 + 1) * 512],
                            start=(kch == 0), stop=(kch == KCH - 1))
                        if kch == wh * 8 + 7 and tc == 3:
                            p.inc("pe", "pe", 1, mm)
                        else:
                            p.emit("pe", mm)
                ring.consumed(i, "pe")
            p.wait("act", "pe")
            for tc in range(4):
                p.inc("act", "ac", 1, lambda E, tc=tc, dvq=dvq: E.activation(
                    v_stage[:, tc, dvq * 512:(dvq + 1) * 512], vps(tc), AF.Copy))
            free = {"ac": p.ct["ac"]}
        p.wait("sync", "ac")
        for tc in range(4):
            r0 = row0 + halfq * 512 + tc * 128
            stv.append(p.inc("sync", "sta", 16, lambda E, tc=tc, r0=r0: E.dma_start(
                out=v_out[r0:r0 + 128, :], in_=v_stage[:, tc, :])))
        if halfq == 0:
            p.wait("act", "sta", stv[-1])
    return free


def _f32(a):
    return np.ascontiguousarray(np.asarray(a, dtype=np.float32))


def chunk_wT(W):
    return np.ascontiguousarray(
        _f32(W).reshape(16, 128, 16, 128).transpose(0, 3, 2, 1)).reshape(16, 128, 2048)


def rope_tables():
    angle = 1.0 / 10000 ** np.linspace(0.0, 1.0, 32)
    freqs = np.arange(S, dtype=np.float64)[:, None] * angle
    cos, sin = np.cos(freqs), np.sin(freqs)
    Cc = np.empty((64, S), np.float64)
    Sc = np.empty((64, S), np.float64)
    for i in range(32):
        Cc[2 * i] = cos[:, i]
        Cc[2 * i + 1] = cos[:, i]
        Sc[2 * i] = -sin[:, i]
        Sc[2 * i + 1] = sin[:, i]
    C = np.vstack([Cc, Cc]).astype(NPBF)
    Sg = np.vstack([Sc, Sc]).astype(NPBF)
    return np.ascontiguousarray(C), np.ascontiguousarray(Sg)


MODE = "F"


def kernel(L0_attn_x, L0_attn_Wq, L0_attn_Wk, L0_attn_Wv, L0_attn_Wo,
           L0_attn_lambda_q1, L0_attn_lambda_k1, L0_attn_lambda_q2, L0_attn_lambda_k2,
           L0_attn_gamma, L0_ffn_gate_w, L0_ffn_up_w, L0_ffn_down_w,
           L0_ffn_gate_s, L0_ffn_up_s, L0_ffn_down_s):
    x = _f32(L0_attn_x)
    wq_dev = chunk_wT(L0_attn_Wq)
    wk_dev = chunk_wT(L0_attn_Wk)
    wo_dev = chunk_wT(L0_attn_Wo)
    wv_dev = np.ascontiguousarray(
        _f32(L0_attn_Wv).reshape(4, 512, 16, 128).transpose(0, 3, 2, 1)).reshape(4, 128, 8192)
    g4 = _f32(L0_ffn_gate_w).reshape(44, 128, 16, 128).transpose(0, 3, 2, 1)
    u4 = _f32(L0_ffn_up_w).reshape(44, 128, 16, 128).transpose(0, 3, 2, 1)
    gu_dev = np.ascontiguousarray(
        np.stack([g4, u4], axis=1).transpose(0, 2, 1, 3, 4)).reshape(44, 128, 4096)
    dn_dev = np.ascontiguousarray(
        _f32(L0_ffn_down_w).reshape(16, 128, 44, 128).transpose(0, 3, 2, 1)).reshape(16, 128, 5632)
    gs_dev = np.ascontiguousarray(_f32(L0_ffn_gate_s).reshape(44, 128).T)
    us_dev = np.ascontiguousarray(_f32(L0_ffn_up_s).reshape(44, 128).T)
    ds_dev = np.ascontiguousarray(_f32(L0_ffn_down_s).reshape(16, 128).T)
    gam_dev = _f32(L0_attn_gamma).reshape(128, 1)
    lq1 = _f32(L0_attn_lambda_q1).reshape(1, 64)
    lk1 = _f32(L0_attn_lambda_k1).reshape(1, 64)
    lq2 = _f32(L0_attn_lambda_q2).reshape(1, 64)
    lk2 = _f32(L0_attn_lambda_k2).reshape(1, 64)
    Ct, St = rope_tables()
    common = {"wq": wq_dev, "wo": wo_dev, "gu": gu_dev, "dn": dn_dev,
              "gs": gs_dev, "us": us_dev, "ds": ds_dev, "gam": gam_dev,
              "lq1": lq1, "lk1": lk1, "lq2": lq2, "lk2": lk2}

    if MODE == "F":
        ncF = build_F()
        in_F = []
        for c in range(8):
            b, j = c // 4, c % 4
            order = [j] + [i for i in range(4) if i != j]
            xTb = np.ascontiguousarray(np.concatenate(
                [x[b, 1024 * i:1024 * (i + 1), :] for i in order], axis=0).T)
            ctb = np.ascontiguousarray(np.concatenate(
                [Ct[:, 1024 * i:1024 * (i + 1)] for i in order], axis=1))
            stbt = np.ascontiguousarray(np.concatenate(
                [St[:, 1024 * i:1024 * (i + 1)] for i in order], axis=1))
            m = dict(common)
            m.update({"xTb": xTb, "wk": wk_dev, "wv": wv_dev,
                      "ctb": ctb, "stbt": stbt})
            in_F.append(m)
        res = run_bass_kernel_spmd(ncF, in_F, core_ids=list(range(8)))
        out = np.empty((2, S, E), np.float32)
        for c in range(8):
            b, r0 = c // 4, (c % 4) * TQ
            out[b, r0:r0 + TQ, :] = np.asarray(res.results[c]["outT"]).T
        return out

    xT = []
    for c in range(8):
        b, r0 = c // 4, (c % 4) * TQ
        xT.append(np.ascontiguousarray(x[b, r0:r0 + TQ, :].T))
    ncA = build_A()
    in_A = []
    for c in range(8):
        r0 = (c % 4) * TQ
        in_A.append({"xT": xT[c], "wk": wk_dev, "wv": wv_dev,
                     "ct": np.ascontiguousarray(Ct[:, r0:r0 + TQ]),
                     "stb": np.ascontiguousarray(St[:, r0:r0 + TQ])})
    resA = run_bass_kernel_spmd(ncA, in_A, core_ids=list(range(8)))
    kT_full, v_full, ks4 = {}, {}, {}
    for g in range(2):
        cs = [4 * g + i for i in range(4)]
        kT_full[g] = np.ascontiguousarray(
            np.concatenate([np.asarray(resA.results[c]["kT_out"]) for c in cs], axis=1))
        v_full[g] = np.ascontiguousarray(
            np.concatenate([np.asarray(resA.results[c]["v_out"]) for c in cs], axis=0))
        ks4[g] = np.ascontiguousarray(
            np.concatenate([np.asarray(resA.results[c]["kstat_out"]) for c in cs], axis=0))
    ncB = build_B()
    in_B = []
    for c in range(8):
        g, r0 = c // 4, (c % 4) * TQ
        m = dict(common)
        m.update({"xT": xT[c], "kT": kT_full[g], "vf": v_full[g], "ks": ks4[g],
                  "ct": np.ascontiguousarray(Ct[:, r0:r0 + TQ]),
                  "stb": np.ascontiguousarray(St[:, r0:r0 + TQ])})
        in_B.append(m)
    resB = run_bass_kernel_spmd(ncB, in_B, core_ids=list(range(8)))
    out = np.empty((2, S, E), np.float32)
    for c in range(8):
        b, r0 = c // 4, (c % 4) * TQ
        out[b, r0:r0 + TQ, :] = np.asarray(resB.results[c]["outT"]).T
    return out


# version 10
# speedup vs baseline: 1.0272x; 1.0272x over previous
import math
import numpy as np
import ml_dtypes

import concourse.bass as bass
import concourse.mybir as mybir
from concourse.bass_utils import run_bass_kernel_spmd

F32 = mybir.dt.float32
BF16 = mybir.dt.bfloat16
AF = mybir.ActivationFunctionType
ALU = mybir.AluOpType
AX = mybir.AxisListType

S, E, H = 4096, 2048, 16
TQ = 1024
KCH = 16
ICH = 44
LAMBDA_INIT = 0.8 - 0.6 * math.exp(-0.3 * 12)
ONE_MINUS_LI = 1.0 - LAMBDA_INIT
EPS = 1e-5
NPBF = ml_dtypes.bfloat16
SWAP_MASK = [i ^ 1 for i in range(32)]


class Prog:
    def __init__(self, nc, ent):
        self.nc = nc
        self.ent = ent
        self.ops = {e: [] for e in ("pe", "act", "dve", "pool", "sync")}
        self.sems = {}
        self.ct = {}
        for s in ("pe", "ac", "dv", "pl", "lds", "ldx0", "ldx1", "ldw0", "ldw1",
                  "ldk0", "ldk1", "st0", "st1", "sta"):
            self.sems[s] = ent(nc.semaphore(s))
            self.ct[s] = 0

    def emit(self, eng, fn):
        self.ops[eng].append(fn)

    def wait(self, eng, sem, val=None):
        v = self.ct[sem] if val is None else val
        if v <= 0:
            return
        s = self.sems[sem]
        self.emit(eng, lambda E, s=s, v=v: E.wait_ge(s, v))

    OWN = {"act": "ac", "dve": "dv", "pool": "pl"}

    def inc(self, eng, sem, n, mk):
        own = self.OWN.get(eng)
        if own is not None and self.ct[own] > 0:
            self.wait(eng, own)
        if n >= 16 and sem == "lds" and self.ct[sem] > 0:
            self.wait(eng, sem)
        s = self.sems[sem]
        self.ct[sem] += n
        v = self.ct[sem]
        self.emit(eng, lambda E, mk=mk, s=s, n=n: mk(E).then_inc(s, n))
        return v


class Ring:

    def __init__(self, p, name, nslots, queue):
        self.p, self.queue, self.n, self.name = p, queue, nslots, name
        for j in range(nslots):
            s = f"{name}{j}"
            if s not in p.sems:
                p.sems[s] = p.ent(p.nc.semaphore(s))
                p.ct[s] = 0
        self.i = 0
        self.done = []

    def load(self, mk_of_slot):
        i, n = self.i, self.n
        slot = i % n
        if i >= n:
            for sem, val in self.done[i - n].items():
                self.p.wait(self.queue, sem, val)
        v = self.p.inc(self.queue, f"{self.name}{slot}", 16, mk_of_slot(slot))
        self.done.append({})
        self.i += 1
        return i, slot, (f"{self.name}{slot}", v)

    def consumed(self, i, *sems):
        for s in sems:
            self.done[i][s] = self.p.ct[s]


def scope_barrier(p):
    for eng in ("pe", "act", "dve", "pool", "sync"):
        for sem in list(p.ct):
            if Prog.OWN.get(eng) == sem:
                continue
            p.wait(eng, sem)


def attach(p, block):
    @block.tensor
    def _(t):
        for fn in p.ops["pe"]:
            fn(t)

    @block.scalar
    def _(sc):
        for fn in p.ops["act"]:
            fn(sc)

    @block.vector
    def _(v):
        for fn in p.ops["dve"]:
            fn(v)

    @block.gpsimd
    def _(g):
        for fn in p.ops["pool"]:
            fn(g)

    @block.sync
    def _(s):
        for fn in p.ops["sync"]:
            fn(s)


def alloc_common(nc, ent, t):
    t["ps_a"] = ent(nc.psum_tensor("ps_a", [128, 2048], F32))
    t["ps_b"] = ent(nc.psum_tensor("ps_b", [128, 1024], F32))
    t["ps_d1"] = ent(nc.psum_tensor("ps_d1", [1, 512], F32))
    t["ps_d2"] = ent(nc.psum_tensor("ps_d2", [1, 512], F32))
    t["ones_col_bf"] = ent(nc.sbuf_tensor("ones_col_bf", [128, 1], BF16))
    t["ones_row_f"] = ent(nc.sbuf_tensor("ones_row_f", [1, 128], F32))
    t["neg_row_f"] = ent(nc.sbuf_tensor("neg_row_f", [1, 128], F32))
    t["ones128_bf"] = ent(nc.sbuf_tensor("ones128_bf", [128, 128], BF16))
    t["mask1_bf"] = ent(nc.sbuf_tensor("mask1_bf", [128, 1], BF16))
    t["mask2_bf"] = ent(nc.sbuf_tensor("mask2_bf", [128, 1], BF16))
    t["xc"] = ent(nc.sbuf_tensor("xc", [128, 4096], F32))
    t["xbb"] = ent(nc.sbuf_tensor("xbb", [128, 4096], BF16))
    t["x2b"] = ent(nc.sbuf_tensor("x2b", [128, 4096], BF16))
    t["row"] = ent(nc.sbuf_tensor("row", [1, 4096], F32))
    t["tmp0"] = ent(nc.sbuf_tensor("tmp0", [128, 1024], F32))
    t["tmp1"] = ent(nc.sbuf_tensor("tmp1", [128, 1024], F32))
    t["sq_b"] = ent(nc.sbuf_tensor("sq_b", [128, 1024], BF16))
    t["srow"] = ent(nc.sbuf_tensor("srow", [1, 64], F32))
    t["eps_col"] = ent(nc.sbuf_tensor("eps_col", [128, 1], F32))
    t["nrow"] = ent(nc.sbuf_tensor("nrow", [1, 1024], F32))
    t["trow"] = ent(nc.sbuf_tensor("trow", [1, 16], F32))


def emit_init(p, t):
    for name, sl, val in (("ones_col_bf", None, 1.0), ("ones_row_f", None, 1.0),
                          ("neg_row_f", None, -1.0), ("ones128_bf", None, 1.0),
                          ("mask1_bf", (0, 64), 1.0), ("mask1_bf", (64, 128), 0.0),
                          ("mask2_bf", (0, 64), 0.0), ("mask2_bf", (64, 128), 1.0),
                          ("srow", None, 0.0), ("eps_col", None, EPS)):
        def mk(E, name=name, sl=sl, val=val):
            ap = t[name][:] if sl is None else t[name][sl[0]:sl[1], :]
            return E.memset(ap, val)
        p.inc("pool", "pl", 1, mk)


def emit_ln(p, t, x_hbm, xn_bf, xn_seg, xn_col0, ring=None):
    xc, xbb, x2b, row = t["xc"], t["xbb"], t["x2b"], t["row"]
    ps_b, ps_d1, ps_d2 = t["ps_b"], t["ps_d1"], t["ps_d2"]
    if ring is None:
        ring = Ring(p, "rx", 2, "sync")
    p.wait("pe", "pl")

    ringb = t.setdefault("_lnringb", Ring(p, "rxb", 2, "pool"))

    def qload(q, tile):
        return ring.load(lambda s: (lambda E, q=q, tile=tile, s=s: E.dma_start(
            out=xc[:, s * 2048:(s + 1) * 2048].rearrange("p (c t) -> p c t", c=4),
            in_=x_hbm[q * 512:(q + 1) * 512,
                      tile * 512:(tile + 1) * 512].rearrange("(c p) t -> p c t", p=128))))

    def qload_bf(q, tile):
        return ringb.load(lambda s: (lambda E, q=q, tile=tile, s=s: E.dma_start(
            out=xbb[:, s * 2048:(s + 1) * 2048].rearrange("p (c t) -> p c t", c=4),
            in_=x_hbm[q * 512:(q + 1) * 512,
                      tile * 512:(tile + 1) * 512].rearrange("(c p) t -> p c t", p=128))))

    for tile in range(2):
        for q in range(4):
            i, sl, key = qload_bf(q, tile)
            p.wait("act", *key)
            for j in range(4):
                kch = q * 4 + j
                co = sl * 2048 + j * 512
                p.inc("act", "ac", 1, lambda E, co=co: E.activation(
                    x2b[:, co:co + 512], xbb[:, co:co + 512], AF.Square))
                p.wait("pe", *key)
                p.wait("pe", "ac")
                p.emit("pe", lambda E, co=co, kch=kch: E.matmul(
                    ps_d1[:], t["ones_col_bf"][:], xbb[:, co:co + 512],
                    start=(kch == 0), stop=(kch == KCH - 1)))
                mmv = p.inc("pe", "pe", 1, lambda E, co=co, kch=kch: E.matmul(
                    ps_d2[:], t["ones_col_bf"][:], x2b[:, co:co + 512],
                    start=(kch == 0), stop=(kch == KCH - 1)))
            ringb.consumed(i, "ac", "pe")
        p.wait("act", "pe", mmv)
        p.inc("act", "ac", 1, lambda E: E.activation(row[0:1, 0:512], ps_d1[:], AF.Copy, scale=1.0 / 2048.0))
        p.inc("act", "ac", 1, lambda E: E.activation(row[0:1, 512:1024], ps_d2[:], AF.Copy, scale=1.0 / 2048.0))
        p.inc("act", "ac", 1, lambda E: E.activation(row[0:1, 1024:1536], row[0:1, 0:512], AF.Square))
        p.wait("dve", "ac")
        p.inc("dve", "dv", 1, lambda E: E.tensor_tensor(
            row[0:1, 512:1024], row[0:1, 512:1024], row[0:1, 1024:1536], ALU.subtract))
        p.wait("act", "dv")
        p.inc("act", "ac", 1, lambda E: E.activation(row[0:1, 1536:2048], row[0:1, 512:1024], AF.Ln, bias=t["eps_col"][0:1, :]))
        p.inc("act", "ac", 1, lambda E: E.activation(row[0:1, 1536:2048], row[0:1, 1536:2048], AF.Exp, scale=-0.5))
        p.wait("pe", "ac")
        p.wait("pe", "dv")
        p.emit("pe", lambda E: E.matmul(ps_b[:, 0:512], t["ones_row_f"][:], row[0:1, 0:512],
                                        start=True, stop=True))
        bcv = p.inc("pe", "pe", 1, lambda E: E.matmul(
            ps_b[:, 512:1024], t["ones_row_f"][:], row[0:1, 1536:2048], start=True, stop=True))
        for q in range(4):
            i, sl, key = qload(q, tile)
            p.wait("dve", *key)
            p.wait("dve", "pe", bcv)
            for j in range(4):
                kch = q * 4 + j
                co = sl * 2048 + j * 512
                c0 = xn_col0 + tile * 512
                p.inc("dve", "dv", 1, lambda E, co=co: E.tensor_tensor(
                    t["tmp0"][:, 0:512], xc[:, co:co + 512], ps_b[:, 0:512], ALU.subtract))
                p.inc("dve", "dv", 1, lambda E, kch=kch, c0=c0: E.tensor_tensor(
                    xn_bf[:, kch * xn_seg + c0: kch * xn_seg + c0 + 512],
                    t["tmp0"][:, 0:512], ps_b[:, 512:1024], ALU.mult))
            ring.consumed(i, "dv")
    return {"dv": p.ct["dv"]}


def emit_rope(p, t, ps_view, ctab, stab, ccol0, dst_bf, dst_col0):
    qr, sw, ta = t["qraw_bf"], t["tswp_bf"], t["tmpa_bf"]
    p.wait("act", "pe")
    psfree = p.inc("act", "ac", 1, lambda E: E.activation(qr[:], ps_view, AF.Copy))
    p.wait("dve", "ac", psfree)
    p.inc("dve", "dv", 1, lambda E: E.stream_shuffle(sw[:], qr[:], SWAP_MASK))
    p.inc("dve", "dv", 1, lambda E: E.tensor_tensor(
        ta[:], qr[:], ctab[:, ccol0:ccol0 + 1024], ALU.mult))
    p.inc("dve", "dv", 1, lambda E: E.tensor_tensor(
        sw[:], sw[:], stab[:, ccol0:ccol0 + 1024], ALU.mult))
    wv = p.inc("dve", "dv", 1, lambda E: E.tensor_tensor(
        dst_bf[:, dst_col0:dst_col0 + 1024], ta[:], sw[:], ALU.add))
    return ("ac", psfree), wv


def emit_rope_sb(p, t, src_view, ctab, stab, ccol0, dst_bf, dst_col0):
    sw, ta = t["tswp_bf"], t["tmpa_bf"]
    p.inc("dve", "dv", 1, lambda E: E.stream_shuffle(sw[:], src_view, SWAP_MASK))
    srcfree = p.inc("dve", "dv", 1, lambda E: E.tensor_tensor(
        ta[:], src_view, ctab[:, ccol0:ccol0 + 1024], ALU.mult))
    p.inc("dve", "dv", 1, lambda E: E.tensor_tensor(
        sw[:], sw[:], stab[:, ccol0:ccol0 + 1024], ALU.mult))
    wv = p.inc("dve", "dv", 1, lambda E: E.tensor_tensor(
        dst_bf[:, dst_col0:dst_col0 + 1024], ta[:], sw[:], ALU.add))
    return srcfree, wv


def emit_v_group(p, t, wv_hbm, wv_cache, ring, xn_bf, halfq, dvq,
                 v_stage, v_out, row0, vfree, fill_cache, stv):
    ps_b = t["ps_b"]
    slots = []
    for wh in range(2):
        if fill_cache and halfq == 0:
            i, slot, key = ring.load(lambda s: (lambda E, dvq=dvq, wh=wh, s=s: E.dma_start(
                out=t["wvbuf"][:, s * 4096:(s + 1) * 4096],
                in_=wv_hbm[dvq][:, wh * 4096:(wh + 1) * 4096])))
            p.wait("pool", *key)
            wcv = p.inc("pool", f"rvw{slot}", 16, lambda E, dvq=dvq, wh=wh, s=slot: E.dma_start(
                out=wv_cache[dvq][:, wh * 4096:(wh + 1) * 4096],
                in_=t["wvbuf"][:, s * 4096:(s + 1) * 4096]))
            ring.done[i][f"rvw{slot}"] = wcv
        else:
            i, slot, key = ring.load(lambda s: (lambda E, dvq=dvq, wh=wh, s=s: E.dma_start(
                out=t["wvbuf"][:, s * 4096:(s + 1) * 4096],
                in_=wv_cache[dvq][:, wh * 4096:(wh + 1) * 4096])))
        slots.append((i, slot, key))
    for tc2 in range(2):
        for sem, val in vfree.items():
            p.wait("pe", sem, val)
        for wh in range(2):
            i, slot, key = slots[wh]
            p.wait("pe", key[0], key[1])
            for kch in range(wh * 8, wh * 8 + 8):
                for half in range(2):
                    tc = tc2 * 2 + half
                    tok = halfq * 512 + tc * 128
                    mm = lambda E, kch=kch, tok=tok, slot=slot, half=half: E.matmul(
                        ps_b[:, half * 512:(half + 1) * 512],
                        xn_bf[:, kch * 1024 + tok: kch * 1024 + tok + 128],
                        t["wvbuf"][:, slot * 4096 + (kch % 8) * 512:
                                   slot * 4096 + (kch % 8 + 1) * 512],
                        start=(kch == 0), stop=(kch == KCH - 1))
                    if kch == wh * 8 + 7 and half == 1:
                        p.inc("pe", "pe", 1, mm)
                    else:
                        p.emit("pe", mm)
            if tc2 == 1:
                ring.consumed(i, "pe")
        p.wait("act", "pe")
        if stv:
            p.wait("act", "sta", stv[-1][1])
        for half in range(2):
            tc = tc2 * 2 + half
            p.inc("act", "ac", 1, lambda E, tc=tc, dvq=dvq, half=half: E.activation(
                v_stage[:, tc, dvq * 512:(dvq + 1) * 512],
                ps_b[:, half * 512:(half + 1) * 512], AF.Copy))
        vfree = {"ac": p.ct["ac"]}
    if dvq == 3:
        p.wait("sync", "ac")
        for tc in range(4):
            r0 = row0 + halfq * 512 + tc * 128
            stv.append((halfq, p.inc("sync", "sta", 16, lambda E, tc=tc, r0=r0: E.dma_start(
                out=v_out[r0:r0 + 128, :], in_=v_stage[:, tc, :]))))
    return vfree


def emit_headnorm(p, t, src_bf, col0, och, statbase=0):
    sq_b, ps_d1, ps_d2 = t["sq_b"], t["ps_d1"], t["ps_d2"]
    p.wait("act", "dv")
    p.wait("act", "pe")
    p.inc("act", "ac", 1, lambda E: E.activation(
        sq_b[:], src_bf[:, col0:col0 + 1024], AF.Square))
    p.wait("pe", "ac")
    for half, mask in ((0, "mask1_bf"), (1, "mask2_bf")):
        p.wait("pe", "ac", t.get("hn_ac", 0))
        p.emit("pe", lambda E, mask=mask: E.matmul(
            ps_d1[:], t[mask][:], sq_b[:, 0:512], start=True, stop=True))
        mv = p.inc("pe", "pe", 1, lambda E, mask=mask: E.matmul(
            ps_d2[:], t[mask][:], sq_b[:, 512:1024], start=True, stop=True))
        p.wait("act", "pe", mv)
        p.wait("act", "dv", t.get("hn_rv", 0))
        p.inc("act", "ac", 1, lambda E: E.activation(t["nrow"][0:1, 0:512], ps_d1[:], AF.Copy))
        nv = p.inc("act", "ac", 1, lambda E: E.activation(t["nrow"][0:1, 512:1024], ps_d2[:], AF.Copy))
        t["hn_ac"] = nv
        p.wait("dve", "ac", nv)
        p.wait("dve", "pl")
        rv_ = p.inc("dve", "dv", 1, lambda E: E.tensor_reduce(
            t["trow"][0:1, 2:3], t["nrow"][0:1, 0:1024], AX.X, ALU.max))
        t["hn_rv"] = rv_
        col = statbase + och + 16 * half
        p.inc("dve", "dv", 1, lambda E, col=col: E.tensor_tensor(
            t["srow"][0:1, col:col + 1], t["srow"][0:1, col:col + 1],
            t["trow"][0:1, 2:3], ALU.max))
    return p.ct["pe"]


def emit_fmproj(p, t, w_hbm, wbuf, xn_bf, xn_seg, xn_col0, consume, psfree0):
    ps_a = t["ps_a"]
    free = [dict(psfree0), dict(psfree0)]
    ldv = []
    use_done = []
    for och in range(16):
        par = och % 2
        if och >= 2:
            for sem, val in use_done[och - 2].items():
                p.wait("pool", sem, val)
        v = p.inc("pool", f"ldw{par}", 16, lambda E, och=och, par=par: E.dma_start(
            out=wbuf[:, par * 2048:(par + 1) * 2048], in_=w_hbm[och]))
        ldv.append(v)
        for sem, val in free[par].items():
            p.wait("pe", sem, val)
        p.wait("pe", f"ldw{par}", v)
        for kch in range(KCH):
            for half in range(2):
                mm = lambda E, och=och, kch=kch, half=half, par=par: E.matmul(
                    ps_a[:, par * 1024 + half * 512: par * 1024 + (half + 1) * 512],
                    wbuf[:, par * 2048 + kch * 128: par * 2048 + (kch + 1) * 128],
                    xn_bf[:, kch * xn_seg + xn_col0 + half * 512:
                          kch * xn_seg + xn_col0 + (half + 1) * 512],
                    start=(kch == 0), stop=(kch == KCH - 1))
                if kch == KCH - 1 and half == 1:
                    mmv = p.inc("pe", "pe", 1, mm)
                else:
                    p.emit("pe", mm)
        use_done.append({"pe": p.ct["pe"]})
        p.wait("dve", "pe", mmv)
        free[par] = consume(och, mmv)
    return free[0], free[1]


def emit_vproj(p, t, wv_hbm, wvbuf, xn_bf, xn_seg, xn_col0, v_stage, v_out, row0, psfree0):
    ps_a = t["ps_a"]

    def vps(j):
        return ps_a[:, j * 512:(j + 1) * 512]

    free = dict(psfree0)
    use_done = []
    gi = 0
    stv = []
    for halfq in range(2):
        for dvq in range(4):
            par = gi % 2
            if gi >= 2:
                for sem, val in use_done[gi - 2].items():
                    p.wait("pool", sem, val)
            v = p.inc("pool", f"ldw{par}", 16, lambda E, dvq=dvq, par=par: E.dma_start(
                out=wvbuf[:, par * 8192:(par + 1) * 8192], in_=wv_hbm[dvq]))
            for sem, val in free.items():
                p.wait("pe", sem, val)
            p.wait("pe", f"ldw{par}", v)
            for kch in range(KCH):
                for tc in range(4):
                    tok = xn_col0 + halfq * 512 + tc * 128
                    mm = lambda E, kch=kch, tc=tc, tok=tok, par=par: E.matmul(
                        vps(tc),
                        xn_bf[:, kch * xn_seg + tok: kch * xn_seg + tok + 128],
                        wvbuf[:, par * 8192 + kch * 512: par * 8192 + (kch + 1) * 512],
                        start=(kch == 0), stop=(kch == KCH - 1))
                    if kch == KCH - 1 and tc == 3:
                        p.inc("pe", "pe", 1, mm)
                    else:
                        p.emit("pe", mm)
            use_done.append({"pe": p.ct["pe"]})
            p.wait("dve", "pe")
            for tc in range(4):
                p.inc("dve", "dv", 1, lambda E, tc=tc, dvq=dvq: E.tensor_copy(
                    v_stage[:, tc, dvq * 512:(dvq + 1) * 512], vps(tc)))
            free = {"dv": p.ct["dv"]}
            gi += 1
        p.wait("sync", "dv")
        for tc in range(4):
            r0 = row0 + halfq * 512 + tc * 128
            stv.append(p.inc("sync", "sta", 16, lambda E, tc=tc, r0=r0: E.dma_start(
                out=v_out[r0:r0 + 128, :], in_=v_stage[:, tc, :])))
        if halfq == 0:
            p.wait("dve", "sta", stv[-1])
    return free


def build_A():
    nc = bass.Bass("TRN2", target_bir_lowering=False)
    xT = nc.dram_tensor("xT", [E, TQ], F32, kind="ExternalInput").ap()
    wk = nc.dram_tensor("wk", [16, 128, 2048], F32, kind="ExternalInput").ap()
    wv = nc.dram_tensor("wv", [4, 128, 8192], F32, kind="ExternalInput").ap()
    ct = nc.dram_tensor("ct", [128, TQ], BF16, kind="ExternalInput").ap()
    stb = nc.dram_tensor("stb", [128, TQ], BF16, kind="ExternalInput").ap()
    kT_out = nc.dram_tensor("kT_out", [E, TQ], BF16, kind="ExternalOutput").ap()
    v_out = nc.dram_tensor("v_out", [TQ, E], BF16, kind="ExternalOutput").ap()
    kstat_out = nc.dram_tensor("kstat_out", [1, 64], F32, kind="ExternalOutput").ap()

    from contextlib import ExitStack
    ctx = ExitStack()
    ent = ctx.enter_context
    t = {}
    alloc_common(nc, ent, t)
    t["xn_bf"] = ent(nc.sbuf_tensor("xn_bf", [128, 16 * 1024], BF16))
    t["wbuf"] = ent(nc.sbuf_tensor("wbuf", [128, 4096], BF16))
    t["wvbuf"] = ent(nc.sbuf_tensor("wvbuf", [128, 16384], BF16))
    t["v_stage"] = ent(nc.sbuf_tensor("v_stage", [128, 4, 2048], BF16))
    t["kstage"] = ent(nc.sbuf_tensor("kstage", [128, 2048], BF16))
    t["ctab"] = ent(nc.sbuf_tensor("ctab", [128, TQ], BF16))
    t["qraw_bf"] = ent(nc.sbuf_tensor("qraw_bf", [128, 1024], BF16))
    t["tswp_bf"] = ent(nc.sbuf_tensor("tswp_bf", [128, 1024], BF16))
    t["tmpa_bf"] = ent(nc.sbuf_tensor("tmpa_bf", [128, 1024], BF16))
    t["stab"] = ent(nc.sbuf_tensor("stab", [128, TQ], BF16))

    p = Prog(nc, ent)
    emit_init(p, t)
    p.inc("sync", "lds", 16, lambda E: E.dma_start(out=t["ctab"][:], in_=ct))
    tabv = p.inc("sync", "lds", 16, lambda E: E.dma_start(out=t["stab"][:], in_=stb))
    psfree = emit_ln(p, t, xT, t["xn_bf"], 1024, 0)
    p.wait("dve", "lds", tabv)
    p.wait("pool", "lds", tabv)

    kst_store = []
    ksq_done = []

    def consume_k(och, mmv):
        half = och % 2
        if och >= 2:
            p.wait("dve", f"st{half}", kst_store[och - 2])
            p.wait("dve", "ac", ksq_done[och - 2])
        fr, wv_ = emit_rope(p, t, t["ps_a"][:, half * 1024:(half + 1) * 1024],
                            t["ctab"], t["stab"], 0, t["kstage"], half * 1024)
        emit_headnorm(p, t, t["kstage"], half * 1024, och)
        ksq_done.append(p.ct["ac"])
        p.wait("sync", "dv", wv_)
        kst_store.append(p.inc("sync", f"st{half}", 16, lambda E, och=och, half=half: E.dma_start(
            out=kT_out[och * 128:(och + 1) * 128, :],
            in_=t["kstage"][:, half * 1024:(half + 1) * 1024])))
        return {fr[0]: fr[1]}

    f0, f1 = emit_fmproj(p, t, wk, t["wbuf"], t["xn_bf"], 1024, 0, consume_k, psfree)
    both = {k: max(f0.get(k, 0), f1.get(k, 0)) for k in set(f0) | set(f1)}
    emit_vproj(p, t, wv, t["wvbuf"], t["xn_bf"], 1024, 0, t["v_stage"], v_out, 0, both)
    p.wait("sync", "dv")
    p.inc("sync", "sta", 16, lambda E: E.dma_start(out=kstat_out, in_=t["srow"][:]))
    p.wait("sync", "st0")
    p.wait("sync", "st1")
    p.wait("sync", "sta")
    block = ent(nc.Block())
    attach(p, block)
    ctx.close()
    return nc


def build_B(debug=False):
    nc = bass.Bass("TRN2", target_bir_lowering=False)
    xT = nc.dram_tensor("xT", [E, TQ], F32, kind="ExternalInput").ap()
    kT = nc.dram_tensor("kT", [E, S], BF16, kind="ExternalInput").ap()
    vf = nc.dram_tensor("vf", [S, E], BF16, kind="ExternalInput").ap()
    ks = nc.dram_tensor("ks", [4, 64], F32, kind="ExternalInput").ap()
    wq = nc.dram_tensor("wq", [16, 128, 2048], F32, kind="ExternalInput").ap()
    wo = nc.dram_tensor("wo", [16, 128, 2048], F32, kind="ExternalInput").ap()
    gu = nc.dram_tensor("gu", [44, 128, 4096], F32, kind="ExternalInput").ap()
    dn = nc.dram_tensor("dn", [16, 128, 5632], F32, kind="ExternalInput").ap()
    gs = nc.dram_tensor("gs", [128, 44], F32, kind="ExternalInput").ap()
    us = nc.dram_tensor("us", [128, 44], F32, kind="ExternalInput").ap()
    ds = nc.dram_tensor("ds", [128, 16], F32, kind="ExternalInput").ap()
    gam = nc.dram_tensor("gam", [128, 1], F32, kind="ExternalInput").ap()
    lq1 = nc.dram_tensor("lq1", [1, 64], F32, kind="ExternalInput").ap()
    lk1 = nc.dram_tensor("lk1", [1, 64], F32, kind="ExternalInput").ap()
    lq2 = nc.dram_tensor("lq2", [1, 64], F32, kind="ExternalInput").ap()
    lk2 = nc.dram_tensor("lk2", [1, 64], F32, kind="ExternalInput").ap()
    ct = nc.dram_tensor("ct", [128, TQ], BF16, kind="ExternalInput").ap()
    stb = nc.dram_tensor("stb", [128, TQ], BF16, kind="ExternalInput").ap()
    outT = nc.dram_tensor("outT", [E, TQ], F32, kind="ExternalOutput").ap()
    dbg = {}
    if debug:
        for nm, shp, dt in (("d_xn", [E, TQ], BF16), ("d_q", [E, TQ], BF16),
                            ("d_attn", [E, TQ], BF16), ("d_acc", [E, TQ], F32),
                            ("d_bias", [128, 16], F32), ("d_srow", [1, 64], F32)):
            dbg[nm] = nc.dram_tensor(nm, shp, dt, kind="ExternalOutput").ap()

    from contextlib import ExitStack
    ctx = ExitStack()
    ent = ctx.enter_context
    t = {}
    t["ps_a"] = ent(nc.psum_tensor("ps_a", [128, 2048], F32))
    t["ps_b"] = ent(nc.psum_tensor("ps_b", [128, 1024], F32))
    t["ps_d1"] = ent(nc.psum_tensor("ps_d1", [1, 512], F32))
    t["ps_d2"] = ent(nc.psum_tensor("ps_d2", [1, 512], F32))
    t["ones_col_bf"] = ent(nc.sbuf_tensor("ones_col_bf", [128, 1], BF16))
    t["ones_row_f"] = ent(nc.sbuf_tensor("ones_row_f", [1, 128], F32))
    t["neg_row_f"] = ent(nc.sbuf_tensor("neg_row_f", [1, 128], F32))
    t["ones128_bf"] = ent(nc.sbuf_tensor("ones128_bf", [128, 128], BF16))
    t["mask1_bf"] = ent(nc.sbuf_tensor("mask1_bf", [128, 1], BF16))
    t["mask2_bf"] = ent(nc.sbuf_tensor("mask2_bf", [128, 1], BF16))
    t["tmp0"] = ent(nc.sbuf_tensor("tmp0", [128, 1024], F32))
    t["tmp1"] = ent(nc.sbuf_tensor("tmp1", [128, 1024], F32))
    t["srow"] = ent(nc.sbuf_tensor("srow", [1, 64], F32))
    t["trow"] = ent(nc.sbuf_tensor("trow", [1, 16], F32))
    t["eps_col"] = ent(nc.sbuf_tensor("eps_col", [128, 1], F32))
    t["nrow"] = ent(nc.sbuf_tensor("nrow", [1, 1024], F32))
    t["bias_sb"] = ent(nc.sbuf_tensor("bias_sb", [128, 16], F32))
    t["lam_vec"] = ent(nc.sbuf_tensor("lam_vec", [128, 1], F32))
    t["gam_eff"] = ent(nc.sbuf_tensor("gam_eff", [128, 1], F32))
    t["gs_sb"] = ent(nc.sbuf_tensor("gs_sb", [128, 44], F32))
    t["us_sb"] = ent(nc.sbuf_tensor("us_sb", [128, 44], F32))
    t["ds_sb"] = ent(nc.sbuf_tensor("ds_sb", [128, 16], F32))
    t["lrow"] = ent(nc.sbuf_tensor("lrow", [1, 256], F32))
    t["krow"] = ent(nc.sbuf_tensor("krow", [1, 256], F32))
    t["xn_bf"] = ent(nc.sbuf_tensor("xn_bf", [128, 16 * 1024], BF16))
    t["q_bf"] = ent(nc.sbuf_tensor("q_bf", [128, 16 * 1024], BF16))
    t["acc"] = ent(nc.sbuf_tensor("acc", [128, 16 * 1024], F32))

    p = Prog(nc, ent)
    emit_init(p, t)
    p.wait("pe", "pl")
    for name, src_ap in (("gs_sb", gs), ("us_sb", us), ("ds_sb", ds)):
        p.inc("sync", "lds", 16, lambda E, src_ap=src_ap, name=name: E.dma_start(
            out=t[name][:], in_=src_ap))
    p.inc("sync", "lds", 16, lambda E: E.dma_start(out=t["lrow"][0:1, 0:64], in_=lq1))
    p.inc("sync", "lds", 16, lambda E: E.dma_start(out=t["lrow"][0:1, 64:128], in_=lk1))
    p.inc("sync", "lds", 16, lambda E: E.dma_start(out=t["lrow"][0:1, 128:192], in_=lq2))
    p.inc("sync", "lds", 16, lambda E: E.dma_start(out=t["lrow"][0:1, 192:256], in_=lk2))
    for r in range(4):
        p.inc("sync", "lds", 16, lambda E, r=r: E.dma_start(
            out=t["krow"][0:1, r * 64:(r + 1) * 64], in_=ks[r:r + 1, :]))
    gamv = p.inc("sync", "lds", 16, lambda E: E.dma_start(out=t["tmp0"][:, 0:1], in_=gam))
    p.wait("dve", "lds", gamv)
    p.wait("pool", "lds", gamv)
    p.wait("act", "lds", gamv)
    p.inc("dve", "dv", 1, lambda E: E.tensor_scalar(
        t["gam_eff"][:], t["tmp0"][:, 0:1], ONE_MINUS_LI, 0.0, ALU.mult, ALU.add))
    p.inc("dve", "dv", 1, lambda E: E.tensor_tensor(
        t["lrow"][0:1, 0:64], t["lrow"][0:1, 0:64], t["lrow"][0:1, 64:128], ALU.mult))
    p.inc("dve", "dv", 1, lambda E: E.tensor_tensor(
        t["lrow"][0:1, 128:192], t["lrow"][0:1, 128:192], t["lrow"][0:1, 192:256], ALU.mult))
    p.inc("dve", "dv", 1, lambda E: E.tensor_reduce(
        t["trow"][0:1, 0:1], t["lrow"][0:1, 0:64], AX.X, ALU.add))
    p.inc("dve", "dv", 1, lambda E: E.tensor_reduce(
        t["trow"][0:1, 1:2], t["lrow"][0:1, 128:192], AX.X, ALU.add))
    p.wait("act", "dv")
    p.inc("act", "ac", 1, lambda E: E.activation(t["trow"][0:1, 2:3], t["trow"][0:1, 0:1], AF.Exp))
    p.inc("act", "ac", 1, lambda E: E.activation(t["trow"][0:1, 3:4], t["trow"][0:1, 1:2], AF.Exp))
    p.wait("dve", "ac")
    p.inc("dve", "dv", 1, lambda E: E.tensor_tensor(
        t["trow"][0:1, 2:3], t["trow"][0:1, 2:3], t["trow"][0:1, 3:4], ALU.subtract))
    lamv = p.inc("dve", "dv", 1, lambda E: E.tensor_scalar(
        t["trow"][0:1, 2:3], t["trow"][0:1, 2:3], 1.0, LAMBDA_INIT, ALU.mult, ALU.add))
    p.wait("pe", "dv", lamv)
    lbv = p.inc("pe", "pe", 1, lambda E: E.matmul(
        t["ps_b"][:, 0:1], t["ones_row_f"][:], t["trow"][0:1, 2:3], start=True, stop=True))
    p.wait("dve", "pe", lbv)
    p.inc("dve", "dv", 1, lambda E: E.tensor_copy(t["lam_vec"][:], t["ps_b"][:, 0:1]))

    ctx1 = ExitStack()
    ent1 = ctx1.enter_context
    t["xc"] = ent1(nc.sbuf_tensor("xc", [128, 4096], F32, side="right"))
    t["xbb"] = ent1(nc.sbuf_tensor("xbb", [128, 4096], BF16, side="right"))
    t["x2b"] = ent1(nc.sbuf_tensor("x2b", [128, 4096], BF16, side="right"))
    t["row"] = ent1(nc.sbuf_tensor("row", [1, 2048], F32, side="right"))
    t["sq_b"] = ent1(nc.sbuf_tensor("sq_b", [128, 1024], BF16, side="right"))
    t["ctab"] = ent1(nc.sbuf_tensor("ctab", [128, TQ], BF16, side="right"))
    t["qraw_bf"] = ent1(nc.sbuf_tensor("qraw_bf", [128, 1024], BF16, side="right"))
    t["tswp_bf"] = ent1(nc.sbuf_tensor("tswp_bf", [128, 1024], BF16, side="right"))
    t["tmpa_bf"] = ent1(nc.sbuf_tensor("tmpa_bf", [128, 1024], BF16, side="right"))
    t["stab"] = ent1(nc.sbuf_tensor("stab", [128, TQ], BF16, side="right"))
    t["wbuf"] = ent1(nc.sbuf_tensor("wbuf", [128, 4096], BF16, side="right"))

    p.inc("sync", "lds", 16, lambda E: E.dma_start(out=t["ctab"][:], in_=ct))
    tabv = p.inc("sync", "lds", 16, lambda E: E.dma_start(out=t["stab"][:], in_=stb))
    psfree = emit_ln(p, t, xT, t["xn_bf"], 1024, 0)
    p.wait("dve", "lds", tabv)
    p.wait("pool", "lds", tabv)
    if debug:
        p.wait("sync", "dv")
        for kch in range(KCH):
            p.inc("sync", "sta", 16, lambda E, kch=kch: E.dma_start(
                out=dbg["d_xn"][kch * 128:(kch + 1) * 128, :],
                in_=t["xn_bf"][:, kch * 1024:(kch + 1) * 1024]))

    def consume_q(och, mmv):
        fr, wv_ = emit_rope(p, t, t["ps_a"][:, (och % 2) * 1024:(och % 2 + 1) * 1024],
                            t["ctab"], t["stab"], 0, t["q_bf"], och * 1024)
        emit_headnorm(p, t, t["q_bf"], och * 1024, och)
        return {fr[0]: fr[1]}

    emit_fmproj(p, t, wq, t["wbuf"], t["xn_bf"], 1024, 0, consume_q, psfree)
    if debug:
        p.wait("sync", "dv")
        for kch in range(KCH):
            p.inc("sync", "sta", 16, lambda E, kch=kch: E.dma_start(
                out=dbg["d_q"][kch * 128:(kch + 1) * 128, :],
                in_=t["q_bf"][:, kch * 1024:(kch + 1) * 1024]))
        p.inc("sync", "sta", 16, lambda E: E.dma_start(out=dbg["d_srow"], in_=t["srow"][:]))

    p.wait("dve", "lds")
    for r in (1, 2, 3):
        p.inc("dve", "dv", 1, lambda E, r=r: E.tensor_tensor(
            t["krow"][0:1, 0:64], t["krow"][0:1, 0:64],
            t["krow"][0:1, r * 64:(r + 1) * 64], ALU.max))
    p.inc("dve", "dv", 1, lambda E: E.tensor_tensor(
        t["lrow"][0:1, 0:16], t["srow"][0:1, 0:16], t["krow"][0:1, 0:16], ALU.mult))
    p.inc("dve", "dv", 1, lambda E: E.tensor_tensor(
        t["lrow"][0:1, 16:32], t["srow"][0:1, 16:32], t["krow"][0:1, 16:32], ALU.mult))
    p.inc("dve", "dv", 1, lambda E: E.tensor_tensor(
        t["lrow"][0:1, 64:80], t["lrow"][0:1, 0:16], t["lrow"][0:1, 16:32], ALU.max))
    p.wait("act", "dv")
    brv = p.inc("act", "ac", 1, lambda E: E.activation(
        t["lrow"][0:1, 128:144], t["lrow"][0:1, 64:80], AF.Sqrt, scale=1.0 / 64.0))
    p.wait("pe", "ac", brv)
    p.wait("pe", "dv")
    bbv = p.inc("pe", "pe", 1, lambda E: E.matmul(
        t["ps_b"][:, 0:16], t["neg_row_f"][:], t["lrow"][0:1, 128:144], start=True, stop=True))
    p.wait("dve", "pe", bbv)
    biasv = p.inc("dve", "dv", 1, lambda E: E.tensor_copy(t["bias_sb"][:], t["ps_b"][:, 0:16]))
    if debug:
        p.wait("sync", "dv", biasv)
        p.inc("sync", "sta", 16, lambda E: E.dma_start(out=dbg["d_bias"], in_=t["bias_sb"][:]))
    ctx1.close()

    ctx2 = ExitStack()
    ent2 = ctx2.enter_context
    scope_barrier(p)
    t["h_bf"] = ent2(nc.sbuf_tensor("h_bf", [128, ICH * 512], BF16, side="right"))
    t["wffn"] = ent2(nc.sbuf_tensor("wffn", [128, 8448], BF16, side="right"))
    gring = Ring(p, "rg", 2, "pool")
    uring = Ring(p, "ru", 2, "pool")
    dring = Ring(p, "rd", 3, "pool")
    wffn = t["wffn"]
    UOFF = 4096
    for tile in range(2):
        gufree = {"dv": p.ct["dv"]}
        p.wait("dve", "pe")
        p.wait("pool", "pe")
        for ich in range(ICH):
            par = ich % 2
            gi, gsl, gkey = gring.load(lambda s: (lambda E, ich=ich, s=s: E.dma_start(
                out=wffn[:, s * 2048:(s + 1) * 2048], in_=gu[ich][:, 0:2048])))
            ui, usl, ukey = uring.load(lambda s: (lambda E, ich=ich, s=s: E.dma_start(
                out=wffn[:, UOFF + s * 2048:UOFF + (s + 1) * 2048], in_=gu[ich][:, 2048:4096])))
            for sem, val in gufree.items():
                p.wait("pe", sem, val)
            p.wait("pe", *gkey)
            p.wait("pe", *ukey)
            for kch in range(KCH):
                p.emit("pe", lambda E, kch=kch, par=par, tile=tile, gsl=gsl: E.matmul(
                    t["ps_a"][:, par * 1024: par * 1024 + 512],
                    wffn[:, gsl * 2048 + kch * 128: gsl * 2048 + (kch + 1) * 128],
                    t["xn_bf"][:, kch * 1024 + tile * 512: kch * 1024 + (tile + 1) * 512],
                    start=(kch == 0), stop=(kch == KCH - 1)))
                mm = lambda E, kch=kch, par=par, tile=tile, usl=usl: E.matmul(
                    t["ps_a"][:, par * 1024 + 512: par * 1024 + 1024],
                    wffn[:, UOFF + usl * 2048 + kch * 128: UOFF + usl * 2048 + (kch + 1) * 128],
                    t["xn_bf"][:, kch * 1024 + tile * 512: kch * 1024 + (tile + 1) * 512],
                    start=(kch == 0), stop=(kch == KCH - 1))
                if kch == KCH - 1:
                    gv = p.inc("pe", "pe", 1, mm)
                else:
                    p.emit("pe", mm)
            gring.consumed(gi, "pe")
            uring.consumed(ui, "pe")
            p.wait("act", "pe", gv)
            p.wait("act", "dv")
            p.inc("act", "ac", 1, lambda E, par=par, ich=ich: E.activation(
                t["tmp1"][:, 512:1024],
                t["ps_a"][:, par * 1024: par * 1024 + 512],
                AF.Silu, scale=t["gs_sb"][:, ich:ich + 1]))
            p.wait("dve", "ac")
            p.inc("dve", "dv", 1, lambda E, par=par, ich=ich: E.tensor_scalar(
                t["tmp0"][:, 0:512], t["ps_a"][:, par * 1024 + 512: par * 1024 + 1024],
                t["us_sb"][:, ich:ich + 1], 0.0, ALU.mult, ALU.add))
            p.inc("dve", "dv", 1, lambda E, par=par, ich=ich: E.tensor_tensor(
                t["h_bf"][:, ich * 512:(ich + 1) * 512], t["tmp0"][:, 0:512],
                t["tmp1"][:, 512:1024], ALU.mult))
            gufree = {"dv": p.ct["dv"]}
        dfree = [{"dv": p.ct["dv"]}, {"dv": p.ct["dv"]}]
        p.wait("pool", "pe")
        for hch in range(16):
            par = hch % 2
            d0, s0, k0 = dring.load(lambda s: (lambda E, hch=hch, s=s: E.dma_start(
                out=wffn[:, s * 2816:(s + 1) * 2816], in_=dn[hch][:, 0:2816])))
            d1, s1, k1 = dring.load(lambda s: (lambda E, hch=hch, s=s: E.dma_start(
                out=wffn[:, s * 2816:(s + 1) * 2816], in_=dn[hch][:, 2816:5632])))
            for sem, val in dfree[par].items():
                p.wait("pe", sem, val)
            if hch == 0:
                p.wait("pe", "dv", gufree["dv"])
            p.wait("pe", *k0)
            for ich in range(22):
                p.emit("pe", lambda E, ich=ich, par=par, s0=s0: E.matmul(
                    t["ps_b"][:, par * 512:(par + 1) * 512],
                    wffn[:, s0 * 2816 + ich * 128: s0 * 2816 + (ich + 1) * 128],
                    t["h_bf"][:, ich * 512:(ich + 1) * 512],
                    start=(ich == 0), stop=False))
            p.wait("pe", *k1)
            for ich in range(22, ICH):
                mm = lambda E, ich=ich, par=par, s1=s1: E.matmul(
                    t["ps_b"][:, par * 512:(par + 1) * 512],
                    wffn[:, s1 * 2816 + (ich - 22) * 128: s1 * 2816 + (ich - 21) * 128],
                    t["h_bf"][:, ich * 512:(ich + 1) * 512],
                    start=False, stop=(ich == ICH - 1))
                if ich == ICH - 1:
                    dv_ = p.inc("pe", "pe", 1, mm)
                else:
                    p.emit("pe", mm)
            dring.consumed(d0, "pe")
            dring.consumed(d1, "pe")
            p.wait("dve", "pe", dv_)
            p.inc("dve", "dv", 1, lambda E, hch=hch, par=par, tile=tile: E.tensor_scalar(
                t["acc"][:, hch * 1024 + tile * 512: hch * 1024 + (tile + 1) * 512],
                t["ps_b"][:, par * 512:(par + 1) * 512],
                t["ds_sb"][:, hch:hch + 1], 0.0, ALU.mult, ALU.add))
            dfree[par] = {"dv": p.ct["dv"]}
    ctx2.close()

    ctx3 = ExitStack()
    ent3 = ctx3.enter_context
    scope_barrier(p)
    t["e12"] = ent3(nc.sbuf_tensor("e12", [128, 2048], BF16, side="right"))
    t["ktb"] = ent3(nc.sbuf_tensor("ktb", [128, 8192], BF16, side="right"))
    t["vtb"] = ent3(nc.sbuf_tensor("vtb", [128, 8192], BF16, side="right"))
    t["r1B"] = ent3(nc.sbuf_tensor("r1B", [128, 512], F32, side="right"))
    t["r2B"] = ent3(nc.sbuf_tensor("r2B", [128, 512], F32, side="right"))
    t["uB"] = ent3(nc.sbuf_tensor("uB", [128, 512], F32, side="right"))
    t["rstdB"] = ent3(nc.sbuf_tensor("rstdB", [128, 512], F32, side="right"))
    t["sqa"] = ent3(nc.sbuf_tensor("sqa", [128, 512], BF16, side="right"))
    p.wait("act", "dv", biasv)
    attn_bf = t["xn_bf"]
    ps_a, ps_b, ps_d1, ps_d2 = t["ps_a"], t["ps_b"], t["ps_d1"], t["ps_d2"]
    kv_use = []
    ac_free = p.ct["ac"]
    dv_free = p.ct["dv"]
    for h in range(16):
        par_h = h % 2
        if h >= 2:
            p.wait("sync", "pe", kv_use[h - 2])
        v1 = p.inc("sync", f"ldk{par_h}", 16, lambda E, h=h, par_h=par_h: E.dma_start(
            out=t["ktb"][:, par_h * 4096:(par_h + 1) * 4096],
            in_=kT[h * 128:(h + 1) * 128, :]))
        v2 = p.inc("sync", f"ldk{par_h}", 16, lambda E, h=h, par_h=par_h: E.dma_start(
            out=t["vtb"].ap()[:, par_h * 4096:(par_h + 1) * 4096].rearrange(
                "p (b d) -> p b d", b=32),
            in_=vf.rearrange("(b q) d -> q b d", q=128)[:, :, h * 128:(h + 1) * 128]))
        for tile in range(2):
            koff = par_h * 4096
            qcol = h * 1024 + tile * 512
            p.wait("pe", f"ldk{par_h}", v2)
            p.wait("pe", "ac", ac_free)
            p.wait("pe", "dv", dv_free)
            for b in range(32):
                par = b % 2
                p.emit("pe", lambda E, b=b, par=par, koff=koff, qcol=qcol: E.matmul(
                    ps_a[:, par * 1024: par * 1024 + 512],
                    t["ktb"][0:64, koff + b * 128: koff + (b + 1) * 128],
                    t["q_bf"][0:64, qcol:qcol + 512],
                    start=True, stop=True, tile_position=(0, 0)))
                sv = p.inc("pe", "pe", 1, lambda E, b=b, par=par, koff=koff, qcol=qcol: E.matmul(
                    ps_a[:, par * 1024 + 512: par * 1024 + 1024],
                    t["ktb"][64:128, koff + b * 128: koff + (b + 1) * 128],
                    t["q_bf"][64:128, qcol:qcol + 512],
                    start=True, stop=True, tile_position=(64, 0)))
                p.wait("act", "pe", sv)
                ev = p.inc("act", "ac", 1, lambda E, par=par, h=h: E.activation(
                    t["e12"][:, par * 1024:(par + 1) * 1024],
                    ps_a[:, par * 1024:(par + 1) * 1024],
                    AF.Exp, bias=t["bias_sb"][:, h:h + 1], scale=0.125))
                p.wait("pe", "ac", ev)
                first, last = (b == 0), (b == 31)
                p.emit("pe", lambda E, b=b, par=par, koff=koff, first=first, last=last: E.matmul(
                    ps_b[:, 0:512],
                    t["vtb"][:, koff + b * 128: koff + (b + 1) * 128],
                    t["e12"][:, par * 1024: par * 1024 + 512],
                    start=first, stop=last))
                p.emit("pe", lambda E, b=b, par=par, koff=koff, first=first, last=last: E.matmul(
                    ps_b[:, 512:1024],
                    t["vtb"][:, koff + b * 128: koff + (b + 1) * 128],
                    t["e12"][:, par * 1024 + 512: par * 1024 + 1024],
                    start=first, stop=last))
                p.emit("pe", lambda E, par=par, first=first, last=last: E.matmul(
                    ps_d1[:], t["ones_col_bf"][:],
                    t["e12"][:, par * 1024: par * 1024 + 512],
                    start=first, stop=last))
                mmD = lambda E, par=par, first=first, last=last: E.matmul(
                    ps_d2[:], t["ones_col_bf"][:],
                    t["e12"][:, par * 1024 + 512: par * 1024 + 1024],
                    start=first, stop=last)
                if last:
                    pvv = p.inc("pe", "pe", 1, mmD)
                else:
                    p.emit("pe", mmD)
            p.wait("act", "pe", pvv)
            p.inc("act", "ac", 1, lambda E: E.activation(t["tmp1"][0:1, 0:512], ps_d1[:], AF.Copy))
            dcp = p.inc("act", "ac", 1, lambda E: E.activation(t["tmp1"][0:1, 512:1024], ps_d2[:], AF.Copy))
            p.wait("pe", "ac", dcp)
            p.emit("pe", lambda E: E.matmul(ps_a[:, 0:512], t["ones_row_f"][:],
                                            t["tmp1"][0:1, 0:512], start=True, stop=True))
            bv = p.inc("pe", "pe", 1, lambda E: E.matmul(
                ps_a[:, 512:1024], t["ones_row_f"][:], t["tmp1"][0:1, 512:1024],
                start=True, stop=True))
            p.wait("act", "pe", bv)
            p.inc("act", "ac", 1, lambda E: E.activation(t["uB"][:], ps_a[:, 0:512], AF.Ln))
            p.inc("act", "ac", 1, lambda E: E.activation(t["r1B"][:], t["uB"][:], AF.Exp, scale=-1.0))
            p.inc("act", "ac", 1, lambda E: E.activation(t["uB"][:], ps_a[:, 512:1024], AF.Ln))
            rv = p.inc("act", "ac", 1, lambda E: E.activation(t["r2B"][:], t["uB"][:], AF.Exp, scale=-1.0))
            p.wait("dve", "ac", rv)
            p.inc("dve", "dv", 1, lambda E: E.tensor_scalar(
                t["r2B"][:], t["r2B"][:], t["lam_vec"][:], 0.0, ALU.mult, ALU.add))
            p.inc("dve", "dv", 1, lambda E: E.tensor_tensor(
                t["tmp0"][:, 0:512], ps_b[:, 0:512], t["r1B"][:], ALU.mult))
            p.inc("dve", "dv", 1, lambda E: E.tensor_tensor(
                t["tmp0"][:, 512:1024], ps_b[:, 512:1024], t["r2B"][:], ALU.mult))
            yv = p.inc("dve", "dv", 1, lambda E: E.tensor_tensor(
                t["tmp0"][:, 0:512], t["tmp0"][:, 0:512], t["tmp0"][:, 512:1024], ALU.subtract))
            p.wait("act", "dv", yv)
            sqv = p.inc("act", "ac", 1, lambda E: E.activation(
                t["sqa"][:], t["tmp0"][:, 0:512], AF.Square))
            p.wait("pe", "ac", sqv)
            msv = p.inc("pe", "pe", 1, lambda E: E.matmul(
                ps_a[:, 1024:1536], t["ones128_bf"][:], t["sqa"][:], start=True, stop=True))
            p.wait("act", "pe", msv)
            p.inc("act", "ac", 1, lambda E: E.activation(
                t["uB"][:], ps_a[:, 1024:1536], AF.Ln, scale=1.0 / 128.0, bias=t["eps_col"][:]))
            rsv = p.inc("act", "ac", 1, lambda E: E.activation(
                t["rstdB"][:], t["uB"][:], AF.Exp, scale=-0.5))
            p.wait("dve", "ac", rsv)
            p.inc("dve", "dv", 1, lambda E: E.tensor_tensor(
                t["tmp1"][:, 0:512], t["tmp0"][:, 0:512], t["rstdB"][:], ALU.mult))
            dvf = p.inc("dve", "dv", 1, lambda E, h=h, tile=tile: E.tensor_scalar(
                attn_bf[:, h * 1024 + tile * 512: h * 1024 + (tile + 1) * 512],
                t["tmp1"][:, 0:512], t["gam_eff"][:], 0.0, ALU.mult, ALU.add))
            ac_free = rsv
            dv_free = dvf
        kv_use.append(p.ct["pe"])
    ctx3.close()

    if debug:
        p.wait("sync", "dv")
        for kch in range(KCH):
            p.inc("sync", "sta", 16, lambda E, kch=kch: E.dma_start(
                out=dbg["d_attn"][kch * 128:(kch + 1) * 128, :],
                in_=attn_bf[:, kch * 1024:(kch + 1) * 1024]))

    ctx4 = ExitStack()
    ent4 = ctx4.enter_context
    scope_barrier(p)
    t["wbuf2"] = ent4(nc.sbuf_tensor("wbuf2", [128, 4096], BF16, side="right"))
    wring = Ring(p, "rw", 2, "pool")
    wofree = [{"ac": ac_free, "dv": dv_free}, {"ac": ac_free, "dv": dv_free}]
    for och in range(16):
        par = och % 2
        wi, wsl, wkey = wring.load(lambda s: (lambda E, och=och, s=s: E.dma_start(
            out=t["wbuf2"][:, s * 2048:(s + 1) * 2048], in_=wo[och])))
        for sem, val in wofree[par].items():
            p.wait("pe", sem, val)
        p.wait("pe", *wkey)
        for kch in range(KCH):
            for half in range(2):
                mm = lambda E, och=och, kch=kch, half=half, wsl=wsl: E.matmul(
                    ps_a[:, (och % 2) * 1024 + half * 512: (och % 2) * 1024 + (half + 1) * 512],
                    t["wbuf2"][:, wsl * 2048 + kch * 128: wsl * 2048 + (kch + 1) * 128],
                    attn_bf[:, kch * 1024 + half * 512: kch * 1024 + (half + 1) * 512],
                    start=(kch == 0), stop=(kch == KCH - 1))
                if kch == KCH - 1 and half == 1:
                    ov = p.inc("pe", "pe", 1, mm)
                else:
                    p.emit("pe", mm)
        wring.consumed(wi, "pe")
        p.wait("dve", "pe", ov)
        fv = p.inc("dve", "dv", 1, lambda E, och=och, par=par: E.tensor_tensor(
            t["acc"][:, och * 1024:(och + 1) * 1024],
            t["acc"][:, och * 1024:(och + 1) * 1024],
            ps_a[:, par * 1024:(par + 1) * 1024], ALU.add))
        wofree[par] = {"dv": fv}
    ctx4.close()

    if debug:
        p.wait("sync", "dv")
        for kch in range(KCH):
            p.inc("sync", "sta", 16, lambda E, kch=kch: E.dma_start(
                out=dbg["d_acc"][kch * 128:(kch + 1) * 128, :],
                in_=t["acc"][:, kch * 1024:(kch + 1) * 1024]))

    ctx5 = ExitStack()
    ent5 = ctx5.enter_context
    scope_barrier(p)
    t["xe"] = ent5(nc.sbuf_tensor("xe", [128, 2048], F32, side="right"))
    t["out_sb"] = ent5(nc.sbuf_tensor("out_sb", [128, 2048], F32, side="right"))
    stv = []
    for och in range(16):
        par = och % 2
        if och >= 2:
            p.wait("sync", f"st{par}", stv[och - 2])
        v = p.inc("sync", f"ldx{par}", 16, lambda E, och=och, par=par: E.dma_start(
            out=t["xe"][:, par * 1024:(par + 1) * 1024],
            in_=xT[och * 128:(och + 1) * 128, :]))
        p.wait("dve", f"ldx{par}", v)
        p.wait("dve", "dv", wofree[par].get("dv", 0))
        ev = p.inc("dve", "dv", 1, lambda E, och=och, par=par: E.tensor_tensor(
            t["out_sb"][:, par * 1024:(par + 1) * 1024],
            t["acc"][:, och * 1024:(och + 1) * 1024],
            t["xe"][:, par * 1024:(par + 1) * 1024], ALU.add))
        p.wait("sync", "dv", ev)
        stv.append(p.inc("sync", f"st{par}", 16, lambda E, och=och, par=par: E.dma_start(
            out=outT[och * 128:(och + 1) * 128, :],
            in_=t["out_sb"][:, par * 1024:(par + 1) * 1024])))
    ctx5.close()
    p.wait("sync", "st0")
    p.wait("sync", "st1")
    if debug:
        p.wait("sync", "sta")
    block = ent(nc.Block())
    attach(p, block)
    ctx.close()
    return nc


def build_F(debug=False):
    nc = bass.Bass("TRN2", target_bir_lowering=False)
    xTb = nc.dram_tensor("xTb", [E, S], F32, kind="ExternalInput").ap()
    wq = nc.dram_tensor("wq", [16, 128, 2048], F32, kind="ExternalInput").ap()
    wk = nc.dram_tensor("wk", [16, 128, 2048], F32, kind="ExternalInput").ap()
    wo = nc.dram_tensor("wo", [16, 128, 2048], F32, kind="ExternalInput").ap()
    wv = nc.dram_tensor("wv", [4, 128, 8192], F32, kind="ExternalInput").ap()
    gu = nc.dram_tensor("gu", [44, 128, 4096], F32, kind="ExternalInput").ap()
    dn = nc.dram_tensor("dn", [16, 128, 5632], F32, kind="ExternalInput").ap()
    gs = nc.dram_tensor("gs", [128, 44], F32, kind="ExternalInput").ap()
    us = nc.dram_tensor("us", [128, 44], F32, kind="ExternalInput").ap()
    ds = nc.dram_tensor("ds", [128, 16], F32, kind="ExternalInput").ap()
    gam = nc.dram_tensor("gam", [128, 1], F32, kind="ExternalInput").ap()
    lq1 = nc.dram_tensor("lq1", [1, 64], F32, kind="ExternalInput").ap()
    lk1 = nc.dram_tensor("lk1", [1, 64], F32, kind="ExternalInput").ap()
    lq2 = nc.dram_tensor("lq2", [1, 64], F32, kind="ExternalInput").ap()
    lk2 = nc.dram_tensor("lk2", [1, 64], F32, kind="ExternalInput").ap()
    ctb = nc.dram_tensor("ctb", [128, S], BF16, kind="ExternalInput").ap()
    stbt = nc.dram_tensor("stbt", [128, S], BF16, kind="ExternalInput").ap()
    outT = nc.dram_tensor("outT", [E, TQ], F32, kind="ExternalOutput").ap()
    kT_s = nc.dram_tensor("kT_s", [E, S], BF16, kind="Internal").ap()
    v_s = nc.dram_tensor("v_s", [S, E], BF16, kind="Internal").ap()
    wk_c = nc.dram_tensor("wk_c", [16, 128, 2048], BF16, kind="Internal").ap()
    wv_c = nc.dram_tensor("wv_c", [4, 128, 8192], BF16, kind="Internal").ap()
    dbg = {}
    if debug:
        for nm, shp, dt in (("d_xn", [E, TQ], BF16), ("d_q", [E, TQ], BF16),
                            ("d_attn", [E, TQ], BF16), ("d_kt", [E, S], BF16),
                            ("d_vs", [S, E], BF16), ("d_srow", [1, 64], F32)):
            dbg[nm] = nc.dram_tensor(nm, shp, dt, kind="ExternalOutput").ap()

    from contextlib import ExitStack
    ctx = ExitStack()
    ent = ctx.enter_context
    t = {}
    t["ps_a"] = ent(nc.psum_tensor("ps_a", [128, 2048], F32))
    t["ps_b"] = ent(nc.psum_tensor("ps_b", [128, 1024], F32))
    t["ps_d1"] = ent(nc.psum_tensor("ps_d1", [1, 512], F32))
    t["ps_d2"] = ent(nc.psum_tensor("ps_d2", [1, 512], F32))
    t["ones_col_bf"] = ent(nc.sbuf_tensor("ones_col_bf", [128, 1], BF16))
    t["ones_row_f"] = ent(nc.sbuf_tensor("ones_row_f", [1, 128], F32))
    t["neg_row_f"] = ent(nc.sbuf_tensor("neg_row_f", [1, 128], F32))
    t["ones128_bf"] = ent(nc.sbuf_tensor("ones128_bf", [128, 128], BF16))
    t["mask1_bf"] = ent(nc.sbuf_tensor("mask1_bf", [128, 1], BF16))
    t["mask2_bf"] = ent(nc.sbuf_tensor("mask2_bf", [128, 1], BF16))
    t["tmp0"] = ent(nc.sbuf_tensor("tmp0", [128, 1024], F32))
    t["tmp1"] = ent(nc.sbuf_tensor("tmp1", [128, 1024], F32))
    t["srow"] = ent(nc.sbuf_tensor("srow", [1, 64], F32))
    t["trow"] = ent(nc.sbuf_tensor("trow", [1, 16], F32))
    t["eps_col"] = ent(nc.sbuf_tensor("eps_col", [128, 1], F32))
    t["nrow"] = ent(nc.sbuf_tensor("nrow", [1, 1024], F32))
    t["bias_sb"] = ent(nc.sbuf_tensor("bias_sb", [128, 16], F32))
    t["lam_vec"] = ent(nc.sbuf_tensor("lam_vec", [128, 1], F32))
    t["gam_eff"] = ent(nc.sbuf_tensor("gam_eff", [128, 1], F32))
    t["gs_sb"] = ent(nc.sbuf_tensor("gs_sb", [128, 44], F32))
    t["us_sb"] = ent(nc.sbuf_tensor("us_sb", [128, 44], F32))
    t["ds_sb"] = ent(nc.sbuf_tensor("ds_sb", [128, 16], F32))
    t["lrow"] = ent(nc.sbuf_tensor("lrow", [1, 256], F32))
    t["xn_bf"] = ent(nc.sbuf_tensor("xn_bf", [128, 16 * 1024], BF16))

    p = Prog(nc, ent)
    emit_init(p, t)
    p.wait("pe", "pl")
    for name, src_ap in (("gs_sb", gs), ("us_sb", us), ("ds_sb", ds)):
        p.inc("sync", "lds", 16, lambda E, src_ap=src_ap, name=name: E.dma_start(
            out=t[name][:], in_=src_ap))
    p.inc("sync", "lds", 16, lambda E: E.dma_start(out=t["lrow"][0:1, 0:64], in_=lq1))
    p.inc("sync", "lds", 16, lambda E: E.dma_start(out=t["lrow"][0:1, 64:128], in_=lk1))
    p.inc("sync", "lds", 16, lambda E: E.dma_start(out=t["lrow"][0:1, 128:192], in_=lq2))
    p.inc("sync", "lds", 16, lambda E: E.dma_start(out=t["lrow"][0:1, 192:256], in_=lk2))
    gamv = p.inc("sync", "lds", 16, lambda E: E.dma_start(out=t["tmp0"][:, 0:1], in_=gam))
    p.wait("dve", "lds", gamv)
    p.wait("pool", "lds", gamv)
    p.wait("act", "lds", gamv)
    p.inc("dve", "dv", 1, lambda E: E.tensor_scalar(
        t["gam_eff"][:], t["tmp0"][:, 0:1], ONE_MINUS_LI, 0.0, ALU.mult, ALU.add))
    p.inc("dve", "dv", 1, lambda E: E.tensor_tensor(
        t["lrow"][0:1, 0:64], t["lrow"][0:1, 0:64], t["lrow"][0:1, 64:128], ALU.mult))
    p.inc("dve", "dv", 1, lambda E: E.tensor_tensor(
        t["lrow"][0:1, 128:192], t["lrow"][0:1, 128:192], t["lrow"][0:1, 192:256], ALU.mult))
    p.inc("dve", "dv", 1, lambda E: E.tensor_reduce(
        t["trow"][0:1, 0:1], t["lrow"][0:1, 0:64], AX.X, ALU.add))
    p.inc("dve", "dv", 1, lambda E: E.tensor_reduce(
        t["trow"][0:1, 1:2], t["lrow"][0:1, 128:192], AX.X, ALU.add))
    p.wait("act", "dv")
    p.inc("act", "ac", 1, lambda E: E.activation(t["trow"][0:1, 2:3], t["trow"][0:1, 0:1], AF.Exp))
    p.inc("act", "ac", 1, lambda E: E.activation(t["trow"][0:1, 3:4], t["trow"][0:1, 1:2], AF.Exp))
    p.wait("dve", "ac")
    p.inc("dve", "dv", 1, lambda E: E.tensor_tensor(
        t["trow"][0:1, 2:3], t["trow"][0:1, 2:3], t["trow"][0:1, 3:4], ALU.subtract))
    lamv = p.inc("dve", "dv", 1, lambda E: E.tensor_scalar(
        t["trow"][0:1, 2:3], t["trow"][0:1, 2:3], 1.0, LAMBDA_INIT, ALU.mult, ALU.add))
    p.wait("pe", "dv", lamv)
    lbv = p.inc("pe", "pe", 1, lambda E: E.matmul(
        t["ps_b"][:, 0:1], t["ones_row_f"][:], t["trow"][0:1, 2:3], start=True, stop=True))
    p.wait("dve", "pe", lbv)
    p.inc("dve", "dv", 1, lambda E: E.tensor_copy(t["lam_vec"][:], t["ps_b"][:, 0:1]))

    ctx_tab = ExitStack()
    entt = ctx_tab.enter_context
    t["ctab"] = entt(nc.sbuf_tensor("ctab", [128, S], BF16, side="right"))
    t["qraw_bf"] = entt(nc.sbuf_tensor("qraw_bf", [128, 1024], BF16, side="right"))
    t["tswp_bf"] = entt(nc.sbuf_tensor("tswp_bf", [128, 1024], BF16, side="right"))
    t["tmpa_bf"] = entt(nc.sbuf_tensor("tmpa_bf", [128, 1024], BF16, side="right"))
    t["stab"] = entt(nc.sbuf_tensor("stab", [128, S], BF16, side="right"))
    t["sq_b"] = entt(nc.sbuf_tensor("sq_b", [128, 1024], BF16, side="right"))
    p.inc("sync", "lds", 16, lambda E: E.dma_start(out=t["ctab"][:], in_=ctb))
    tabv = p.inc("sync", "lds", 16, lambda E: E.dma_start(out=t["stab"][:], in_=stbt))
    p.wait("dve", "lds", tabv)

    ctx_kv = ExitStack()
    entk = ctx_kv.enter_context
    t["xc"] = entk(nc.sbuf_tensor("xc", [128, 4096], F32, side="right"))
    t["xbb"] = entk(nc.sbuf_tensor("xbb", [128, 4096], BF16, side="right"))
    t["x2b"] = entk(nc.sbuf_tensor("x2b", [128, 4096], BF16, side="right"))
    t["row"] = entk(nc.sbuf_tensor("row", [1, 2048], F32, side="right"))
    t["xnb"] = entk(nc.sbuf_tensor("xnb", [128, 16 * 1024], BF16, side="right"))
    t["kstage"] = entk(nc.sbuf_tensor("kstage", [128, 2048], BF16, side="right"))
    t["kraw"] = entk(nc.sbuf_tensor("kraw", [128, 4096], BF16, side="right"))
    t["v_stage"] = entk(nc.sbuf_tensor("v_stage", [128, 4, 2048], BF16, side="right"))
    t["wbuf"] = entk(nc.sbuf_tensor("wbuf", [128, 4096], BF16, side="right"))
    t["wvbuf"] = entk(nc.sbuf_tensor("wvbuf", [128, 12288], BF16, side="right"))
    kwring = Ring(p, "rkw", 2, "pool")
    vwring = Ring(p, "rvw", 3, "pool")
    lnring = Ring(p, "rx", 2, "sync")
    kst_store = []
    ksq_done = []
    kraw_free = {}
    vstv = []

    for chunk in range(4):
        c0 = chunk * 1024
        xn_dst = t["xn_bf"] if chunk == 0 else t["xnb"]
        psfree = emit_ln(p, t, xTb[:, c0:c0 + 1024], xn_dst, 1024, 0, ring=lnring)

        def consume_k(och, o8, chunk=chunk, c0=c0):
            gidx = chunk * 16 + och
            half = och % 2
            if gidx >= 2:
                p.wait("dve", f"st{half}", kst_store[gidx - 2])
                p.wait("dve", "ac", ksq_done[gidx - 2])
            srcfree, wv_ = emit_rope_sb(p, t, t["kraw"][:, o8 * 1024:(o8 + 1) * 1024],
                                        t["ctab"], t["stab"], c0, t["kstage"], half * 1024)
            emit_headnorm(p, t, t["kstage"], half * 1024, och, statbase=0)
            ksq_done.append(p.ct["ac"])
            p.wait("sync", "dv", wv_)
            kst_store.append(p.inc("sync", f"st{half}", 16,
                                   lambda E, och=och, half=half, c0=c0: E.dma_start(
                out=kT_s[och * 128:(och + 1) * 128, c0:c0 + 1024],
                in_=t["kstage"][:, half * 1024:(half + 1) * 1024])))
            return srcfree

        free = [dict(psfree), dict(psfree)]
        vfree = {"dv": psfree.get("dv", 0), "ac": p.ct["ac"]}
        for wave in range(4):
            cpvs = []
            for o8 in range(4):
                och = wave * 4 + o8
                par = och % 2
                if chunk == 0:
                    i, slot, key = kwring.load(lambda s: (lambda E, och=och, s=s: E.dma_start(
                        out=t["wbuf"][:, s * 2048:(s + 1) * 2048], in_=wk[och])))
                    p.wait("pool", *key)
                    wcv = p.inc("pool", f"rkw{slot}", 16, lambda E, och=och, s=slot: E.dma_start(
                        out=wk_c[och], in_=t["wbuf"][:, s * 2048:(s + 1) * 2048]))
                    kwring.done[i][f"rkw{slot}"] = wcv
                else:
                    i, slot, key = kwring.load(lambda s: (lambda E, och=och, s=s: E.dma_start(
                        out=t["wbuf"][:, s * 2048:(s + 1) * 2048], in_=wk_c[och])))
                for sem, val in free[par].items():
                    p.wait("pe", sem, val)
                p.wait("pe", key[0], key[1])
                for kch in range(KCH):
                    for half in range(2):
                        mm = lambda E, och=och, kch=kch, half=half, par=par, slot=slot, xn_dst=xn_dst: E.matmul(
                            t["ps_a"][:, par * 1024 + half * 512: par * 1024 + (half + 1) * 512],
                            t["wbuf"][:, slot * 2048 + kch * 128: slot * 2048 + (kch + 1) * 128],
                            xn_dst[:, kch * 1024 + half * 512: kch * 1024 + (half + 1) * 512],
                            start=(kch == 0), stop=(kch == KCH - 1))
                        if kch == KCH - 1 and half == 1:
                            mmv = p.inc("pe", "pe", 1, mm)
                        else:
                            p.emit("pe", mm)
                kwring.consumed(i, "pe")
                p.wait("act", "pe", mmv)
                if o8 in kraw_free:
                    p.wait("act", "dv", kraw_free[o8])
                cpv = p.inc("act", "ac", 1, lambda E, par=par, o8=o8: E.activation(
                    t["kraw"][:, o8 * 1024:(o8 + 1) * 1024],
                    t["ps_a"][:, par * 1024:(par + 1) * 1024], AF.Copy))
                cpvs.append(cpv)
                free[par] = {"ac": cpv}
            for o8 in range(4):
                och = wave * 4 + o8
                p.wait("dve", "ac", cpvs[o8])
                kraw_free[o8] = consume_k(och, o8)
            for vg in range(2):
                gidx = wave * 2 + vg
                halfq, dvq = gidx // 4, gidx % 4
                vfree = emit_v_group(p, t, wv, wv_c, vwring, xn_dst, halfq, dvq,
                                     t["v_stage"], v_s, c0, vfree,
                                     fill_cache=(chunk == 0), stv=vstv)
    ctx_kv.close()
    if debug:
        p.wait("sync", "dv")
        p.wait("sync", "st0")
        p.wait("sync", "st1")
        p.wait("sync", "sta")
        p.inc("sync", "sta", 16, lambda E: E.dma_start(out=dbg["d_kt"], in_=kT_s))
        p.inc("sync", "sta", 16, lambda E: E.dma_start(out=dbg["d_vs"], in_=v_s))

    ctx_q = ExitStack()
    entq = ctx_q.enter_context
    scope_barrier(p)
    t["wbufq"] = entq(nc.sbuf_tensor("wbufq", [128, 4096], BF16, side="right"))
    t["q_bf"] = ent(nc.sbuf_tensor("q_bf", [128, 16 * 1024], BF16))

    def consume_q(och, mmv):
        fr, wv_ = emit_rope(p, t, t["ps_a"][:, (och % 2) * 1024:(och % 2 + 1) * 1024],
                            t["ctab"], t["stab"], 0, t["q_bf"], och * 1024)
        emit_headnorm(p, t, t["q_bf"], och * 1024, och, statbase=32)
        return {fr[0]: fr[1]}

    qfree = {"pe": p.ct["pe"], "dv": p.ct["dv"], "ac": p.ct["ac"]}
    emit_fmproj(p, t, wq, t["wbufq"], t["xn_bf"], 1024, 0, consume_q, qfree)
    if debug:
        p.wait("sync", "dv")
        for kch in range(KCH):
            p.inc("sync", "sta", 16, lambda E, kch=kch: E.dma_start(
                out=dbg["d_q"][kch * 128:(kch + 1) * 128, :],
                in_=t["q_bf"][:, kch * 1024:(kch + 1) * 1024]))
        p.inc("sync", "sta", 16, lambda E: E.dma_start(out=dbg["d_srow"], in_=t["srow"][:]))

    p.wait("dve", "lds")
    p.wait("dve", "pl")
    p.inc("dve", "dv", 1, lambda E: E.tensor_tensor(
        t["lrow"][0:1, 0:16], t["srow"][0:1, 32:48], t["srow"][0:1, 0:16], ALU.mult))
    p.inc("dve", "dv", 1, lambda E: E.tensor_tensor(
        t["lrow"][0:1, 16:32], t["srow"][0:1, 48:64], t["srow"][0:1, 16:32], ALU.mult))
    p.inc("dve", "dv", 1, lambda E: E.tensor_tensor(
        t["lrow"][0:1, 64:80], t["lrow"][0:1, 0:16], t["lrow"][0:1, 16:32], ALU.max))
    p.wait("act", "dv")
    brv = p.inc("act", "ac", 1, lambda E: E.activation(
        t["lrow"][0:1, 128:144], t["lrow"][0:1, 64:80], AF.Sqrt, scale=1.0 / 64.0))
    p.wait("pe", "ac", brv)
    p.wait("pe", "dv")
    bbv = p.inc("pe", "pe", 1, lambda E: E.matmul(
        t["ps_b"][:, 0:16], t["neg_row_f"][:], t["lrow"][0:1, 128:144], start=True, stop=True))
    p.wait("dve", "pe", bbv)
    biasv = p.inc("dve", "dv", 1, lambda E: E.tensor_copy(t["bias_sb"][:], t["ps_b"][:, 0:16]))
    ctx_q.close()
    ctx_tab.close()

    ctx2 = ExitStack()
    ent2 = ctx2.enter_context
    scope_barrier(p)
    t["acc"] = ent(nc.sbuf_tensor("acc", [128, 16 * 1024], F32))
    t["h_bf"] = ent2(nc.sbuf_tensor("h_bf", [128, ICH * 512], BF16, side="right"))
    t["wffn"] = ent2(nc.sbuf_tensor("wffn", [128, 8448], BF16, side="right"))
    gring = Ring(p, "rg", 2, "pool")
    uring = Ring(p, "ru", 2, "pool")
    dring = Ring(p, "rd", 3, "pool")
    wffn = t["wffn"]
    UOFF = 4096
    for tile in range(2):
        gufree = {"dv": p.ct["dv"]}
        p.wait("dve", "pe")
        p.wait("pool", "pe")
        for ich in range(ICH):
            par = ich % 2
            gi, gsl, gkey = gring.load(lambda s: (lambda E, ich=ich, s=s: E.dma_start(
                out=wffn[:, s * 2048:(s + 1) * 2048], in_=gu[ich][:, 0:2048])))
            ui, usl, ukey = uring.load(lambda s: (lambda E, ich=ich, s=s: E.dma_start(
                out=wffn[:, UOFF + s * 2048:UOFF + (s + 1) * 2048], in_=gu[ich][:, 2048:4096])))
            for sem, val in gufree.items():
                p.wait("pe", sem, val)
            p.wait("pe", *gkey)
            p.wait("pe", *ukey)
            for kch in range(KCH):
                p.emit("pe", lambda E, kch=kch, par=par, tile=tile, gsl=gsl: E.matmul(
                    t["ps_a"][:, par * 1024: par * 1024 + 512],
                    wffn[:, gsl * 2048 + kch * 128: gsl * 2048 + (kch + 1) * 128],
                    t["xn_bf"][:, kch * 1024 + tile * 512: kch * 1024 + (tile + 1) * 512],
                    start=(kch == 0), stop=(kch == KCH - 1)))
                mm = lambda E, kch=kch, par=par, tile=tile, usl=usl: E.matmul(
                    t["ps_a"][:, par * 1024 + 512: par * 1024 + 1024],
                    wffn[:, UOFF + usl * 2048 + kch * 128: UOFF + usl * 2048 + (kch + 1) * 128],
                    t["xn_bf"][:, kch * 1024 + tile * 512: kch * 1024 + (tile + 1) * 512],
                    start=(kch == 0), stop=(kch == KCH - 1))
                if kch == KCH - 1:
                    gv = p.inc("pe", "pe", 1, mm)
                else:
                    p.emit("pe", mm)
            gring.consumed(gi, "pe")
            uring.consumed(ui, "pe")
            p.wait("act", "pe", gv)
            p.wait("act", "dv")
            p.inc("act", "ac", 1, lambda E, par=par, ich=ich: E.activation(
                t["tmp1"][:, 512:1024],
                t["ps_a"][:, par * 1024: par * 1024 + 512],
                AF.Silu, scale=t["gs_sb"][:, ich:ich + 1]))
            p.wait("dve", "ac")
            p.inc("dve", "dv", 1, lambda E, par=par, ich=ich: E.tensor_scalar(
                t["tmp0"][:, 0:512], t["ps_a"][:, par * 1024 + 512: par * 1024 + 1024],
                t["us_sb"][:, ich:ich + 1], 0.0, ALU.mult, ALU.add))
            p.inc("dve", "dv", 1, lambda E, par=par, ich=ich: E.tensor_tensor(
                t["h_bf"][:, ich * 512:(ich + 1) * 512], t["tmp0"][:, 0:512],
                t["tmp1"][:, 512:1024], ALU.mult))
            gufree = {"dv": p.ct["dv"]}
        dfree = [{"dv": p.ct["dv"]}, {"dv": p.ct["dv"]}]
        p.wait("pool", "pe")
        for hch in range(16):
            par = hch % 2
            d0, s0, k0 = dring.load(lambda s: (lambda E, hch=hch, s=s: E.dma_start(
                out=wffn[:, s * 2816:(s + 1) * 2816], in_=dn[hch][:, 0:2816])))
            d1, s1, k1 = dring.load(lambda s: (lambda E, hch=hch, s=s: E.dma_start(
                out=wffn[:, s * 2816:(s + 1) * 2816], in_=dn[hch][:, 2816:5632])))
            for sem, val in dfree[par].items():
                p.wait("pe", sem, val)
            if hch == 0:
                p.wait("pe", "dv", gufree["dv"])
            p.wait("pe", *k0)
            for ich in range(22):
                p.emit("pe", lambda E, ich=ich, par=par, s0=s0: E.matmul(
                    t["ps_b"][:, par * 512:(par + 1) * 512],
                    wffn[:, s0 * 2816 + ich * 128: s0 * 2816 + (ich + 1) * 128],
                    t["h_bf"][:, ich * 512:(ich + 1) * 512],
                    start=(ich == 0), stop=False))
            p.wait("pe", *k1)
            for ich in range(22, ICH):
                mm = lambda E, ich=ich, par=par, s1=s1: E.matmul(
                    t["ps_b"][:, par * 512:(par + 1) * 512],
                    wffn[:, s1 * 2816 + (ich - 22) * 128: s1 * 2816 + (ich - 21) * 128],
                    t["h_bf"][:, ich * 512:(ich + 1) * 512],
                    start=False, stop=(ich == ICH - 1))
                if ich == ICH - 1:
                    dv_ = p.inc("pe", "pe", 1, mm)
                else:
                    p.emit("pe", mm)
            dring.consumed(d0, "pe")
            dring.consumed(d1, "pe")
            p.wait("dve", "pe", dv_)
            p.inc("dve", "dv", 1, lambda E, hch=hch, par=par, tile=tile: E.tensor_scalar(
                t["acc"][:, hch * 1024 + tile * 512: hch * 1024 + (tile + 1) * 512],
                t["ps_b"][:, par * 512:(par + 1) * 512],
                t["ds_sb"][:, hch:hch + 1], 0.0, ALU.mult, ALU.add))
            dfree[par] = {"dv": p.ct["dv"]}
    ctx2.close()

    ctx3 = ExitStack()
    ent3 = ctx3.enter_context
    scope_barrier(p)
    t["e12"] = ent3(nc.sbuf_tensor("e12", [128, 2048], BF16, side="right"))
    t["ktb"] = ent3(nc.sbuf_tensor("ktb", [128, 8192], BF16, side="right"))
    t["vtb"] = ent3(nc.sbuf_tensor("vtb", [128, 8192], BF16, side="right"))
    t["r1B"] = ent3(nc.sbuf_tensor("r1B", [128, 512], F32, side="right"))
    t["r2B"] = ent3(nc.sbuf_tensor("r2B", [128, 512], F32, side="right"))
    t["uB"] = ent3(nc.sbuf_tensor("uB", [128, 512], F32, side="right"))
    t["rstdB"] = ent3(nc.sbuf_tensor("rstdB", [128, 512], F32, side="right"))
    t["sqa"] = ent3(nc.sbuf_tensor("sqa", [128, 512], BF16, side="right"))
    p.wait("act", "dv", biasv)
    p.wait("sync", "st0")
    p.wait("sync", "st1")
    p.wait("sync", "sta")
    attn_bf = t["xn_bf"]
    ps_a, ps_b, ps_d1, ps_d2 = t["ps_a"], t["ps_b"], t["ps_d1"], t["ps_d2"]
    kv_use = []
    ac_free = p.ct["ac"]
    dv_free = p.ct["dv"]
    for h in range(16):
        par_h = h % 2
        if h >= 2:
            p.wait("sync", "pe", kv_use[h - 2])
        v1 = p.inc("sync", f"ldk{par_h}", 16, lambda E, h=h, par_h=par_h: E.dma_start(
            out=t["ktb"][:, par_h * 4096:(par_h + 1) * 4096],
            in_=kT_s[h * 128:(h + 1) * 128, :]))
        v2 = p.inc("sync", f"ldk{par_h}", 16, lambda E, h=h, par_h=par_h: E.dma_start(
            out=t["vtb"].ap()[:, par_h * 4096:(par_h + 1) * 4096].rearrange(
                "p (b d) -> p b d", b=32),
            in_=v_s.rearrange("(b q) d -> q b d", q=128)[:, :, h * 128:(h + 1) * 128]))
        for tile in range(2):
            koff = par_h * 4096
            qcol = h * 1024 + tile * 512
            p.wait("pe", f"ldk{par_h}", v2)
            p.wait("pe", "ac", ac_free)
            p.wait("pe", "dv", dv_free)

            def emit_S(b):
                par = b % 2
                p.emit("pe", lambda E, b=b, par=par, koff=koff, qcol=qcol: E.matmul(
                    ps_a[:, par * 1024: par * 1024 + 512],
                    t["ktb"][0:64, koff + b * 128: koff + (b + 1) * 128],
                    t["q_bf"][0:64, qcol:qcol + 512],
                    start=True, stop=True, tile_position=(0, 0)))
                return p.inc("pe", "pe", 1, lambda E, b=b, par=par, koff=koff, qcol=qcol: E.matmul(
                    ps_a[:, par * 1024 + 512: par * 1024 + 1024],
                    t["ktb"][64:128, koff + b * 128: koff + (b + 1) * 128],
                    t["q_bf"][64:128, qcol:qcol + 512],
                    start=True, stop=True, tile_position=(64, 0)))

            sv = [None] * 32
            pvv_l = [None] * 32
            sv[0] = emit_S(0)
            sv[1] = emit_S(1)
            for b in range(32):
                par = b % 2
                if b >= 2:
                    p.wait("act", "pe", pvv_l[b - 2])
                p.wait("act", "pe", sv[b])
                ev = p.inc("act", "ac", 1, lambda E, par=par, h=h: E.activation(
                    t["e12"][:, par * 1024:(par + 1) * 1024],
                    ps_a[:, par * 1024:(par + 1) * 1024],
                    AF.Exp, bias=t["bias_sb"][:, h:h + 1], scale=0.125))
                p.wait("pe", "ac", ev)
                first, last = (b == 0), (b == 31)
                p.emit("pe", lambda E, b=b, par=par, koff=koff, first=first, last=last: E.matmul(
                    ps_b[:, 0:512],
                    t["vtb"][:, koff + b * 128: koff + (b + 1) * 128],
                    t["e12"][:, par * 1024: par * 1024 + 512],
                    start=first, stop=last))
                p.emit("pe", lambda E, b=b, par=par, koff=koff, first=first, last=last: E.matmul(
                    ps_b[:, 512:1024],
                    t["vtb"][:, koff + b * 128: koff + (b + 1) * 128],
                    t["e12"][:, par * 1024 + 512: par * 1024 + 1024],
                    start=first, stop=last))
                p.emit("pe", lambda E, par=par, first=first, last=last: E.matmul(
                    ps_d1[:], t["ones_col_bf"][:],
                    t["e12"][:, par * 1024: par * 1024 + 512],
                    start=first, stop=last))
                mmD = lambda E, par=par, first=first, last=last: E.matmul(
                    ps_d2[:], t["ones_col_bf"][:],
                    t["e12"][:, par * 1024 + 512: par * 1024 + 1024],
                    start=first, stop=last)
                pvv_l[b] = p.inc("pe", "pe", 1, mmD)
                if b + 2 < 32:
                    sv[b + 2] = emit_S(b + 2)
            pvv = pvv_l[31]
            p.wait("act", "pe", pvv)
            p.inc("act", "ac", 1, lambda E: E.activation(t["tmp1"][0:1, 0:512], ps_d1[:], AF.Copy))
            dcp = p.inc("act", "ac", 1, lambda E: E.activation(t["tmp1"][0:1, 512:1024], ps_d2[:], AF.Copy))
            p.wait("pe", "ac", dcp)
            p.emit("pe", lambda E: E.matmul(ps_a[:, 0:512], t["ones_row_f"][:],
                                            t["tmp1"][0:1, 0:512], start=True, stop=True))
            bv = p.inc("pe", "pe", 1, lambda E: E.matmul(
                ps_a[:, 512:1024], t["ones_row_f"][:], t["tmp1"][0:1, 512:1024],
                start=True, stop=True))
            p.wait("act", "pe", bv)
            p.inc("act", "ac", 1, lambda E: E.activation(t["uB"][:], ps_a[:, 0:512], AF.Ln))
            p.inc("act", "ac", 1, lambda E: E.activation(t["r1B"][:], t["uB"][:], AF.Exp, scale=-1.0))
            p.inc("act", "ac", 1, lambda E: E.activation(t["uB"][:], ps_a[:, 512:1024], AF.Ln))
            rv = p.inc("act", "ac", 1, lambda E: E.activation(t["r2B"][:], t["uB"][:], AF.Exp, scale=-1.0))
            p.wait("dve", "ac", rv)
            p.inc("dve", "dv", 1, lambda E: E.tensor_scalar(
                t["r2B"][:], t["r2B"][:], t["lam_vec"][:], 0.0, ALU.mult, ALU.add))
            p.inc("dve", "dv", 1, lambda E: E.tensor_tensor(
                t["tmp0"][:, 0:512], ps_b[:, 0:512], t["r1B"][:], ALU.mult))
            p.inc("dve", "dv", 1, lambda E: E.tensor_tensor(
                t["tmp0"][:, 512:1024], ps_b[:, 512:1024], t["r2B"][:], ALU.mult))
            yv = p.inc("dve", "dv", 1, lambda E: E.tensor_tensor(
                t["tmp0"][:, 0:512], t["tmp0"][:, 0:512], t["tmp0"][:, 512:1024], ALU.subtract))
            p.wait("act", "dv", yv)
            sqv = p.inc("act", "ac", 1, lambda E: E.activation(
                t["sqa"][:], t["tmp0"][:, 0:512], AF.Square))
            p.wait("pe", "ac", sqv)
            msv = p.inc("pe", "pe", 1, lambda E: E.matmul(
                ps_a[:, 1024:1536], t["ones128_bf"][:], t["sqa"][:], start=True, stop=True))
            p.wait("act", "pe", msv)
            p.inc("act", "ac", 1, lambda E: E.activation(
                t["uB"][:], ps_a[:, 1024:1536], AF.Ln, scale=1.0 / 128.0, bias=t["eps_col"][:]))
            rsv = p.inc("act", "ac", 1, lambda E: E.activation(
                t["rstdB"][:], t["uB"][:], AF.Exp, scale=-0.5))
            p.wait("dve", "ac", rsv)
            p.inc("dve", "dv", 1, lambda E: E.tensor_tensor(
                t["tmp1"][:, 0:512], t["tmp0"][:, 0:512], t["rstdB"][:], ALU.mult))
            dvf = p.inc("dve", "dv", 1, lambda E, h=h, tile=tile: E.tensor_scalar(
                attn_bf[:, h * 1024 + tile * 512: h * 1024 + (tile + 1) * 512],
                t["tmp1"][:, 0:512], t["gam_eff"][:], 0.0, ALU.mult, ALU.add))
            ac_free = rsv
            dv_free = dvf
        kv_use.append(p.ct["pe"])
    ctx3.close()
    if debug:
        p.wait("sync", "dv")
        for kch in range(KCH):
            p.inc("sync", "sta", 16, lambda E, kch=kch: E.dma_start(
                out=dbg["d_attn"][kch * 128:(kch + 1) * 128, :],
                in_=attn_bf[:, kch * 1024:(kch + 1) * 1024]))

    ctx4 = ExitStack()
    ent4 = ctx4.enter_context
    scope_barrier(p)
    t["wbuf2"] = ent4(nc.sbuf_tensor("wbuf2", [128, 4096], BF16, side="right"))
    t["xe"] = ent4(nc.sbuf_tensor("xe", [128, 2048], F32, side="right"))
    t["out_sb"] = ent4(nc.sbuf_tensor("out_sb", [128, 2048], F32, side="right"))
    wring = Ring(p, "rw", 2, "pool")
    wofree = [{"ac": ac_free, "dv": dv_free}, {"ac": ac_free, "dv": dv_free}]
    stv = []
    for och in range(16):
        par = och % 2
        wi, wsl, wkey = wring.load(lambda s: (lambda E, och=och, s=s: E.dma_start(
            out=t["wbuf2"][:, s * 2048:(s + 1) * 2048], in_=wo[och])))
        if och >= 2:
            p.wait("sync", f"st{par}", stv[och - 2])
        xv = p.inc("sync", f"ldx{par}", 16, lambda E, och=och, par=par: E.dma_start(
            out=t["xe"][:, par * 1024:(par + 1) * 1024],
            in_=xTb[och * 128:(och + 1) * 128, 0:1024]))
        for sem, val in wofree[par].items():
            p.wait("pe", sem, val)
        p.wait("pe", *wkey)
        for kch in range(KCH):
            for half in range(2):
                mm = lambda E, och=och, kch=kch, half=half, wsl=wsl: E.matmul(
                    ps_a[:, (och % 2) * 1024 + half * 512: (och % 2) * 1024 + (half + 1) * 512],
                    t["wbuf2"][:, wsl * 2048 + kch * 128: wsl * 2048 + (kch + 1) * 128],
                    attn_bf[:, kch * 1024 + half * 512: kch * 1024 + (half + 1) * 512],
                    start=(kch == 0), stop=(kch == KCH - 1))
                if kch == KCH - 1 and half == 1:
                    ov = p.inc("pe", "pe", 1, mm)
                else:
                    p.emit("pe", mm)
        wring.consumed(wi, "pe")
        p.wait("dve", "pe", ov)
        p.wait("dve", f"ldx{par}", xv)
        if och >= 2:
            p.wait("dve", f"st{par}", stv[och - 2])
        p.inc("dve", "dv", 1, lambda E, och=och, par=par: E.tensor_tensor(
            t["out_sb"][:, par * 1024:(par + 1) * 1024],
            t["acc"][:, och * 1024:(och + 1) * 1024],
            ps_a[:, par * 1024:(par + 1) * 1024], ALU.add))
        fv = p.inc("dve", "dv", 1, lambda E, och=och, par=par: E.tensor_tensor(
            t["out_sb"][:, par * 1024:(par + 1) * 1024],
            t["out_sb"][:, par * 1024:(par + 1) * 1024],
            t["xe"][:, par * 1024:(par + 1) * 1024], ALU.add))
        p.wait("sync", "dv", fv)
        stv.append(p.inc("sync", f"st{par}", 16, lambda E, och=och, par=par: E.dma_start(
            out=outT[och * 128:(och + 1) * 128, :],
            in_=t["out_sb"][:, par * 1024:(par + 1) * 1024])))
        wofree[par] = {"dv": fv}
    ctx4.close()
    p.wait("sync", "st0")
    p.wait("sync", "st1")
    if debug:
        p.wait("sync", "sta")
    block = ent(nc.Block())
    attach(p, block)
    ctx.close()
    return nc


def emit_vproj_cache(p, t, wv_hbm, wv_cache, ring, xn_bf, xn_seg, xn_col0,
                     v_stage, v_out, row0, psfree0, fill_cache):
    ps_a = t["ps_a"]

    def vps(j):
        return ps_a[:, j * 512:(j + 1) * 512]

    free = dict(psfree0)
    stv = []
    p.wait("act", "sta")
    for halfq in range(2):
        for dvq in range(4):
            slots = []
            for wh in range(2):
                if fill_cache and halfq == 0:
                    i, slot, key = ring.load(lambda s: (lambda E, dvq=dvq, wh=wh, s=s: E.dma_start(
                        out=t["wvbuf"][:, s * 4096:(s + 1) * 4096],
                        in_=wv_hbm[dvq][:, wh * 4096:(wh + 1) * 4096])))
                    p.wait("pool", *key)
                    wcv = p.inc("pool", f"rvw{slot}", 16, lambda E, dvq=dvq, wh=wh, s=slot: E.dma_start(
                        out=wv_cache[dvq][:, wh * 4096:(wh + 1) * 4096],
                        in_=t["wvbuf"][:, s * 4096:(s + 1) * 4096]))
                    ring.done[i][f"rvw{slot}"] = wcv
                else:
                    i, slot, key = ring.load(lambda s: (lambda E, dvq=dvq, wh=wh, s=s: E.dma_start(
                        out=t["wvbuf"][:, s * 4096:(s + 1) * 4096],
                        in_=wv_cache[dvq][:, wh * 4096:(wh + 1) * 4096])))
                slots.append((i, slot, key))
            for sem, val in free.items():
                p.wait("pe", sem, val)
            for wh in range(2):
                i, slot, key = slots[wh]
                p.wait("pe", key[0], key[1])
                for kch in range(wh * 8, wh * 8 + 8):
                    for tc in range(4):
                        tok = xn_col0 + halfq * 512 + tc * 128
                        mm = lambda E, kch=kch, tc=tc, tok=tok, slot=slot: E.matmul(
                            vps(tc),
                            xn_bf[:, kch * xn_seg + tok: kch * xn_seg + tok + 128],
                            t["wvbuf"][:, slot * 4096 + (kch % 8) * 512:
                                       slot * 4096 + (kch % 8 + 1) * 512],
                            start=(kch == 0), stop=(kch == KCH - 1))
                        if kch == wh * 8 + 7 and tc == 3:
                            p.inc("pe", "pe", 1, mm)
                        else:
                            p.emit("pe", mm)
                ring.consumed(i, "pe")
            p.wait("act", "pe")
            for tc in range(4):
                p.inc("act", "ac", 1, lambda E, tc=tc, dvq=dvq: E.activation(
                    v_stage[:, tc, dvq * 512:(dvq + 1) * 512], vps(tc), AF.Copy))
            free = {"ac": p.ct["ac"]}
        p.wait("sync", "ac")
        for tc in range(4):
            r0 = row0 + halfq * 512 + tc * 128
            stv.append(p.inc("sync", "sta", 16, lambda E, tc=tc, r0=r0: E.dma_start(
                out=v_out[r0:r0 + 128, :], in_=v_stage[:, tc, :])))
        if halfq == 0:
            p.wait("act", "sta", stv[-1])
    return free


def _f32(a):
    return np.ascontiguousarray(np.asarray(a, dtype=np.float32))


def chunk_wT(W):
    return np.ascontiguousarray(
        _f32(W).reshape(16, 128, 16, 128).transpose(0, 3, 2, 1)).reshape(16, 128, 2048)


def rope_tables():
    angle = 1.0 / 10000 ** np.linspace(0.0, 1.0, 32)
    freqs = np.arange(S, dtype=np.float64)[:, None] * angle
    cos, sin = np.cos(freqs), np.sin(freqs)
    Cc = np.empty((64, S), np.float64)
    Sc = np.empty((64, S), np.float64)
    for i in range(32):
        Cc[2 * i] = cos[:, i]
        Cc[2 * i + 1] = cos[:, i]
        Sc[2 * i] = -sin[:, i]
        Sc[2 * i + 1] = sin[:, i]
    C = np.vstack([Cc, Cc]).astype(NPBF)
    Sg = np.vstack([Sc, Sc]).astype(NPBF)
    return np.ascontiguousarray(C), np.ascontiguousarray(Sg)


MODE = "F"


def kernel(L0_attn_x, L0_attn_Wq, L0_attn_Wk, L0_attn_Wv, L0_attn_Wo,
           L0_attn_lambda_q1, L0_attn_lambda_k1, L0_attn_lambda_q2, L0_attn_lambda_k2,
           L0_attn_gamma, L0_ffn_gate_w, L0_ffn_up_w, L0_ffn_down_w,
           L0_ffn_gate_s, L0_ffn_up_s, L0_ffn_down_s):
    x = _f32(L0_attn_x)
    wq_dev = chunk_wT(L0_attn_Wq)
    wk_dev = chunk_wT(L0_attn_Wk)
    wo_dev = chunk_wT(L0_attn_Wo)
    wv_dev = np.ascontiguousarray(
        _f32(L0_attn_Wv).reshape(4, 512, 16, 128).transpose(0, 3, 2, 1)).reshape(4, 128, 8192)
    g4 = _f32(L0_ffn_gate_w).reshape(44, 128, 16, 128).transpose(0, 3, 2, 1)
    u4 = _f32(L0_ffn_up_w).reshape(44, 128, 16, 128).transpose(0, 3, 2, 1)
    gu_dev = np.ascontiguousarray(
        np.stack([g4, u4], axis=1).transpose(0, 2, 1, 3, 4)).reshape(44, 128, 4096)
    dn_dev = np.ascontiguousarray(
        _f32(L0_ffn_down_w).reshape(16, 128, 44, 128).transpose(0, 3, 2, 1)).reshape(16, 128, 5632)
    gs_dev = np.ascontiguousarray(_f32(L0_ffn_gate_s).reshape(44, 128).T)
    us_dev = np.ascontiguousarray(_f32(L0_ffn_up_s).reshape(44, 128).T)
    ds_dev = np.ascontiguousarray(_f32(L0_ffn_down_s).reshape(16, 128).T)
    gam_dev = _f32(L0_attn_gamma).reshape(128, 1)
    lq1 = _f32(L0_attn_lambda_q1).reshape(1, 64)
    lk1 = _f32(L0_attn_lambda_k1).reshape(1, 64)
    lq2 = _f32(L0_attn_lambda_q2).reshape(1, 64)
    lk2 = _f32(L0_attn_lambda_k2).reshape(1, 64)
    Ct, St = rope_tables()
    common = {"wq": wq_dev, "wo": wo_dev, "gu": gu_dev, "dn": dn_dev,
              "gs": gs_dev, "us": us_dev, "ds": ds_dev, "gam": gam_dev,
              "lq1": lq1, "lk1": lk1, "lq2": lq2, "lk2": lk2}

    if MODE == "F":
        ncF = build_F()
        in_F = []
        for c in range(8):
            b, j = c // 4, c % 4
            order = [j] + [i for i in range(4) if i != j]
            xTb = np.ascontiguousarray(np.concatenate(
                [x[b, 1024 * i:1024 * (i + 1), :] for i in order], axis=0).T)
            ctb = np.ascontiguousarray(np.concatenate(
                [Ct[:, 1024 * i:1024 * (i + 1)] for i in order], axis=1))
            stbt = np.ascontiguousarray(np.concatenate(
                [St[:, 1024 * i:1024 * (i + 1)] for i in order], axis=1))
            m = dict(common)
            m.update({"xTb": xTb, "wk": wk_dev, "wv": wv_dev,
                      "ctb": ctb, "stbt": stbt})
            in_F.append(m)
        res = run_bass_kernel_spmd(ncF, in_F, core_ids=list(range(8)))
        out = np.empty((2, S, E), np.float32)
        for c in range(8):
            b, r0 = c // 4, (c % 4) * TQ
            out[b, r0:r0 + TQ, :] = np.asarray(res.results[c]["outT"]).T
        return out

    xT = []
    for c in range(8):
        b, r0 = c // 4, (c % 4) * TQ
        xT.append(np.ascontiguousarray(x[b, r0:r0 + TQ, :].T))
    ncA = build_A()
    in_A = []
    for c in range(8):
        r0 = (c % 4) * TQ
        in_A.append({"xT": xT[c], "wk": wk_dev, "wv": wv_dev,
                     "ct": np.ascontiguousarray(Ct[:, r0:r0 + TQ]),
                     "stb": np.ascontiguousarray(St[:, r0:r0 + TQ])})
    resA = run_bass_kernel_spmd(ncA, in_A, core_ids=list(range(8)))
    kT_full, v_full, ks4 = {}, {}, {}
    for g in range(2):
        cs = [4 * g + i for i in range(4)]
        kT_full[g] = np.ascontiguousarray(
            np.concatenate([np.asarray(resA.results[c]["kT_out"]) for c in cs], axis=1))
        v_full[g] = np.ascontiguousarray(
            np.concatenate([np.asarray(resA.results[c]["v_out"]) for c in cs], axis=0))
        ks4[g] = np.ascontiguousarray(
            np.concatenate([np.asarray(resA.results[c]["kstat_out"]) for c in cs], axis=0))
    ncB = build_B()
    in_B = []
    for c in range(8):
        g, r0 = c // 4, (c % 4) * TQ
        m = dict(common)
        m.update({"xT": xT[c], "kT": kT_full[g], "vf": v_full[g], "ks": ks4[g],
                  "ct": np.ascontiguousarray(Ct[:, r0:r0 + TQ]),
                  "stb": np.ascontiguousarray(St[:, r0:r0 + TQ])})
        in_B.append(m)
    resB = run_bass_kernel_spmd(ncB, in_B, core_ids=list(range(8)))
    out = np.empty((2, S, E), np.float32)
    for c in range(8):
        b, r0 = c // 4, (c % 4) * TQ
        out[b, r0:r0 + TQ, :] = np.asarray(resB.results[c]["outT"]).T
    return out


# version 11
# speedup vs baseline: 1.0393x; 1.0118x over previous
import math
import numpy as np
import ml_dtypes

import concourse.bass as bass
import concourse.mybir as mybir
from concourse.bass_utils import run_bass_kernel_spmd

F32 = mybir.dt.float32
BF16 = mybir.dt.bfloat16
AF = mybir.ActivationFunctionType
ALU = mybir.AluOpType
AX = mybir.AxisListType

S, E, H = 4096, 2048, 16
TQ = 1024
KCH = 16
ICH = 44
LAMBDA_INIT = 0.8 - 0.6 * math.exp(-0.3 * 12)
ONE_MINUS_LI = 1.0 - LAMBDA_INIT
EPS = 1e-5
NPBF = ml_dtypes.bfloat16
SWAP_MASK = [i ^ 1 for i in range(32)]


class Prog:
    def __init__(self, nc, ent):
        self.nc = nc
        self.ent = ent
        self.ops = {e: [] for e in ("pe", "act", "dve", "pool", "sync")}
        self.sems = {}
        self.ct = {}
        for s in ("pe", "ac", "dv", "pl", "lds", "ldx0", "ldx1", "ldw0", "ldw1",
                  "ldk0", "ldk1", "st0", "st1", "sta"):
            self.sems[s] = ent(nc.semaphore(s))
            self.ct[s] = 0

    def emit(self, eng, fn):
        self.ops[eng].append(fn)

    def wait(self, eng, sem, val=None):
        v = self.ct[sem] if val is None else val
        if v <= 0:
            return
        s = self.sems[sem]
        self.emit(eng, lambda E, s=s, v=v: E.wait_ge(s, v))

    OWN = {"act": "ac", "dve": "dv", "pool": "pl"}

    def inc(self, eng, sem, n, mk):
        own = self.OWN.get(eng)
        if own is not None and self.ct[own] > 0:
            self.wait(eng, own)
        if n >= 16 and sem == "lds" and self.ct[sem] > 0:
            self.wait(eng, sem)
        s = self.sems[sem]
        self.ct[sem] += n
        v = self.ct[sem]
        self.emit(eng, lambda E, mk=mk, s=s, n=n: mk(E).then_inc(s, n))
        return v


class Ring:

    def __init__(self, p, name, nslots, queue):
        self.p, self.queue, self.n, self.name = p, queue, nslots, name
        for j in range(nslots):
            s = f"{name}{j}"
            if s not in p.sems:
                p.sems[s] = p.ent(p.nc.semaphore(s))
                p.ct[s] = 0
        self.i = 0
        self.done = []

    def load(self, mk_of_slot):
        i, n = self.i, self.n
        slot = i % n
        if i >= n:
            for sem, val in self.done[i - n].items():
                self.p.wait(self.queue, sem, val)
        v = self.p.inc(self.queue, f"{self.name}{slot}", 16, mk_of_slot(slot))
        self.done.append({})
        self.i += 1
        return i, slot, (f"{self.name}{slot}", v)

    def consumed(self, i, *sems):
        for s in sems:
            self.done[i][s] = self.p.ct[s]


def scope_barrier(p):
    for eng in ("pe", "act", "dve", "pool", "sync"):
        for sem in list(p.ct):
            if Prog.OWN.get(eng) == sem:
                continue
            p.wait(eng, sem)


def attach(p, block):
    @block.tensor
    def _(t):
        for fn in p.ops["pe"]:
            fn(t)

    @block.scalar
    def _(sc):
        for fn in p.ops["act"]:
            fn(sc)

    @block.vector
    def _(v):
        for fn in p.ops["dve"]:
            fn(v)

    @block.gpsimd
    def _(g):
        for fn in p.ops["pool"]:
            fn(g)

    @block.sync
    def _(s):
        for fn in p.ops["sync"]:
            fn(s)


def alloc_common(nc, ent, t):
    t["ps_a"] = ent(nc.psum_tensor("ps_a", [128, 2048], F32))
    t["ps_b"] = ent(nc.psum_tensor("ps_b", [128, 1024], F32))
    t["ps_d1"] = ent(nc.psum_tensor("ps_d1", [1, 512], F32))
    t["ps_d2"] = ent(nc.psum_tensor("ps_d2", [1, 512], F32))
    t["ones_col_bf"] = ent(nc.sbuf_tensor("ones_col_bf", [128, 1], BF16))
    t["ones_row_f"] = ent(nc.sbuf_tensor("ones_row_f", [1, 128], F32))
    t["neg_row_f"] = ent(nc.sbuf_tensor("neg_row_f", [1, 128], F32))
    t["ones128_bf"] = ent(nc.sbuf_tensor("ones128_bf", [128, 128], BF16))
    t["mask1_bf"] = ent(nc.sbuf_tensor("mask1_bf", [128, 1], BF16))
    t["mask2_bf"] = ent(nc.sbuf_tensor("mask2_bf", [128, 1], BF16))
    t["xc"] = ent(nc.sbuf_tensor("xc", [128, 4096], F32))
    t["xbb"] = ent(nc.sbuf_tensor("xbb", [128, 8192], BF16))
    t["x2b"] = ent(nc.sbuf_tensor("x2b", [128, 4096], BF16))
    t["row"] = ent(nc.sbuf_tensor("row", [1, 4096], F32))
    t["tmp0"] = ent(nc.sbuf_tensor("tmp0", [128, 1024], F32))
    t["tmp1"] = ent(nc.sbuf_tensor("tmp1", [128, 1024], F32))
    t["sq_b"] = ent(nc.sbuf_tensor("sq_b", [128, 1024], BF16))
    t["srow"] = ent(nc.sbuf_tensor("srow", [1, 64], F32))
    t["eps_col"] = ent(nc.sbuf_tensor("eps_col", [128, 1], F32))
    t["nrow"] = ent(nc.sbuf_tensor("nrow", [1, 1024], F32))
    t["mBb"] = ent(nc.sbuf_tensor("mBb", [128, 512], BF16))
    t["rBb"] = ent(nc.sbuf_tensor("rBb", [128, 512], BF16))
    t["trow"] = ent(nc.sbuf_tensor("trow", [1, 16], F32))


def emit_init(p, t):
    for name, sl, val in (("ones_col_bf", None, 1.0), ("ones_row_f", None, 1.0),
                          ("neg_row_f", None, -1.0), ("ones128_bf", None, 1.0),
                          ("mask1_bf", (0, 64), 1.0), ("mask1_bf", (64, 128), 0.0),
                          ("mask2_bf", (0, 64), 0.0), ("mask2_bf", (64, 128), 1.0),
                          ("srow", None, 0.0), ("eps_col", None, EPS)):
        def mk(E, name=name, sl=sl, val=val):
            ap = t[name][:] if sl is None else t[name][sl[0]:sl[1], :]
            return E.memset(ap, val)
        p.inc("pool", "pl", 1, mk)


def emit_ln(p, t, x_hbm, xn_bf, xn_seg, xn_col0, ring=None):
    xbb, x2b, row = t["xbb"], t["x2b"], t["row"]
    ps_b, ps_d1, ps_d2 = t["ps_b"], t["ps_d1"], t["ps_d2"]
    ringb = t.setdefault("_lnringb", Ring(p, "rxb", 4, "pool"))
    p.wait("pe", "pl")
    x2i = 0
    for tile in range(2):
        qidx = {}
        for q in range(4):
            i, sl, key = ringb.load(lambda s: (lambda E, q=q, tile=tile: E.dma_start(
                out=xbb[:, q * 2048:(q + 1) * 2048].rearrange("p (c t) -> p c t", c=4),
                in_=x_hbm[q * 512:(q + 1) * 512,
                          tile * 512:(tile + 1) * 512].rearrange("(c p) t -> p c t", p=128))))
            p.wait("act", *key)
            for j in range(4):
                kch = q * 4 + j
                co = q * 2048 + j * 512
                x2o = (x2i % 8) * 512
                x2i += 1
                p.wait("act", "pe", t.get("_x2pe", 0))
                p.inc("act", "ac", 1, lambda E, co=co, x2o=x2o: E.activation(
                    x2b[:, x2o:x2o + 512], xbb[:, co:co + 512], AF.Square))
                p.wait("pe", *key)
                p.wait("pe", "ac")
                p.emit("pe", lambda E, co=co, kch=kch: E.matmul(
                    ps_d1[:], t["ones_col_bf"][:], xbb[:, co:co + 512],
                    start=(kch == 0), stop=(kch == KCH - 1)))
                mmv = p.inc("pe", "pe", 1, lambda E, x2o=x2o, kch=kch: E.matmul(
                    ps_d2[:], t["ones_col_bf"][:], x2b[:, x2o:x2o + 512],
                    start=(kch == 0), stop=(kch == KCH - 1)))
                if x2i % 8 == 0:
                    t["_x2pe"] = p.ct["pe"]
            ringb.consumed(i, "pe")
            qidx[q] = i
        p.wait("act", "pe", mmv)
        p.inc("act", "ac", 1, lambda E: E.activation(row[0:1, 0:512], ps_d1[:], AF.Copy, scale=1.0 / 2048.0))
        p.inc("act", "ac", 1, lambda E: E.activation(row[0:1, 512:1024], ps_d2[:], AF.Copy, scale=1.0 / 2048.0))
        p.inc("act", "ac", 1, lambda E: E.activation(row[0:1, 1024:1536], row[0:1, 0:512], AF.Square))
        p.wait("dve", "ac")
        p.inc("dve", "dv", 1, lambda E: E.tensor_tensor(
            row[0:1, 512:1024], row[0:1, 512:1024], row[0:1, 1024:1536], ALU.subtract))
        p.wait("act", "dv")
        p.inc("act", "ac", 1, lambda E: E.activation(row[0:1, 1536:2048], row[0:1, 512:1024], AF.Ln, bias=t["eps_col"][0:1, :]))
        p.inc("act", "ac", 1, lambda E: E.activation(row[0:1, 1536:2048], row[0:1, 1536:2048], AF.Exp, scale=-0.5))
        p.wait("pe", "ac")
        p.wait("pe", "dv")
        p.emit("pe", lambda E: E.matmul(ps_b[:, 0:512], t["ones_row_f"][:], row[0:1, 0:512],
                                        start=True, stop=True))
        bcv = p.inc("pe", "pe", 1, lambda E: E.matmul(
            ps_b[:, 512:1024], t["ones_row_f"][:], row[0:1, 1536:2048], start=True, stop=True))
        p.wait("act", "pe", bcv)
        p.inc("act", "ac", 1, lambda E: E.activation(t["mBb"][:], ps_b[:, 0:512], AF.Copy))
        cbv = p.inc("act", "ac", 1, lambda E: E.activation(t["rBb"][:], ps_b[:, 512:1024], AF.Copy))
        p.wait("dve", "ac", cbv)
        for kch in range(KCH):
            q, j = kch // 4, kch % 4
            co = q * 2048 + j * 512
            c0 = xn_col0 + tile * 512
            p.inc("dve", "dv", 1, lambda E, co=co: E.tensor_tensor(
                t["tmpa_bf"][:, 0:512], xbb[:, co:co + 512], t["mBb"][:], ALU.subtract))
            p.inc("dve", "dv", 1, lambda E, kch=kch, c0=c0: E.tensor_tensor(
                xn_bf[:, kch * xn_seg + c0: kch * xn_seg + c0 + 512],
                t["tmpa_bf"][:, 0:512], t["rBb"][:], ALU.mult))
        for q in range(4):
            ringb.done[qidx[q]]["dv"] = p.ct["dv"]
    return {"dv": p.ct["dv"]}


def emit_rope(p, t, ps_view, ctab, stab, ccol0, dst_bf, dst_col0):
    qr, sw, ta = t["qraw_bf"], t["tswp_bf"], t["tmpa_bf"]
    p.wait("act", "pe")
    psfree = p.inc("act", "ac", 1, lambda E: E.activation(qr[:], ps_view, AF.Copy))
    p.wait("dve", "ac", psfree)
    p.inc("dve", "dv", 1, lambda E: E.stream_shuffle(sw[:], qr[:], SWAP_MASK))
    p.inc("dve", "dv", 1, lambda E: E.tensor_tensor(
        ta[:], qr[:], ctab[:, ccol0:ccol0 + 1024], ALU.mult))
    p.inc("dve", "dv", 1, lambda E: E.tensor_tensor(
        sw[:], sw[:], stab[:, ccol0:ccol0 + 1024], ALU.mult))
    wv = p.inc("dve", "dv", 1, lambda E: E.tensor_tensor(
        dst_bf[:, dst_col0:dst_col0 + 1024], ta[:], sw[:], ALU.add))
    return ("ac", psfree), wv


def emit_rope_sb(p, t, src_view, ctab, stab, ccol0, dst_bf, dst_col0):
    sw, ta = t["tswp_bf"], t["tmpa_bf"]
    p.inc("dve", "dv", 1, lambda E: E.stream_shuffle(sw[:], src_view, SWAP_MASK))
    srcfree = p.inc("dve", "dv", 1, lambda E: E.tensor_tensor(
        ta[:], src_view, ctab[:, ccol0:ccol0 + 1024], ALU.mult))
    p.inc("dve", "dv", 1, lambda E: E.tensor_tensor(
        sw[:], sw[:], stab[:, ccol0:ccol0 + 1024], ALU.mult))
    wv = p.inc("dve", "dv", 1, lambda E: E.tensor_tensor(
        dst_bf[:, dst_col0:dst_col0 + 1024], ta[:], sw[:], ALU.add))
    return srcfree, wv


def emit_v_group(p, t, wv_hbm, wv_cache, ring, xn_bf, halfq, dvq,
                 v_stage, v_out, row0, vfree, fill_cache, stv):
    ps_b = t["ps_b"]
    slots = []
    for wh in range(2):
        if fill_cache and halfq == 0:
            i, slot, key = ring.load(lambda s: (lambda E, dvq=dvq, wh=wh, s=s: E.dma_start(
                out=t["wvbuf"][:, s * 4096:(s + 1) * 4096],
                in_=wv_hbm[dvq][:, wh * 4096:(wh + 1) * 4096])))
            p.wait("pool", *key)
            wcv = p.inc("pool", f"rvw{slot}", 16, lambda E, dvq=dvq, wh=wh, s=slot: E.dma_start(
                out=wv_cache[dvq][:, wh * 4096:(wh + 1) * 4096],
                in_=t["wvbuf"][:, s * 4096:(s + 1) * 4096]))
            ring.done[i][f"rvw{slot}"] = wcv
        else:
            i, slot, key = ring.load(lambda s: (lambda E, dvq=dvq, wh=wh, s=s: E.dma_start(
                out=t["wvbuf"][:, s * 4096:(s + 1) * 4096],
                in_=wv_cache[dvq][:, wh * 4096:(wh + 1) * 4096])))
        slots.append((i, slot, key))
    for tc2 in range(2):
        for sem, val in vfree.items():
            p.wait("pe", sem, val)
        for wh in range(2):
            i, slot, key = slots[wh]
            p.wait("pe", key[0], key[1])
            for kch in range(wh * 8, wh * 8 + 8):
                for half in range(2):
                    tc = tc2 * 2 + half
                    tok = halfq * 512 + tc * 128
                    mm = lambda E, kch=kch, tok=tok, slot=slot, half=half: E.matmul(
                        ps_b[:, half * 512:(half + 1) * 512],
                        xn_bf[:, kch * 1024 + tok: kch * 1024 + tok + 128],
                        t["wvbuf"][:, slot * 4096 + (kch % 8) * 512:
                                   slot * 4096 + (kch % 8 + 1) * 512],
                        start=(kch == 0), stop=(kch == KCH - 1))
                    if kch == wh * 8 + 7 and half == 1:
                        p.inc("pe", "pe", 1, mm)
                    else:
                        p.emit("pe", mm)
            if tc2 == 1:
                ring.consumed(i, "pe")
        p.wait("act", "pe")
        if stv:
            p.wait("act", "sta", stv[-1][1])
        for half in range(2):
            tc = tc2 * 2 + half
            p.inc("act", "ac", 1, lambda E, tc=tc, dvq=dvq, half=half: E.activation(
                v_stage[:, tc, dvq * 512:(dvq + 1) * 512],
                ps_b[:, half * 512:(half + 1) * 512], AF.Copy))
        vfree = {"ac": p.ct["ac"]}
    if dvq == 3:
        p.wait("sync", "ac")
        for tc in range(4):
            r0 = row0 + halfq * 512 + tc * 128
            stv.append((halfq, p.inc("sync", "sta", 16, lambda E, tc=tc, r0=r0: E.dma_start(
                out=v_out[r0:r0 + 128, :], in_=v_stage[:, tc, :]))))
    return vfree


def emit_headnorm(p, t, src_bf, col0, och, statbase=0):
    sq_b, ps_d1, ps_d2 = t["sq_b"], t["ps_d1"], t["ps_d2"]
    p.wait("act", "dv")
    p.wait("act", "pe")
    p.inc("act", "ac", 1, lambda E: E.activation(
        sq_b[:], src_bf[:, col0:col0 + 1024], AF.Square))
    p.wait("pe", "ac")
    for half, mask in ((0, "mask1_bf"), (1, "mask2_bf")):
        p.wait("pe", "ac", t.get("hn_ac", 0))
        p.emit("pe", lambda E, mask=mask: E.matmul(
            ps_d1[:], t[mask][:], sq_b[:, 0:512], start=True, stop=True))
        mv = p.inc("pe", "pe", 1, lambda E, mask=mask: E.matmul(
            ps_d2[:], t[mask][:], sq_b[:, 512:1024], start=True, stop=True))
        p.wait("act", "pe", mv)
        p.wait("act", "dv", t.get("hn_rv", 0))
        p.inc("act", "ac", 1, lambda E: E.activation(t["nrow"][0:1, 0:512], ps_d1[:], AF.Copy))
        nv = p.inc("act", "ac", 1, lambda E: E.activation(t["nrow"][0:1, 512:1024], ps_d2[:], AF.Copy))
        t["hn_ac"] = nv
        p.wait("dve", "ac", nv)
        p.wait("dve", "pl")
        rv_ = p.inc("dve", "dv", 1, lambda E: E.tensor_reduce(
            t["trow"][0:1, 2:3], t["nrow"][0:1, 0:1024], AX.X, ALU.max))
        t["hn_rv"] = rv_
        col = statbase + och + 16 * half
        p.inc("dve", "dv", 1, lambda E, col=col: E.tensor_tensor(
            t["srow"][0:1, col:col + 1], t["srow"][0:1, col:col + 1],
            t["trow"][0:1, 2:3], ALU.max))
    return p.ct["pe"]


def emit_fmproj(p, t, w_hbm, wbuf, xn_bf, xn_seg, xn_col0, consume, psfree0):
    ps_a = t["ps_a"]
    free = [dict(psfree0), dict(psfree0)]
    ldv = []
    use_done = []
    for och in range(16):
        par = och % 2
        if och >= 2:
            for sem, val in use_done[och - 2].items():
                p.wait("pool", sem, val)
        v = p.inc("pool", f"ldw{par}", 16, lambda E, och=och, par=par: E.dma_start(
            out=wbuf[:, par * 2048:(par + 1) * 2048], in_=w_hbm[och]))
        ldv.append(v)
        for sem, val in free[par].items():
            p.wait("pe", sem, val)
        p.wait("pe", f"ldw{par}", v)
        for kch in range(KCH):
            for half in range(2):
                mm = lambda E, och=och, kch=kch, half=half, par=par: E.matmul(
                    ps_a[:, par * 1024 + half * 512: par * 1024 + (half + 1) * 512],
                    wbuf[:, par * 2048 + kch * 128: par * 2048 + (kch + 1) * 128],
                    xn_bf[:, kch * xn_seg + xn_col0 + half * 512:
                          kch * xn_seg + xn_col0 + (half + 1) * 512],
                    start=(kch == 0), stop=(kch == KCH - 1))
                if kch == KCH - 1 and half == 1:
                    mmv = p.inc("pe", "pe", 1, mm)
                else:
                    p.emit("pe", mm)
        use_done.append({"pe": p.ct["pe"]})
        p.wait("dve", "pe", mmv)
        free[par] = consume(och, mmv)
    return free[0], free[1]


def emit_vproj(p, t, wv_hbm, wvbuf, xn_bf, xn_seg, xn_col0, v_stage, v_out, row0, psfree0):
    ps_a = t["ps_a"]

    def vps(j):
        return ps_a[:, j * 512:(j + 1) * 512]

    free = dict(psfree0)
    use_done = []
    gi = 0
    stv = []
    for halfq in range(2):
        for dvq in range(4):
            par = gi % 2
            if gi >= 2:
                for sem, val in use_done[gi - 2].items():
                    p.wait("pool", sem, val)
            v = p.inc("pool", f"ldw{par}", 16, lambda E, dvq=dvq, par=par: E.dma_start(
                out=wvbuf[:, par * 8192:(par + 1) * 8192], in_=wv_hbm[dvq]))
            for sem, val in free.items():
                p.wait("pe", sem, val)
            p.wait("pe", f"ldw{par}", v)
            for kch in range(KCH):
                for tc in range(4):
                    tok = xn_col0 + halfq * 512 + tc * 128
                    mm = lambda E, kch=kch, tc=tc, tok=tok, par=par: E.matmul(
                        vps(tc),
                        xn_bf[:, kch * xn_seg + tok: kch * xn_seg + tok + 128],
                        wvbuf[:, par * 8192 + kch * 512: par * 8192 + (kch + 1) * 512],
                        start=(kch == 0), stop=(kch == KCH - 1))
                    if kch == KCH - 1 and tc == 3:
                        p.inc("pe", "pe", 1, mm)
                    else:
                        p.emit("pe", mm)
            use_done.append({"pe": p.ct["pe"]})
            p.wait("dve", "pe")
            for tc in range(4):
                p.inc("dve", "dv", 1, lambda E, tc=tc, dvq=dvq: E.tensor_copy(
                    v_stage[:, tc, dvq * 512:(dvq + 1) * 512], vps(tc)))
            free = {"dv": p.ct["dv"]}
            gi += 1
        p.wait("sync", "dv")
        for tc in range(4):
            r0 = row0 + halfq * 512 + tc * 128
            stv.append(p.inc("sync", "sta", 16, lambda E, tc=tc, r0=r0: E.dma_start(
                out=v_out[r0:r0 + 128, :], in_=v_stage[:, tc, :])))
        if halfq == 0:
            p.wait("dve", "sta", stv[-1])
    return free


def build_A():
    nc = bass.Bass("TRN2", target_bir_lowering=False)
    xT = nc.dram_tensor("xT", [E, TQ], F32, kind="ExternalInput").ap()
    wk = nc.dram_tensor("wk", [16, 128, 2048], F32, kind="ExternalInput").ap()
    wv = nc.dram_tensor("wv", [4, 128, 8192], F32, kind="ExternalInput").ap()
    ct = nc.dram_tensor("ct", [128, TQ], BF16, kind="ExternalInput").ap()
    stb = nc.dram_tensor("stb", [128, TQ], BF16, kind="ExternalInput").ap()
    kT_out = nc.dram_tensor("kT_out", [E, TQ], BF16, kind="ExternalOutput").ap()
    v_out = nc.dram_tensor("v_out", [TQ, E], BF16, kind="ExternalOutput").ap()
    kstat_out = nc.dram_tensor("kstat_out", [1, 64], F32, kind="ExternalOutput").ap()

    from contextlib import ExitStack
    ctx = ExitStack()
    ent = ctx.enter_context
    t = {}
    alloc_common(nc, ent, t)
    t["xn_bf"] = ent(nc.sbuf_tensor("xn_bf", [128, 16 * 1024], BF16))
    t["wbuf"] = ent(nc.sbuf_tensor("wbuf", [128, 4096], BF16))
    t["wvbuf"] = ent(nc.sbuf_tensor("wvbuf", [128, 16384], BF16))
    t["v_stage"] = ent(nc.sbuf_tensor("v_stage", [128, 4, 2048], BF16))
    t["kstage"] = ent(nc.sbuf_tensor("kstage", [128, 2048], BF16))
    t["ctab"] = ent(nc.sbuf_tensor("ctab", [128, TQ], BF16))
    t["qraw_bf"] = ent(nc.sbuf_tensor("qraw_bf", [128, 1024], BF16))
    t["tswp_bf"] = ent(nc.sbuf_tensor("tswp_bf", [128, 1024], BF16))
    t["tmpa_bf"] = ent(nc.sbuf_tensor("tmpa_bf", [128, 1024], BF16))
    t["stab"] = ent(nc.sbuf_tensor("stab", [128, TQ], BF16))

    p = Prog(nc, ent)
    emit_init(p, t)
    p.inc("sync", "lds", 16, lambda E: E.dma_start(out=t["ctab"][:], in_=ct))
    tabv = p.inc("sync", "lds", 16, lambda E: E.dma_start(out=t["stab"][:], in_=stb))
    psfree = emit_ln(p, t, xT, t["xn_bf"], 1024, 0)
    p.wait("dve", "lds", tabv)
    p.wait("pool", "lds", tabv)

    kst_store = []
    ksq_done = []

    def consume_k(och, mmv):
        half = och % 2
        if och >= 2:
            p.wait("dve", f"st{half}", kst_store[och - 2])
            p.wait("dve", "ac", ksq_done[och - 2])
        fr, wv_ = emit_rope(p, t, t["ps_a"][:, half * 1024:(half + 1) * 1024],
                            t["ctab"], t["stab"], 0, t["kstage"], half * 1024)
        emit_headnorm(p, t, t["kstage"], half * 1024, och)
        ksq_done.append(p.ct["ac"])
        p.wait("sync", "dv", wv_)
        kst_store.append(p.inc("sync", f"st{half}", 16, lambda E, och=och, half=half: E.dma_start(
            out=kT_out[och * 128:(och + 1) * 128, :],
            in_=t["kstage"][:, half * 1024:(half + 1) * 1024])))
        return {fr[0]: fr[1]}

    f0, f1 = emit_fmproj(p, t, wk, t["wbuf"], t["xn_bf"], 1024, 0, consume_k, psfree)
    both = {k: max(f0.get(k, 0), f1.get(k, 0)) for k in set(f0) | set(f1)}
    emit_vproj(p, t, wv, t["wvbuf"], t["xn_bf"], 1024, 0, t["v_stage"], v_out, 0, both)
    p.wait("sync", "dv")
    p.inc("sync", "sta", 16, lambda E: E.dma_start(out=kstat_out, in_=t["srow"][:]))
    p.wait("sync", "st0")
    p.wait("sync", "st1")
    p.wait("sync", "sta")
    block = ent(nc.Block())
    attach(p, block)
    ctx.close()
    return nc


def build_B(debug=False):
    nc = bass.Bass("TRN2", target_bir_lowering=False)
    xT = nc.dram_tensor("xT", [E, TQ], F32, kind="ExternalInput").ap()
    kT = nc.dram_tensor("kT", [E, S], BF16, kind="ExternalInput").ap()
    vf = nc.dram_tensor("vf", [S, E], BF16, kind="ExternalInput").ap()
    ks = nc.dram_tensor("ks", [4, 64], F32, kind="ExternalInput").ap()
    wq = nc.dram_tensor("wq", [16, 128, 2048], F32, kind="ExternalInput").ap()
    wo = nc.dram_tensor("wo", [16, 128, 2048], F32, kind="ExternalInput").ap()
    gu = nc.dram_tensor("gu", [44, 128, 4096], F32, kind="ExternalInput").ap()
    dn = nc.dram_tensor("dn", [16, 128, 5632], F32, kind="ExternalInput").ap()
    gs = nc.dram_tensor("gs", [128, 44], F32, kind="ExternalInput").ap()
    us = nc.dram_tensor("us", [128, 44], F32, kind="ExternalInput").ap()
    ds = nc.dram_tensor("ds", [128, 16], F32, kind="ExternalInput").ap()
    gam = nc.dram_tensor("gam", [128, 1], F32, kind="ExternalInput").ap()
    lq1 = nc.dram_tensor("lq1", [1, 64], F32, kind="ExternalInput").ap()
    lk1 = nc.dram_tensor("lk1", [1, 64], F32, kind="ExternalInput").ap()
    lq2 = nc.dram_tensor("lq2", [1, 64], F32, kind="ExternalInput").ap()
    lk2 = nc.dram_tensor("lk2", [1, 64], F32, kind="ExternalInput").ap()
    ct = nc.dram_tensor("ct", [128, TQ], BF16, kind="ExternalInput").ap()
    stb = nc.dram_tensor("stb", [128, TQ], BF16, kind="ExternalInput").ap()
    outT = nc.dram_tensor("outT", [E, TQ], F32, kind="ExternalOutput").ap()
    dbg = {}
    if debug:
        for nm, shp, dt in (("d_xn", [E, TQ], BF16), ("d_q", [E, TQ], BF16),
                            ("d_attn", [E, TQ], BF16), ("d_acc", [E, TQ], F32),
                            ("d_bias", [128, 16], F32), ("d_srow", [1, 64], F32)):
            dbg[nm] = nc.dram_tensor(nm, shp, dt, kind="ExternalOutput").ap()

    from contextlib import ExitStack
    ctx = ExitStack()
    ent = ctx.enter_context
    t = {}
    t["ps_a"] = ent(nc.psum_tensor("ps_a", [128, 2048], F32))
    t["ps_b"] = ent(nc.psum_tensor("ps_b", [128, 1024], F32))
    t["ps_d1"] = ent(nc.psum_tensor("ps_d1", [1, 512], F32))
    t["ps_d2"] = ent(nc.psum_tensor("ps_d2", [1, 512], F32))
    t["ones_col_bf"] = ent(nc.sbuf_tensor("ones_col_bf", [128, 1], BF16))
    t["ones_row_f"] = ent(nc.sbuf_tensor("ones_row_f", [1, 128], F32))
    t["neg_row_f"] = ent(nc.sbuf_tensor("neg_row_f", [1, 128], F32))
    t["ones128_bf"] = ent(nc.sbuf_tensor("ones128_bf", [128, 128], BF16))
    t["mask1_bf"] = ent(nc.sbuf_tensor("mask1_bf", [128, 1], BF16))
    t["mask2_bf"] = ent(nc.sbuf_tensor("mask2_bf", [128, 1], BF16))
    t["tmp0"] = ent(nc.sbuf_tensor("tmp0", [128, 1024], F32))
    t["tmp1"] = ent(nc.sbuf_tensor("tmp1", [128, 1024], F32))
    t["srow"] = ent(nc.sbuf_tensor("srow", [1, 64], F32))
    t["trow"] = ent(nc.sbuf_tensor("trow", [1, 16], F32))
    t["eps_col"] = ent(nc.sbuf_tensor("eps_col", [128, 1], F32))
    t["nrow"] = ent(nc.sbuf_tensor("nrow", [1, 1024], F32))
    t["mBb"] = ent(nc.sbuf_tensor("mBb", [128, 512], BF16))
    t["rBb"] = ent(nc.sbuf_tensor("rBb", [128, 512], BF16))
    t["bias_sb"] = ent(nc.sbuf_tensor("bias_sb", [128, 16], F32))
    t["lam_vec"] = ent(nc.sbuf_tensor("lam_vec", [128, 1], F32))
    t["gam_eff"] = ent(nc.sbuf_tensor("gam_eff", [128, 1], F32))
    t["gs_sb"] = ent(nc.sbuf_tensor("gs_sb", [128, 44], F32))
    t["us_sb"] = ent(nc.sbuf_tensor("us_sb", [128, 44], F32))
    t["ds_sb"] = ent(nc.sbuf_tensor("ds_sb", [128, 16], F32))
    t["lrow"] = ent(nc.sbuf_tensor("lrow", [1, 256], F32))
    t["krow"] = ent(nc.sbuf_tensor("krow", [1, 256], F32))
    t["xn_bf"] = ent(nc.sbuf_tensor("xn_bf", [128, 16 * 1024], BF16))
    t["q_bf"] = ent(nc.sbuf_tensor("q_bf", [128, 16 * 1024], BF16))
    t["acc"] = ent(nc.sbuf_tensor("acc", [128, 16 * 1024], F32))

    p = Prog(nc, ent)
    emit_init(p, t)
    p.wait("pe", "pl")
    for name, src_ap in (("gs_sb", gs), ("us_sb", us), ("ds_sb", ds)):
        p.inc("sync", "lds", 16, lambda E, src_ap=src_ap, name=name: E.dma_start(
            out=t[name][:], in_=src_ap))
    p.inc("sync", "lds", 16, lambda E: E.dma_start(out=t["lrow"][0:1, 0:64], in_=lq1))
    p.inc("sync", "lds", 16, lambda E: E.dma_start(out=t["lrow"][0:1, 64:128], in_=lk1))
    p.inc("sync", "lds", 16, lambda E: E.dma_start(out=t["lrow"][0:1, 128:192], in_=lq2))
    p.inc("sync", "lds", 16, lambda E: E.dma_start(out=t["lrow"][0:1, 192:256], in_=lk2))
    for r in range(4):
        p.inc("sync", "lds", 16, lambda E, r=r: E.dma_start(
            out=t["krow"][0:1, r * 64:(r + 1) * 64], in_=ks[r:r + 1, :]))
    gamv = p.inc("sync", "lds", 16, lambda E: E.dma_start(out=t["tmp0"][:, 0:1], in_=gam))
    p.wait("dve", "lds", gamv)
    p.wait("pool", "lds", gamv)
    p.wait("act", "lds", gamv)
    p.inc("dve", "dv", 1, lambda E: E.tensor_scalar(
        t["gam_eff"][:], t["tmp0"][:, 0:1], ONE_MINUS_LI, 0.0, ALU.mult, ALU.add))
    p.inc("dve", "dv", 1, lambda E: E.tensor_tensor(
        t["lrow"][0:1, 0:64], t["lrow"][0:1, 0:64], t["lrow"][0:1, 64:128], ALU.mult))
    p.inc("dve", "dv", 1, lambda E: E.tensor_tensor(
        t["lrow"][0:1, 128:192], t["lrow"][0:1, 128:192], t["lrow"][0:1, 192:256], ALU.mult))
    p.inc("dve", "dv", 1, lambda E: E.tensor_reduce(
        t["trow"][0:1, 0:1], t["lrow"][0:1, 0:64], AX.X, ALU.add))
    p.inc("dve", "dv", 1, lambda E: E.tensor_reduce(
        t["trow"][0:1, 1:2], t["lrow"][0:1, 128:192], AX.X, ALU.add))
    p.wait("act", "dv")
    p.inc("act", "ac", 1, lambda E: E.activation(t["trow"][0:1, 2:3], t["trow"][0:1, 0:1], AF.Exp))
    p.inc("act", "ac", 1, lambda E: E.activation(t["trow"][0:1, 3:4], t["trow"][0:1, 1:2], AF.Exp))
    p.wait("dve", "ac")
    p.inc("dve", "dv", 1, lambda E: E.tensor_tensor(
        t["trow"][0:1, 2:3], t["trow"][0:1, 2:3], t["trow"][0:1, 3:4], ALU.subtract))
    lamv = p.inc("dve", "dv", 1, lambda E: E.tensor_scalar(
        t["trow"][0:1, 2:3], t["trow"][0:1, 2:3], 1.0, LAMBDA_INIT, ALU.mult, ALU.add))
    p.wait("pe", "dv", lamv)
    lbv = p.inc("pe", "pe", 1, lambda E: E.matmul(
        t["ps_b"][:, 0:1], t["ones_row_f"][:], t["trow"][0:1, 2:3], start=True, stop=True))
    p.wait("dve", "pe", lbv)
    p.inc("dve", "dv", 1, lambda E: E.tensor_copy(t["lam_vec"][:], t["ps_b"][:, 0:1]))

    ctx1 = ExitStack()
    ent1 = ctx1.enter_context
    t["xc"] = ent1(nc.sbuf_tensor("xc", [128, 4096], F32, side="right"))
    t["xbb"] = ent1(nc.sbuf_tensor("xbb", [128, 8192], BF16, side="right"))
    t["x2b"] = ent1(nc.sbuf_tensor("x2b", [128, 4096], BF16, side="right"))
    t["row"] = ent1(nc.sbuf_tensor("row", [1, 2048], F32, side="right"))
    t["sq_b"] = ent1(nc.sbuf_tensor("sq_b", [128, 1024], BF16, side="right"))
    t["ctab"] = ent1(nc.sbuf_tensor("ctab", [128, TQ], BF16, side="right"))
    t["qraw_bf"] = ent1(nc.sbuf_tensor("qraw_bf", [128, 1024], BF16, side="right"))
    t["tswp_bf"] = ent1(nc.sbuf_tensor("tswp_bf", [128, 1024], BF16, side="right"))
    t["tmpa_bf"] = ent1(nc.sbuf_tensor("tmpa_bf", [128, 1024], BF16, side="right"))
    t["stab"] = ent1(nc.sbuf_tensor("stab", [128, TQ], BF16, side="right"))
    t["wbuf"] = ent1(nc.sbuf_tensor("wbuf", [128, 4096], BF16, side="right"))

    p.inc("sync", "lds", 16, lambda E: E.dma_start(out=t["ctab"][:], in_=ct))
    tabv = p.inc("sync", "lds", 16, lambda E: E.dma_start(out=t["stab"][:], in_=stb))
    psfree = emit_ln(p, t, xT, t["xn_bf"], 1024, 0)
    p.wait("dve", "lds", tabv)
    p.wait("pool", "lds", tabv)
    if debug:
        p.wait("sync", "dv")
        for kch in range(KCH):
            p.inc("sync", "sta", 16, lambda E, kch=kch: E.dma_start(
                out=dbg["d_xn"][kch * 128:(kch + 1) * 128, :],
                in_=t["xn_bf"][:, kch * 1024:(kch + 1) * 1024]))

    def consume_q(och, mmv):
        fr, wv_ = emit_rope(p, t, t["ps_a"][:, (och % 2) * 1024:(och % 2 + 1) * 1024],
                            t["ctab"], t["stab"], 0, t["q_bf"], och * 1024)
        emit_headnorm(p, t, t["q_bf"], och * 1024, och)
        return {fr[0]: fr[1]}

    emit_fmproj(p, t, wq, t["wbuf"], t["xn_bf"], 1024, 0, consume_q, psfree)
    if debug:
        p.wait("sync", "dv")
        for kch in range(KCH):
            p.inc("sync", "sta", 16, lambda E, kch=kch: E.dma_start(
                out=dbg["d_q"][kch * 128:(kch + 1) * 128, :],
                in_=t["q_bf"][:, kch * 1024:(kch + 1) * 1024]))
        p.inc("sync", "sta", 16, lambda E: E.dma_start(out=dbg["d_srow"], in_=t["srow"][:]))

    p.wait("dve", "lds")
    for r in (1, 2, 3):
        p.inc("dve", "dv", 1, lambda E, r=r: E.tensor_tensor(
            t["krow"][0:1, 0:64], t["krow"][0:1, 0:64],
            t["krow"][0:1, r * 64:(r + 1) * 64], ALU.max))
    p.inc("dve", "dv", 1, lambda E: E.tensor_tensor(
        t["lrow"][0:1, 0:16], t["srow"][0:1, 0:16], t["krow"][0:1, 0:16], ALU.mult))
    p.inc("dve", "dv", 1, lambda E: E.tensor_tensor(
        t["lrow"][0:1, 16:32], t["srow"][0:1, 16:32], t["krow"][0:1, 16:32], ALU.mult))
    p.inc("dve", "dv", 1, lambda E: E.tensor_tensor(
        t["lrow"][0:1, 64:80], t["lrow"][0:1, 0:16], t["lrow"][0:1, 16:32], ALU.max))
    p.wait("act", "dv")
    brv = p.inc("act", "ac", 1, lambda E: E.activation(
        t["lrow"][0:1, 128:144], t["lrow"][0:1, 64:80], AF.Sqrt, scale=1.0 / 64.0))
    p.wait("pe", "ac", brv)
    p.wait("pe", "dv")
    bbv = p.inc("pe", "pe", 1, lambda E: E.matmul(
        t["ps_b"][:, 0:16], t["neg_row_f"][:], t["lrow"][0:1, 128:144], start=True, stop=True))
    p.wait("dve", "pe", bbv)
    biasv = p.inc("dve", "dv", 1, lambda E: E.tensor_copy(t["bias_sb"][:], t["ps_b"][:, 0:16]))
    if debug:
        p.wait("sync", "dv", biasv)
        p.inc("sync", "sta", 16, lambda E: E.dma_start(out=dbg["d_bias"], in_=t["bias_sb"][:]))
    ctx1.close()

    ctx2 = ExitStack()
    ent2 = ctx2.enter_context
    scope_barrier(p)
    t["h_bf"] = ent2(nc.sbuf_tensor("h_bf", [128, ICH * 512], BF16, side="right"))
    t["wffn"] = ent2(nc.sbuf_tensor("wffn", [128, 8448], BF16, side="right"))
    gring = Ring(p, "rg", 2, "pool")
    uring = Ring(p, "ru", 2, "pool")
    dring = Ring(p, "rd", 3, "pool")
    wffn = t["wffn"]
    UOFF = 4096
    for tile in range(2):
        gufree = {"dv": p.ct["dv"]}
        p.wait("dve", "pe")
        p.wait("pool", "pe")
        for ich in range(ICH):
            par = ich % 2
            gi, gsl, gkey = gring.load(lambda s: (lambda E, ich=ich, s=s: E.dma_start(
                out=wffn[:, s * 2048:(s + 1) * 2048], in_=gu[ich][:, 0:2048])))
            ui, usl, ukey = uring.load(lambda s: (lambda E, ich=ich, s=s: E.dma_start(
                out=wffn[:, UOFF + s * 2048:UOFF + (s + 1) * 2048], in_=gu[ich][:, 2048:4096])))
            for sem, val in gufree.items():
                p.wait("pe", sem, val)
            p.wait("pe", *gkey)
            p.wait("pe", *ukey)
            for kch in range(KCH):
                p.emit("pe", lambda E, kch=kch, par=par, tile=tile, gsl=gsl: E.matmul(
                    t["ps_a"][:, par * 1024: par * 1024 + 512],
                    wffn[:, gsl * 2048 + kch * 128: gsl * 2048 + (kch + 1) * 128],
                    t["xn_bf"][:, kch * 1024 + tile * 512: kch * 1024 + (tile + 1) * 512],
                    start=(kch == 0), stop=(kch == KCH - 1)))
                mm = lambda E, kch=kch, par=par, tile=tile, usl=usl: E.matmul(
                    t["ps_a"][:, par * 1024 + 512: par * 1024 + 1024],
                    wffn[:, UOFF + usl * 2048 + kch * 128: UOFF + usl * 2048 + (kch + 1) * 128],
                    t["xn_bf"][:, kch * 1024 + tile * 512: kch * 1024 + (tile + 1) * 512],
                    start=(kch == 0), stop=(kch == KCH - 1))
                if kch == KCH - 1:
                    gv = p.inc("pe", "pe", 1, mm)
                else:
                    p.emit("pe", mm)
            gring.consumed(gi, "pe")
            uring.consumed(ui, "pe")
            p.wait("act", "pe", gv)
            p.wait("act", "dv")
            p.inc("act", "ac", 1, lambda E, par=par, ich=ich: E.activation(
                t["tmp1"][:, 512:1024],
                t["ps_a"][:, par * 1024: par * 1024 + 512],
                AF.Silu, scale=t["gs_sb"][:, ich:ich + 1]))
            p.wait("dve", "ac")
            p.inc("dve", "dv", 1, lambda E, par=par, ich=ich: E.tensor_scalar(
                t["tmp0"][:, 0:512], t["ps_a"][:, par * 1024 + 512: par * 1024 + 1024],
                t["us_sb"][:, ich:ich + 1], 0.0, ALU.mult, ALU.add))
            p.inc("dve", "dv", 1, lambda E, par=par, ich=ich: E.tensor_tensor(
                t["h_bf"][:, ich * 512:(ich + 1) * 512], t["tmp0"][:, 0:512],
                t["tmp1"][:, 512:1024], ALU.mult))
            gufree = {"dv": p.ct["dv"]}
        dfree = [{"dv": p.ct["dv"]}, {"dv": p.ct["dv"]}]
        p.wait("pool", "pe")
        for hch in range(16):
            par = hch % 2
            d0, s0, k0 = dring.load(lambda s: (lambda E, hch=hch, s=s: E.dma_start(
                out=wffn[:, s * 2816:(s + 1) * 2816], in_=dn[hch][:, 0:2816])))
            d1, s1, k1 = dring.load(lambda s: (lambda E, hch=hch, s=s: E.dma_start(
                out=wffn[:, s * 2816:(s + 1) * 2816], in_=dn[hch][:, 2816:5632])))
            for sem, val in dfree[par].items():
                p.wait("pe", sem, val)
            if hch == 0:
                p.wait("pe", "dv", gufree["dv"])
            p.wait("pe", *k0)
            for ich in range(22):
                p.emit("pe", lambda E, ich=ich, par=par, s0=s0: E.matmul(
                    t["ps_b"][:, par * 512:(par + 1) * 512],
                    wffn[:, s0 * 2816 + ich * 128: s0 * 2816 + (ich + 1) * 128],
                    t["h_bf"][:, ich * 512:(ich + 1) * 512],
                    start=(ich == 0), stop=False))
            p.wait("pe", *k1)
            for ich in range(22, ICH):
                mm = lambda E, ich=ich, par=par, s1=s1: E.matmul(
                    t["ps_b"][:, par * 512:(par + 1) * 512],
                    wffn[:, s1 * 2816 + (ich - 22) * 128: s1 * 2816 + (ich - 21) * 128],
                    t["h_bf"][:, ich * 512:(ich + 1) * 512],
                    start=False, stop=(ich == ICH - 1))
                if ich == ICH - 1:
                    dv_ = p.inc("pe", "pe", 1, mm)
                else:
                    p.emit("pe", mm)
            dring.consumed(d0, "pe")
            dring.consumed(d1, "pe")
            p.wait("dve", "pe", dv_)
            p.inc("dve", "dv", 1, lambda E, hch=hch, par=par, tile=tile: E.tensor_scalar(
                t["acc"][:, hch * 1024 + tile * 512: hch * 1024 + (tile + 1) * 512],
                t["ps_b"][:, par * 512:(par + 1) * 512],
                t["ds_sb"][:, hch:hch + 1], 0.0, ALU.mult, ALU.add))
            dfree[par] = {"dv": p.ct["dv"]}
    ctx2.close()

    ctx3 = ExitStack()
    ent3 = ctx3.enter_context
    scope_barrier(p)
    t["e12"] = ent3(nc.sbuf_tensor("e12", [128, 2048], BF16, side="right"))
    t["ktb"] = ent3(nc.sbuf_tensor("ktb", [128, 8192], BF16, side="right"))
    t["vtb"] = ent3(nc.sbuf_tensor("vtb", [128, 8192], BF16, side="right"))
    t["r1B"] = ent3(nc.sbuf_tensor("r1B", [128, 512], F32, side="right"))
    t["r2B"] = ent3(nc.sbuf_tensor("r2B", [128, 512], F32, side="right"))
    t["uB"] = ent3(nc.sbuf_tensor("uB", [128, 512], F32, side="right"))
    t["rstdB"] = ent3(nc.sbuf_tensor("rstdB", [128, 512], F32, side="right"))
    t["sqa"] = ent3(nc.sbuf_tensor("sqa", [128, 512], BF16, side="right"))
    p.wait("act", "dv", biasv)
    attn_bf = t["xn_bf"]
    ps_a, ps_b, ps_d1, ps_d2 = t["ps_a"], t["ps_b"], t["ps_d1"], t["ps_d2"]
    kv_use = []
    ac_free = p.ct["ac"]
    dv_free = p.ct["dv"]
    for h in range(16):
        par_h = h % 2
        if h >= 2:
            p.wait("sync", "pe", kv_use[h - 2])
        v1 = p.inc("sync", f"ldk{par_h}", 16, lambda E, h=h, par_h=par_h: E.dma_start(
            out=t["ktb"][:, par_h * 4096:(par_h + 1) * 4096],
            in_=kT[h * 128:(h + 1) * 128, :]))
        v2 = p.inc("sync", f"ldk{par_h}", 16, lambda E, h=h, par_h=par_h: E.dma_start(
            out=t["vtb"].ap()[:, par_h * 4096:(par_h + 1) * 4096].rearrange(
                "p (b d) -> p b d", b=32),
            in_=vf.rearrange("(b q) d -> q b d", q=128)[:, :, h * 128:(h + 1) * 128]))
        for tile in range(2):
            koff = par_h * 4096
            qcol = h * 1024 + tile * 512
            p.wait("pe", f"ldk{par_h}", v2)
            p.wait("pe", "ac", ac_free)
            p.wait("pe", "dv", dv_free)
            for b in range(32):
                par = b % 2
                p.emit("pe", lambda E, b=b, par=par, koff=koff, qcol=qcol: E.matmul(
                    ps_a[:, par * 1024: par * 1024 + 512],
                    t["ktb"][0:64, koff + b * 128: koff + (b + 1) * 128],
                    t["q_bf"][0:64, qcol:qcol + 512],
                    start=True, stop=True, tile_position=(0, 0)))
                sv = p.inc("pe", "pe", 1, lambda E, b=b, par=par, koff=koff, qcol=qcol: E.matmul(
                    ps_a[:, par * 1024 + 512: par * 1024 + 1024],
                    t["ktb"][64:128, koff + b * 128: koff + (b + 1) * 128],
                    t["q_bf"][64:128, qcol:qcol + 512],
                    start=True, stop=True, tile_position=(64, 0)))
                p.wait("act", "pe", sv)
                ev = p.inc("act", "ac", 1, lambda E, par=par, h=h: E.activation(
                    t["e12"][:, par * 1024:(par + 1) * 1024],
                    ps_a[:, par * 1024:(par + 1) * 1024],
                    AF.Exp, bias=t["bias_sb"][:, h:h + 1], scale=0.125))
                p.wait("pe", "ac", ev)
                first, last = (b == 0), (b == 31)
                p.emit("pe", lambda E, b=b, par=par, koff=koff, first=first, last=last: E.matmul(
                    ps_b[:, 0:512],
                    t["vtb"][:, koff + b * 128: koff + (b + 1) * 128],
                    t["e12"][:, par * 1024: par * 1024 + 512],
                    start=first, stop=last))
                p.emit("pe", lambda E, b=b, par=par, koff=koff, first=first, last=last: E.matmul(
                    ps_b[:, 512:1024],
                    t["vtb"][:, koff + b * 128: koff + (b + 1) * 128],
                    t["e12"][:, par * 1024 + 512: par * 1024 + 1024],
                    start=first, stop=last))
                p.emit("pe", lambda E, par=par, first=first, last=last: E.matmul(
                    ps_d1[:], t["ones_col_bf"][:],
                    t["e12"][:, par * 1024: par * 1024 + 512],
                    start=first, stop=last))
                mmD = lambda E, par=par, first=first, last=last: E.matmul(
                    ps_d2[:], t["ones_col_bf"][:],
                    t["e12"][:, par * 1024 + 512: par * 1024 + 1024],
                    start=first, stop=last)
                if last:
                    pvv = p.inc("pe", "pe", 1, mmD)
                else:
                    p.emit("pe", mmD)
            p.wait("act", "pe", pvv)
            p.inc("act", "ac", 1, lambda E: E.activation(t["tmp1"][0:1, 0:512], ps_d1[:], AF.Copy))
            dcp = p.inc("act", "ac", 1, lambda E: E.activation(t["tmp1"][0:1, 512:1024], ps_d2[:], AF.Copy))
            p.wait("pe", "ac", dcp)
            p.emit("pe", lambda E: E.matmul(ps_a[:, 0:512], t["ones_row_f"][:],
                                            t["tmp1"][0:1, 0:512], start=True, stop=True))
            bv = p.inc("pe", "pe", 1, lambda E: E.matmul(
                ps_a[:, 512:1024], t["ones_row_f"][:], t["tmp1"][0:1, 512:1024],
                start=True, stop=True))
            p.wait("act", "pe", bv)
            p.inc("act", "ac", 1, lambda E: E.activation(t["uB"][:], ps_a[:, 0:512], AF.Ln))
            p.inc("act", "ac", 1, lambda E: E.activation(t["r1B"][:], t["uB"][:], AF.Exp, scale=-1.0))
            p.inc("act", "ac", 1, lambda E: E.activation(t["uB"][:], ps_a[:, 512:1024], AF.Ln))
            rv = p.inc("act", "ac", 1, lambda E: E.activation(t["r2B"][:], t["uB"][:], AF.Exp, scale=-1.0))
            p.wait("dve", "ac", rv)
            p.inc("dve", "dv", 1, lambda E: E.tensor_scalar(
                t["r2B"][:], t["r2B"][:], t["lam_vec"][:], 0.0, ALU.mult, ALU.add))
            p.inc("dve", "dv", 1, lambda E: E.tensor_tensor(
                t["tmp0"][:, 0:512], ps_b[:, 0:512], t["r1B"][:], ALU.mult))
            p.inc("dve", "dv", 1, lambda E: E.tensor_tensor(
                t["tmp0"][:, 512:1024], ps_b[:, 512:1024], t["r2B"][:], ALU.mult))
            yv = p.inc("dve", "dv", 1, lambda E: E.tensor_tensor(
                t["tmp0"][:, 0:512], t["tmp0"][:, 0:512], t["tmp0"][:, 512:1024], ALU.subtract))
            p.wait("act", "dv", yv)
            sqv = p.inc("act", "ac", 1, lambda E: E.activation(
                t["sqa"][:], t["tmp0"][:, 0:512], AF.Square))
            p.wait("pe", "ac", sqv)
            msv = p.inc("pe", "pe", 1, lambda E: E.matmul(
                ps_a[:, 1024:1536], t["ones128_bf"][:], t["sqa"][:], start=True, stop=True))
            p.wait("act", "pe", msv)
            p.inc("act", "ac", 1, lambda E: E.activation(
                t["uB"][:], ps_a[:, 1024:1536], AF.Ln, scale=1.0 / 128.0, bias=t["eps_col"][:]))
            rsv = p.inc("act", "ac", 1, lambda E: E.activation(
                t["rstdB"][:], t["uB"][:], AF.Exp, scale=-0.5))
            p.wait("dve", "ac", rsv)
            p.inc("dve", "dv", 1, lambda E: E.tensor_tensor(
                t["tmp1"][:, 0:512], t["tmp0"][:, 0:512], t["rstdB"][:], ALU.mult))
            dvf = p.inc("dve", "dv", 1, lambda E, h=h, tile=tile: E.tensor_scalar(
                attn_bf[:, h * 1024 + tile * 512: h * 1024 + (tile + 1) * 512],
                t["tmp1"][:, 0:512], t["gam_eff"][:], 0.0, ALU.mult, ALU.add))
            ac_free = rsv
            dv_free = dvf
        kv_use.append(p.ct["pe"])
    ctx3.close()

    if debug:
        p.wait("sync", "dv")
        for kch in range(KCH):
            p.inc("sync", "sta", 16, lambda E, kch=kch: E.dma_start(
                out=dbg["d_attn"][kch * 128:(kch + 1) * 128, :],
                in_=attn_bf[:, kch * 1024:(kch + 1) * 1024]))

    ctx4 = ExitStack()
    ent4 = ctx4.enter_context
    scope_barrier(p)
    t["wbuf2"] = ent4(nc.sbuf_tensor("wbuf2", [128, 4096], BF16, side="right"))
    wring = Ring(p, "rw", 2, "pool")
    wofree = [{"ac": ac_free, "dv": dv_free}, {"ac": ac_free, "dv": dv_free}]
    for och in range(16):
        par = och % 2
        wi, wsl, wkey = wring.load(lambda s: (lambda E, och=och, s=s: E.dma_start(
            out=t["wbuf2"][:, s * 2048:(s + 1) * 2048], in_=wo[och])))
        for sem, val in wofree[par].items():
            p.wait("pe", sem, val)
        p.wait("pe", *wkey)
        for kch in range(KCH):
            for half in range(2):
                mm = lambda E, och=och, kch=kch, half=half, wsl=wsl: E.matmul(
                    ps_a[:, (och % 2) * 1024 + half * 512: (och % 2) * 1024 + (half + 1) * 512],
                    t["wbuf2"][:, wsl * 2048 + kch * 128: wsl * 2048 + (kch + 1) * 128],
                    attn_bf[:, kch * 1024 + half * 512: kch * 1024 + (half + 1) * 512],
                    start=(kch == 0), stop=(kch == KCH - 1))
                if kch == KCH - 1 and half == 1:
                    ov = p.inc("pe", "pe", 1, mm)
                else:
                    p.emit("pe", mm)
        wring.consumed(wi, "pe")
        p.wait("dve", "pe", ov)
        fv = p.inc("dve", "dv", 1, lambda E, och=och, par=par: E.tensor_tensor(
            t["acc"][:, och * 1024:(och + 1) * 1024],
            t["acc"][:, och * 1024:(och + 1) * 1024],
            ps_a[:, par * 1024:(par + 1) * 1024], ALU.add))
        wofree[par] = {"dv": fv}
    ctx4.close()

    if debug:
        p.wait("sync", "dv")
        for kch in range(KCH):
            p.inc("sync", "sta", 16, lambda E, kch=kch: E.dma_start(
                out=dbg["d_acc"][kch * 128:(kch + 1) * 128, :],
                in_=t["acc"][:, kch * 1024:(kch + 1) * 1024]))

    ctx5 = ExitStack()
    ent5 = ctx5.enter_context
    scope_barrier(p)
    t["xe"] = ent5(nc.sbuf_tensor("xe", [128, 2048], F32, side="right"))
    t["out_sb"] = ent5(nc.sbuf_tensor("out_sb", [128, 2048], F32, side="right"))
    stv = []
    for och in range(16):
        par = och % 2
        if och >= 2:
            p.wait("sync", f"st{par}", stv[och - 2])
        v = p.inc("sync", f"ldx{par}", 16, lambda E, och=och, par=par: E.dma_start(
            out=t["xe"][:, par * 1024:(par + 1) * 1024],
            in_=xT[och * 128:(och + 1) * 128, :]))
        p.wait("dve", f"ldx{par}", v)
        p.wait("dve", "dv", wofree[par].get("dv", 0))
        ev = p.inc("dve", "dv", 1, lambda E, och=och, par=par: E.tensor_tensor(
            t["out_sb"][:, par * 1024:(par + 1) * 1024],
            t["acc"][:, och * 1024:(och + 1) * 1024],
            t["xe"][:, par * 1024:(par + 1) * 1024], ALU.add))
        p.wait("sync", "dv", ev)
        stv.append(p.inc("sync", f"st{par}", 16, lambda E, och=och, par=par: E.dma_start(
            out=outT[och * 128:(och + 1) * 128, :],
            in_=t["out_sb"][:, par * 1024:(par + 1) * 1024])))
    ctx5.close()
    p.wait("sync", "st0")
    p.wait("sync", "st1")
    if debug:
        p.wait("sync", "sta")
    block = ent(nc.Block())
    attach(p, block)
    ctx.close()
    return nc


def build_F(debug=False):
    nc = bass.Bass("TRN2", target_bir_lowering=False)
    xTb = nc.dram_tensor("xTb", [E, S], F32, kind="ExternalInput").ap()
    wq = nc.dram_tensor("wq", [16, 128, 2048], F32, kind="ExternalInput").ap()
    wk = nc.dram_tensor("wk", [16, 128, 2048], F32, kind="ExternalInput").ap()
    wo = nc.dram_tensor("wo", [16, 128, 2048], F32, kind="ExternalInput").ap()
    wv = nc.dram_tensor("wv", [4, 128, 8192], F32, kind="ExternalInput").ap()
    gu = nc.dram_tensor("gu", [44, 128, 4096], F32, kind="ExternalInput").ap()
    dn = nc.dram_tensor("dn", [16, 128, 5632], F32, kind="ExternalInput").ap()
    gs = nc.dram_tensor("gs", [128, 44], F32, kind="ExternalInput").ap()
    us = nc.dram_tensor("us", [128, 44], F32, kind="ExternalInput").ap()
    ds = nc.dram_tensor("ds", [128, 16], F32, kind="ExternalInput").ap()
    gam = nc.dram_tensor("gam", [128, 1], F32, kind="ExternalInput").ap()
    lq1 = nc.dram_tensor("lq1", [1, 64], F32, kind="ExternalInput").ap()
    lk1 = nc.dram_tensor("lk1", [1, 64], F32, kind="ExternalInput").ap()
    lq2 = nc.dram_tensor("lq2", [1, 64], F32, kind="ExternalInput").ap()
    lk2 = nc.dram_tensor("lk2", [1, 64], F32, kind="ExternalInput").ap()
    ctb = nc.dram_tensor("ctb", [128, S], BF16, kind="ExternalInput").ap()
    stbt = nc.dram_tensor("stbt", [128, S], BF16, kind="ExternalInput").ap()
    outT = nc.dram_tensor("outT", [E, TQ], F32, kind="ExternalOutput").ap()
    kT_s = nc.dram_tensor("kT_s", [E, S], BF16, kind="Internal").ap()
    v_s = nc.dram_tensor("v_s", [S, E], BF16, kind="Internal").ap()
    wk_c = nc.dram_tensor("wk_c", [16, 128, 2048], BF16, kind="Internal").ap()
    wv_c = nc.dram_tensor("wv_c", [4, 128, 8192], BF16, kind="Internal").ap()
    dbg = {}
    if debug:
        for nm, shp, dt in (("d_xn", [E, TQ], BF16), ("d_q", [E, TQ], BF16),
                            ("d_attn", [E, TQ], BF16), ("d_kt", [E, S], BF16),
                            ("d_vs", [S, E], BF16), ("d_srow", [1, 64], F32)):
            dbg[nm] = nc.dram_tensor(nm, shp, dt, kind="ExternalOutput").ap()

    from contextlib import ExitStack
    ctx = ExitStack()
    ent = ctx.enter_context
    t = {}
    t["ps_a"] = ent(nc.psum_tensor("ps_a", [128, 2048], F32))
    t["ps_b"] = ent(nc.psum_tensor("ps_b", [128, 1024], F32))
    t["ps_d1"] = ent(nc.psum_tensor("ps_d1", [1, 512], F32))
    t["ps_d2"] = ent(nc.psum_tensor("ps_d2", [1, 512], F32))
    t["ones_col_bf"] = ent(nc.sbuf_tensor("ones_col_bf", [128, 1], BF16))
    t["ones_row_f"] = ent(nc.sbuf_tensor("ones_row_f", [1, 128], F32))
    t["neg_row_f"] = ent(nc.sbuf_tensor("neg_row_f", [1, 128], F32))
    t["ones128_bf"] = ent(nc.sbuf_tensor("ones128_bf", [128, 128], BF16))
    t["mask1_bf"] = ent(nc.sbuf_tensor("mask1_bf", [128, 1], BF16))
    t["mask2_bf"] = ent(nc.sbuf_tensor("mask2_bf", [128, 1], BF16))
    t["tmp0"] = ent(nc.sbuf_tensor("tmp0", [128, 1024], F32))
    t["tmp1"] = ent(nc.sbuf_tensor("tmp1", [128, 1024], F32))
    t["srow"] = ent(nc.sbuf_tensor("srow", [1, 64], F32))
    t["trow"] = ent(nc.sbuf_tensor("trow", [1, 16], F32))
    t["eps_col"] = ent(nc.sbuf_tensor("eps_col", [128, 1], F32))
    t["nrow"] = ent(nc.sbuf_tensor("nrow", [1, 1024], F32))
    t["mBb"] = ent(nc.sbuf_tensor("mBb", [128, 512], BF16))
    t["rBb"] = ent(nc.sbuf_tensor("rBb", [128, 512], BF16))
    t["bias_sb"] = ent(nc.sbuf_tensor("bias_sb", [128, 16], F32))
    t["lam_vec"] = ent(nc.sbuf_tensor("lam_vec", [128, 1], F32))
    t["gam_eff"] = ent(nc.sbuf_tensor("gam_eff", [128, 1], F32))
    t["gs_sb"] = ent(nc.sbuf_tensor("gs_sb", [128, 44], F32))
    t["us_sb"] = ent(nc.sbuf_tensor("us_sb", [128, 44], F32))
    t["ds_sb"] = ent(nc.sbuf_tensor("ds_sb", [128, 16], F32))
    t["lrow"] = ent(nc.sbuf_tensor("lrow", [1, 256], F32))
    t["xn_bf"] = ent(nc.sbuf_tensor("xn_bf", [128, 16 * 1024], BF16))

    p = Prog(nc, ent)
    emit_init(p, t)
    p.wait("pe", "pl")
    for name, src_ap in (("gs_sb", gs), ("us_sb", us), ("ds_sb", ds)):
        p.inc("sync", "lds", 16, lambda E, src_ap=src_ap, name=name: E.dma_start(
            out=t[name][:], in_=src_ap))
    p.inc("sync", "lds", 16, lambda E: E.dma_start(out=t["lrow"][0:1, 0:64], in_=lq1))
    p.inc("sync", "lds", 16, lambda E: E.dma_start(out=t["lrow"][0:1, 64:128], in_=lk1))
    p.inc("sync", "lds", 16, lambda E: E.dma_start(out=t["lrow"][0:1, 128:192], in_=lq2))
    p.inc("sync", "lds", 16, lambda E: E.dma_start(out=t["lrow"][0:1, 192:256], in_=lk2))
    gamv = p.inc("sync", "lds", 16, lambda E: E.dma_start(out=t["tmp0"][:, 0:1], in_=gam))
    p.wait("dve", "lds", gamv)
    p.wait("pool", "lds", gamv)
    p.wait("act", "lds", gamv)
    p.inc("dve", "dv", 1, lambda E: E.tensor_scalar(
        t["gam_eff"][:], t["tmp0"][:, 0:1], ONE_MINUS_LI, 0.0, ALU.mult, ALU.add))
    p.inc("dve", "dv", 1, lambda E: E.tensor_tensor(
        t["lrow"][0:1, 0:64], t["lrow"][0:1, 0:64], t["lrow"][0:1, 64:128], ALU.mult))
    p.inc("dve", "dv", 1, lambda E: E.tensor_tensor(
        t["lrow"][0:1, 128:192], t["lrow"][0:1, 128:192], t["lrow"][0:1, 192:256], ALU.mult))
    p.inc("dve", "dv", 1, lambda E: E.tensor_reduce(
        t["trow"][0:1, 0:1], t["lrow"][0:1, 0:64], AX.X, ALU.add))
    p.inc("dve", "dv", 1, lambda E: E.tensor_reduce(
        t["trow"][0:1, 1:2], t["lrow"][0:1, 128:192], AX.X, ALU.add))
    p.wait("act", "dv")
    p.inc("act", "ac", 1, lambda E: E.activation(t["trow"][0:1, 2:3], t["trow"][0:1, 0:1], AF.Exp))
    p.inc("act", "ac", 1, lambda E: E.activation(t["trow"][0:1, 3:4], t["trow"][0:1, 1:2], AF.Exp))
    p.wait("dve", "ac")
    p.inc("dve", "dv", 1, lambda E: E.tensor_tensor(
        t["trow"][0:1, 2:3], t["trow"][0:1, 2:3], t["trow"][0:1, 3:4], ALU.subtract))
    lamv = p.inc("dve", "dv", 1, lambda E: E.tensor_scalar(
        t["trow"][0:1, 2:3], t["trow"][0:1, 2:3], 1.0, LAMBDA_INIT, ALU.mult, ALU.add))
    p.wait("pe", "dv", lamv)
    lbv = p.inc("pe", "pe", 1, lambda E: E.matmul(
        t["ps_b"][:, 0:1], t["ones_row_f"][:], t["trow"][0:1, 2:3], start=True, stop=True))
    p.wait("dve", "pe", lbv)
    p.inc("dve", "dv", 1, lambda E: E.tensor_copy(t["lam_vec"][:], t["ps_b"][:, 0:1]))

    ctx_tab = ExitStack()
    entt = ctx_tab.enter_context
    t["ctab"] = entt(nc.sbuf_tensor("ctab", [128, S], BF16, side="right"))
    t["qraw_bf"] = entt(nc.sbuf_tensor("qraw_bf", [128, 1024], BF16, side="right"))
    t["tswp_bf"] = entt(nc.sbuf_tensor("tswp_bf", [128, 1024], BF16, side="right"))
    t["tmpa_bf"] = entt(nc.sbuf_tensor("tmpa_bf", [128, 1024], BF16, side="right"))
    t["stab"] = entt(nc.sbuf_tensor("stab", [128, S], BF16, side="right"))
    t["sq_b"] = entt(nc.sbuf_tensor("sq_b", [128, 1024], BF16, side="right"))
    p.inc("sync", "lds", 16, lambda E: E.dma_start(out=t["ctab"][:], in_=ctb))
    tabv = p.inc("sync", "lds", 16, lambda E: E.dma_start(out=t["stab"][:], in_=stbt))
    p.wait("dve", "lds", tabv)

    ctx_kv = ExitStack()
    entk = ctx_kv.enter_context
    t["xbb"] = entk(nc.sbuf_tensor("xbb", [128, 8192], BF16, side="right"))
    t["x2b"] = entk(nc.sbuf_tensor("x2b", [128, 4096], BF16, side="right"))
    t["row"] = entk(nc.sbuf_tensor("row", [1, 2048], F32, side="right"))
    t["xnb"] = entk(nc.sbuf_tensor("xnb", [128, 16 * 1024], BF16, side="right"))
    t["kstage"] = entk(nc.sbuf_tensor("kstage", [128, 2048], BF16, side="right"))
    t["kraw"] = entk(nc.sbuf_tensor("kraw", [128, 4096], BF16, side="right"))
    t["v_stage"] = entk(nc.sbuf_tensor("v_stage", [128, 4, 2048], BF16, side="right"))
    t["wbuf"] = entk(nc.sbuf_tensor("wbuf", [128, 4096], BF16, side="right"))
    t["wvbuf"] = entk(nc.sbuf_tensor("wvbuf", [128, 12288], BF16, side="right"))
    kwring = Ring(p, "rkw", 2, "pool")
    vwring = Ring(p, "rvw", 3, "pool")
    lnring = Ring(p, "rx", 2, "sync")
    kst_store = []
    ksq_done = []
    kraw_free = {}
    vstv = []

    for chunk in range(4):
        c0 = chunk * 1024
        xn_dst = t["xn_bf"] if chunk == 0 else t["xnb"]
        psfree = emit_ln(p, t, xTb[:, c0:c0 + 1024], xn_dst, 1024, 0, ring=lnring)

        def consume_k(och, o8, chunk=chunk, c0=c0):
            gidx = chunk * 16 + och
            half = och % 2
            if gidx >= 2:
                p.wait("dve", f"st{half}", kst_store[gidx - 2])
                p.wait("dve", "ac", ksq_done[gidx - 2])
            srcfree, wv_ = emit_rope_sb(p, t, t["kraw"][:, o8 * 1024:(o8 + 1) * 1024],
                                        t["ctab"], t["stab"], c0, t["kstage"], half * 1024)
            emit_headnorm(p, t, t["kstage"], half * 1024, och, statbase=0)
            ksq_done.append(p.ct["ac"])
            p.wait("sync", "dv", wv_)
            kst_store.append(p.inc("sync", f"st{half}", 16,
                                   lambda E, och=och, half=half, c0=c0: E.dma_start(
                out=kT_s[och * 128:(och + 1) * 128, c0:c0 + 1024],
                in_=t["kstage"][:, half * 1024:(half + 1) * 1024])))
            return srcfree

        free = [dict(psfree), dict(psfree)]
        vfree = {"dv": psfree.get("dv", 0), "ac": p.ct["ac"]}
        for wave in range(4):
            cpvs = []
            for o8 in range(4):
                och = wave * 4 + o8
                par = och % 2
                if chunk == 0:
                    i, slot, key = kwring.load(lambda s: (lambda E, och=och, s=s: E.dma_start(
                        out=t["wbuf"][:, s * 2048:(s + 1) * 2048], in_=wk[och])))
                    p.wait("pool", *key)
                    wcv = p.inc("pool", f"rkw{slot}", 16, lambda E, och=och, s=slot: E.dma_start(
                        out=wk_c[och], in_=t["wbuf"][:, s * 2048:(s + 1) * 2048]))
                    kwring.done[i][f"rkw{slot}"] = wcv
                else:
                    i, slot, key = kwring.load(lambda s: (lambda E, och=och, s=s: E.dma_start(
                        out=t["wbuf"][:, s * 2048:(s + 1) * 2048], in_=wk_c[och])))
                for sem, val in free[par].items():
                    p.wait("pe", sem, val)
                p.wait("pe", key[0], key[1])
                for kch in range(KCH):
                    for half in range(2):
                        mm = lambda E, och=och, kch=kch, half=half, par=par, slot=slot, xn_dst=xn_dst: E.matmul(
                            t["ps_a"][:, par * 1024 + half * 512: par * 1024 + (half + 1) * 512],
                            t["wbuf"][:, slot * 2048 + kch * 128: slot * 2048 + (kch + 1) * 128],
                            xn_dst[:, kch * 1024 + half * 512: kch * 1024 + (half + 1) * 512],
                            start=(kch == 0), stop=(kch == KCH - 1))
                        if kch == KCH - 1 and half == 1:
                            mmv = p.inc("pe", "pe", 1, mm)
                        else:
                            p.emit("pe", mm)
                kwring.consumed(i, "pe")
                p.wait("act", "pe", mmv)
                if o8 in kraw_free:
                    p.wait("act", "dv", kraw_free[o8])
                cpv = p.inc("act", "ac", 1, lambda E, par=par, o8=o8: E.activation(
                    t["kraw"][:, o8 * 1024:(o8 + 1) * 1024],
                    t["ps_a"][:, par * 1024:(par + 1) * 1024], AF.Copy))
                cpvs.append(cpv)
                free[par] = {"ac": cpv}
            for o8 in range(4):
                och = wave * 4 + o8
                p.wait("dve", "ac", cpvs[o8])
                kraw_free[o8] = consume_k(och, o8)
            for vg in range(2):
                gidx = wave * 2 + vg
                halfq, dvq = gidx // 4, gidx % 4
                vfree = emit_v_group(p, t, wv, wv_c, vwring, xn_dst, halfq, dvq,
                                     t["v_stage"], v_s, c0, vfree,
                                     fill_cache=(chunk == 0), stv=vstv)
    ctx_kv.close()
    if debug:
        p.wait("sync", "dv")
        p.wait("sync", "st0")
        p.wait("sync", "st1")
        p.wait("sync", "sta")
        p.inc("sync", "sta", 16, lambda E: E.dma_start(out=dbg["d_kt"], in_=kT_s))
        p.inc("sync", "sta", 16, lambda E: E.dma_start(out=dbg["d_vs"], in_=v_s))

    ctx_q = ExitStack()
    entq = ctx_q.enter_context
    scope_barrier(p)
    t["wbufq"] = entq(nc.sbuf_tensor("wbufq", [128, 4096], BF16, side="right"))
    t["q_bf"] = ent(nc.sbuf_tensor("q_bf", [128, 16 * 1024], BF16))

    def consume_q(och, mmv):
        fr, wv_ = emit_rope(p, t, t["ps_a"][:, (och % 2) * 1024:(och % 2 + 1) * 1024],
                            t["ctab"], t["stab"], 0, t["q_bf"], och * 1024)
        emit_headnorm(p, t, t["q_bf"], och * 1024, och, statbase=32)
        return {fr[0]: fr[1]}

    qfree = {"pe": p.ct["pe"], "dv": p.ct["dv"], "ac": p.ct["ac"]}
    emit_fmproj(p, t, wq, t["wbufq"], t["xn_bf"], 1024, 0, consume_q, qfree)
    if debug:
        p.wait("sync", "dv")
        for kch in range(KCH):
            p.inc("sync", "sta", 16, lambda E, kch=kch: E.dma_start(
                out=dbg["d_q"][kch * 128:(kch + 1) * 128, :],
                in_=t["q_bf"][:, kch * 1024:(kch + 1) * 1024]))
        p.inc("sync", "sta", 16, lambda E: E.dma_start(out=dbg["d_srow"], in_=t["srow"][:]))

    p.wait("dve", "lds")
    p.wait("dve", "pl")
    p.inc("dve", "dv", 1, lambda E: E.tensor_tensor(
        t["lrow"][0:1, 0:16], t["srow"][0:1, 32:48], t["srow"][0:1, 0:16], ALU.mult))
    p.inc("dve", "dv", 1, lambda E: E.tensor_tensor(
        t["lrow"][0:1, 16:32], t["srow"][0:1, 48:64], t["srow"][0:1, 16:32], ALU.mult))
    p.inc("dve", "dv", 1, lambda E: E.tensor_tensor(
        t["lrow"][0:1, 64:80], t["lrow"][0:1, 0:16], t["lrow"][0:1, 16:32], ALU.max))
    p.wait("act", "dv")
    brv = p.inc("act", "ac", 1, lambda E: E.activation(
        t["lrow"][0:1, 128:144], t["lrow"][0:1, 64:80], AF.Sqrt, scale=1.0 / 64.0))
    p.wait("pe", "ac", brv)
    p.wait("pe", "dv")
    bbv = p.inc("pe", "pe", 1, lambda E: E.matmul(
        t["ps_b"][:, 0:16], t["neg_row_f"][:], t["lrow"][0:1, 128:144], start=True, stop=True))
    p.wait("dve", "pe", bbv)
    biasv = p.inc("dve", "dv", 1, lambda E: E.tensor_copy(t["bias_sb"][:], t["ps_b"][:, 0:16]))
    ctx_q.close()
    ctx_tab.close()

    ctx2 = ExitStack()
    ent2 = ctx2.enter_context
    scope_barrier(p)
    t["acc"] = ent(nc.sbuf_tensor("acc", [128, 16 * 1024], F32))
    t["h_bf"] = ent2(nc.sbuf_tensor("h_bf", [128, ICH * 512], BF16, side="right"))
    t["wffn"] = ent2(nc.sbuf_tensor("wffn", [128, 8448], BF16, side="right"))
    gring = Ring(p, "rg", 2, "pool")
    uring = Ring(p, "ru", 2, "pool")
    dring = Ring(p, "rd", 3, "pool")
    wffn = t["wffn"]
    UOFF = 4096
    for tile in range(2):
        gufree = {"dv": p.ct["dv"]}
        p.wait("dve", "pe")
        p.wait("pool", "pe")
        for ich in range(ICH):
            par = ich % 2
            gi, gsl, gkey = gring.load(lambda s: (lambda E, ich=ich, s=s: E.dma_start(
                out=wffn[:, s * 2048:(s + 1) * 2048], in_=gu[ich][:, 0:2048])))
            ui, usl, ukey = uring.load(lambda s: (lambda E, ich=ich, s=s: E.dma_start(
                out=wffn[:, UOFF + s * 2048:UOFF + (s + 1) * 2048], in_=gu[ich][:, 2048:4096])))
            for sem, val in gufree.items():
                p.wait("pe", sem, val)
            p.wait("pe", *gkey)
            p.wait("pe", *ukey)
            for kch in range(KCH):
                p.emit("pe", lambda E, kch=kch, par=par, tile=tile, gsl=gsl: E.matmul(
                    t["ps_a"][:, par * 1024: par * 1024 + 512],
                    wffn[:, gsl * 2048 + kch * 128: gsl * 2048 + (kch + 1) * 128],
                    t["xn_bf"][:, kch * 1024 + tile * 512: kch * 1024 + (tile + 1) * 512],
                    start=(kch == 0), stop=(kch == KCH - 1)))
                mm = lambda E, kch=kch, par=par, tile=tile, usl=usl: E.matmul(
                    t["ps_a"][:, par * 1024 + 512: par * 1024 + 1024],
                    wffn[:, UOFF + usl * 2048 + kch * 128: UOFF + usl * 2048 + (kch + 1) * 128],
                    t["xn_bf"][:, kch * 1024 + tile * 512: kch * 1024 + (tile + 1) * 512],
                    start=(kch == 0), stop=(kch == KCH - 1))
                if kch == KCH - 1:
                    gv = p.inc("pe", "pe", 1, mm)
                else:
                    p.emit("pe", mm)
            gring.consumed(gi, "pe")
            uring.consumed(ui, "pe")
            p.wait("act", "pe", gv)
            p.wait("act", "dv")
            p.inc("act", "ac", 1, lambda E, par=par, ich=ich: E.activation(
                t["tmp1"][:, 512:1024],
                t["ps_a"][:, par * 1024: par * 1024 + 512],
                AF.Silu, scale=t["gs_sb"][:, ich:ich + 1]))
            p.wait("dve", "ac")
            p.inc("dve", "dv", 1, lambda E, par=par, ich=ich: E.tensor_scalar(
                t["tmp0"][:, 0:512], t["ps_a"][:, par * 1024 + 512: par * 1024 + 1024],
                t["us_sb"][:, ich:ich + 1], 0.0, ALU.mult, ALU.add))
            p.inc("dve", "dv", 1, lambda E, par=par, ich=ich: E.tensor_tensor(
                t["h_bf"][:, ich * 512:(ich + 1) * 512], t["tmp0"][:, 0:512],
                t["tmp1"][:, 512:1024], ALU.mult))
            gufree = {"dv": p.ct["dv"]}
        dfree = [{"dv": p.ct["dv"]}, {"dv": p.ct["dv"]}]
        p.wait("pool", "pe")
        for hch in range(16):
            par = hch % 2
            d0, s0, k0 = dring.load(lambda s: (lambda E, hch=hch, s=s: E.dma_start(
                out=wffn[:, s * 2816:(s + 1) * 2816], in_=dn[hch][:, 0:2816])))
            d1, s1, k1 = dring.load(lambda s: (lambda E, hch=hch, s=s: E.dma_start(
                out=wffn[:, s * 2816:(s + 1) * 2816], in_=dn[hch][:, 2816:5632])))
            for sem, val in dfree[par].items():
                p.wait("pe", sem, val)
            if hch == 0:
                p.wait("pe", "dv", gufree["dv"])
            p.wait("pe", *k0)
            for ich in range(22):
                p.emit("pe", lambda E, ich=ich, par=par, s0=s0: E.matmul(
                    t["ps_b"][:, par * 512:(par + 1) * 512],
                    wffn[:, s0 * 2816 + ich * 128: s0 * 2816 + (ich + 1) * 128],
                    t["h_bf"][:, ich * 512:(ich + 1) * 512],
                    start=(ich == 0), stop=False))
            p.wait("pe", *k1)
            for ich in range(22, ICH):
                mm = lambda E, ich=ich, par=par, s1=s1: E.matmul(
                    t["ps_b"][:, par * 512:(par + 1) * 512],
                    wffn[:, s1 * 2816 + (ich - 22) * 128: s1 * 2816 + (ich - 21) * 128],
                    t["h_bf"][:, ich * 512:(ich + 1) * 512],
                    start=False, stop=(ich == ICH - 1))
                if ich == ICH - 1:
                    dv_ = p.inc("pe", "pe", 1, mm)
                else:
                    p.emit("pe", mm)
            dring.consumed(d0, "pe")
            dring.consumed(d1, "pe")
            p.wait("dve", "pe", dv_)
            p.inc("dve", "dv", 1, lambda E, hch=hch, par=par, tile=tile: E.tensor_scalar(
                t["acc"][:, hch * 1024 + tile * 512: hch * 1024 + (tile + 1) * 512],
                t["ps_b"][:, par * 512:(par + 1) * 512],
                t["ds_sb"][:, hch:hch + 1], 0.0, ALU.mult, ALU.add))
            dfree[par] = {"dv": p.ct["dv"]}
    ctx2.close()

    ctx3 = ExitStack()
    ent3 = ctx3.enter_context
    scope_barrier(p)
    t["e12"] = ent3(nc.sbuf_tensor("e12", [128, 2048], BF16, side="right"))
    t["ktb"] = ent3(nc.sbuf_tensor("ktb", [128, 8192], BF16, side="right"))
    t["vtb"] = ent3(nc.sbuf_tensor("vtb", [128, 8192], BF16, side="right"))
    t["r1B"] = ent3(nc.sbuf_tensor("r1B", [128, 512], F32, side="right"))
    t["r2B"] = ent3(nc.sbuf_tensor("r2B", [128, 512], F32, side="right"))
    t["uB"] = ent3(nc.sbuf_tensor("uB", [128, 512], F32, side="right"))
    t["rstdB"] = ent3(nc.sbuf_tensor("rstdB", [128, 512], F32, side="right"))
    t["sqa"] = ent3(nc.sbuf_tensor("sqa", [128, 512], BF16, side="right"))
    p.wait("act", "dv", biasv)
    p.wait("sync", "st0")
    p.wait("sync", "st1")
    p.wait("sync", "sta")
    attn_bf = t["xn_bf"]
    ps_a, ps_b, ps_d1, ps_d2 = t["ps_a"], t["ps_b"], t["ps_d1"], t["ps_d2"]
    kv_use = []
    ac_free = p.ct["ac"]
    dv_free = p.ct["dv"]
    for h in range(16):
        par_h = h % 2
        if h >= 2:
            p.wait("sync", "pe", kv_use[h - 2])
        v1 = p.inc("sync", f"ldk{par_h}", 16, lambda E, h=h, par_h=par_h: E.dma_start(
            out=t["ktb"][:, par_h * 4096:(par_h + 1) * 4096],
            in_=kT_s[h * 128:(h + 1) * 128, :]))
        v2 = p.inc("sync", f"ldk{par_h}", 16, lambda E, h=h, par_h=par_h: E.dma_start(
            out=t["vtb"].ap()[:, par_h * 4096:(par_h + 1) * 4096].rearrange(
                "p (b d) -> p b d", b=32),
            in_=v_s.rearrange("(b q) d -> q b d", q=128)[:, :, h * 128:(h + 1) * 128]))
        for tile in range(2):
            koff = par_h * 4096
            qcol = h * 1024 + tile * 512
            p.wait("pe", f"ldk{par_h}", v2)
            p.wait("pe", "ac", ac_free)
            p.wait("pe", "dv", dv_free)

            def emit_S(b):
                par = b % 2
                p.emit("pe", lambda E, b=b, par=par, koff=koff, qcol=qcol: E.matmul(
                    ps_a[:, par * 1024: par * 1024 + 512],
                    t["ktb"][0:64, koff + b * 128: koff + (b + 1) * 128],
                    t["q_bf"][0:64, qcol:qcol + 512],
                    start=True, stop=True, tile_position=(0, 0)))
                return p.inc("pe", "pe", 1, lambda E, b=b, par=par, koff=koff, qcol=qcol: E.matmul(
                    ps_a[:, par * 1024 + 512: par * 1024 + 1024],
                    t["ktb"][64:128, koff + b * 128: koff + (b + 1) * 128],
                    t["q_bf"][64:128, qcol:qcol + 512],
                    start=True, stop=True, tile_position=(64, 0)))

            sv = [None] * 32
            pvv_l = [None] * 32
            sv[0] = emit_S(0)
            sv[1] = emit_S(1)
            for b in range(32):
                par = b % 2
                if b >= 2:
                    p.wait("act", "pe", pvv_l[b - 2])
                p.wait("act", "pe", sv[b])
                ev = p.inc("act", "ac", 1, lambda E, par=par, h=h: E.activation(
                    t["e12"][:, par * 1024:(par + 1) * 1024],
                    ps_a[:, par * 1024:(par + 1) * 1024],
                    AF.Exp, bias=t["bias_sb"][:, h:h + 1], scale=0.125))
                p.wait("pe", "ac", ev)
                first, last = (b == 0), (b == 31)
                p.emit("pe", lambda E, b=b, par=par, koff=koff, first=first, last=last: E.matmul(
                    ps_b[:, 0:512],
                    t["vtb"][:, koff + b * 128: koff + (b + 1) * 128],
                    t["e12"][:, par * 1024: par * 1024 + 512],
                    start=first, stop=last))
                p.emit("pe", lambda E, b=b, par=par, koff=koff, first=first, last=last: E.matmul(
                    ps_b[:, 512:1024],
                    t["vtb"][:, koff + b * 128: koff + (b + 1) * 128],
                    t["e12"][:, par * 1024 + 512: par * 1024 + 1024],
                    start=first, stop=last))
                p.emit("pe", lambda E, par=par, first=first, last=last: E.matmul(
                    ps_d1[:], t["ones_col_bf"][:],
                    t["e12"][:, par * 1024: par * 1024 + 512],
                    start=first, stop=last))
                mmD = lambda E, par=par, first=first, last=last: E.matmul(
                    ps_d2[:], t["ones_col_bf"][:],
                    t["e12"][:, par * 1024 + 512: par * 1024 + 1024],
                    start=first, stop=last)
                pvv_l[b] = p.inc("pe", "pe", 1, mmD)
                if b + 2 < 32:
                    sv[b + 2] = emit_S(b + 2)
            pvv = pvv_l[31]
            p.wait("act", "pe", pvv)
            p.inc("act", "ac", 1, lambda E: E.activation(t["tmp1"][0:1, 0:512], ps_d1[:], AF.Copy))
            dcp = p.inc("act", "ac", 1, lambda E: E.activation(t["tmp1"][0:1, 512:1024], ps_d2[:], AF.Copy))
            p.wait("pe", "ac", dcp)
            p.emit("pe", lambda E: E.matmul(ps_a[:, 0:512], t["ones_row_f"][:],
                                            t["tmp1"][0:1, 0:512], start=True, stop=True))
            bv = p.inc("pe", "pe", 1, lambda E: E.matmul(
                ps_a[:, 512:1024], t["ones_row_f"][:], t["tmp1"][0:1, 512:1024],
                start=True, stop=True))
            p.wait("act", "pe", bv)
            p.inc("act", "ac", 1, lambda E: E.activation(t["uB"][:], ps_a[:, 0:512], AF.Ln))
            p.inc("act", "ac", 1, lambda E: E.activation(t["r1B"][:], t["uB"][:], AF.Exp, scale=-1.0))
            p.inc("act", "ac", 1, lambda E: E.activation(t["uB"][:], ps_a[:, 512:1024], AF.Ln))
            rv = p.inc("act", "ac", 1, lambda E: E.activation(t["r2B"][:], t["uB"][:], AF.Exp, scale=-1.0))
            p.wait("dve", "ac", rv)
            p.inc("dve", "dv", 1, lambda E: E.tensor_scalar(
                t["r2B"][:], t["r2B"][:], t["lam_vec"][:], 0.0, ALU.mult, ALU.add))
            p.inc("dve", "dv", 1, lambda E: E.tensor_tensor(
                t["tmp0"][:, 0:512], ps_b[:, 0:512], t["r1B"][:], ALU.mult))
            p.inc("dve", "dv", 1, lambda E: E.tensor_tensor(
                t["tmp0"][:, 512:1024], ps_b[:, 512:1024], t["r2B"][:], ALU.mult))
            yv = p.inc("dve", "dv", 1, lambda E: E.tensor_tensor(
                t["tmp0"][:, 0:512], t["tmp0"][:, 0:512], t["tmp0"][:, 512:1024], ALU.subtract))
            p.wait("act", "dv", yv)
            sqv = p.inc("act", "ac", 1, lambda E: E.activation(
                t["sqa"][:], t["tmp0"][:, 0:512], AF.Square))
            p.wait("pe", "ac", sqv)
            msv = p.inc("pe", "pe", 1, lambda E: E.matmul(
                ps_a[:, 1024:1536], t["ones128_bf"][:], t["sqa"][:], start=True, stop=True))
            p.wait("act", "pe", msv)
            p.inc("act", "ac", 1, lambda E: E.activation(
                t["uB"][:], ps_a[:, 1024:1536], AF.Ln, scale=1.0 / 128.0, bias=t["eps_col"][:]))
            rsv = p.inc("act", "ac", 1, lambda E: E.activation(
                t["rstdB"][:], t["uB"][:], AF.Exp, scale=-0.5))
            p.wait("dve", "ac", rsv)
            p.inc("dve", "dv", 1, lambda E: E.tensor_tensor(
                t["tmp1"][:, 0:512], t["tmp0"][:, 0:512], t["rstdB"][:], ALU.mult))
            dvf = p.inc("dve", "dv", 1, lambda E, h=h, tile=tile: E.tensor_scalar(
                attn_bf[:, h * 1024 + tile * 512: h * 1024 + (tile + 1) * 512],
                t["tmp1"][:, 0:512], t["gam_eff"][:], 0.0, ALU.mult, ALU.add))
            ac_free = rsv
            dv_free = dvf
        kv_use.append(p.ct["pe"])
    ctx3.close()
    if debug:
        p.wait("sync", "dv")
        for kch in range(KCH):
            p.inc("sync", "sta", 16, lambda E, kch=kch: E.dma_start(
                out=dbg["d_attn"][kch * 128:(kch + 1) * 128, :],
                in_=attn_bf[:, kch * 1024:(kch + 1) * 1024]))

    ctx4 = ExitStack()
    ent4 = ctx4.enter_context
    scope_barrier(p)
    t["wbuf2"] = ent4(nc.sbuf_tensor("wbuf2", [128, 4096], BF16, side="right"))
    t["xe"] = ent4(nc.sbuf_tensor("xe", [128, 2048], F32, side="right"))
    t["out_sb"] = ent4(nc.sbuf_tensor("out_sb", [128, 2048], F32, side="right"))
    wring = Ring(p, "rw", 2, "pool")
    wofree = [{"ac": ac_free, "dv": dv_free}, {"ac": ac_free, "dv": dv_free}]
    stv = []
    for och in range(16):
        par = och % 2
        wi, wsl, wkey = wring.load(lambda s: (lambda E, och=och, s=s: E.dma_start(
            out=t["wbuf2"][:, s * 2048:(s + 1) * 2048], in_=wo[och])))
        if och >= 2:
            p.wait("sync", f"st{par}", stv[och - 2])
        xv = p.inc("sync", f"ldx{par}", 16, lambda E, och=och, par=par: E.dma_start(
            out=t["xe"][:, par * 1024:(par + 1) * 1024],
            in_=xTb[och * 128:(och + 1) * 128, 0:1024]))
        for sem, val in wofree[par].items():
            p.wait("pe", sem, val)
        p.wait("pe", *wkey)
        for kch in range(KCH):
            for half in range(2):
                mm = lambda E, och=och, kch=kch, half=half, wsl=wsl: E.matmul(
                    ps_a[:, (och % 2) * 1024 + half * 512: (och % 2) * 1024 + (half + 1) * 512],
                    t["wbuf2"][:, wsl * 2048 + kch * 128: wsl * 2048 + (kch + 1) * 128],
                    attn_bf[:, kch * 1024 + half * 512: kch * 1024 + (half + 1) * 512],
                    start=(kch == 0), stop=(kch == KCH - 1))
                if kch == KCH - 1 and half == 1:
                    ov = p.inc("pe", "pe", 1, mm)
                else:
                    p.emit("pe", mm)
        wring.consumed(wi, "pe")
        p.wait("dve", "pe", ov)
        p.wait("dve", f"ldx{par}", xv)
        if och >= 2:
            p.wait("dve", f"st{par}", stv[och - 2])
        p.inc("dve", "dv", 1, lambda E, och=och, par=par: E.tensor_tensor(
            t["out_sb"][:, par * 1024:(par + 1) * 1024],
            t["acc"][:, och * 1024:(och + 1) * 1024],
            ps_a[:, par * 1024:(par + 1) * 1024], ALU.add))
        fv = p.inc("dve", "dv", 1, lambda E, och=och, par=par: E.tensor_tensor(
            t["out_sb"][:, par * 1024:(par + 1) * 1024],
            t["out_sb"][:, par * 1024:(par + 1) * 1024],
            t["xe"][:, par * 1024:(par + 1) * 1024], ALU.add))
        p.wait("sync", "dv", fv)
        stv.append(p.inc("sync", f"st{par}", 16, lambda E, och=och, par=par: E.dma_start(
            out=outT[och * 128:(och + 1) * 128, :],
            in_=t["out_sb"][:, par * 1024:(par + 1) * 1024])))
        wofree[par] = {"dv": fv}
    ctx4.close()
    p.wait("sync", "st0")
    p.wait("sync", "st1")
    if debug:
        p.wait("sync", "sta")
    block = ent(nc.Block())
    attach(p, block)
    ctx.close()
    return nc


def emit_vproj_cache(p, t, wv_hbm, wv_cache, ring, xn_bf, xn_seg, xn_col0,
                     v_stage, v_out, row0, psfree0, fill_cache):
    ps_a = t["ps_a"]

    def vps(j):
        return ps_a[:, j * 512:(j + 1) * 512]

    free = dict(psfree0)
    stv = []
    p.wait("act", "sta")
    for halfq in range(2):
        for dvq in range(4):
            slots = []
            for wh in range(2):
                if fill_cache and halfq == 0:
                    i, slot, key = ring.load(lambda s: (lambda E, dvq=dvq, wh=wh, s=s: E.dma_start(
                        out=t["wvbuf"][:, s * 4096:(s + 1) * 4096],
                        in_=wv_hbm[dvq][:, wh * 4096:(wh + 1) * 4096])))
                    p.wait("pool", *key)
                    wcv = p.inc("pool", f"rvw{slot}", 16, lambda E, dvq=dvq, wh=wh, s=slot: E.dma_start(
                        out=wv_cache[dvq][:, wh * 4096:(wh + 1) * 4096],
                        in_=t["wvbuf"][:, s * 4096:(s + 1) * 4096]))
                    ring.done[i][f"rvw{slot}"] = wcv
                else:
                    i, slot, key = ring.load(lambda s: (lambda E, dvq=dvq, wh=wh, s=s: E.dma_start(
                        out=t["wvbuf"][:, s * 4096:(s + 1) * 4096],
                        in_=wv_cache[dvq][:, wh * 4096:(wh + 1) * 4096])))
                slots.append((i, slot, key))
            for sem, val in free.items():
                p.wait("pe", sem, val)
            for wh in range(2):
                i, slot, key = slots[wh]
                p.wait("pe", key[0], key[1])
                for kch in range(wh * 8, wh * 8 + 8):
                    for tc in range(4):
                        tok = xn_col0 + halfq * 512 + tc * 128
                        mm = lambda E, kch=kch, tc=tc, tok=tok, slot=slot: E.matmul(
                            vps(tc),
                            xn_bf[:, kch * xn_seg + tok: kch * xn_seg + tok + 128],
                            t["wvbuf"][:, slot * 4096 + (kch % 8) * 512:
                                       slot * 4096 + (kch % 8 + 1) * 512],
                            start=(kch == 0), stop=(kch == KCH - 1))
                        if kch == wh * 8 + 7 and tc == 3:
                            p.inc("pe", "pe", 1, mm)
                        else:
                            p.emit("pe", mm)
                ring.consumed(i, "pe")
            p.wait("act", "pe")
            for tc in range(4):
                p.inc("act", "ac", 1, lambda E, tc=tc, dvq=dvq: E.activation(
                    v_stage[:, tc, dvq * 512:(dvq + 1) * 512], vps(tc), AF.Copy))
            free = {"ac": p.ct["ac"]}
        p.wait("sync", "ac")
        for tc in range(4):
            r0 = row0 + halfq * 512 + tc * 128
            stv.append(p.inc("sync", "sta", 16, lambda E, tc=tc, r0=r0: E.dma_start(
                out=v_out[r0:r0 + 128, :], in_=v_stage[:, tc, :])))
        if halfq == 0:
            p.wait("act", "sta", stv[-1])
    return free


def _f32(a):
    return np.ascontiguousarray(np.asarray(a, dtype=np.float32))


def chunk_wT(W):
    return np.ascontiguousarray(
        _f32(W).reshape(16, 128, 16, 128).transpose(0, 3, 2, 1)).reshape(16, 128, 2048)


def rope_tables():
    angle = 1.0 / 10000 ** np.linspace(0.0, 1.0, 32)
    freqs = np.arange(S, dtype=np.float64)[:, None] * angle
    cos, sin = np.cos(freqs), np.sin(freqs)
    Cc = np.empty((64, S), np.float64)
    Sc = np.empty((64, S), np.float64)
    for i in range(32):
        Cc[2 * i] = cos[:, i]
        Cc[2 * i + 1] = cos[:, i]
        Sc[2 * i] = -sin[:, i]
        Sc[2 * i + 1] = sin[:, i]
    C = np.vstack([Cc, Cc]).astype(NPBF)
    Sg = np.vstack([Sc, Sc]).astype(NPBF)
    return np.ascontiguousarray(C), np.ascontiguousarray(Sg)


MODE = "F"


def kernel(L0_attn_x, L0_attn_Wq, L0_attn_Wk, L0_attn_Wv, L0_attn_Wo,
           L0_attn_lambda_q1, L0_attn_lambda_k1, L0_attn_lambda_q2, L0_attn_lambda_k2,
           L0_attn_gamma, L0_ffn_gate_w, L0_ffn_up_w, L0_ffn_down_w,
           L0_ffn_gate_s, L0_ffn_up_s, L0_ffn_down_s):
    x = _f32(L0_attn_x)
    wq_dev = chunk_wT(L0_attn_Wq)
    wk_dev = chunk_wT(L0_attn_Wk)
    wo_dev = chunk_wT(L0_attn_Wo)
    wv_dev = np.ascontiguousarray(
        _f32(L0_attn_Wv).reshape(4, 512, 16, 128).transpose(0, 3, 2, 1)).reshape(4, 128, 8192)
    g4 = _f32(L0_ffn_gate_w).reshape(44, 128, 16, 128).transpose(0, 3, 2, 1)
    u4 = _f32(L0_ffn_up_w).reshape(44, 128, 16, 128).transpose(0, 3, 2, 1)
    gu_dev = np.ascontiguousarray(
        np.stack([g4, u4], axis=1).transpose(0, 2, 1, 3, 4)).reshape(44, 128, 4096)
    dn_dev = np.ascontiguousarray(
        _f32(L0_ffn_down_w).reshape(16, 128, 44, 128).transpose(0, 3, 2, 1)).reshape(16, 128, 5632)
    gs_dev = np.ascontiguousarray(_f32(L0_ffn_gate_s).reshape(44, 128).T)
    us_dev = np.ascontiguousarray(_f32(L0_ffn_up_s).reshape(44, 128).T)
    ds_dev = np.ascontiguousarray(_f32(L0_ffn_down_s).reshape(16, 128).T)
    gam_dev = _f32(L0_attn_gamma).reshape(128, 1)
    lq1 = _f32(L0_attn_lambda_q1).reshape(1, 64)
    lk1 = _f32(L0_attn_lambda_k1).reshape(1, 64)
    lq2 = _f32(L0_attn_lambda_q2).reshape(1, 64)
    lk2 = _f32(L0_attn_lambda_k2).reshape(1, 64)
    Ct, St = rope_tables()
    common = {"wq": wq_dev, "wo": wo_dev, "gu": gu_dev, "dn": dn_dev,
              "gs": gs_dev, "us": us_dev, "ds": ds_dev, "gam": gam_dev,
              "lq1": lq1, "lk1": lk1, "lq2": lq2, "lk2": lk2}

    if MODE == "F":
        ncF = build_F()
        in_F = []
        for c in range(8):
            b, j = c // 4, c % 4
            order = [j] + [i for i in range(4) if i != j]
            xTb = np.ascontiguousarray(np.concatenate(
                [x[b, 1024 * i:1024 * (i + 1), :] for i in order], axis=0).T)
            ctb = np.ascontiguousarray(np.concatenate(
                [Ct[:, 1024 * i:1024 * (i + 1)] for i in order], axis=1))
            stbt = np.ascontiguousarray(np.concatenate(
                [St[:, 1024 * i:1024 * (i + 1)] for i in order], axis=1))
            m = dict(common)
            m.update({"xTb": xTb, "wk": wk_dev, "wv": wv_dev,
                      "ctb": ctb, "stbt": stbt})
            in_F.append(m)
        res = run_bass_kernel_spmd(ncF, in_F, core_ids=list(range(8)))
        out = np.empty((2, S, E), np.float32)
        for c in range(8):
            b, r0 = c // 4, (c % 4) * TQ
            out[b, r0:r0 + TQ, :] = np.asarray(res.results[c]["outT"]).T
        return out

    xT = []
    for c in range(8):
        b, r0 = c // 4, (c % 4) * TQ
        xT.append(np.ascontiguousarray(x[b, r0:r0 + TQ, :].T))
    ncA = build_A()
    in_A = []
    for c in range(8):
        r0 = (c % 4) * TQ
        in_A.append({"xT": xT[c], "wk": wk_dev, "wv": wv_dev,
                     "ct": np.ascontiguousarray(Ct[:, r0:r0 + TQ]),
                     "stb": np.ascontiguousarray(St[:, r0:r0 + TQ])})
    resA = run_bass_kernel_spmd(ncA, in_A, core_ids=list(range(8)))
    kT_full, v_full, ks4 = {}, {}, {}
    for g in range(2):
        cs = [4 * g + i for i in range(4)]
        kT_full[g] = np.ascontiguousarray(
            np.concatenate([np.asarray(resA.results[c]["kT_out"]) for c in cs], axis=1))
        v_full[g] = np.ascontiguousarray(
            np.concatenate([np.asarray(resA.results[c]["v_out"]) for c in cs], axis=0))
        ks4[g] = np.ascontiguousarray(
            np.concatenate([np.asarray(resA.results[c]["kstat_out"]) for c in cs], axis=0))
    ncB = build_B()
    in_B = []
    for c in range(8):
        g, r0 = c // 4, (c % 4) * TQ
        m = dict(common)
        m.update({"xT": xT[c], "kT": kT_full[g], "vf": v_full[g], "ks": ks4[g],
                  "ct": np.ascontiguousarray(Ct[:, r0:r0 + TQ]),
                  "stb": np.ascontiguousarray(St[:, r0:r0 + TQ])})
        in_B.append(m)
    resB = run_bass_kernel_spmd(ncB, in_B, core_ids=list(range(8)))
    out = np.empty((2, S, E), np.float32)
    for c in range(8):
        b, r0 = c // 4, (c % 4) * TQ
        out[b, r0:r0 + TQ, :] = np.asarray(resB.results[c]["outT"]).T
    return out


# version 12
# speedup vs baseline: 1.0703x; 1.0298x over previous
import math
import numpy as np
import ml_dtypes

import concourse.bass as bass
import concourse.mybir as mybir
from concourse.bass_utils import run_bass_kernel_spmd

F32 = mybir.dt.float32
BF16 = mybir.dt.bfloat16
AF = mybir.ActivationFunctionType
ALU = mybir.AluOpType
AX = mybir.AxisListType

S, E, H = 4096, 2048, 16
TQ = 1024
KCH = 16
ICH = 44
LAMBDA_INIT = 0.8 - 0.6 * math.exp(-0.3 * 12)
ONE_MINUS_LI = 1.0 - LAMBDA_INIT
EPS = 1e-5
NPBF = ml_dtypes.bfloat16
SWAP_MASK = [i ^ 1 for i in range(32)]


class Prog:
    def __init__(self, nc, ent):
        self.nc = nc
        self.ent = ent
        self.ops = {e: [] for e in ("pe", "act", "dve", "pool", "sync")}
        self.sems = {}
        self.ct = {}
        for s in ("pe", "ac", "dv", "pl", "lds", "ldx0", "ldx1", "ldw0", "ldw1",
                  "ldk0", "ldk1", "st0", "st1", "sta"):
            self.sems[s] = ent(nc.semaphore(s))
            self.ct[s] = 0

    def emit(self, eng, fn):
        self.ops[eng].append(fn)

    def wait(self, eng, sem, val=None):
        v = self.ct[sem] if val is None else val
        if v <= 0:
            return
        s = self.sems[sem]
        self.emit(eng, lambda E, s=s, v=v: E.wait_ge(s, v))

    OWN = {"act": "ac", "dve": "dv", "pool": "pl"}

    def inc(self, eng, sem, n, mk):
        own = self.OWN.get(eng)
        if own is not None and self.ct[own] > 0:
            self.wait(eng, own)
        if n >= 16 and sem == "lds" and self.ct[sem] > 0:
            self.wait(eng, sem)
        s = self.sems[sem]
        self.ct[sem] += n
        v = self.ct[sem]
        self.emit(eng, lambda E, mk=mk, s=s, n=n: mk(E).then_inc(s, n))
        return v


class Ring:

    def __init__(self, p, name, nslots, queue):
        self.p, self.queue, self.n, self.name = p, queue, nslots, name
        for j in range(nslots):
            s = f"{name}{j}"
            if s not in p.sems:
                p.sems[s] = p.ent(p.nc.semaphore(s))
                p.ct[s] = 0
        self.i = 0
        self.done = []

    def load(self, mk_of_slot):
        i, n = self.i, self.n
        slot = i % n
        if i >= n:
            for sem, val in self.done[i - n].items():
                self.p.wait(self.queue, sem, val)
        v = self.p.inc(self.queue, f"{self.name}{slot}", 16, mk_of_slot(slot))
        self.done.append({})
        self.i += 1
        return i, slot, (f"{self.name}{slot}", v)

    def consumed(self, i, *sems):
        for s in sems:
            self.done[i][s] = self.p.ct[s]


def scope_barrier(p):
    for eng in ("pe", "act", "dve", "pool", "sync"):
        for sem in list(p.ct):
            if Prog.OWN.get(eng) == sem:
                continue
            p.wait(eng, sem)


def attach(p, block):
    @block.tensor
    def _(t):
        for fn in p.ops["pe"]:
            fn(t)

    @block.scalar
    def _(sc):
        for fn in p.ops["act"]:
            fn(sc)

    @block.vector
    def _(v):
        for fn in p.ops["dve"]:
            fn(v)

    @block.gpsimd
    def _(g):
        for fn in p.ops["pool"]:
            fn(g)

    @block.sync
    def _(s):
        for fn in p.ops["sync"]:
            fn(s)


def alloc_common(nc, ent, t):
    t["ps_a"] = ent(nc.psum_tensor("ps_a", [128, 2048], F32))
    t["ps_b"] = ent(nc.psum_tensor("ps_b", [128, 1024], F32))
    t["ps_d1"] = ent(nc.psum_tensor("ps_d1", [1, 512], F32))
    t["ps_d2"] = ent(nc.psum_tensor("ps_d2", [1, 512], F32))
    t["ones_col_bf"] = ent(nc.sbuf_tensor("ones_col_bf", [128, 1], BF16))
    t["ones_row_f"] = ent(nc.sbuf_tensor("ones_row_f", [1, 128], F32))
    t["neg_row_f"] = ent(nc.sbuf_tensor("neg_row_f", [1, 128], F32))
    t["ones128_bf"] = ent(nc.sbuf_tensor("ones128_bf", [128, 128], BF16))
    t["mask1_bf"] = ent(nc.sbuf_tensor("mask1_bf", [128, 1], BF16))
    t["mask2_bf"] = ent(nc.sbuf_tensor("mask2_bf", [128, 1], BF16))
    t["xc"] = ent(nc.sbuf_tensor("xc", [128, 4096], F32))
    t["xbb"] = ent(nc.sbuf_tensor("xbb", [128, 8192], BF16))
    t["x2b"] = ent(nc.sbuf_tensor("x2b", [128, 4096], BF16))
    t["row"] = ent(nc.sbuf_tensor("row", [1, 4096], F32))
    t["tmp0"] = ent(nc.sbuf_tensor("tmp0", [128, 1024], F32))
    t["tmp1"] = ent(nc.sbuf_tensor("tmp1", [128, 1024], F32))
    t["sq_b"] = ent(nc.sbuf_tensor("sq_b", [128, 1024], BF16))
    t["srow"] = ent(nc.sbuf_tensor("srow", [1, 64], F32))
    t["eps_col"] = ent(nc.sbuf_tensor("eps_col", [128, 1], F32))
    t["nrow"] = ent(nc.sbuf_tensor("nrow", [1, 1024], F32))
    t["mBb"] = ent(nc.sbuf_tensor("mBb", [128, 512], BF16))
    t["rBb"] = ent(nc.sbuf_tensor("rBb", [128, 512], BF16))
    t["trow"] = ent(nc.sbuf_tensor("trow", [1, 16], F32))


def emit_init(p, t):
    for name, sl, val in (("ones_col_bf", None, 1.0), ("ones_row_f", None, 1.0),
                          ("neg_row_f", None, -1.0), ("ones128_bf", None, 1.0),
                          ("mask1_bf", (0, 64), 1.0), ("mask1_bf", (64, 128), 0.0),
                          ("mask2_bf", (0, 64), 0.0), ("mask2_bf", (64, 128), 1.0),
                          ("srow", None, 0.0), ("eps_col", None, EPS)):
        def mk(E, name=name, sl=sl, val=val):
            ap = t[name][:] if sl is None else t[name][sl[0]:sl[1], :]
            return E.memset(ap, val)
        p.inc("pool", "pl", 1, mk)


def emit_ln(p, t, x_hbm, xn_bf, xn_seg, xn_col0, ring=None):
    xbb, x2b, row = t["xbb"], t["x2b"], t["row"]
    ps_b, ps_d1, ps_d2 = t["ps_b"], t["ps_d1"], t["ps_d2"]
    ringb = t.setdefault("_lnringb", Ring(p, "rxb", 4, "pool"))
    p.wait("pe", "pl")
    x2i = 0
    for tile in range(2):
        qidx = {}
        for q in range(4):
            i, sl, key = ringb.load(lambda s: (lambda E, q=q, tile=tile: E.dma_start(
                out=xbb[:, q * 2048:(q + 1) * 2048].rearrange("p (c t) -> p c t", c=4),
                in_=x_hbm[q * 512:(q + 1) * 512,
                          tile * 512:(tile + 1) * 512].rearrange("(c p) t -> p c t", p=128))))
            p.wait("act", *key)
            for j in range(4):
                kch = q * 4 + j
                co = q * 2048 + j * 512
                x2o = (x2i % 8) * 512
                x2i += 1
                p.wait("act", "pe", t.get("_x2pe", 0))
                p.inc("act", "ac", 1, lambda E, co=co, x2o=x2o: E.activation(
                    x2b[:, x2o:x2o + 512], xbb[:, co:co + 512], AF.Square))
                p.wait("pe", *key)
                p.wait("pe", "ac")
                p.emit("pe", lambda E, co=co, kch=kch: E.matmul(
                    ps_d1[:], t["ones_col_bf"][:], xbb[:, co:co + 512],
                    start=(kch == 0), stop=(kch == KCH - 1)))
                mmv = p.inc("pe", "pe", 1, lambda E, x2o=x2o, kch=kch: E.matmul(
                    ps_d2[:], t["ones_col_bf"][:], x2b[:, x2o:x2o + 512],
                    start=(kch == 0), stop=(kch == KCH - 1)))
                if x2i % 8 == 0:
                    t["_x2pe"] = p.ct["pe"]
            ringb.consumed(i, "pe")
            qidx[q] = i
        p.wait("act", "pe", mmv)
        p.inc("act", "ac", 1, lambda E: E.activation(row[0:1, 0:512], ps_d1[:], AF.Copy, scale=1.0 / 2048.0))
        p.inc("act", "ac", 1, lambda E: E.activation(row[0:1, 512:1024], ps_d2[:], AF.Copy, scale=1.0 / 2048.0))
        p.inc("act", "ac", 1, lambda E: E.activation(row[0:1, 1024:1536], row[0:1, 0:512], AF.Square))
        p.wait("dve", "ac")
        p.inc("dve", "dv", 1, lambda E: E.tensor_tensor(
            row[0:1, 512:1024], row[0:1, 512:1024], row[0:1, 1024:1536], ALU.subtract))
        p.wait("act", "dv")
        p.inc("act", "ac", 1, lambda E: E.activation(row[0:1, 1536:2048], row[0:1, 512:1024], AF.Ln, bias=t["eps_col"][0:1, :]))
        p.inc("act", "ac", 1, lambda E: E.activation(row[0:1, 1536:2048], row[0:1, 1536:2048], AF.Exp, scale=-0.5))
        p.wait("pe", "ac")
        p.wait("pe", "dv")
        p.emit("pe", lambda E: E.matmul(ps_b[:, 0:512], t["ones_row_f"][:], row[0:1, 0:512],
                                        start=True, stop=True))
        bcv = p.inc("pe", "pe", 1, lambda E: E.matmul(
            ps_b[:, 512:1024], t["ones_row_f"][:], row[0:1, 1536:2048], start=True, stop=True))
        p.wait("act", "pe", bcv)
        p.inc("act", "ac", 1, lambda E: E.activation(t["mBb"][:], ps_b[:, 0:512], AF.Copy))
        cbv = p.inc("act", "ac", 1, lambda E: E.activation(t["rBb"][:], ps_b[:, 512:1024], AF.Copy))
        p.wait("dve", "ac", cbv)
        for kch in range(KCH):
            q, j = kch // 4, kch % 4
            co = q * 2048 + j * 512
            c0 = xn_col0 + tile * 512
            p.inc("dve", "dv", 1, lambda E, co=co: E.tensor_tensor(
                t["tmpa_bf"][:, 0:512], xbb[:, co:co + 512], t["mBb"][:], ALU.subtract))
            p.inc("dve", "dv", 1, lambda E, kch=kch, c0=c0: E.tensor_tensor(
                xn_bf[:, kch * xn_seg + c0: kch * xn_seg + c0 + 512],
                t["tmpa_bf"][:, 0:512], t["rBb"][:], ALU.mult))
        for q in range(4):
            ringb.done[qidx[q]]["dv"] = p.ct["dv"]
    return {"dv": p.ct["dv"]}


def emit_rope(p, t, ps_view, ctab, stab, ccol0, dst_bf, dst_col0):
    qr, sw, ta = t["qraw_bf"], t["tswp_bf"], t["tmpa_bf"]
    p.wait("act", "pe")
    psfree = p.inc("act", "ac", 1, lambda E: E.activation(qr[:], ps_view, AF.Copy))
    p.wait("dve", "ac", psfree)
    p.inc("dve", "dv", 1, lambda E: E.stream_shuffle(sw[:], qr[:], SWAP_MASK))
    p.inc("dve", "dv", 1, lambda E: E.tensor_tensor(
        ta[:], qr[:], ctab[:, ccol0:ccol0 + 1024], ALU.mult))
    p.inc("dve", "dv", 1, lambda E: E.tensor_tensor(
        sw[:], sw[:], stab[:, ccol0:ccol0 + 1024], ALU.mult))
    wv = p.inc("dve", "dv", 1, lambda E: E.tensor_tensor(
        dst_bf[:, dst_col0:dst_col0 + 1024], ta[:], sw[:], ALU.add))
    return ("ac", psfree), wv


def emit_rope_sb(p, t, src_view, ctab, stab, ccol0, dst_bf, dst_col0):
    sw, ta = t["tswp_bf"], t["tmpa_bf"]
    p.inc("dve", "dv", 1, lambda E: E.stream_shuffle(sw[:], src_view, SWAP_MASK))
    srcfree = p.inc("dve", "dv", 1, lambda E: E.tensor_tensor(
        ta[:], src_view, ctab[:, ccol0:ccol0 + 1024], ALU.mult))
    p.inc("dve", "dv", 1, lambda E: E.tensor_tensor(
        sw[:], sw[:], stab[:, ccol0:ccol0 + 1024], ALU.mult))
    wv = p.inc("dve", "dv", 1, lambda E: E.tensor_tensor(
        dst_bf[:, dst_col0:dst_col0 + 1024], ta[:], sw[:], ALU.add))
    return srcfree, wv


def emit_v_group(p, t, wv_hbm, wv_cache, ring, xn_bf, halfq, dvq,
                 v_stage, v_out, row0, vfree, fill_cache, stv):
    ps_b = t["ps_b"]
    slots = []
    for wh in range(2):
        if fill_cache and halfq == 0:
            i, slot, key = ring.load(lambda s: (lambda E, dvq=dvq, wh=wh, s=s: E.dma_start(
                out=t["wvbuf"][:, s * 4096:(s + 1) * 4096],
                in_=wv_hbm[dvq][:, wh * 4096:(wh + 1) * 4096])))
            p.wait("pool", *key)
            wcv = p.inc("pool", f"rvw{slot}", 16, lambda E, dvq=dvq, wh=wh, s=slot: E.dma_start(
                out=wv_cache[dvq][:, wh * 4096:(wh + 1) * 4096],
                in_=t["wvbuf"][:, s * 4096:(s + 1) * 4096]))
            ring.done[i][f"rvw{slot}"] = wcv
        else:
            i, slot, key = ring.load(lambda s: (lambda E, dvq=dvq, wh=wh, s=s: E.dma_start(
                out=t["wvbuf"][:, s * 4096:(s + 1) * 4096],
                in_=wv_cache[dvq][:, wh * 4096:(wh + 1) * 4096])))
        slots.append((i, slot, key))
    for tc2 in range(2):
        for sem, val in vfree.items():
            p.wait("pe", sem, val)
        for wh in range(2):
            i, slot, key = slots[wh]
            p.wait("pe", key[0], key[1])
            for kch in range(wh * 8, wh * 8 + 8):
                for half in range(2):
                    tc = tc2 * 2 + half
                    tok = halfq * 512 + tc * 128
                    mm = lambda E, kch=kch, tok=tok, slot=slot, half=half: E.matmul(
                        ps_b[:, half * 512:(half + 1) * 512],
                        xn_bf[:, kch * 1024 + tok: kch * 1024 + tok + 128],
                        t["wvbuf"][:, slot * 4096 + (kch % 8) * 512:
                                   slot * 4096 + (kch % 8 + 1) * 512],
                        start=(kch == 0), stop=(kch == KCH - 1))
                    if kch == wh * 8 + 7 and half == 1:
                        p.inc("pe", "pe", 1, mm)
                    else:
                        p.emit("pe", mm)
            if tc2 == 1:
                ring.consumed(i, "pe")
        p.wait("act", "pe")
        if stv:
            p.wait("act", "sta", stv[-1][1])
        for half in range(2):
            tc = tc2 * 2 + half
            p.inc("act", "ac", 1, lambda E, tc=tc, dvq=dvq, half=half: E.activation(
                v_stage[:, tc, dvq * 512:(dvq + 1) * 512],
                ps_b[:, half * 512:(half + 1) * 512], AF.Copy))
        vfree = {"ac": p.ct["ac"]}
    if dvq == 3:
        p.wait("sync", "ac")
        for tc in range(4):
            r0 = row0 + halfq * 512 + tc * 128
            stv.append((halfq, p.inc("sync", "sta", 16, lambda E, tc=tc, r0=r0: E.dma_start(
                out=v_out[r0:r0 + 128, :], in_=v_stage[:, tc, :]))))
    return vfree


def emit_headnorm(p, t, src_bf, col0, och, statbase=0):
    sq_b, ps_d1, ps_d2 = t["sq_b"], t["ps_d1"], t["ps_d2"]
    p.wait("act", "dv")
    p.wait("act", "pe")
    p.inc("act", "ac", 1, lambda E: E.activation(
        sq_b[:], src_bf[:, col0:col0 + 1024], AF.Square))
    p.wait("pe", "ac")
    for half, mask in ((0, "mask1_bf"), (1, "mask2_bf")):
        p.wait("pe", "ac", t.get("hn_ac", 0))
        p.emit("pe", lambda E, mask=mask: E.matmul(
            ps_d1[:], t[mask][:], sq_b[:, 0:512], start=True, stop=True))
        mv = p.inc("pe", "pe", 1, lambda E, mask=mask: E.matmul(
            ps_d2[:], t[mask][:], sq_b[:, 512:1024], start=True, stop=True))
        p.wait("act", "pe", mv)
        p.wait("act", "dv", t.get("hn_rv", 0))
        p.inc("act", "ac", 1, lambda E: E.activation(t["nrow"][0:1, 0:512], ps_d1[:], AF.Copy))
        nv = p.inc("act", "ac", 1, lambda E: E.activation(t["nrow"][0:1, 512:1024], ps_d2[:], AF.Copy))
        t["hn_ac"] = nv
        p.wait("dve", "ac", nv)
        p.wait("dve", "pl")
        rv_ = p.inc("dve", "dv", 1, lambda E: E.tensor_reduce(
            t["trow"][0:1, 2:3], t["nrow"][0:1, 0:1024], AX.X, ALU.max))
        t["hn_rv"] = rv_
        col = statbase + och + 16 * half
        p.inc("dve", "dv", 1, lambda E, col=col: E.tensor_tensor(
            t["srow"][0:1, col:col + 1], t["srow"][0:1, col:col + 1],
            t["trow"][0:1, 2:3], ALU.max))
    return p.ct["pe"]


def emit_fmproj(p, t, w_hbm, wbuf, xn_bf, xn_seg, xn_col0, consume, psfree0):
    ps_a = t["ps_a"]
    free = [dict(psfree0), dict(psfree0)]
    ldv = []
    use_done = []
    for och in range(16):
        par = och % 2
        if och >= 2:
            for sem, val in use_done[och - 2].items():
                p.wait("pool", sem, val)
        v = p.inc("pool", f"ldw{par}", 16, lambda E, och=och, par=par: E.dma_start(
            out=wbuf[:, par * 2048:(par + 1) * 2048], in_=w_hbm[och]))
        ldv.append(v)
        for sem, val in free[par].items():
            p.wait("pe", sem, val)
        p.wait("pe", f"ldw{par}", v)
        for kch in range(KCH):
            for half in range(2):
                mm = lambda E, och=och, kch=kch, half=half, par=par: E.matmul(
                    ps_a[:, par * 1024 + half * 512: par * 1024 + (half + 1) * 512],
                    wbuf[:, par * 2048 + kch * 128: par * 2048 + (kch + 1) * 128],
                    xn_bf[:, kch * xn_seg + xn_col0 + half * 512:
                          kch * xn_seg + xn_col0 + (half + 1) * 512],
                    start=(kch == 0), stop=(kch == KCH - 1))
                if kch == KCH - 1 and half == 1:
                    mmv = p.inc("pe", "pe", 1, mm)
                else:
                    p.emit("pe", mm)
        use_done.append({"pe": p.ct["pe"]})
        p.wait("dve", "pe", mmv)
        free[par] = consume(och, mmv)
    return free[0], free[1]


def emit_vproj(p, t, wv_hbm, wvbuf, xn_bf, xn_seg, xn_col0, v_stage, v_out, row0, psfree0):
    ps_a = t["ps_a"]

    def vps(j):
        return ps_a[:, j * 512:(j + 1) * 512]

    free = dict(psfree0)
    use_done = []
    gi = 0
    stv = []
    for halfq in range(2):
        for dvq in range(4):
            par = gi % 2
            if gi >= 2:
                for sem, val in use_done[gi - 2].items():
                    p.wait("pool", sem, val)
            v = p.inc("pool", f"ldw{par}", 16, lambda E, dvq=dvq, par=par: E.dma_start(
                out=wvbuf[:, par * 8192:(par + 1) * 8192], in_=wv_hbm[dvq]))
            for sem, val in free.items():
                p.wait("pe", sem, val)
            p.wait("pe", f"ldw{par}", v)
            for kch in range(KCH):
                for tc in range(4):
                    tok = xn_col0 + halfq * 512 + tc * 128
                    mm = lambda E, kch=kch, tc=tc, tok=tok, par=par: E.matmul(
                        vps(tc),
                        xn_bf[:, kch * xn_seg + tok: kch * xn_seg + tok + 128],
                        wvbuf[:, par * 8192 + kch * 512: par * 8192 + (kch + 1) * 512],
                        start=(kch == 0), stop=(kch == KCH - 1))
                    if kch == KCH - 1 and tc == 3:
                        p.inc("pe", "pe", 1, mm)
                    else:
                        p.emit("pe", mm)
            use_done.append({"pe": p.ct["pe"]})
            p.wait("dve", "pe")
            for tc in range(4):
                p.inc("dve", "dv", 1, lambda E, tc=tc, dvq=dvq: E.tensor_copy(
                    v_stage[:, tc, dvq * 512:(dvq + 1) * 512], vps(tc)))
            free = {"dv": p.ct["dv"]}
            gi += 1
        p.wait("sync", "dv")
        for tc in range(4):
            r0 = row0 + halfq * 512 + tc * 128
            stv.append(p.inc("sync", "sta", 16, lambda E, tc=tc, r0=r0: E.dma_start(
                out=v_out[r0:r0 + 128, :], in_=v_stage[:, tc, :])))
        if halfq == 0:
            p.wait("dve", "sta", stv[-1])
    return free


def build_A():
    nc = bass.Bass("TRN2", target_bir_lowering=False)
    xT = nc.dram_tensor("xT", [E, TQ], F32, kind="ExternalInput").ap()
    wk = nc.dram_tensor("wk", [16, 128, 2048], F32, kind="ExternalInput").ap()
    wv = nc.dram_tensor("wv", [4, 128, 8192], F32, kind="ExternalInput").ap()
    ct = nc.dram_tensor("ct", [128, TQ], BF16, kind="ExternalInput").ap()
    stb = nc.dram_tensor("stb", [128, TQ], BF16, kind="ExternalInput").ap()
    kT_out = nc.dram_tensor("kT_out", [E, TQ], BF16, kind="ExternalOutput").ap()
    v_out = nc.dram_tensor("v_out", [TQ, E], BF16, kind="ExternalOutput").ap()
    kstat_out = nc.dram_tensor("kstat_out", [1, 64], F32, kind="ExternalOutput").ap()

    from contextlib import ExitStack
    ctx = ExitStack()
    ent = ctx.enter_context
    t = {}
    alloc_common(nc, ent, t)
    t["xn_bf"] = ent(nc.sbuf_tensor("xn_bf", [128, 16 * 1024], BF16))
    t["wbuf"] = ent(nc.sbuf_tensor("wbuf", [128, 4096], BF16))
    t["wvbuf"] = ent(nc.sbuf_tensor("wvbuf", [128, 16384], BF16))
    t["v_stage"] = ent(nc.sbuf_tensor("v_stage", [128, 4, 2048], BF16))
    t["kstage"] = ent(nc.sbuf_tensor("kstage", [128, 2048], BF16))
    t["ctab"] = ent(nc.sbuf_tensor("ctab", [128, TQ], BF16))
    t["qraw_bf"] = ent(nc.sbuf_tensor("qraw_bf", [128, 1024], BF16))
    t["tswp_bf"] = ent(nc.sbuf_tensor("tswp_bf", [128, 1024], BF16))
    t["tmpa_bf"] = ent(nc.sbuf_tensor("tmpa_bf", [128, 1024], BF16))
    t["stab"] = ent(nc.sbuf_tensor("stab", [128, TQ], BF16))

    p = Prog(nc, ent)
    emit_init(p, t)
    p.inc("sync", "lds", 16, lambda E: E.dma_start(out=t["ctab"][:], in_=ct))
    tabv = p.inc("sync", "lds", 16, lambda E: E.dma_start(out=t["stab"][:], in_=stb))
    psfree = emit_ln(p, t, xT, t["xn_bf"], 1024, 0)
    p.wait("dve", "lds", tabv)
    p.wait("pool", "lds", tabv)

    kst_store = []
    ksq_done = []

    def consume_k(och, mmv):
        half = och % 2
        if och >= 2:
            p.wait("dve", f"st{half}", kst_store[och - 2])
            p.wait("dve", "ac", ksq_done[och - 2])
        fr, wv_ = emit_rope(p, t, t["ps_a"][:, half * 1024:(half + 1) * 1024],
                            t["ctab"], t["stab"], 0, t["kstage"], half * 1024)
        emit_headnorm(p, t, t["kstage"], half * 1024, och)
        ksq_done.append(p.ct["ac"])
        p.wait("sync", "dv", wv_)
        kst_store.append(p.inc("sync", f"st{half}", 16, lambda E, och=och, half=half: E.dma_start(
            out=kT_out[och * 128:(och + 1) * 128, :],
            in_=t["kstage"][:, half * 1024:(half + 1) * 1024])))
        return {fr[0]: fr[1]}

    f0, f1 = emit_fmproj(p, t, wk, t["wbuf"], t["xn_bf"], 1024, 0, consume_k, psfree)
    both = {k: max(f0.get(k, 0), f1.get(k, 0)) for k in set(f0) | set(f1)}
    emit_vproj(p, t, wv, t["wvbuf"], t["xn_bf"], 1024, 0, t["v_stage"], v_out, 0, both)
    p.wait("sync", "dv")
    p.inc("sync", "sta", 16, lambda E: E.dma_start(out=kstat_out, in_=t["srow"][:]))
    p.wait("sync", "st0")
    p.wait("sync", "st1")
    p.wait("sync", "sta")
    block = ent(nc.Block())
    attach(p, block)
    ctx.close()
    return nc


def build_B(debug=False):
    nc = bass.Bass("TRN2", target_bir_lowering=False)
    xT = nc.dram_tensor("xT", [E, TQ], F32, kind="ExternalInput").ap()
    kT = nc.dram_tensor("kT", [E, S], BF16, kind="ExternalInput").ap()
    vf = nc.dram_tensor("vf", [S, E], BF16, kind="ExternalInput").ap()
    ks = nc.dram_tensor("ks", [4, 64], F32, kind="ExternalInput").ap()
    wq = nc.dram_tensor("wq", [16, 128, 2048], F32, kind="ExternalInput").ap()
    wo = nc.dram_tensor("wo", [16, 128, 2048], F32, kind="ExternalInput").ap()
    gu = nc.dram_tensor("gu", [44, 128, 4096], F32, kind="ExternalInput").ap()
    dn = nc.dram_tensor("dn", [16, 128, 5632], F32, kind="ExternalInput").ap()
    gs = nc.dram_tensor("gs", [128, 44], F32, kind="ExternalInput").ap()
    us = nc.dram_tensor("us", [128, 44], F32, kind="ExternalInput").ap()
    ds = nc.dram_tensor("ds", [128, 16], F32, kind="ExternalInput").ap()
    gam = nc.dram_tensor("gam", [128, 1], F32, kind="ExternalInput").ap()
    lq1 = nc.dram_tensor("lq1", [1, 64], F32, kind="ExternalInput").ap()
    lk1 = nc.dram_tensor("lk1", [1, 64], F32, kind="ExternalInput").ap()
    lq2 = nc.dram_tensor("lq2", [1, 64], F32, kind="ExternalInput").ap()
    lk2 = nc.dram_tensor("lk2", [1, 64], F32, kind="ExternalInput").ap()
    ct = nc.dram_tensor("ct", [128, TQ], BF16, kind="ExternalInput").ap()
    stb = nc.dram_tensor("stb", [128, TQ], BF16, kind="ExternalInput").ap()
    outT = nc.dram_tensor("outT", [E, TQ], F32, kind="ExternalOutput").ap()
    dbg = {}
    if debug:
        for nm, shp, dt in (("d_xn", [E, TQ], BF16), ("d_q", [E, TQ], BF16),
                            ("d_attn", [E, TQ], BF16), ("d_acc", [E, TQ], F32),
                            ("d_bias", [128, 16], F32), ("d_srow", [1, 64], F32)):
            dbg[nm] = nc.dram_tensor(nm, shp, dt, kind="ExternalOutput").ap()

    from contextlib import ExitStack
    ctx = ExitStack()
    ent = ctx.enter_context
    t = {}
    t["ps_a"] = ent(nc.psum_tensor("ps_a", [128, 2048], F32))
    t["ps_b"] = ent(nc.psum_tensor("ps_b", [128, 1024], F32))
    t["ps_d1"] = ent(nc.psum_tensor("ps_d1", [1, 512], F32))
    t["ps_d2"] = ent(nc.psum_tensor("ps_d2", [1, 512], F32))
    t["ones_col_bf"] = ent(nc.sbuf_tensor("ones_col_bf", [128, 1], BF16))
    t["ones_row_f"] = ent(nc.sbuf_tensor("ones_row_f", [1, 128], F32))
    t["neg_row_f"] = ent(nc.sbuf_tensor("neg_row_f", [1, 128], F32))
    t["ones128_bf"] = ent(nc.sbuf_tensor("ones128_bf", [128, 128], BF16))
    t["mask1_bf"] = ent(nc.sbuf_tensor("mask1_bf", [128, 1], BF16))
    t["mask2_bf"] = ent(nc.sbuf_tensor("mask2_bf", [128, 1], BF16))
    t["tmp0"] = ent(nc.sbuf_tensor("tmp0", [128, 1024], F32))
    t["tmp1"] = ent(nc.sbuf_tensor("tmp1", [128, 1024], F32))
    t["srow"] = ent(nc.sbuf_tensor("srow", [1, 64], F32))
    t["trow"] = ent(nc.sbuf_tensor("trow", [1, 16], F32))
    t["eps_col"] = ent(nc.sbuf_tensor("eps_col", [128, 1], F32))
    t["nrow"] = ent(nc.sbuf_tensor("nrow", [1, 1024], F32))
    t["mBb"] = ent(nc.sbuf_tensor("mBb", [128, 512], BF16))
    t["rBb"] = ent(nc.sbuf_tensor("rBb", [128, 512], BF16))
    t["bias_sb"] = ent(nc.sbuf_tensor("bias_sb", [128, 16], F32))
    t["lam_vec"] = ent(nc.sbuf_tensor("lam_vec", [128, 1], F32))
    t["gam_eff"] = ent(nc.sbuf_tensor("gam_eff", [128, 1], F32))
    t["gs_sb"] = ent(nc.sbuf_tensor("gs_sb", [128, 44], F32))
    t["us_sb"] = ent(nc.sbuf_tensor("us_sb", [128, 44], F32))
    t["ds_sb"] = ent(nc.sbuf_tensor("ds_sb", [128, 16], F32))
    t["lrow"] = ent(nc.sbuf_tensor("lrow", [1, 256], F32))
    t["krow"] = ent(nc.sbuf_tensor("krow", [1, 256], F32))
    t["xn_bf"] = ent(nc.sbuf_tensor("xn_bf", [128, 16 * 1024], BF16))
    t["q_bf"] = ent(nc.sbuf_tensor("q_bf", [128, 16 * 1024], BF16))
    t["acc"] = ent(nc.sbuf_tensor("acc", [128, 16 * 1024], F32))

    p = Prog(nc, ent)
    emit_init(p, t)
    p.wait("pe", "pl")
    for name, src_ap in (("gs_sb", gs), ("us_sb", us), ("ds_sb", ds)):
        p.inc("sync", "lds", 16, lambda E, src_ap=src_ap, name=name: E.dma_start(
            out=t[name][:], in_=src_ap))
    p.inc("sync", "lds", 16, lambda E: E.dma_start(out=t["lrow"][0:1, 0:64], in_=lq1))
    p.inc("sync", "lds", 16, lambda E: E.dma_start(out=t["lrow"][0:1, 64:128], in_=lk1))
    p.inc("sync", "lds", 16, lambda E: E.dma_start(out=t["lrow"][0:1, 128:192], in_=lq2))
    p.inc("sync", "lds", 16, lambda E: E.dma_start(out=t["lrow"][0:1, 192:256], in_=lk2))
    for r in range(4):
        p.inc("sync", "lds", 16, lambda E, r=r: E.dma_start(
            out=t["krow"][0:1, r * 64:(r + 1) * 64], in_=ks[r:r + 1, :]))
    gamv = p.inc("sync", "lds", 16, lambda E: E.dma_start(out=t["tmp0"][:, 0:1], in_=gam))
    p.wait("dve", "lds", gamv)
    p.wait("pool", "lds", gamv)
    p.wait("act", "lds", gamv)
    p.inc("dve", "dv", 1, lambda E: E.tensor_scalar(
        t["gam_eff"][:], t["tmp0"][:, 0:1], ONE_MINUS_LI, 0.0, ALU.mult, ALU.add))
    p.inc("dve", "dv", 1, lambda E: E.tensor_tensor(
        t["lrow"][0:1, 0:64], t["lrow"][0:1, 0:64], t["lrow"][0:1, 64:128], ALU.mult))
    p.inc("dve", "dv", 1, lambda E: E.tensor_tensor(
        t["lrow"][0:1, 128:192], t["lrow"][0:1, 128:192], t["lrow"][0:1, 192:256], ALU.mult))
    p.inc("dve", "dv", 1, lambda E: E.tensor_reduce(
        t["trow"][0:1, 0:1], t["lrow"][0:1, 0:64], AX.X, ALU.add))
    p.inc("dve", "dv", 1, lambda E: E.tensor_reduce(
        t["trow"][0:1, 1:2], t["lrow"][0:1, 128:192], AX.X, ALU.add))
    p.wait("act", "dv")
    p.inc("act", "ac", 1, lambda E: E.activation(t["trow"][0:1, 2:3], t["trow"][0:1, 0:1], AF.Exp))
    p.inc("act", "ac", 1, lambda E: E.activation(t["trow"][0:1, 3:4], t["trow"][0:1, 1:2], AF.Exp))
    p.wait("dve", "ac")
    p.inc("dve", "dv", 1, lambda E: E.tensor_tensor(
        t["trow"][0:1, 2:3], t["trow"][0:1, 2:3], t["trow"][0:1, 3:4], ALU.subtract))
    lamv = p.inc("dve", "dv", 1, lambda E: E.tensor_scalar(
        t["trow"][0:1, 2:3], t["trow"][0:1, 2:3], 1.0, LAMBDA_INIT, ALU.mult, ALU.add))
    p.wait("pe", "dv", lamv)
    lbv = p.inc("pe", "pe", 1, lambda E: E.matmul(
        t["ps_b"][:, 0:1], t["ones_row_f"][:], t["trow"][0:1, 2:3], start=True, stop=True))
    p.wait("dve", "pe", lbv)
    p.inc("dve", "dv", 1, lambda E: E.tensor_copy(t["lam_vec"][:], t["ps_b"][:, 0:1]))

    ctx1 = ExitStack()
    ent1 = ctx1.enter_context
    t["xc"] = ent1(nc.sbuf_tensor("xc", [128, 4096], F32, side="right"))
    t["xbb"] = ent1(nc.sbuf_tensor("xbb", [128, 8192], BF16, side="right"))
    t["x2b"] = ent1(nc.sbuf_tensor("x2b", [128, 4096], BF16, side="right"))
    t["row"] = ent1(nc.sbuf_tensor("row", [1, 2048], F32, side="right"))
    t["sq_b"] = ent1(nc.sbuf_tensor("sq_b", [128, 1024], BF16, side="right"))
    t["ctab"] = ent1(nc.sbuf_tensor("ctab", [128, TQ], BF16, side="right"))
    t["qraw_bf"] = ent1(nc.sbuf_tensor("qraw_bf", [128, 1024], BF16, side="right"))
    t["tswp_bf"] = ent1(nc.sbuf_tensor("tswp_bf", [128, 1024], BF16, side="right"))
    t["tmpa_bf"] = ent1(nc.sbuf_tensor("tmpa_bf", [128, 1024], BF16, side="right"))
    t["stab"] = ent1(nc.sbuf_tensor("stab", [128, TQ], BF16, side="right"))
    t["wbuf"] = ent1(nc.sbuf_tensor("wbuf", [128, 4096], BF16, side="right"))

    p.inc("sync", "lds", 16, lambda E: E.dma_start(out=t["ctab"][:], in_=ct))
    tabv = p.inc("sync", "lds", 16, lambda E: E.dma_start(out=t["stab"][:], in_=stb))
    psfree = emit_ln(p, t, xT, t["xn_bf"], 1024, 0)
    p.wait("dve", "lds", tabv)
    p.wait("pool", "lds", tabv)
    if debug:
        p.wait("sync", "dv")
        for kch in range(KCH):
            p.inc("sync", "sta", 16, lambda E, kch=kch: E.dma_start(
                out=dbg["d_xn"][kch * 128:(kch + 1) * 128, :],
                in_=t["xn_bf"][:, kch * 1024:(kch + 1) * 1024]))

    def consume_q(och, mmv):
        fr, wv_ = emit_rope(p, t, t["ps_a"][:, (och % 2) * 1024:(och % 2 + 1) * 1024],
                            t["ctab"], t["stab"], 0, t["q_bf"], och * 1024)
        emit_headnorm(p, t, t["q_bf"], och * 1024, och)
        return {fr[0]: fr[1]}

    emit_fmproj(p, t, wq, t["wbuf"], t["xn_bf"], 1024, 0, consume_q, psfree)
    if debug:
        p.wait("sync", "dv")
        for kch in range(KCH):
            p.inc("sync", "sta", 16, lambda E, kch=kch: E.dma_start(
                out=dbg["d_q"][kch * 128:(kch + 1) * 128, :],
                in_=t["q_bf"][:, kch * 1024:(kch + 1) * 1024]))
        p.inc("sync", "sta", 16, lambda E: E.dma_start(out=dbg["d_srow"], in_=t["srow"][:]))

    p.wait("dve", "lds")
    for r in (1, 2, 3):
        p.inc("dve", "dv", 1, lambda E, r=r: E.tensor_tensor(
            t["krow"][0:1, 0:64], t["krow"][0:1, 0:64],
            t["krow"][0:1, r * 64:(r + 1) * 64], ALU.max))
    p.inc("dve", "dv", 1, lambda E: E.tensor_tensor(
        t["lrow"][0:1, 0:16], t["srow"][0:1, 0:16], t["krow"][0:1, 0:16], ALU.mult))
    p.inc("dve", "dv", 1, lambda E: E.tensor_tensor(
        t["lrow"][0:1, 16:32], t["srow"][0:1, 16:32], t["krow"][0:1, 16:32], ALU.mult))
    p.inc("dve", "dv", 1, lambda E: E.tensor_tensor(
        t["lrow"][0:1, 64:80], t["lrow"][0:1, 0:16], t["lrow"][0:1, 16:32], ALU.max))
    p.wait("act", "dv")
    brv = p.inc("act", "ac", 1, lambda E: E.activation(
        t["lrow"][0:1, 128:144], t["lrow"][0:1, 64:80], AF.Sqrt, scale=1.0 / 64.0))
    p.wait("pe", "ac", brv)
    p.wait("pe", "dv")
    bbv = p.inc("pe", "pe", 1, lambda E: E.matmul(
        t["ps_b"][:, 0:16], t["neg_row_f"][:], t["lrow"][0:1, 128:144], start=True, stop=True))
    p.wait("dve", "pe", bbv)
    biasv = p.inc("dve", "dv", 1, lambda E: E.tensor_copy(t["bias_sb"][:], t["ps_b"][:, 0:16]))
    if debug:
        p.wait("sync", "dv", biasv)
        p.inc("sync", "sta", 16, lambda E: E.dma_start(out=dbg["d_bias"], in_=t["bias_sb"][:]))
    ctx1.close()

    ctx2 = ExitStack()
    ent2 = ctx2.enter_context
    scope_barrier(p)
    t["h_bf"] = ent2(nc.sbuf_tensor("h_bf", [128, ICH * 512], BF16, side="right"))
    t["wffn"] = ent2(nc.sbuf_tensor("wffn", [128, 8448], BF16, side="right"))
    gring = Ring(p, "rg", 2, "pool")
    uring = Ring(p, "ru", 2, "pool")
    dring = Ring(p, "rd", 3, "pool")
    wffn = t["wffn"]
    UOFF = 4096
    for tile in range(2):
        gufree = {"dv": p.ct["dv"]}
        p.wait("dve", "pe")
        p.wait("pool", "pe")
        for ich in range(ICH):
            par = ich % 2
            gi, gsl, gkey = gring.load(lambda s: (lambda E, ich=ich, s=s: E.dma_start(
                out=wffn[:, s * 2048:(s + 1) * 2048], in_=gu[ich][:, 0:2048])))
            ui, usl, ukey = uring.load(lambda s: (lambda E, ich=ich, s=s: E.dma_start(
                out=wffn[:, UOFF + s * 2048:UOFF + (s + 1) * 2048], in_=gu[ich][:, 2048:4096])))
            for sem, val in gufree.items():
                p.wait("pe", sem, val)
            p.wait("pe", *gkey)
            p.wait("pe", *ukey)
            for kch in range(KCH):
                p.emit("pe", lambda E, kch=kch, par=par, tile=tile, gsl=gsl: E.matmul(
                    t["ps_a"][:, par * 1024: par * 1024 + 512],
                    wffn[:, gsl * 2048 + kch * 128: gsl * 2048 + (kch + 1) * 128],
                    t["xn_bf"][:, kch * 1024 + tile * 512: kch * 1024 + (tile + 1) * 512],
                    start=(kch == 0), stop=(kch == KCH - 1)))
                mm = lambda E, kch=kch, par=par, tile=tile, usl=usl: E.matmul(
                    t["ps_a"][:, par * 1024 + 512: par * 1024 + 1024],
                    wffn[:, UOFF + usl * 2048 + kch * 128: UOFF + usl * 2048 + (kch + 1) * 128],
                    t["xn_bf"][:, kch * 1024 + tile * 512: kch * 1024 + (tile + 1) * 512],
                    start=(kch == 0), stop=(kch == KCH - 1))
                if kch == KCH - 1:
                    gv = p.inc("pe", "pe", 1, mm)
                else:
                    p.emit("pe", mm)
            gring.consumed(gi, "pe")
            uring.consumed(ui, "pe")
            p.wait("act", "pe", gv)
            p.wait("act", "dv")
            p.inc("act", "ac", 1, lambda E, par=par, ich=ich: E.activation(
                t["tmp1"][:, 512:1024],
                t["ps_a"][:, par * 1024: par * 1024 + 512],
                AF.Silu, scale=t["gs_sb"][:, ich:ich + 1]))
            p.wait("dve", "ac")
            p.inc("dve", "dv", 1, lambda E, par=par, ich=ich: E.tensor_scalar(
                t["tmp0"][:, 0:512], t["ps_a"][:, par * 1024 + 512: par * 1024 + 1024],
                t["us_sb"][:, ich:ich + 1], 0.0, ALU.mult, ALU.add))
            p.inc("dve", "dv", 1, lambda E, par=par, ich=ich: E.tensor_tensor(
                t["h_bf"][:, ich * 512:(ich + 1) * 512], t["tmp0"][:, 0:512],
                t["tmp1"][:, 512:1024], ALU.mult))
            gufree = {"dv": p.ct["dv"]}
        dfree = [{"dv": p.ct["dv"]}, {"dv": p.ct["dv"]}]
        p.wait("pool", "pe")
        for hch in range(16):
            par = hch % 2
            d0, s0, k0 = dring.load(lambda s: (lambda E, hch=hch, s=s: E.dma_start(
                out=wffn[:, s * 2816:(s + 1) * 2816], in_=dn[hch][:, 0:2816])))
            d1, s1, k1 = dring.load(lambda s: (lambda E, hch=hch, s=s: E.dma_start(
                out=wffn[:, s * 2816:(s + 1) * 2816], in_=dn[hch][:, 2816:5632])))
            for sem, val in dfree[par].items():
                p.wait("pe", sem, val)
            if hch == 0:
                p.wait("pe", "dv", gufree["dv"])
            p.wait("pe", *k0)
            for ich in range(22):
                p.emit("pe", lambda E, ich=ich, par=par, s0=s0: E.matmul(
                    t["ps_b"][:, par * 512:(par + 1) * 512],
                    wffn[:, s0 * 2816 + ich * 128: s0 * 2816 + (ich + 1) * 128],
                    t["h_bf"][:, ich * 512:(ich + 1) * 512],
                    start=(ich == 0), stop=False))
            p.wait("pe", *k1)
            for ich in range(22, ICH):
                mm = lambda E, ich=ich, par=par, s1=s1: E.matmul(
                    t["ps_b"][:, par * 512:(par + 1) * 512],
                    wffn[:, s1 * 2816 + (ich - 22) * 128: s1 * 2816 + (ich - 21) * 128],
                    t["h_bf"][:, ich * 512:(ich + 1) * 512],
                    start=False, stop=(ich == ICH - 1))
                if ich == ICH - 1:
                    dv_ = p.inc("pe", "pe", 1, mm)
                else:
                    p.emit("pe", mm)
            dring.consumed(d0, "pe")
            dring.consumed(d1, "pe")
            p.wait("dve", "pe", dv_)
            p.inc("dve", "dv", 1, lambda E, hch=hch, par=par, tile=tile: E.tensor_scalar(
                t["acc"][:, hch * 1024 + tile * 512: hch * 1024 + (tile + 1) * 512],
                t["ps_b"][:, par * 512:(par + 1) * 512],
                t["ds_sb"][:, hch:hch + 1], 0.0, ALU.mult, ALU.add))
            dfree[par] = {"dv": p.ct["dv"]}
    ctx2.close()

    ctx3 = ExitStack()
    ent3 = ctx3.enter_context
    scope_barrier(p)
    t["e12"] = ent3(nc.sbuf_tensor("e12", [128, 2048], BF16, side="right"))
    t["ktb"] = ent3(nc.sbuf_tensor("ktb", [128, 8192], BF16, side="right"))
    t["vtb"] = ent3(nc.sbuf_tensor("vtb", [128, 8192], BF16, side="right"))
    t["r1B"] = ent3(nc.sbuf_tensor("r1B", [128, 512], F32, side="right"))
    t["r2B"] = ent3(nc.sbuf_tensor("r2B", [128, 512], F32, side="right"))
    t["uB"] = ent3(nc.sbuf_tensor("uB", [128, 512], F32, side="right"))
    t["rstdB"] = ent3(nc.sbuf_tensor("rstdB", [128, 512], F32, side="right"))
    t["sqa"] = ent3(nc.sbuf_tensor("sqa", [128, 512], BF16, side="right"))
    p.wait("act", "dv", biasv)
    attn_bf = t["xn_bf"]
    ps_a, ps_b, ps_d1, ps_d2 = t["ps_a"], t["ps_b"], t["ps_d1"], t["ps_d2"]
    kv_use = []
    ac_free = p.ct["ac"]
    dv_free = p.ct["dv"]
    for h in range(16):
        par_h = h % 2
        if h >= 2:
            p.wait("sync", "pe", kv_use[h - 2])
        v1 = p.inc("sync", f"ldk{par_h}", 16, lambda E, h=h, par_h=par_h: E.dma_start(
            out=t["ktb"][:, par_h * 4096:(par_h + 1) * 4096],
            in_=kT[h * 128:(h + 1) * 128, :]))
        v2 = p.inc("sync", f"ldk{par_h}", 16, lambda E, h=h, par_h=par_h: E.dma_start(
            out=t["vtb"].ap()[:, par_h * 4096:(par_h + 1) * 4096].rearrange(
                "p (b d) -> p b d", b=32),
            in_=vf.rearrange("(b q) d -> q b d", q=128)[:, :, h * 128:(h + 1) * 128]))
        for tile in range(2):
            koff = par_h * 4096
            qcol = h * 1024 + tile * 512
            p.wait("pe", f"ldk{par_h}", v2)
            p.wait("pe", "ac", ac_free)
            p.wait("pe", "dv", dv_free)
            for b in range(32):
                par = b % 2
                p.emit("pe", lambda E, b=b, par=par, koff=koff, qcol=qcol: E.matmul(
                    ps_a[:, par * 1024: par * 1024 + 512],
                    t["ktb"][0:64, koff + b * 128: koff + (b + 1) * 128],
                    t["q_bf"][0:64, qcol:qcol + 512],
                    start=True, stop=True, tile_position=(0, 0)))
                sv = p.inc("pe", "pe", 1, lambda E, b=b, par=par, koff=koff, qcol=qcol: E.matmul(
                    ps_a[:, par * 1024 + 512: par * 1024 + 1024],
                    t["ktb"][64:128, koff + b * 128: koff + (b + 1) * 128],
                    t["q_bf"][64:128, qcol:qcol + 512],
                    start=True, stop=True, tile_position=(64, 0)))
                p.wait("act", "pe", sv)
                ev = p.inc("act", "ac", 1, lambda E, par=par, h=h: E.activation(
                    t["e12"][:, par * 1024:(par + 1) * 1024],
                    ps_a[:, par * 1024:(par + 1) * 1024],
                    AF.Exp, bias=t["bias_sb"][:, h:h + 1], scale=0.125))
                p.wait("pe", "ac", ev)
                first, last = (b == 0), (b == 31)
                p.emit("pe", lambda E, b=b, par=par, koff=koff, first=first, last=last: E.matmul(
                    ps_b[:, 0:512],
                    t["vtb"][:, koff + b * 128: koff + (b + 1) * 128],
                    t["e12"][:, par * 1024: par * 1024 + 512],
                    start=first, stop=last))
                p.emit("pe", lambda E, b=b, par=par, koff=koff, first=first, last=last: E.matmul(
                    ps_b[:, 512:1024],
                    t["vtb"][:, koff + b * 128: koff + (b + 1) * 128],
                    t["e12"][:, par * 1024 + 512: par * 1024 + 1024],
                    start=first, stop=last))
                p.emit("pe", lambda E, par=par, first=first, last=last: E.matmul(
                    ps_d1[:], t["ones_col_bf"][:],
                    t["e12"][:, par * 1024: par * 1024 + 512],
                    start=first, stop=last))
                mmD = lambda E, par=par, first=first, last=last: E.matmul(
                    ps_d2[:], t["ones_col_bf"][:],
                    t["e12"][:, par * 1024 + 512: par * 1024 + 1024],
                    start=first, stop=last)
                if last:
                    pvv = p.inc("pe", "pe", 1, mmD)
                else:
                    p.emit("pe", mmD)
            p.wait("act", "pe", pvv)
            p.inc("act", "ac", 1, lambda E: E.activation(t["tmp1"][0:1, 0:512], ps_d1[:], AF.Copy))
            dcp = p.inc("act", "ac", 1, lambda E: E.activation(t["tmp1"][0:1, 512:1024], ps_d2[:], AF.Copy))
            p.wait("pe", "ac", dcp)
            p.emit("pe", lambda E: E.matmul(ps_a[:, 0:512], t["ones_row_f"][:],
                                            t["tmp1"][0:1, 0:512], start=True, stop=True))
            bv = p.inc("pe", "pe", 1, lambda E: E.matmul(
                ps_a[:, 512:1024], t["ones_row_f"][:], t["tmp1"][0:1, 512:1024],
                start=True, stop=True))
            p.wait("act", "pe", bv)
            p.inc("act", "ac", 1, lambda E: E.activation(t["uB"][:], ps_a[:, 0:512], AF.Ln))
            p.inc("act", "ac", 1, lambda E: E.activation(t["r1B"][:], t["uB"][:], AF.Exp, scale=-1.0))
            p.inc("act", "ac", 1, lambda E: E.activation(t["uB"][:], ps_a[:, 512:1024], AF.Ln))
            rv = p.inc("act", "ac", 1, lambda E: E.activation(t["r2B"][:], t["uB"][:], AF.Exp, scale=-1.0))
            p.wait("dve", "ac", rv)
            p.inc("dve", "dv", 1, lambda E: E.tensor_scalar(
                t["r2B"][:], t["r2B"][:], t["lam_vec"][:], 0.0, ALU.mult, ALU.add))
            p.inc("dve", "dv", 1, lambda E: E.tensor_tensor(
                t["tmp0"][:, 0:512], ps_b[:, 0:512], t["r1B"][:], ALU.mult))
            p.inc("dve", "dv", 1, lambda E: E.tensor_tensor(
                t["tmp0"][:, 512:1024], ps_b[:, 512:1024], t["r2B"][:], ALU.mult))
            yv = p.inc("dve", "dv", 1, lambda E: E.tensor_tensor(
                t["tmp0"][:, 0:512], t["tmp0"][:, 0:512], t["tmp0"][:, 512:1024], ALU.subtract))
            p.wait("act", "dv", yv)
            sqv = p.inc("act", "ac", 1, lambda E: E.activation(
                t["sqa"][:], t["tmp0"][:, 0:512], AF.Square))
            p.wait("pe", "ac", sqv)
            msv = p.inc("pe", "pe", 1, lambda E: E.matmul(
                ps_a[:, 1024:1536], t["ones128_bf"][:], t["sqa"][:], start=True, stop=True))
            p.wait("act", "pe", msv)
            p.inc("act", "ac", 1, lambda E: E.activation(
                t["uB"][:], ps_a[:, 1024:1536], AF.Ln, scale=1.0 / 128.0, bias=t["eps_col"][:]))
            rsv = p.inc("act", "ac", 1, lambda E: E.activation(
                t["rstdB"][:], t["uB"][:], AF.Exp, scale=-0.5))
            p.wait("dve", "ac", rsv)
            p.inc("dve", "dv", 1, lambda E: E.tensor_tensor(
                t["tmp1"][:, 0:512], t["tmp0"][:, 0:512], t["rstdB"][:], ALU.mult))
            dvf = p.inc("dve", "dv", 1, lambda E, h=h, tile=tile: E.tensor_scalar(
                attn_bf[:, h * 1024 + tile * 512: h * 1024 + (tile + 1) * 512],
                t["tmp1"][:, 0:512], t["gam_eff"][:], 0.0, ALU.mult, ALU.add))
            ac_free = rsv
            dv_free = dvf
        kv_use.append(p.ct["pe"])
    ctx3.close()

    if debug:
        p.wait("sync", "dv")
        for kch in range(KCH):
            p.inc("sync", "sta", 16, lambda E, kch=kch: E.dma_start(
                out=dbg["d_attn"][kch * 128:(kch + 1) * 128, :],
                in_=attn_bf[:, kch * 1024:(kch + 1) * 1024]))

    ctx4 = ExitStack()
    ent4 = ctx4.enter_context
    scope_barrier(p)
    t["wbuf2"] = ent4(nc.sbuf_tensor("wbuf2", [128, 4096], BF16, side="right"))
    wring = Ring(p, "rw", 2, "pool")
    wofree = [{"ac": ac_free, "dv": dv_free}, {"ac": ac_free, "dv": dv_free}]
    for och in range(16):
        par = och % 2
        wi, wsl, wkey = wring.load(lambda s: (lambda E, och=och, s=s: E.dma_start(
            out=t["wbuf2"][:, s * 2048:(s + 1) * 2048], in_=wo[och])))
        for sem, val in wofree[par].items():
            p.wait("pe", sem, val)
        p.wait("pe", *wkey)
        for kch in range(KCH):
            for half in range(2):
                mm = lambda E, och=och, kch=kch, half=half, wsl=wsl: E.matmul(
                    ps_a[:, (och % 2) * 1024 + half * 512: (och % 2) * 1024 + (half + 1) * 512],
                    t["wbuf2"][:, wsl * 2048 + kch * 128: wsl * 2048 + (kch + 1) * 128],
                    attn_bf[:, kch * 1024 + half * 512: kch * 1024 + (half + 1) * 512],
                    start=(kch == 0), stop=(kch == KCH - 1))
                if kch == KCH - 1 and half == 1:
                    ov = p.inc("pe", "pe", 1, mm)
                else:
                    p.emit("pe", mm)
        wring.consumed(wi, "pe")
        p.wait("dve", "pe", ov)
        fv = p.inc("dve", "dv", 1, lambda E, och=och, par=par: E.tensor_tensor(
            t["acc"][:, och * 1024:(och + 1) * 1024],
            t["acc"][:, och * 1024:(och + 1) * 1024],
            ps_a[:, par * 1024:(par + 1) * 1024], ALU.add))
        wofree[par] = {"dv": fv}
    ctx4.close()

    if debug:
        p.wait("sync", "dv")
        for kch in range(KCH):
            p.inc("sync", "sta", 16, lambda E, kch=kch: E.dma_start(
                out=dbg["d_acc"][kch * 128:(kch + 1) * 128, :],
                in_=t["acc"][:, kch * 1024:(kch + 1) * 1024]))

    ctx5 = ExitStack()
    ent5 = ctx5.enter_context
    scope_barrier(p)
    t["xe"] = ent5(nc.sbuf_tensor("xe", [128, 2048], F32, side="right"))
    t["out_sb"] = ent5(nc.sbuf_tensor("out_sb", [128, 2048], F32, side="right"))
    stv = []
    for och in range(16):
        par = och % 2
        if och >= 2:
            p.wait("sync", f"st{par}", stv[och - 2])
        v = p.inc("sync", f"ldx{par}", 16, lambda E, och=och, par=par: E.dma_start(
            out=t["xe"][:, par * 1024:(par + 1) * 1024],
            in_=xT[och * 128:(och + 1) * 128, :]))
        p.wait("dve", f"ldx{par}", v)
        p.wait("dve", "dv", wofree[par].get("dv", 0))
        ev = p.inc("dve", "dv", 1, lambda E, och=och, par=par: E.tensor_tensor(
            t["out_sb"][:, par * 1024:(par + 1) * 1024],
            t["acc"][:, och * 1024:(och + 1) * 1024],
            t["xe"][:, par * 1024:(par + 1) * 1024], ALU.add))
        p.wait("sync", "dv", ev)
        stv.append(p.inc("sync", f"st{par}", 16, lambda E, och=och, par=par: E.dma_start(
            out=outT[och * 128:(och + 1) * 128, :],
            in_=t["out_sb"][:, par * 1024:(par + 1) * 1024])))
    ctx5.close()
    p.wait("sync", "st0")
    p.wait("sync", "st1")
    if debug:
        p.wait("sync", "sta")
    block = ent(nc.Block())
    attach(p, block)
    ctx.close()
    return nc


def build_F(debug=False):
    nc = bass.Bass("TRN2", target_bir_lowering=False)
    xTb = nc.dram_tensor("xTb", [E, S], F32, kind="ExternalInput").ap()
    wq = nc.dram_tensor("wq", [16, 128, 2048], F32, kind="ExternalInput").ap()
    wk = nc.dram_tensor("wk", [16, 128, 2048], F32, kind="ExternalInput").ap()
    wo = nc.dram_tensor("wo", [16, 128, 2048], F32, kind="ExternalInput").ap()
    wv = nc.dram_tensor("wv", [4, 128, 8192], F32, kind="ExternalInput").ap()
    gu = nc.dram_tensor("gu", [44, 128, 4096], F32, kind="ExternalInput").ap()
    dn = nc.dram_tensor("dn", [16, 128, 5632], F32, kind="ExternalInput").ap()
    gs = nc.dram_tensor("gs", [128, 44], F32, kind="ExternalInput").ap()
    us = nc.dram_tensor("us", [128, 44], F32, kind="ExternalInput").ap()
    ds = nc.dram_tensor("ds", [128, 16], F32, kind="ExternalInput").ap()
    gam = nc.dram_tensor("gam", [128, 1], F32, kind="ExternalInput").ap()
    lq1 = nc.dram_tensor("lq1", [1, 64], F32, kind="ExternalInput").ap()
    lk1 = nc.dram_tensor("lk1", [1, 64], F32, kind="ExternalInput").ap()
    lq2 = nc.dram_tensor("lq2", [1, 64], F32, kind="ExternalInput").ap()
    lk2 = nc.dram_tensor("lk2", [1, 64], F32, kind="ExternalInput").ap()
    ctb = nc.dram_tensor("ctb", [128, S], BF16, kind="ExternalInput").ap()
    stbt = nc.dram_tensor("stbt", [128, S], BF16, kind="ExternalInput").ap()
    outT = nc.dram_tensor("outT", [E, TQ], F32, kind="ExternalOutput").ap()
    kT_s = nc.dram_tensor("kT_s", [E, S], BF16, kind="Internal").ap()
    v_s = nc.dram_tensor("v_s", [S, E], BF16, kind="Internal").ap()
    wk_c = nc.dram_tensor("wk_c", [16, 128, 2048], BF16, kind="Internal").ap()
    wv_c = nc.dram_tensor("wv_c", [4, 128, 8192], BF16, kind="Internal").ap()
    dbg = {}
    if debug:
        for nm, shp, dt in (("d_xn", [E, TQ], BF16), ("d_q", [E, TQ], BF16),
                            ("d_attn", [E, TQ], BF16), ("d_kt", [E, S], BF16),
                            ("d_vs", [S, E], BF16), ("d_srow", [1, 64], F32)):
            dbg[nm] = nc.dram_tensor(nm, shp, dt, kind="ExternalOutput").ap()

    from contextlib import ExitStack
    ctx = ExitStack()
    ent = ctx.enter_context
    t = {}
    t["ps_a"] = ent(nc.psum_tensor("ps_a", [128, 2048], F32))
    t["ps_b"] = ent(nc.psum_tensor("ps_b", [128, 1024], F32))
    t["ps_d1"] = ent(nc.psum_tensor("ps_d1", [1, 512], F32))
    t["ps_d2"] = ent(nc.psum_tensor("ps_d2", [1, 512], F32))
    t["ones_col_bf"] = ent(nc.sbuf_tensor("ones_col_bf", [128, 1], BF16))
    t["ones_row_f"] = ent(nc.sbuf_tensor("ones_row_f", [1, 128], F32))
    t["neg_row_f"] = ent(nc.sbuf_tensor("neg_row_f", [1, 128], F32))
    t["ones128_bf"] = ent(nc.sbuf_tensor("ones128_bf", [128, 128], BF16))
    t["mask1_bf"] = ent(nc.sbuf_tensor("mask1_bf", [128, 1], BF16))
    t["mask2_bf"] = ent(nc.sbuf_tensor("mask2_bf", [128, 1], BF16))
    t["tmp0"] = ent(nc.sbuf_tensor("tmp0", [128, 1024], F32))
    t["tmp1"] = ent(nc.sbuf_tensor("tmp1", [128, 1024], F32))
    t["srow"] = ent(nc.sbuf_tensor("srow", [1, 64], F32))
    t["trow"] = ent(nc.sbuf_tensor("trow", [1, 16], F32))
    t["eps_col"] = ent(nc.sbuf_tensor("eps_col", [128, 1], F32))
    t["nrow"] = ent(nc.sbuf_tensor("nrow", [1, 1024], F32))
    t["mBb"] = ent(nc.sbuf_tensor("mBb", [128, 512], BF16))
    t["rBb"] = ent(nc.sbuf_tensor("rBb", [128, 512], BF16))
    t["bias_sb"] = ent(nc.sbuf_tensor("bias_sb", [128, 16], F32))
    t["lam_vec"] = ent(nc.sbuf_tensor("lam_vec", [128, 1], F32))
    t["gam_eff"] = ent(nc.sbuf_tensor("gam_eff", [128, 1], F32))
    t["gs_sb"] = ent(nc.sbuf_tensor("gs_sb", [128, 44], F32))
    t["us_sb"] = ent(nc.sbuf_tensor("us_sb", [128, 44], F32))
    t["ds_sb"] = ent(nc.sbuf_tensor("ds_sb", [128, 16], F32))
    t["lrow"] = ent(nc.sbuf_tensor("lrow", [1, 256], F32))
    t["xn_bf"] = ent(nc.sbuf_tensor("xn_bf", [128, 16 * 1024], BF16))

    p = Prog(nc, ent)
    emit_init(p, t)
    p.wait("pe", "pl")
    for name, src_ap in (("gs_sb", gs), ("us_sb", us), ("ds_sb", ds)):
        p.inc("sync", "lds", 16, lambda E, src_ap=src_ap, name=name: E.dma_start(
            out=t[name][:], in_=src_ap))
    p.inc("sync", "lds", 16, lambda E: E.dma_start(out=t["lrow"][0:1, 0:64], in_=lq1))
    p.inc("sync", "lds", 16, lambda E: E.dma_start(out=t["lrow"][0:1, 64:128], in_=lk1))
    p.inc("sync", "lds", 16, lambda E: E.dma_start(out=t["lrow"][0:1, 128:192], in_=lq2))
    p.inc("sync", "lds", 16, lambda E: E.dma_start(out=t["lrow"][0:1, 192:256], in_=lk2))
    gamv = p.inc("sync", "lds", 16, lambda E: E.dma_start(out=t["tmp0"][:, 0:1], in_=gam))
    p.wait("dve", "lds", gamv)
    p.wait("pool", "lds", gamv)
    p.wait("act", "lds", gamv)
    p.inc("dve", "dv", 1, lambda E: E.tensor_scalar(
        t["gam_eff"][:], t["tmp0"][:, 0:1], ONE_MINUS_LI, 0.0, ALU.mult, ALU.add))
    p.inc("dve", "dv", 1, lambda E: E.tensor_tensor(
        t["lrow"][0:1, 0:64], t["lrow"][0:1, 0:64], t["lrow"][0:1, 64:128], ALU.mult))
    p.inc("dve", "dv", 1, lambda E: E.tensor_tensor(
        t["lrow"][0:1, 128:192], t["lrow"][0:1, 128:192], t["lrow"][0:1, 192:256], ALU.mult))
    p.inc("dve", "dv", 1, lambda E: E.tensor_reduce(
        t["trow"][0:1, 0:1], t["lrow"][0:1, 0:64], AX.X, ALU.add))
    p.inc("dve", "dv", 1, lambda E: E.tensor_reduce(
        t["trow"][0:1, 1:2], t["lrow"][0:1, 128:192], AX.X, ALU.add))
    p.wait("act", "dv")
    p.inc("act", "ac", 1, lambda E: E.activation(t["trow"][0:1, 2:3], t["trow"][0:1, 0:1], AF.Exp))
    p.inc("act", "ac", 1, lambda E: E.activation(t["trow"][0:1, 3:4], t["trow"][0:1, 1:2], AF.Exp))
    p.wait("dve", "ac")
    p.inc("dve", "dv", 1, lambda E: E.tensor_tensor(
        t["trow"][0:1, 2:3], t["trow"][0:1, 2:3], t["trow"][0:1, 3:4], ALU.subtract))
    lamv = p.inc("dve", "dv", 1, lambda E: E.tensor_scalar(
        t["trow"][0:1, 2:3], t["trow"][0:1, 2:3], 1.0, LAMBDA_INIT, ALU.mult, ALU.add))
    p.wait("pe", "dv", lamv)
    lbv = p.inc("pe", "pe", 1, lambda E: E.matmul(
        t["ps_b"][:, 0:1], t["ones_row_f"][:], t["trow"][0:1, 2:3], start=True, stop=True))
    p.wait("dve", "pe", lbv)
    p.inc("dve", "dv", 1, lambda E: E.tensor_copy(t["lam_vec"][:], t["ps_b"][:, 0:1]))

    ctx_tab = ExitStack()
    entt = ctx_tab.enter_context
    t["ctab"] = entt(nc.sbuf_tensor("ctab", [128, S], BF16, side="right"))
    t["qraw_bf"] = entt(nc.sbuf_tensor("qraw_bf", [128, 1024], BF16, side="right"))
    t["tswp_bf"] = entt(nc.sbuf_tensor("tswp_bf", [128, 1024], BF16, side="right"))
    t["tmpa_bf"] = entt(nc.sbuf_tensor("tmpa_bf", [128, 1024], BF16, side="right"))
    t["stab"] = entt(nc.sbuf_tensor("stab", [128, S], BF16, side="right"))
    t["sq_b"] = entt(nc.sbuf_tensor("sq_b", [128, 1024], BF16, side="right"))
    p.inc("sync", "lds", 16, lambda E: E.dma_start(out=t["ctab"][:], in_=ctb))
    tabv = p.inc("sync", "lds", 16, lambda E: E.dma_start(out=t["stab"][:], in_=stbt))
    p.wait("dve", "lds", tabv)

    ctx_kv = ExitStack()
    entk = ctx_kv.enter_context
    t["xbb"] = entk(nc.sbuf_tensor("xbb", [128, 8192], BF16, side="right"))
    t["x2b"] = entk(nc.sbuf_tensor("x2b", [128, 4096], BF16, side="right"))
    t["row"] = entk(nc.sbuf_tensor("row", [1, 2048], F32, side="right"))
    t["xnb"] = entk(nc.sbuf_tensor("xnb", [128, 16 * 1024], BF16, side="right"))
    t["kstage"] = entk(nc.sbuf_tensor("kstage", [128, 2048], BF16, side="right"))
    t["kraw"] = entk(nc.sbuf_tensor("kraw", [128, 4096], BF16, side="right"))
    t["v_stage"] = entk(nc.sbuf_tensor("v_stage", [128, 4, 2048], BF16, side="right"))
    t["wbuf"] = entk(nc.sbuf_tensor("wbuf", [128, 4096], BF16, side="right"))
    t["wvbuf"] = entk(nc.sbuf_tensor("wvbuf", [128, 12288], BF16, side="right"))
    kwring = Ring(p, "rkw", 2, "pool")
    vwring = Ring(p, "rvw", 3, "pool")
    lnring = Ring(p, "rx", 2, "sync")
    kst_store = []
    ksq_done = []
    kraw_free = {}
    vstv = []

    for chunk in range(4):
        c0 = chunk * 1024
        xn_dst = t["xn_bf"] if chunk == 0 else t["xnb"]
        psfree = emit_ln(p, t, xTb[:, c0:c0 + 1024], xn_dst, 1024, 0, ring=lnring)

        def consume_k(och, o8, chunk=chunk, c0=c0):
            gidx = chunk * 16 + och
            half = och % 2
            if gidx >= 2:
                p.wait("dve", f"st{half}", kst_store[gidx - 2])
                p.wait("dve", "ac", ksq_done[gidx - 2])
            srcfree, wv_ = emit_rope_sb(p, t, t["kraw"][:, o8 * 1024:(o8 + 1) * 1024],
                                        t["ctab"], t["stab"], c0, t["kstage"], half * 1024)
            emit_headnorm(p, t, t["kstage"], half * 1024, och, statbase=0)
            ksq_done.append(p.ct["ac"])
            p.wait("sync", "dv", wv_)
            kst_store.append(p.inc("sync", f"st{half}", 16,
                                   lambda E, och=och, half=half, c0=c0: E.dma_start(
                out=kT_s[och * 128:(och + 1) * 128, c0:c0 + 1024],
                in_=t["kstage"][:, half * 1024:(half + 1) * 1024])))
            return srcfree

        free = [dict(psfree), dict(psfree)]
        vfree = {"dv": psfree.get("dv", 0), "ac": p.ct["ac"]}
        for wave in range(4):
            cpvs = []
            for o8 in range(4):
                och = wave * 4 + o8
                par = och % 2
                if chunk == 0:
                    i, slot, key = kwring.load(lambda s: (lambda E, och=och, s=s: E.dma_start(
                        out=t["wbuf"][:, s * 2048:(s + 1) * 2048], in_=wk[och])))
                    p.wait("pool", *key)
                    wcv = p.inc("pool", f"rkw{slot}", 16, lambda E, och=och, s=slot: E.dma_start(
                        out=wk_c[och], in_=t["wbuf"][:, s * 2048:(s + 1) * 2048]))
                    kwring.done[i][f"rkw{slot}"] = wcv
                else:
                    i, slot, key = kwring.load(lambda s: (lambda E, och=och, s=s: E.dma_start(
                        out=t["wbuf"][:, s * 2048:(s + 1) * 2048], in_=wk_c[och])))
                for sem, val in free[par].items():
                    p.wait("pe", sem, val)
                p.wait("pe", key[0], key[1])
                for kch in range(KCH):
                    for half in range(2):
                        mm = lambda E, och=och, kch=kch, half=half, par=par, slot=slot, xn_dst=xn_dst: E.matmul(
                            t["ps_a"][:, par * 1024 + half * 512: par * 1024 + (half + 1) * 512],
                            t["wbuf"][:, slot * 2048 + kch * 128: slot * 2048 + (kch + 1) * 128],
                            xn_dst[:, kch * 1024 + half * 512: kch * 1024 + (half + 1) * 512],
                            start=(kch == 0), stop=(kch == KCH - 1))
                        if kch == KCH - 1 and half == 1:
                            mmv = p.inc("pe", "pe", 1, mm)
                        else:
                            p.emit("pe", mm)
                kwring.consumed(i, "pe")
                p.wait("act", "pe", mmv)
                if o8 in kraw_free:
                    p.wait("act", "dv", kraw_free[o8])
                cpv = p.inc("act", "ac", 1, lambda E, par=par, o8=o8: E.activation(
                    t["kraw"][:, o8 * 1024:(o8 + 1) * 1024],
                    t["ps_a"][:, par * 1024:(par + 1) * 1024], AF.Copy))
                cpvs.append(cpv)
                free[par] = {"ac": cpv}
            for o8 in range(4):
                och = wave * 4 + o8
                p.wait("dve", "ac", cpvs[o8])
                kraw_free[o8] = consume_k(och, o8)
            for vg in range(2):
                gidx = wave * 2 + vg
                halfq, dvq = gidx // 4, gidx % 4
                vfree = emit_v_group(p, t, wv, wv_c, vwring, xn_dst, halfq, dvq,
                                     t["v_stage"], v_s, c0, vfree,
                                     fill_cache=(chunk == 0), stv=vstv)
    ctx_kv.close()
    if debug:
        p.wait("sync", "dv")
        p.wait("sync", "st0")
        p.wait("sync", "st1")
        p.wait("sync", "sta")
        p.inc("sync", "sta", 16, lambda E: E.dma_start(out=dbg["d_kt"], in_=kT_s))
        p.inc("sync", "sta", 16, lambda E: E.dma_start(out=dbg["d_vs"], in_=v_s))

    ctx_q = ExitStack()
    entq = ctx_q.enter_context
    scope_barrier(p)
    t["wbufq"] = entq(nc.sbuf_tensor("wbufq", [128, 4096], BF16, side="right"))
    t["q_bf"] = ent(nc.sbuf_tensor("q_bf", [128, 16 * 1024], BF16))

    def consume_q(och, mmv):
        fr, wv_ = emit_rope(p, t, t["ps_a"][:, (och % 2) * 1024:(och % 2 + 1) * 1024],
                            t["ctab"], t["stab"], 0, t["q_bf"], och * 1024)
        emit_headnorm(p, t, t["q_bf"], och * 1024, och, statbase=32)
        return {fr[0]: fr[1]}

    qfree = {"pe": p.ct["pe"], "dv": p.ct["dv"], "ac": p.ct["ac"]}
    emit_fmproj(p, t, wq, t["wbufq"], t["xn_bf"], 1024, 0, consume_q, qfree)
    if debug:
        p.wait("sync", "dv")
        for kch in range(KCH):
            p.inc("sync", "sta", 16, lambda E, kch=kch: E.dma_start(
                out=dbg["d_q"][kch * 128:(kch + 1) * 128, :],
                in_=t["q_bf"][:, kch * 1024:(kch + 1) * 1024]))
        p.inc("sync", "sta", 16, lambda E: E.dma_start(out=dbg["d_srow"], in_=t["srow"][:]))

    p.wait("dve", "lds")
    p.wait("dve", "pl")
    p.inc("dve", "dv", 1, lambda E: E.tensor_tensor(
        t["lrow"][0:1, 0:16], t["srow"][0:1, 32:48], t["srow"][0:1, 0:16], ALU.mult))
    p.inc("dve", "dv", 1, lambda E: E.tensor_tensor(
        t["lrow"][0:1, 16:32], t["srow"][0:1, 48:64], t["srow"][0:1, 16:32], ALU.mult))
    p.inc("dve", "dv", 1, lambda E: E.tensor_tensor(
        t["lrow"][0:1, 64:80], t["lrow"][0:1, 0:16], t["lrow"][0:1, 16:32], ALU.max))
    p.wait("act", "dv")
    brv = p.inc("act", "ac", 1, lambda E: E.activation(
        t["lrow"][0:1, 128:144], t["lrow"][0:1, 64:80], AF.Sqrt, scale=1.0 / 64.0))
    p.wait("pe", "ac", brv)
    p.wait("pe", "dv")
    bbv = p.inc("pe", "pe", 1, lambda E: E.matmul(
        t["ps_b"][:, 0:16], t["neg_row_f"][:], t["lrow"][0:1, 128:144], start=True, stop=True))
    p.wait("dve", "pe", bbv)
    biasv = p.inc("dve", "dv", 1, lambda E: E.tensor_copy(t["bias_sb"][:], t["ps_b"][:, 0:16]))
    ctx_q.close()
    ctx_tab.close()

    ctx2 = ExitStack()
    ent2 = ctx2.enter_context
    scope_barrier(p)
    t["acc"] = ent(nc.sbuf_tensor("acc", [128, 16 * 1024], F32))
    t["h_bf"] = ent2(nc.sbuf_tensor("h_bf", [128, 22 * 1024], BF16, side="right"))
    t["wffn"] = ent2(nc.sbuf_tensor("wffn", [128, 8192], BF16, side="right"))
    gring = Ring(p, "rg", 2, "pool")
    dring = Ring(p, "rd", 2, "pool")
    wffn = t["wffn"]
    ps_a, ps_b = t["ps_a"], t["ps_b"]
    for ip in range(2):
        hv_prev = p.ct["dv"]
        p.wait("dve", "pe")
        p.wait("pool", "pe")
        for il in range(22):
            ich = ip * 22 + il
            gi, gsl, gkey = gring.load(lambda s: (lambda E, ich=ich, s=s: E.dma_start(
                out=wffn[:, s * 4096:(s + 1) * 4096], in_=gu[ich])))
            if il % 2 == 0:
                gps = ps_a[:, 0:1024]
                ups = ps_a[:, 1024:2048]
            else:
                gps = ps_b[:, 0:1024]
                ups = ps_a[:, 0:1024]
            p.wait("pe", "dv", hv_prev)
            p.wait("pe", *gkey)
            for kch in range(KCH):
                for tile in range(2):
                    mm = lambda E, kch=kch, tile=tile, gsl=gsl, gps=gps: E.matmul(
                        gps[:, tile * 512:(tile + 1) * 512],
                        wffn[:, gsl * 4096 + kch * 128: gsl * 4096 + (kch + 1) * 128],
                        t["xn_bf"][:, kch * 1024 + tile * 512: kch * 1024 + (tile + 1) * 512],
                        start=(kch == 0), stop=(kch == KCH - 1))
                    p.emit("pe", mm)
            if il % 2 == 1:
                p.wait("pe", "ac")
            for kch in range(KCH):
                for tile in range(2):
                    mm = lambda E, kch=kch, tile=tile, gsl=gsl, ups=ups: E.matmul(
                        ups[:, tile * 512:(tile + 1) * 512],
                        wffn[:, gsl * 4096 + 2048 + kch * 128: gsl * 4096 + 2048 + (kch + 1) * 128],
                        t["xn_bf"][:, kch * 1024 + tile * 512: kch * 1024 + (tile + 1) * 512],
                        start=(kch == 0), stop=(kch == KCH - 1))
                    if kch == KCH - 1 and tile == 1:
                        gv = p.inc("pe", "pe", 1, mm)
                    else:
                        p.emit("pe", mm)
            gring.consumed(gi, "pe")
            p.wait("act", "pe", gv)
            p.wait("act", "dv")
            p.inc("act", "ac", 1, lambda E, ich=ich, gps=gps: E.activation(
                t["tmp1"][:, 0:1024], gps[:], AF.Silu, scale=t["gs_sb"][:, ich:ich + 1]))
            p.wait("dve", "ac")
            p.inc("dve", "dv", 1, lambda E, ich=ich, ups=ups: E.tensor_scalar(
                t["tmp0"][:, 0:1024], ups[:],
                t["us_sb"][:, ich:ich + 1], 0.0, ALU.mult, ALU.add))
            hv = p.inc("dve", "dv", 1, lambda E, il=il: E.tensor_tensor(
                t["h_bf"][:, il * 1024:(il + 1) * 1024], t["tmp0"][:, 0:1024],
                t["tmp1"][:, 0:1024], ALU.mult))
            hv_prev = hv
        dfree = [{"dv": p.ct["dv"]}, {"dv": p.ct["dv"]}]
        p.wait("pool", "pe")
        for hch in range(16):
            par = hch % 2
            di, dsl, dkey = dring.load(lambda s: (lambda E, hch=hch, ip=ip, s=s: E.dma_start(
                out=wffn[:, s * 2816:(s + 1) * 2816], in_=dn[hch][:, ip * 2816:(ip + 1) * 2816])))
            for sem, val in dfree[par].items():
                p.wait("pe", sem, val)
            if hch == 0:
                p.wait("pe", "dv")
            p.wait("pe", *dkey)
            for il in range(22):
                for tile in range(2):
                    mm = lambda E, il=il, tile=tile, par=par, dsl=dsl: E.matmul(
                        ps_b[:, par * 512:(par + 1) * 512] if False else
                        t["ps_a"][:, par * 1024 + tile * 512: par * 1024 + (tile + 1) * 512],
                        wffn[:, dsl * 2816 + il * 128: dsl * 2816 + (il + 1) * 128],
                        t["h_bf"][:, il * 1024 + tile * 512: il * 1024 + (tile + 1) * 512],
                        start=(il == 0), stop=(il == 21))
                    if il == 21 and tile == 1:
                        dv_ = p.inc("pe", "pe", 1, mm)
                    else:
                        p.emit("pe", mm)
            dring.consumed(di, "pe")
            p.wait("dve", "pe", dv_)
            if ip == 0:
                fv = p.inc("dve", "dv", 1, lambda E, hch=hch, par=par: E.tensor_scalar(
                    t["acc"][:, hch * 1024:(hch + 1) * 1024],
                    t["ps_a"][:, par * 1024:(par + 1) * 1024],
                    t["ds_sb"][:, hch:hch + 1], 0.0, ALU.mult, ALU.add))
            else:
                p.inc("dve", "dv", 1, lambda E, hch=hch, par=par: E.tensor_scalar(
                    t["tmp0"][:, 0:1024], t["ps_a"][:, par * 1024:(par + 1) * 1024],
                    t["ds_sb"][:, hch:hch + 1], 0.0, ALU.mult, ALU.add))
                fv = p.inc("dve", "dv", 1, lambda E, hch=hch: E.tensor_tensor(
                    t["acc"][:, hch * 1024:(hch + 1) * 1024],
                    t["acc"][:, hch * 1024:(hch + 1) * 1024],
                    t["tmp0"][:, 0:1024], ALU.add))
            dfree[par] = {"dv": fv}
    ctx2.close()

    ctx3 = ExitStack()
    ent3 = ctx3.enter_context
    scope_barrier(p)
    t["e12"] = ent3(nc.sbuf_tensor("e12", [128, 2048], BF16, side="right"))
    t["ktb"] = ent3(nc.sbuf_tensor("ktb", [128, 8192], BF16, side="right"))
    t["vtb"] = ent3(nc.sbuf_tensor("vtb", [128, 8192], BF16, side="right"))
    t["r1B"] = ent3(nc.sbuf_tensor("r1B", [128, 512], F32, side="right"))
    t["r2B"] = ent3(nc.sbuf_tensor("r2B", [128, 512], F32, side="right"))
    t["uB"] = ent3(nc.sbuf_tensor("uB", [128, 512], F32, side="right"))
    t["rstdB"] = ent3(nc.sbuf_tensor("rstdB", [128, 512], F32, side="right"))
    t["sqa"] = ent3(nc.sbuf_tensor("sqa", [128, 512], BF16, side="right"))
    p.wait("act", "dv", biasv)
    p.wait("sync", "st0")
    p.wait("sync", "st1")
    p.wait("sync", "sta")
    attn_bf = t["xn_bf"]
    ps_a, ps_b, ps_d1, ps_d2 = t["ps_a"], t["ps_b"], t["ps_d1"], t["ps_d2"]
    kv_use = []
    ac_free = p.ct["ac"]
    dv_free = p.ct["dv"]
    for h in range(16):
        par_h = h % 2
        if h >= 2:
            p.wait("sync", "pe", kv_use[h - 2])
        v1 = p.inc("sync", f"ldk{par_h}", 16, lambda E, h=h, par_h=par_h: E.dma_start(
            out=t["ktb"][:, par_h * 4096:(par_h + 1) * 4096],
            in_=kT_s[h * 128:(h + 1) * 128, :]))
        v2 = p.inc("sync", f"ldk{par_h}", 16, lambda E, h=h, par_h=par_h: E.dma_start(
            out=t["vtb"].ap()[:, par_h * 4096:(par_h + 1) * 4096].rearrange(
                "p (b d) -> p b d", b=32),
            in_=v_s.rearrange("(b q) d -> q b d", q=128)[:, :, h * 128:(h + 1) * 128]))
        for tile in range(2):
            koff = par_h * 4096
            qcol = h * 1024 + tile * 512
            p.wait("pe", f"ldk{par_h}", v2)
            p.wait("pe", "ac", ac_free)
            p.wait("pe", "dv", dv_free)

            def emit_S(b):
                par = b % 2
                p.emit("pe", lambda E, b=b, par=par, koff=koff, qcol=qcol: E.matmul(
                    ps_a[:, par * 1024: par * 1024 + 512],
                    t["ktb"][0:64, koff + b * 128: koff + (b + 1) * 128],
                    t["q_bf"][0:64, qcol:qcol + 512],
                    start=True, stop=True, tile_position=(0, 0)))
                return p.inc("pe", "pe", 1, lambda E, b=b, par=par, koff=koff, qcol=qcol: E.matmul(
                    ps_a[:, par * 1024 + 512: par * 1024 + 1024],
                    t["ktb"][64:128, koff + b * 128: koff + (b + 1) * 128],
                    t["q_bf"][64:128, qcol:qcol + 512],
                    start=True, stop=True, tile_position=(64, 0)))

            sv = [None] * 32
            pvv_l = [None] * 32
            sv[0] = emit_S(0)
            sv[1] = emit_S(1)
            for b in range(32):
                par = b % 2
                if b >= 2:
                    p.wait("act", "pe", pvv_l[b - 2])
                p.wait("act", "pe", sv[b])
                ev = p.inc("act", "ac", 1, lambda E, par=par, h=h: E.activation(
                    t["e12"][:, par * 1024:(par + 1) * 1024],
                    ps_a[:, par * 1024:(par + 1) * 1024],
                    AF.Exp, bias=t["bias_sb"][:, h:h + 1], scale=0.125))
                p.wait("pe", "ac", ev)
                first, last = (b == 0), (b == 31)
                p.emit("pe", lambda E, b=b, par=par, koff=koff, first=first, last=last: E.matmul(
                    ps_b[:, 0:512],
                    t["vtb"][:, koff + b * 128: koff + (b + 1) * 128],
                    t["e12"][:, par * 1024: par * 1024 + 512],
                    start=first, stop=last))
                p.emit("pe", lambda E, b=b, par=par, koff=koff, first=first, last=last: E.matmul(
                    ps_b[:, 512:1024],
                    t["vtb"][:, koff + b * 128: koff + (b + 1) * 128],
                    t["e12"][:, par * 1024 + 512: par * 1024 + 1024],
                    start=first, stop=last))
                p.emit("pe", lambda E, par=par, first=first, last=last: E.matmul(
                    ps_d1[:], t["ones_col_bf"][:],
                    t["e12"][:, par * 1024: par * 1024 + 512],
                    start=first, stop=last))
                mmD = lambda E, par=par, first=first, last=last: E.matmul(
                    ps_d2[:], t["ones_col_bf"][:],
                    t["e12"][:, par * 1024 + 512: par * 1024 + 1024],
                    start=first, stop=last)
                pvv_l[b] = p.inc("pe", "pe", 1, mmD)
                if b + 2 < 32:
                    sv[b + 2] = emit_S(b + 2)
            pvv = pvv_l[31]
            p.wait("act", "pe", pvv)
            p.inc("act", "ac", 1, lambda E: E.activation(t["tmp1"][0:1, 0:512], ps_d1[:], AF.Copy))
            dcp = p.inc("act", "ac", 1, lambda E: E.activation(t["tmp1"][0:1, 512:1024], ps_d2[:], AF.Copy))
            p.wait("pe", "ac", dcp)
            p.emit("pe", lambda E: E.matmul(ps_a[:, 0:512], t["ones_row_f"][:],
                                            t["tmp1"][0:1, 0:512], start=True, stop=True))
            bv = p.inc("pe", "pe", 1, lambda E: E.matmul(
                ps_a[:, 512:1024], t["ones_row_f"][:], t["tmp1"][0:1, 512:1024],
                start=True, stop=True))
            p.wait("act", "pe", bv)
            p.inc("act", "ac", 1, lambda E: E.activation(t["uB"][:], ps_a[:, 0:512], AF.Ln))
            p.inc("act", "ac", 1, lambda E: E.activation(t["r1B"][:], t["uB"][:], AF.Exp, scale=-1.0))
            p.inc("act", "ac", 1, lambda E: E.activation(t["uB"][:], ps_a[:, 512:1024], AF.Ln))
            rv = p.inc("act", "ac", 1, lambda E: E.activation(t["r2B"][:], t["uB"][:], AF.Exp, scale=-1.0))
            p.wait("dve", "ac", rv)
            p.inc("dve", "dv", 1, lambda E: E.tensor_scalar(
                t["r2B"][:], t["r2B"][:], t["lam_vec"][:], 0.0, ALU.mult, ALU.add))
            p.inc("dve", "dv", 1, lambda E: E.tensor_tensor(
                t["tmp0"][:, 0:512], ps_b[:, 0:512], t["r1B"][:], ALU.mult))
            p.inc("dve", "dv", 1, lambda E: E.tensor_tensor(
                t["tmp0"][:, 512:1024], ps_b[:, 512:1024], t["r2B"][:], ALU.mult))
            yv = p.inc("dve", "dv", 1, lambda E: E.tensor_tensor(
                t["tmp0"][:, 0:512], t["tmp0"][:, 0:512], t["tmp0"][:, 512:1024], ALU.subtract))
            p.wait("act", "dv", yv)
            sqv = p.inc("act", "ac", 1, lambda E: E.activation(
                t["sqa"][:], t["tmp0"][:, 0:512], AF.Square))
            p.wait("pe", "ac", sqv)
            msv = p.inc("pe", "pe", 1, lambda E: E.matmul(
                ps_a[:, 1024:1536], t["ones128_bf"][:], t["sqa"][:], start=True, stop=True))
            p.wait("act", "pe", msv)
            p.inc("act", "ac", 1, lambda E: E.activation(
                t["uB"][:], ps_a[:, 1024:1536], AF.Ln, scale=1.0 / 128.0, bias=t["eps_col"][:]))
            rsv = p.inc("act", "ac", 1, lambda E: E.activation(
                t["rstdB"][:], t["uB"][:], AF.Exp, scale=-0.5))
            p.wait("dve", "ac", rsv)
            p.inc("dve", "dv", 1, lambda E: E.tensor_tensor(
                t["tmp1"][:, 0:512], t["tmp0"][:, 0:512], t["rstdB"][:], ALU.mult))
            dvf = p.inc("dve", "dv", 1, lambda E, h=h, tile=tile: E.tensor_scalar(
                attn_bf[:, h * 1024 + tile * 512: h * 1024 + (tile + 1) * 512],
                t["tmp1"][:, 0:512], t["gam_eff"][:], 0.0, ALU.mult, ALU.add))
            ac_free = rsv
            dv_free = dvf
        kv_use.append(p.ct["pe"])
    ctx3.close()
    if debug:
        p.wait("sync", "dv")
        for kch in range(KCH):
            p.inc("sync", "sta", 16, lambda E, kch=kch: E.dma_start(
                out=dbg["d_attn"][kch * 128:(kch + 1) * 128, :],
                in_=attn_bf[:, kch * 1024:(kch + 1) * 1024]))

    ctx4 = ExitStack()
    ent4 = ctx4.enter_context
    scope_barrier(p)
    t["wbuf2"] = ent4(nc.sbuf_tensor("wbuf2", [128, 4096], BF16, side="right"))
    t["xe"] = ent4(nc.sbuf_tensor("xe", [128, 2048], F32, side="right"))
    t["out_sb"] = ent4(nc.sbuf_tensor("out_sb", [128, 2048], F32, side="right"))
    wring = Ring(p, "rw", 2, "pool")
    wofree = [{"ac": ac_free, "dv": dv_free}, {"ac": ac_free, "dv": dv_free}]
    stv = []
    for och in range(16):
        par = och % 2
        wi, wsl, wkey = wring.load(lambda s: (lambda E, och=och, s=s: E.dma_start(
            out=t["wbuf2"][:, s * 2048:(s + 1) * 2048], in_=wo[och])))
        if och >= 2:
            p.wait("sync", f"st{par}", stv[och - 2])
        xv = p.inc("sync", f"ldx{par}", 16, lambda E, och=och, par=par: E.dma_start(
            out=t["xe"][:, par * 1024:(par + 1) * 1024],
            in_=xTb[och * 128:(och + 1) * 128, 0:1024]))
        for sem, val in wofree[par].items():
            p.wait("pe", sem, val)
        p.wait("pe", *wkey)
        for kch in range(KCH):
            for half in range(2):
                mm = lambda E, och=och, kch=kch, half=half, wsl=wsl: E.matmul(
                    ps_a[:, (och % 2) * 1024 + half * 512: (och % 2) * 1024 + (half + 1) * 512],
                    t["wbuf2"][:, wsl * 2048 + kch * 128: wsl * 2048 + (kch + 1) * 128],
                    attn_bf[:, kch * 1024 + half * 512: kch * 1024 + (half + 1) * 512],
                    start=(kch == 0), stop=(kch == KCH - 1))
                if kch == KCH - 1 and half == 1:
                    ov = p.inc("pe", "pe", 1, mm)
                else:
                    p.emit("pe", mm)
        wring.consumed(wi, "pe")
        p.wait("dve", "pe", ov)
        p.wait("dve", f"ldx{par}", xv)
        if och >= 2:
            p.wait("dve", f"st{par}", stv[och - 2])
        p.inc("dve", "dv", 1, lambda E, och=och, par=par: E.tensor_tensor(
            t["out_sb"][:, par * 1024:(par + 1) * 1024],
            t["acc"][:, och * 1024:(och + 1) * 1024],
            ps_a[:, par * 1024:(par + 1) * 1024], ALU.add))
        fv = p.inc("dve", "dv", 1, lambda E, och=och, par=par: E.tensor_tensor(
            t["out_sb"][:, par * 1024:(par + 1) * 1024],
            t["out_sb"][:, par * 1024:(par + 1) * 1024],
            t["xe"][:, par * 1024:(par + 1) * 1024], ALU.add))
        p.wait("sync", "dv", fv)
        stv.append(p.inc("sync", f"st{par}", 16, lambda E, och=och, par=par: E.dma_start(
            out=outT[och * 128:(och + 1) * 128, :],
            in_=t["out_sb"][:, par * 1024:(par + 1) * 1024])))
        wofree[par] = {"dv": fv}
    ctx4.close()
    p.wait("sync", "st0")
    p.wait("sync", "st1")
    if debug:
        p.wait("sync", "sta")
    block = ent(nc.Block())
    attach(p, block)
    ctx.close()
    return nc


def emit_vproj_cache(p, t, wv_hbm, wv_cache, ring, xn_bf, xn_seg, xn_col0,
                     v_stage, v_out, row0, psfree0, fill_cache):
    ps_a = t["ps_a"]

    def vps(j):
        return ps_a[:, j * 512:(j + 1) * 512]

    free = dict(psfree0)
    stv = []
    p.wait("act", "sta")
    for halfq in range(2):
        for dvq in range(4):
            slots = []
            for wh in range(2):
                if fill_cache and halfq == 0:
                    i, slot, key = ring.load(lambda s: (lambda E, dvq=dvq, wh=wh, s=s: E.dma_start(
                        out=t["wvbuf"][:, s * 4096:(s + 1) * 4096],
                        in_=wv_hbm[dvq][:, wh * 4096:(wh + 1) * 4096])))
                    p.wait("pool", *key)
                    wcv = p.inc("pool", f"rvw{slot}", 16, lambda E, dvq=dvq, wh=wh, s=slot: E.dma_start(
                        out=wv_cache[dvq][:, wh * 4096:(wh + 1) * 4096],
                        in_=t["wvbuf"][:, s * 4096:(s + 1) * 4096]))
                    ring.done[i][f"rvw{slot}"] = wcv
                else:
                    i, slot, key = ring.load(lambda s: (lambda E, dvq=dvq, wh=wh, s=s: E.dma_start(
                        out=t["wvbuf"][:, s * 4096:(s + 1) * 4096],
                        in_=wv_cache[dvq][:, wh * 4096:(wh + 1) * 4096])))
                slots.append((i, slot, key))
            for sem, val in free.items():
                p.wait("pe", sem, val)
            for wh in range(2):
                i, slot, key = slots[wh]
                p.wait("pe", key[0], key[1])
                for kch in range(wh * 8, wh * 8 + 8):
                    for tc in range(4):
                        tok = xn_col0 + halfq * 512 + tc * 128
                        mm = lambda E, kch=kch, tc=tc, tok=tok, slot=slot: E.matmul(
                            vps(tc),
                            xn_bf[:, kch * xn_seg + tok: kch * xn_seg + tok + 128],
                            t["wvbuf"][:, slot * 4096 + (kch % 8) * 512:
                                       slot * 4096 + (kch % 8 + 1) * 512],
                            start=(kch == 0), stop=(kch == KCH - 1))
                        if kch == wh * 8 + 7 and tc == 3:
                            p.inc("pe", "pe", 1, mm)
                        else:
                            p.emit("pe", mm)
                ring.consumed(i, "pe")
            p.wait("act", "pe")
            for tc in range(4):
                p.inc("act", "ac", 1, lambda E, tc=tc, dvq=dvq: E.activation(
                    v_stage[:, tc, dvq * 512:(dvq + 1) * 512], vps(tc), AF.Copy))
            free = {"ac": p.ct["ac"]}
        p.wait("sync", "ac")
        for tc in range(4):
            r0 = row0 + halfq * 512 + tc * 128
            stv.append(p.inc("sync", "sta", 16, lambda E, tc=tc, r0=r0: E.dma_start(
                out=v_out[r0:r0 + 128, :], in_=v_stage[:, tc, :])))
        if halfq == 0:
            p.wait("act", "sta", stv[-1])
    return free


def _f32(a):
    return np.ascontiguousarray(np.asarray(a, dtype=np.float32))


def chunk_wT(W):
    return np.ascontiguousarray(
        _f32(W).reshape(16, 128, 16, 128).transpose(0, 3, 2, 1)).reshape(16, 128, 2048)


def rope_tables():
    angle = 1.0 / 10000 ** np.linspace(0.0, 1.0, 32)
    freqs = np.arange(S, dtype=np.float64)[:, None] * angle
    cos, sin = np.cos(freqs), np.sin(freqs)
    Cc = np.empty((64, S), np.float64)
    Sc = np.empty((64, S), np.float64)
    for i in range(32):
        Cc[2 * i] = cos[:, i]
        Cc[2 * i + 1] = cos[:, i]
        Sc[2 * i] = -sin[:, i]
        Sc[2 * i + 1] = sin[:, i]
    C = np.vstack([Cc, Cc]).astype(NPBF)
    Sg = np.vstack([Sc, Sc]).astype(NPBF)
    return np.ascontiguousarray(C), np.ascontiguousarray(Sg)


MODE = "F"


def kernel(L0_attn_x, L0_attn_Wq, L0_attn_Wk, L0_attn_Wv, L0_attn_Wo,
           L0_attn_lambda_q1, L0_attn_lambda_k1, L0_attn_lambda_q2, L0_attn_lambda_k2,
           L0_attn_gamma, L0_ffn_gate_w, L0_ffn_up_w, L0_ffn_down_w,
           L0_ffn_gate_s, L0_ffn_up_s, L0_ffn_down_s):
    x = _f32(L0_attn_x)
    wq_dev = chunk_wT(L0_attn_Wq)
    wk_dev = chunk_wT(L0_attn_Wk)
    wo_dev = chunk_wT(L0_attn_Wo)
    wv_dev = np.ascontiguousarray(
        _f32(L0_attn_Wv).reshape(4, 512, 16, 128).transpose(0, 3, 2, 1)).reshape(4, 128, 8192)
    g4 = _f32(L0_ffn_gate_w).reshape(44, 128, 16, 128).transpose(0, 3, 2, 1)
    u4 = _f32(L0_ffn_up_w).reshape(44, 128, 16, 128).transpose(0, 3, 2, 1)
    gu_dev = np.ascontiguousarray(
        np.stack([g4, u4], axis=1).transpose(0, 2, 1, 3, 4)).reshape(44, 128, 4096)
    dn_dev = np.ascontiguousarray(
        _f32(L0_ffn_down_w).reshape(16, 128, 44, 128).transpose(0, 3, 2, 1)).reshape(16, 128, 5632)
    gs_dev = np.ascontiguousarray(_f32(L0_ffn_gate_s).reshape(44, 128).T)
    us_dev = np.ascontiguousarray(_f32(L0_ffn_up_s).reshape(44, 128).T)
    ds_dev = np.ascontiguousarray(_f32(L0_ffn_down_s).reshape(16, 128).T)
    gam_dev = _f32(L0_attn_gamma).reshape(128, 1)
    lq1 = _f32(L0_attn_lambda_q1).reshape(1, 64)
    lk1 = _f32(L0_attn_lambda_k1).reshape(1, 64)
    lq2 = _f32(L0_attn_lambda_q2).reshape(1, 64)
    lk2 = _f32(L0_attn_lambda_k2).reshape(1, 64)
    Ct, St = rope_tables()
    common = {"wq": wq_dev, "wo": wo_dev, "gu": gu_dev, "dn": dn_dev,
              "gs": gs_dev, "us": us_dev, "ds": ds_dev, "gam": gam_dev,
              "lq1": lq1, "lk1": lk1, "lq2": lq2, "lk2": lk2}

    if MODE == "F":
        ncF = build_F()
        in_F = []
        for c in range(8):
            b, j = c // 4, c % 4
            order = [j] + [i for i in range(4) if i != j]
            xTb = np.ascontiguousarray(np.concatenate(
                [x[b, 1024 * i:1024 * (i + 1), :] for i in order], axis=0).T)
            ctb = np.ascontiguousarray(np.concatenate(
                [Ct[:, 1024 * i:1024 * (i + 1)] for i in order], axis=1))
            stbt = np.ascontiguousarray(np.concatenate(
                [St[:, 1024 * i:1024 * (i + 1)] for i in order], axis=1))
            m = dict(common)
            m.update({"xTb": xTb, "wk": wk_dev, "wv": wv_dev,
                      "ctb": ctb, "stbt": stbt})
            in_F.append(m)
        res = run_bass_kernel_spmd(ncF, in_F, core_ids=list(range(8)))
        out = np.empty((2, S, E), np.float32)
        for c in range(8):
            b, r0 = c // 4, (c % 4) * TQ
            out[b, r0:r0 + TQ, :] = np.asarray(res.results[c]["outT"]).T
        return out

    xT = []
    for c in range(8):
        b, r0 = c // 4, (c % 4) * TQ
        xT.append(np.ascontiguousarray(x[b, r0:r0 + TQ, :].T))
    ncA = build_A()
    in_A = []
    for c in range(8):
        r0 = (c % 4) * TQ
        in_A.append({"xT": xT[c], "wk": wk_dev, "wv": wv_dev,
                     "ct": np.ascontiguousarray(Ct[:, r0:r0 + TQ]),
                     "stb": np.ascontiguousarray(St[:, r0:r0 + TQ])})
    resA = run_bass_kernel_spmd(ncA, in_A, core_ids=list(range(8)))
    kT_full, v_full, ks4 = {}, {}, {}
    for g in range(2):
        cs = [4 * g + i for i in range(4)]
        kT_full[g] = np.ascontiguousarray(
            np.concatenate([np.asarray(resA.results[c]["kT_out"]) for c in cs], axis=1))
        v_full[g] = np.ascontiguousarray(
            np.concatenate([np.asarray(resA.results[c]["v_out"]) for c in cs], axis=0))
        ks4[g] = np.ascontiguousarray(
            np.concatenate([np.asarray(resA.results[c]["kstat_out"]) for c in cs], axis=0))
    ncB = build_B()
    in_B = []
    for c in range(8):
        g, r0 = c // 4, (c % 4) * TQ
        m = dict(common)
        m.update({"xT": xT[c], "kT": kT_full[g], "vf": v_full[g], "ks": ks4[g],
                  "ct": np.ascontiguousarray(Ct[:, r0:r0 + TQ]),
                  "stb": np.ascontiguousarray(St[:, r0:r0 + TQ])})
        in_B.append(m)
    resB = run_bass_kernel_spmd(ncB, in_B, core_ids=list(range(8)))
    out = np.empty((2, S, E), np.float32)
    for c in range(8):
        b, r0 = c // 4, (c % 4) * TQ
        out[b, r0:r0 + TQ, :] = np.asarray(resB.results[c]["outT"]).T
    return out
